# version 13
# baseline (speedup 1.0000x reference)
import contextlib
import numpy as np
import concourse.bass as bass
import concourse.mybir as mybir
from concourse.bass_utils import run_bass_kernel_spmd

F32 = mybir.dt.float32
F16 = mybir.dt.float16
BF16 = mybir.dt.bfloat16
AF = mybir.ActivationFunctionType
ALU = mybir.AluOpType
AX = mybir.AxisListType

P = 128
S = 2048
D = 1024
DH = D // P
NST = 16
DCONV = 4
INNER = 4096
IH = INNER // P
TT = 512
NCH = S // TT
SUB = TT // P
NTOK = S // P
LN_EPS = 1e-5
RB = 1536.0


class Sem:
    def __init__(self, h):
        self.h = h
        self.n = 0

    def tick(self, k=1):
        self.n += k
        return self.n


class KB:
    ENGS = ("sync", "tensor", "vector", "scalar", "gpsimd")

    def __init__(self, nc):
        self.nc = nc
        self.q = {e: [] for e in self.ENGS}

    def emit(self, eng, fn, wait=(), inc=()):
        wait = tuple((s.h, v) for s, v in wait)
        inc = tuple((s.h, k) for s, k in inc)

        def g(h):
            for sh, v in wait:
                h.wait_ge(sh, v)
            inst = fn(h)
            for sh, k in inc:
                inst.then_inc(sh, k)
        self.q[eng].append(g)

    def run(self, block):
        for eng in self.ENGS:
            fns = self.q[eng]
            getattr(block, eng)(lambda h, fns=fns: [f(h) for f in fns])


def build_core(debug_scratch=False):
    nc = bass.Bass("TRN2", target_bir_lowering=False, detect_race_conditions=False)

    x_h = nc.dram_tensor("x", [S, D], F32, kind="ExternalInput").ap()
    conv_w_h = nc.dram_tensor("conv_w", [D, DCONV], F32, kind="ExternalInput").ap()
    conv_b_h = nc.dram_tensor("conv_b", [D], F32, kind="ExternalInput").ap()
    A_h = nc.dram_tensor("A", [D, NST], F32, kind="ExternalInput").ap()
    BpT_h = nc.dram_tensor("BpT", [D, NST], F32, kind="ExternalInput").ap()
    Bpb_h = nc.dram_tensor("Bpb", [NST], F32, kind="ExternalInput").ap()
    CpT_h = nc.dram_tensor("CpT", [D, NST], F32, kind="ExternalInput").ap()
    Cpb_h = nc.dram_tensor("Cpb", [NST], F32, kind="ExternalInput").ap()
    Dd_h = nc.dram_tensor("Dd", [D], F32, kind="ExternalInput").ap()
    gwT_h = nc.dram_tensor("gwT", [D, D], F32, kind="ExternalInput").ap()
    gb_h = nc.dram_tensor("gb", [D], F32, kind="ExternalInput").ap()
    owT_h = nc.dram_tensor("owT", [D, D], F32, kind="ExternalInput").ap()
    ob_h = nc.dram_tensor("ob", [D], F32, kind="ExternalInput").ap()
    lng_h = nc.dram_tensor("lng", [D], F32, kind="ExternalInput").ap()
    lnb_h = nc.dram_tensor("lnb", [D], F32, kind="ExternalInput").ap()
    wgT_h = nc.dram_tensor("wgT", [D, INNER], F32, kind="ExternalInput").ap()
    wuT_h = nc.dram_tensor("wuT", [D, INNER], F32, kind="ExternalInput").ap()
    wdT_h = nc.dram_tensor("wdT", [INNER, D], F32, kind="ExternalInput").ap()
    out_h = nc.dram_tensor("out", [S, D], F32, kind="ExternalOutput").ap()
    _scr = "ExternalOutput" if debug_scratch else "Internal"
    x2_d = nc.dram_tensor("x2_scr", [S, D], F32, kind=_scr).ap()
    xq_d = nc.dram_tensor("xq_scr", [S, D], BF16, kind=_scr).ap()
    gq_d = nc.dram_tensor("gq_scr", [S, INNER], BF16, kind=_scr).ap()
    if debug_scratch:
        dbg_st = nc.dram_tensor("dbg_st", [P, 40], F32, kind=_scr).ap()
        dbg_wgq = nc.dram_tensor("dbg_wgq", [P, DH * INNER], BF16, kind=_scr).ap()
        dbg_xqc = nc.dram_tensor("dbg_xqc", [P, DH * P], BF16, kind=_scr).ap()
        dbg_gu = nc.dram_tensor("dbg_gu", [P, INNER], F32, kind=_scr).ap()
        dbg_cols = nc.dram_tensor("dbg_cols", [P, 64], F32, kind=_scr).ap()
        dbg_wdq = nc.dram_tensor("dbg_wdq", [P, IH * D], BF16, kind=_scr).ap()
        dbg_u = nc.dram_tensor("dbg_u", [P, DH * TT], F32, kind=_scr).ap()
        dbg_bt = nc.dram_tensor("dbg_bt", [NST, 2 * TT], F32, kind=_scr).ap()
        dbg_y = nc.dram_tensor("dbg_y", [P, DH * TT], F32, kind=_scr).ap()
        dbg_g = nc.dram_tensor("dbg_g", [P, DH * TT], F32, kind=_scr).ap()
        dbg_z = nc.dram_tensor("dbg_z", [P, DH * TT], F32, kind=_scr).ap()
        dbg_e = nc.dram_tensor("dbg_e", [P, 40], F32, kind=_scr).ap()

    kb = KB(nc)
    top = contextlib.ExitStack()

    def mk(es_):
        sbf = lambda n, shp, dt=F32: es_.enter_context(nc.sbuf_tensor(n, shp, dt))
        return sbf
    sb = mk(top)
    sem = lambda n: Sem(top.enter_context(nc.semaphore(n)))

    sIN = sem("sIN")
    sSP = sem("sSP")
    sXA = sem("sXA")
    sXB = sem("sXB")
    sW0 = sem("sW0")
    sW1 = sem("sW1")
    sWG = sem("sWG")
    sPRM = sem("sPRM")
    sXQ = sem("sXQ")
    sGQT = sem("sGQT")
    sX2 = sem("sX2")
    sX2S = sem("sX2S")
    sXQS = sem("sXQS")
    sGQS = sem("sGQS")
    sOUT = sem("sOUT")
    sT = sem("sT")
    sV = sem("sV")
    sS = sem("sS")
    sG = sem("sG")
    dmo = sem("dmo")

    pes = contextlib.ExitStack()
    top.enter_context(pes)
    ptr0 = pes.enter_context(nc.psum_tensor("ptr0", [P, TT], F32))
    ptr1 = pes.enter_context(nc.psum_tensor("ptr1", [P, TT], F32))
    pA = pes.enter_context(nc.psum_tensor("pA", [P, TT], F32))
    pB = pes.enter_context(nc.psum_tensor("pB", [P, TT], F32))
    pC = pes.enter_context(nc.psum_tensor("pC", [P, TT], F32))
    pD = pes.enter_context(nc.psum_tensor("pD", [P, TT], F32))
    pbt = pes.enter_context(nc.psum_tensor("pbt", [NST, 2 * TT], F32))

    def mm_group(plist, wait, inc):
        n = len(plist)

        def fn(t):
            inst = None
            for i, (o, l, r) in enumerate(plist):
                inst = t.matmul(o, l, r, start=(i == 0),
                                stop=(i % 4 == 3 or i == n - 1),
                                skip_group_check=(i >= 4))
            return inst
        kb.emit("tensor", fn, wait=wait, inc=inc)

    def vfill(v):
        v.tensor_scalar(out=jnk[:, 0:1], in0=onec[:], scalar1=1.0,
                        scalar2=None, op0=ALU.mult)
        v.tensor_scalar(out=jnk[:, 1:2], in0=onec[:], scalar1=1.0,
                        scalar2=None, op0=ALU.mult)

    def dma(s_, eng, out_ap, in_ap, wait=(), **kw):
        v = s_.tick(16)
        kb.emit(eng, lambda e, o=out_ap, i=in_ap, kw=kw: e.dma_start(out=o, in_=i, **kw),
                wait=wait, inc=((s_, 16),))
        return v

    sqs = sb("sqs", [P, D])
    lncol = sb("lncol", [P, 16])
    colt = sb("colt", [P, 8])
    onec = sb("onec", [P, 1])
    jnk = sb("jnk", [P, 2])
    onerow = sb("onerow", [1, P])
    epsc = sb("epsc", [P, 1])
    idf = sb("idf", [P, P])
    wstat = sb("wstat", [P, 96])
    swg, swu, swd = sb("swg", [P, 1]), sb("swu", [P, 1]), sb("swd", [P, 1])
    cwg, cwu, cwd = sb("cwg", [P, 1]), sb("cwu", [P, 1]), sb("cwd", [P, 1])
    dsx = sb("dsx", [P, NTOK])
    dsg = sb("dsg", [P, NTOK])
    h16 = sb("h16", [P, D], F16)
    dbgc = sb("dbgc", [P, 40]) if debug_scratch else None

    kb.emit("gpsimd", lambda g: g.iota(idf[:], [[1, P]], channel_multiplier=-1,
                                       allow_small_or_imprecise_dtypes=True))
    kb.emit("gpsimd", lambda g: g.memset(onec[:], 1.0))
    kb.emit("gpsimd", lambda g: g.memset(onerow[:], 1.0))
    kb.emit("gpsimd", lambda g: g.memset(epsc[:], LN_EPS), inc=((sG, sG.tick() and 1),))
    g_tab0 = sG.n
    kb.emit("vector", lambda v: v.tensor_scalar(out=idf[:], in0=idf[:], scalar1=0.0,
                                                scalar2=None, op0=ALU.is_equal),
            wait=((sG, g_tab0),), inc=((sV, sV.tick() and 1),))
    v_tab = sV.n


    p2 = contextlib.ExitStack()
    sb2 = mk(p2)
    aexp = sb2("aexp", [P, DH * NST])
    cw = sb2("cw", [P, DH * DCONV])
    cb = sb2("cb", [P, DH])
    dD = sb2("dD", [P, DH])
    gbv = sb2("gbv", [P, DH])
    obv = sb2("obv", [P, DH])
    bpb = sb2("bpb", [NST, 1])
    cpb = sb2("cpb", [NST, 1])
    lng_bc = sb2("lng_bc", [P, D])
    lnb_bc = sb2("lnb_bc", [P, D])
    oneh = sb2("oneh", [NST, NST * P])
    cmb = sb2("cmb", [P, 6 * TT])
    hstate = sb2("hstate", [P, NST * DH])
    gwT = sb2("gwT_s", [P, DH * D])
    owT = sb2("owT_s", [P, DH * D])
    bpT = sb2("bpT", [P, DH * NST])
    cpT = sb2("cpT", [P, DH * NST])
    LF = TT + 8
    ln1F = sb2("ln1F", [P, DH * LF])
    ut = sb2("ut", [P, DH * TT])
    ub = sb2("ub", [P, DH * TT])
    hh = sb2("hh", [P, DH * TT])
    yy = sb2("yy", [P, DH * TT])
    gzF = sb2("gzF", [P, DH * TT])
    btT = sb2("btT", [NST, TT])
    ctT = sb2("ctT", [NST, TT])
    etok = sb2("etok", [P, D])
    lncol2 = sb2("lncol2", [P, 16])

    ln1F3 = ln1F[:].rearrange("p (h t) -> p h t", h=DH)
    ut3 = ut[:].rearrange("p (h t) -> p h t", h=DH)
    ub3 = ub[:].rearrange("p (h t) -> p h t", h=DH)
    hh3 = hh[:].rearrange("p (h t) -> p h t", h=DH)
    yy3 = yy[:].rearrange("p (h t) -> p h t", h=DH)
    cmb3 = cmb[:].rearrange("p (h t) -> p h t", h=6)
    g_yb = [0]
    gsl = [0]
    gzF3 = gzF[:].rearrange("p (h t) -> p h t", h=DH)
    bpT3 = bpT[:].rearrange("p (h n) -> p h n", h=DH)
    cpT3 = cpT[:].rearrange("p (h n) -> p h n", h=DH)

    wdT_blk = wdT_h.rearrange("(b p q) d -> b p (q d)", b=8, p=P)
    stat_blocks = []
    for mi, wsrc in enumerate([wgT_h, wuT_h, wdT_h]):
        for b in range(8):
            for q in range(4):
                if mi < 2:
                    ap_ = wsrc[b * P:(b + 1) * P, q * D:(q + 1) * D]
                else:
                    ap_ = wdT_blk[b][:, q * D:(q + 1) * D]
                stat_blocks.append((ap_, mi * 32 + b * 4 + q))
    stat_abs_ticks = []
    stats_gate = [0]

    def emit_stats_boundary(tc, g_free):
        slot_sems = (sW0, sW1, sWG)
        for k in range(24):
            blk = tc * 24 + k
            ap_, col = stat_blocks[blk]
            wt = ((sG, g_free),) if k == 0 else ()
            if blk >= 3:
                wt = wt + ((sS, stat_abs_ticks[blk - 3]),)
            sws = slot_sems[blk % 3]
            w_l = dma(sws, "sync", cmb[:, (blk % 3) * D:(blk % 3 + 1) * D],
                      ap_, wait=wt)
            kb.emit("scalar", lambda sc, blk=blk, col=col: sc.activation(
                out=cmb[:, (blk % 3) * D:(blk % 3 + 1) * D],
                in_=cmb[:, (blk % 3) * D:(blk % 3 + 1) * D],
                func=AF.Abs, accum_out=wstat[:, col:col + 1]),
                wait=((sws, w_l),), inc=((sS, sS.tick() and 1),))
            stat_abs_ticks.append(sS.n)
        stats_gate[0] = sS.n
    v_stats = sV.n
    if False and debug_scratch:
        def dbg_stats(v):
            v.tensor_copy(dbgc[:, 0:24], wstat[:])
            v.tensor_copy(dbgc[:, 24:32], colt[:])
            v.tensor_copy(dbgc[:, 32:33], swg[:])
            v.tensor_copy(dbgc[:, 33:34], cwg[:])
            v.tensor_copy(dbgc[:, 34:37], ptr0[:, 0:3])
            return v.tensor_copy(dbgc[:, 37:38], onec[:])
        kb.emit("vector", dbg_stats, wait=((sV, v_stats),), inc=((sV, sV.tick() and 1),))
        dma(sX2S, "gpsimd", dbg_st, dbgc[:], wait=((sV, sV.n),))

    dma(sPRM, "sync", aexp[:].rearrange("p (h n) -> p h n", h=DH),
        A_h.rearrange("(h p) n -> p h n", p=P))
    dma(sPRM, "sync", cw[:].rearrange("p (h j) -> p h j", h=DH),
        conv_w_h.rearrange("(h p) j -> p h j", p=P))
    dma(sPRM, "sync", cb[:], conv_b_h.rearrange("(h p) -> p h", p=P), allow_slow_non_contiguous=True)
    dma(sPRM, "sync", dD[:], Dd_h.rearrange("(h p) -> p h", p=P), allow_slow_non_contiguous=True)
    dma(sPRM, "sync", gbv[:], gb_h.rearrange("(h p) -> p h", p=P), allow_slow_non_contiguous=True)
    dma(sPRM, "sync", obv[:], ob_h.rearrange("(h p) -> p h", p=P), allow_slow_non_contiguous=True)
    dma(sPRM, "sync", bpb[:], Bpb_h[:, None])
    dma(sPRM, "sync", cpb[:], Cpb_h[:, None])
    dma(sPRM, "sync", lng_bc[:], lng_h[None, :].broadcast_to([P, D]))
    dma(sPRM, "sync", lnb_bc[:], lnb_h[None, :].broadcast_to([P, D]))
    in_params = sPRM.n
    dma(sPRM, "sync", bpT3, BpT_h.rearrange("(h p) n -> p h n", p=P))
    dma(sPRM, "sync", cpT3, CpT_h.rearrange("(h p) n -> p h n", p=P))
    in_bc = sPRM.n
    for c in range(DH):
        dma(sPRM, "sync", gwT[:, c * D:(c + 1) * D], gwT_h[c * P:(c + 1) * P, :])
    for c in range(DH):
        dma(sPRM, "sync", owT[:, c * D:(c + 1) * D], owT_h[c * P:(c + 1) * P, :])
    in_w = sPRM.n
    PRM_ALL = sPRM.n

    kb.emit("gpsimd", lambda g: g.iota(oneh[:], [[1, NST], [0, P]],
                                       channel_multiplier=-1,
                                       allow_small_or_imprecise_dtypes=True))
    kb.emit("gpsimd", lambda g: g.memset(hstate[:], 0.0), inc=((sG, sG.tick() and 1),))
    g_tab = sG.n
    kb.emit("vector", lambda v: v.tensor_scalar(out=oneh[:], in0=oneh[:], scalar1=0.0,
                                                scalar2=None, op0=ALU.is_equal),
            wait=((sG, g_tab),), inc=((sV, sV.tick() and 1),))
    v_oneh = sV.n
    def aexp1(sc):
        inst = sc.activation(out=aexp[:], in_=aexp[:], func=AF.Exp)
        sc.drain()
        return inst
    kb.emit("scalar", aexp1,
            wait=((sPRM, PRM_ALL),), inc=((sS, sS.tick() and 1),))
    kb.emit("scalar", lambda sc: sc.activation(out=aexp[:], in_=aexp[:], func=AF.Exp,
                                               scale=-1.0), inc=((sS, sS.tick() and 1),))
    s_aexp = sS.n

    def ln_pipeline(src_ap, dst_ap, affine):
        kb.emit("scalar", lambda sc: sc.activation(
            out=sqs[:], in_=src_ap, func=AF.Square, accum_out=lncol[:, 8:9]),
            wait=((sV, sV.n),), inc=((sS, sS.tick() and 1),))
        s_a = sS.n

        def cols1(v):
            v.reduce_sum(lncol[:, 9:10], src_ap, axis=AX.X)
            vfill(v)
            v.tensor_scalar(out=lncol[:, 10:11], in0=lncol[:, 9:10],
                            scalar1=1.0 / D, scalar2=None, op0=ALU.mult)
            v.tensor_scalar(out=lncol[:, 11:12], in0=lncol[:, 8:9],
                            scalar1=1.0 / D, scalar2=None, op0=ALU.mult)
            vfill(v)
            v.tensor_mul(lncol[:, 12:13], lncol[:, 10:11], lncol[:, 10:11])
            vfill(v)
            v.tensor_sub(lncol[:, 11:12], lncol[:, 11:12], lncol[:, 12:13])
            vfill(v)
            return v.tensor_scalar(out=lncol[:, 11:12], in0=lncol[:, 11:12],
                                   scalar1=LN_EPS, scalar2=LN_EPS,
                                   op0=ALU.add, op1=ALU.max)
        kb.emit("vector", cols1, wait=((sS, s_a),), inc=((sV, sV.tick() and 1),))
        v_a = sV.n
        kb.emit("scalar", lambda sc: sc.activation(
            out=lncol[:, 12:13], in_=lncol[:, 11:12], func=AF.Sqrt),
            wait=((sV, v_a),), inc=((sS, sS.tick() and 1),))
        s_b = sS.n

        def cols2(v):
            v.reciprocal(lncol[:, 13:14], lncol[:, 12:13])
            vfill(v)
            return v.scalar_tensor_tensor(out=lncol[:, 14:15], in0=lncol[:, 10:11],
                                          scalar=-1.0, in1=lncol[:, 13:14],
                                          op0=ALU.mult, op1=ALU.mult)
        kb.emit("vector", cols2, wait=((sS, s_b),), inc=((sV, sV.tick() and 1),))
        v_b = sV.n
        kb.emit("scalar", lambda sc: sc.activation(
            out=dst_ap, in_=src_ap, func=AF.Identity, scale=lncol[:, 13:14],
            bias=lncol[:, 14:15]), wait=((sV, v_b),), inc=((sS, sS.tick() and 1),))
        s_c = sS.n
        if affine is not None:
            gt, btl = affine

            def aff(v):
                v.tensor_mul(dst_ap, dst_ap, gt[:])
                return v.tensor_add(dst_ap, dst_ap, btl[:])
            kb.emit("vector", aff, wait=((sS, s_c),), inc=((sV, sV.tick() and 1),))


    def ln_cols4(src4, sq_wait, ct=None):
        if ct is None:
            ct = lncol

        def chain1(v, sq_wait=sq_wait, src4=src4, ct=ct):
            v.wait_ge(sS.h, sq_wait)
            v.reduce_sum(ct[:, 4:8], src4, axis=AX.X)
            vfill(v)
            v.tensor_scalar(out=ct[:, 8:12], in0=ct[:, 4:8],
                            scalar1=1.0 / D, scalar2=None, op0=ALU.mult)
            v.tensor_scalar(out=ct[:, 4:8], in0=ct[:, 0:4],
                            scalar1=1.0 / D, scalar2=None, op0=ALU.mult)
            v.tensor_mul(ct[:, 12:16], ct[:, 8:12], ct[:, 8:12])
            vfill(v)
            return v.scalar_tensor_tensor(out=ct[:, 4:8], in0=ct[:, 12:16],
                                          scalar=-1.0, in1=ct[:, 4:8],
                                          op0=ALU.mult, op1=ALU.add)
        kb.emit("vector", chain1, inc=((sV, sV.tick() and 1),))
        v_a = sV.n
        kb.emit("scalar", lambda sc, ct=ct: sc.activation(
            out=ct[:, 12:16], in_=ct[:, 4:8], func=AF.Sqrt, bias=epsc[:]),
            wait=((sV, v_a), (sG, g_tab0)), inc=((sS, sS.tick() and 1),))
        s_b = sS.n

        def chain2(v, s_b=s_b, ct=ct):
            v.wait_ge(sS.h, s_b)
            v.reciprocal(ct[:, 0:4], ct[:, 12:16])
            vfill(v)
            return v.scalar_tensor_tensor(out=ct[:, 4:8], in0=ct[:, 8:12],
                                          scalar=-1.0, in1=ct[:, 0:4],
                                          op0=ALU.mult, op1=ALU.mult)
        kb.emit("vector", chain2, inc=((sV, sV.tick() and 1),))
        return sV.n

    sp_x2_last = 0
    sp_xq_last = 0
    x4n = ut[:].rearrange("p (s d) -> p s d", s=SUB)

    def ln1_stats(t_chunk, extra_wait):
        w = dma(sXA, "sync", x4n,
                x_h[t_chunk * TT:(t_chunk + 1) * TT, :].rearrange(
                    "(s p) d -> p s d", p=P), wait=extra_wait)
        for s_ in range(SUB):
            kb.emit("scalar", lambda sc, s_=s_: sc.activation(
                out=sqs[:], in_=x4n[:, s_, :], func=AF.Square,
                accum_out=lncol2[:, s_:s_ + 1]),
                wait=((sXA, w),) if s_ == 0 else (), inc=((sS, sS.tick() and 1),))
        return ln_cols4(x4n, sS.n, ct=lncol2)

    v_ln1stats = ln1_stats(0, ())

    for tc in range(NCH):
        if tc == 0:
            kb.emit("vector", lambda v: v.memset(ln1F3[:, :, 0:8], 0.0),
                    wait=((sG, g_tab),), inc=((sV, sV.tick() and 1),))
        else:
            kb.emit("vector", lambda v: v.tensor_copy(ln1F3[:, :, 0:8],
                                                      ln1F3[:, :, TT:TT + 8]),
                    inc=((sV, sV.tick() and 1),))
        v_lead = sV.n

        t_tr_last = 0
        for s_ in range(SUB):
            kb.emit("scalar", lambda sc, s_=s_: sc.activation(
                out=etok[:], in_=x4n[:, s_, :], func=AF.Identity,
                scale=lncol2[:, s_:s_ + 1], bias=lncol2[:, 4 + s_:5 + s_]),
                wait=((sV, v_ln1stats), (sT, t_tr_last)), inc=((sS, sS.tick() and 1),))
            s_ln1 = sS.n
            for c in range(DH):
                bank = ptr0 if (c % 2 == 0) else ptr1
                kb.emit("tensor", lambda t, c=c, bank=bank: t.transpose(
                    bank[:, 0:P], etok[:, c * P:(c + 1) * P], idf[:]),
                    wait=((sS, s_ln1), (sV, max(v_tab, v_lead)), (sV, sV.n)),
                    inc=((sT, sT.tick() and 1),))
                t_tr = sT.n
                kb.emit("vector", lambda v, c=c, s_=s_, bank=bank: v.tensor_copy(
                    ln1F3[:, c, 8 + s_ * P:8 + (s_ + 1) * P], bank[:, 0:P]),
                    wait=((sT, t_tr),), inc=((sV, sV.tick() and 1),))
            t_tr_last = sT.n
        v_ln1f = sV.n

        def conv_emit(v):
            inst = None
            for c in range(DH):
                v.tensor_scalar(out=ut3[:, c, :], in0=ln1F3[:, c, 5:5 + TT],
                                scalar1=cw[:, c * DCONV:c * DCONV + 1],
                                scalar2=None, op0=ALU.mult)
                for j in (1, 2, 3):
                    inst = v.scalar_tensor_tensor(
                        out=ut3[:, c, :], in0=ln1F3[:, c, 5 + j:5 + j + TT],
                        scalar=cw[:, c * DCONV + j:c * DCONV + j + 1],
                        in1=ut3[:, c, :], op0=ALU.mult, op1=ALU.add)
            return inst
        kb.emit("vector", conv_emit,
                wait=((sV, v_ln1f), (sS, sS.n)), inc=((sV, sV.tick() and 1),))
        v_conv = sV.n

        def conv_silu(sc):
            inst = None
            for c in range(DH):
                inst = sc.activation(out=ut3[:, c, :], in_=ut3[:, c, :],
                                     func=AF.Silu, bias=cb[:, c:c + 1])
            return inst
        kb.emit("scalar", conv_silu, wait=((sV, v_conv),), inc=((sS, sS.tick() and 1),))
        s_u = sS.n
        if debug_scratch and tc == 1:
            spu = dma(sX2S, "gpsimd", dbg_u, ut[:], wait=((sS, s_u),))

        mm_group([(pbt[:, 0:TT], bpT3[:, c, :], ut3[:, c, :]) for c in range(DH)],
                 wait=((sS, s_u), (sPRM, PRM_ALL), (sS, sS.n)), inc=((sT, sT.tick() and 1),))
        t_bt = sT.n
        mm_group([(pbt[:, TT:2 * TT], cpT3[:, c, :], ut3[:, c, :]) for c in range(DH)],
                 wait=(), inc=((sT, sT.tick() and 1),))
        t_ct = sT.n
        kb.emit("scalar", lambda sc: sc.activation(out=btT[:], in_=pbt[:, 0:TT],
                                                   func=AF.Identity, bias=bpb[:]),
                wait=((sT, t_bt),), inc=((sS, sS.tick() and 1),))
        kb.emit("scalar", lambda sc: sc.activation(out=ctT[:], in_=pbt[:, TT:2 * TT],
                                                   func=AF.Identity, bias=cpb[:]),
                wait=((sT, t_ct),), inc=((sS, sS.tick() and 1),))
        s_btct = sS.n
        if debug_scratch and tc == 1:
            dma(sX2S, "gpsimd", dbg_bt[0:NST, 0:TT], btT[:], wait=((sS, s_btct),))
            dma(sX2S, "gpsimd", dbg_bt[0:NST, TT:2 * TT], ctT[:], wait=((sS, s_btct),))

        kb.emit("vector", lambda v: v.tensor_tensor(
            yy3, ut3, dD[:, :, None].broadcast_to([P, DH, TT]), ALU.mult),
            wait=((sS, s_u),), inc=((sV, sV.tick() and 1),))

        s_gate_evacs = []
        for n in range(NST):
            pb_ = pA if (n % 2 == 0) else pB
            pc_ = pC if (n % 2 == 0) else pD
            kb.emit("tensor", lambda t, n=n, pb_=pb_: t.matmul(
                pb_[:], oneh[:, n * P:(n + 1) * P], btT[:], start=True, stop=True),
                wait=((sS, s_btct), (sV, max(v_oneh, sV.n)), (sS, s_aexp)),
                inc=((sT, sT.tick() and 1),))
            t_b = sT.n
            kb.emit("tensor", lambda t, n=n, pc_=pc_: t.matmul(
                pc_[:], oneh[:, n * P:(n + 1) * P], ctT[:], start=True, stop=True),
                inc=((sT, sT.tick() and 1),))
            t_c = sT.n
            if 1 <= n <= DH:
                g = n - 1
                pg = ptr0 if (g % 2 == 0) else ptr1
                bank_free = () if g < 2 else ((sS, s_gate_evacs[g - 2]),)
                mm_group([(pg[:], gwT[:, c * D + g * P:c * D + (g + 1) * P],
                           ln1F3[:, c, 8:8 + TT]) for c in range(DH)],
                         wait=((sV, v_ln1f), (sPRM, PRM_ALL)) + bank_free,
                         inc=((sT, sT.tick() and 1),))
                t_g = sT.n
                kb.emit("scalar", lambda sc, g=g, pg=pg: sc.activation(
                    out=gzF3[:, g, :], in_=pg[:], func=AF.Silu, bias=gbv[:, g:g + 1]),
                    wait=((sT, t_g),), inc=((sS, sS.tick() and 1),))
                s_gate_evacs.append(sS.n)

            if gsl[0] == 0:
                g_yb[0] = sG.n
            base = gsl[0]
            sgate = stats_gate[0] if n == 0 else 0
            slice_ticks = [sV.tick() for _ in range(DH)]
            slots = [(base + c) % 6 for c in range(DH)]
            rws = [g_yb[0] + (base + c - 6) + 1 if base + c >= 6 else 0
                   for c in range(DH)]

            def scan_n(v, n=n, pb_=pb_, pc_=pc_, t_b=t_b, t_c=t_c,
                       slots=slots, rws=rws, sgate=sgate):
                v.wait_ge(sT.h, t_b)
                v.tensor_tensor(ub3, ut3, pb_[:, None, :].broadcast_to([P, DH, TT]),
                                ALU.mult)
                for c in range(DH):
                    v.tensor_tensor_scan(
                        out=hh3[:, c, :],
                        data0=aexp[:, c * NST + n:c * NST + n + 1].broadcast_to([P, TT]),
                        data1=ub3[:, c, :],
                        initial=hstate[:, n * DH + c:n * DH + c + 1],
                        op0=ALU.mult, op1=ALU.add)
                v.wait_ge(sT.h, t_c)
                if sgate:
                    v.wait_ge(sS.h, sgate)
                for c in range(DH):
                    if rws[c]:
                        v.wait_ge(sG.h, rws[c])
                    v.tensor_tensor(cmb3[:, slots[c], :], hh3[:, c, :], pc_[:],
                                    ALU.mult).then_inc(sV.h, 1)
                return v.tensor_copy(
                    hstate[:].rearrange("p (n h) -> p n h", n=NST)[:, n, :],
                    hh3[:, :, TT - 1])
            kb.emit("vector", scan_n, inc=((sV, sV.tick() and 1),))
            for c in range(DH):
                kb.emit("gpsimd", lambda g, c=c, sl=slots[c]: g.tensor_tensor(
                    yy3[:, c, :], yy3[:, c, :], cmb3[:, sl, :], ALU.add),
                    wait=((sV, slice_ticks[c]),), inc=((sG, sG.tick() and 1),))
                assert sG.n == g_yb[0] + base + c + 1
            gsl[0] = base + DH
        v_scan = sV.n
        g_yacc = sG.n
        dbg_w_y = 0
        if debug_scratch and tc == 1:
            dbg_w_y = dma(sX2S, "gpsimd", dbg_y, yy[:], wait=((sV, v_scan),))

        s_gate = s_gate_evacs[-1]
        dbg_w_g = 0
        if debug_scratch and tc == 1:
            dbg_w_g = dma(sX2S, "gpsimd", dbg_g, gzF[:], wait=((sS, s_gate),))

        kb.emit("vector", lambda v: v.tensor_mul(yy[:], yy[:], gzF[:]),
                wait=((sS, s_gate), (sX2S, dbg_w_y), (sG, g_yacc)),
                inc=((sV, sV.tick() and 1),))
        v_yg = sV.n

        z_evacs = []
        for g in range(DH):
            po = pC if (g % 2 == 0) else pD
            bank_free = ((sV, z_evacs[g - 2]),) if g >= 2 else ()
            mm_group([(po[:], owT[:, c * D + g * P:c * D + (g + 1) * P],
                       yy3[:, c, :]) for c in range(DH)],
                     wait=((sV, v_yg),) + bank_free, inc=((sT, sT.tick() and 1),))
            t_o = sT.n

            def z_evac(v, g=g, po=po, t_o=t_o, dbg_w_g=dbg_w_g):
                v.wait_ge(sT.h, t_o)
                if dbg_w_g:
                    v.wait_ge(sX2S.h, dbg_w_g)
                return v.scalar_tensor_tensor(out=gzF3[:, g, :], in0=po[:],
                                              scalar=obv[:, g:g + 1],
                                              in1=ln1F3[:, g, 8:8 + TT],
                                              op0=ALU.add, op1=ALU.add)
            kb.emit("vector", z_evac, inc=((sV, sV.tick() and 1),))
            z_evacs.append(sV.n)
        v_z = sV.n
        if debug_scratch and tc == 1:
            dbg_w_z = dma(sX2S, "gpsimd", dbg_z, gzF[:], wait=((sV, v_z),))
            sp_x2_last = max(sp_x2_last, dbg_w_z)

        if tc + 1 < NCH:
            v_ln1stats = ln1_stats(tc + 1, ((sV, v_yg),))
        emit_stats_boundary(tc, g_yacc)

        zt4 = hh[:].rearrange("p (s d) -> p s d", s=SUB)
        et4 = ub[:].rearrange("p (s d) -> p s d", s=SUB)
        xr4 = yy[:].rearrange("p (s d) -> p s d", s=SUB)
        qt4 = yy[:].bitcast(BF16)[:, 0:SUB * D].rearrange("p (s d) -> p s d", s=SUB)
        rows = slice(tc * TT, (tc + 1) * TT)
        for s_ in range(SUB):
            for c in range(DH):
                bank = ptr0 if (c % 2 == 0) else ptr1
                kb.emit("tensor", lambda t, c=c, s_=s_, bank=bank: t.transpose(
                    bank[:, 0:P], gzF3[:, c, s_ * P:(s_ + 1) * P], idf[:]),
                    wait=((sV, v_z), (sS, sS.n)), inc=((sT, sT.tick() and 1),))
                t_tr = sT.n
                kb.emit("scalar", lambda sc, c=c, s_=s_, bank=bank: sc.activation(
                    out=hh[:, s_ * D + c * P:s_ * D + (c + 1) * P],
                    in_=bank[:, 0:P], func=AF.Copy),
                    wait=((sT, t_tr), (sX2S, sp_x2_last), (sV, sV.n)),
                    inc=((sS, sS.tick() and 1),))
        for s_ in range(SUB):
            kb.emit("scalar", lambda sc, s_=s_: sc.activation(
                out=sqs[:], in_=zt4[:, s_, :], func=AF.Square,
                accum_out=lncol[:, s_:s_ + 1]), inc=((sS, sS.tick() and 1),))
        v_cols = ln_cols4(zt4, sS.n)
        for s_ in range(SUB):
            kb.emit("scalar", lambda sc, s_=s_: sc.activation(
                out=et4[:, s_, :], in_=zt4[:, s_, :], func=AF.Identity,
                scale=lncol[:, s_:s_ + 1], bias=lncol[:, 4 + s_:5 + s_]),
                wait=((sV, v_cols),) if s_ == 0 else (), inc=((sS, sS.tick() and 1),))
        s_innorm = sS.n

        def aff4(v, s_innorm=s_innorm):
            v.wait_ge(sS.h, s_innorm)
            v.tensor_tensor(et4, et4, lng_bc[:, None, :].broadcast_to([P, SUB, D]),
                            ALU.mult)
            return v.tensor_tensor(et4, et4,
                                   lnb_bc[:, None, :].broadcast_to([P, SUB, D]),
                                   ALU.add)
        kb.emit("vector", aff4, inc=((sV, sV.tick() and 1),))
        for s_ in range(SUB):
            kb.emit("scalar", lambda sc, s_=s_: sc.activation(
                out=sqs[:], in_=et4[:, s_, :], func=AF.Square,
                accum_out=lncol[:, s_:s_ + 1]),
                wait=((sV, sV.n),) if s_ == 0 else (), inc=((sS, sS.tick() and 1),))
        v_cols = ln_cols4(et4, sS.n)
        for s_ in range(SUB):
            kb.emit("scalar", lambda sc, s_=s_: sc.activation(
                out=zt4[:, s_, :], in_=et4[:, s_, :], func=AF.Identity,
                scale=lncol[:, s_:s_ + 1], bias=lncol[:, 4 + s_:5 + s_]),
                wait=((sV, v_cols),) if s_ == 0 else (), inc=((sS, sS.tick() and 1),))
        s_onorm = sS.n
        w_xr = dma(sXB, "sync", xr4,
                   x_h[rows, :].rearrange("(s p) d -> p s d", p=P),
                   wait=((sV, v_z),))
        kb.emit("vector", lambda v: v.tensor_add(hh[:], hh[:], yy[:]),
                wait=((sS, s_onorm), (sXB, w_xr)), inc=((sV, sV.tick() and 1),))
        v_x2 = sV.n
        sp_x2_last = dma(sX2S, "gpsimd",
                         x2_d[rows, :].rearrange("(s p) d -> p s d", p=P),
                         zt4, wait=((sV, v_x2),))
        for s_ in range(SUB):
            kb.emit("scalar", lambda sc, s_=s_: sc.activation(
                out=sqs[:], in_=zt4[:, s_, :], func=AF.Square,
                accum_out=lncol[:, s_:s_ + 1]),
                wait=((sV, v_x2),) if s_ == 0 else (), inc=((sS, sS.tick() and 1),))
        v_cols = ln_cols4(zt4, sS.n)
        for s_ in range(SUB):
            kb.emit("scalar", lambda sc, s_=s_: sc.activation(
                out=et4[:, s_, :], in_=zt4[:, s_, :], func=AF.Identity,
                scale=lncol[:, s_:s_ + 1], bias=lncol[:, 4 + s_:5 + s_]),
                wait=((sV, v_cols),) if s_ == 0 else (), inc=((sS, sS.tick() and 1),))
        s_ln3 = sS.n

        def q_cols4(v, s_ln3=s_ln3, tc=tc):
            v.wait_ge(sS.h, s_ln3)
            v.tensor_reduce(lncol[:, 0:4], et4, axis=AX.X, op=ALU.max,
                            apply_absolute_value=True)
            v.drain()
            v.tensor_scalar(out=lncol[:, 0:4], in0=lncol[:, 0:4],
                            scalar1=1e-5, scalar2=None, op0=ALU.max)
            v.drain()
            v.reciprocal(lncol[:, 4:8], lncol[:, 0:4])
            v.drain()
            v.tensor_scalar(out=lncol[:, 8:12], in0=lncol[:, 4:8],
                            scalar1=127.0, scalar2=None, op0=ALU.mult)
            return v.tensor_scalar(out=dsx[:, tc * SUB:(tc + 1) * SUB],
                                   in0=lncol[:, 0:4], scalar1=1.0 / 127.0,
                                   scalar2=None, op0=ALU.mult)
        kb.emit("vector", q_cols4, inc=((sV, sV.tick() and 1),))
        v_qc = sV.n
        for s_ in range(SUB):
            kb.emit("scalar", lambda sc, s_=s_: sc.activation(
                out=zt4[:, s_, :], in_=et4[:, s_, :], func=AF.Copy,
                scale=lncol[:, 8 + s_:9 + s_], bias=RB),
                wait=(((sV, v_qc), (sX2S, sp_x2_last)) if s_ == 0 else ()),
                inc=((sS, sS.tick() and 1),))
            s_q1 = sS.n

            def q_fin(v, s_=s_, s_q1=s_q1):
                v.wait_ge(sS.h, s_q1)
                v.tensor_copy(h16[:], zt4[:, s_, :])
                return v.tensor_scalar(out=qt4[:, s_, :], in0=h16[:], scalar1=-RB,
                                       scalar2=None, op0=ALU.add)
            kb.emit("vector", q_fin, inc=((sV, sV.tick() and 1),))
        v_q = sV.n
        sp_xq_last = dma(sXQS, "gpsimd",
                         xq_d[rows, :].rearrange("(s p) d -> p s d", p=P),
                         qt4, wait=((sV, v_q),))

    p2_V, p2_S, p2_T = sV.n, sS.n, sT.n
    p2_X2S, p2_XQS = sX2S.n, sXQS.n

    s_stats = stat_abs_ticks[-1]
    for mi in range(3):
        kb.emit("vector",
                lambda v, mi=mi: v.reduce_sum(colt[:, mi:mi + 1],
                                              wstat[:, mi * 32:(mi + 1) * 32],
                                              axis=AX.X),
                wait=((sS, s_stats),), inc=((sV, sV.tick() and 1),))
    v_sr = sV.n
    kb.emit("tensor", lambda t: t.matmul(pbt[0:1, 0:3], onec[:], colt[:, 0:3],
                                         start=True, stop=True),
            wait=((sV, v_sr),), inc=((sT, sT.tick() and 1),))
    t_s1 = sT.n
    kb.emit("vector", lambda v: v.tensor_copy(colt[0:1, 4:7], pbt[0:1, 0:3]),
            wait=((sT, t_s1),), inc=((sV, sV.tick() and 1),))
    v_s2 = sV.n
    kb.emit("tensor", lambda t: t.matmul(ptr0[:, 0:3], onerow[:],
                                         colt[0:1, 4:7], start=True, stop=True),
            wait=((sV, v_s2),), inc=((sT, sT.tick() and 1),))
    t_s2 = sT.n

    def stat_fin(v):
        v.tensor_scalar(out=colt[:, 0:3], in0=ptr0[:, 0:3],
                        scalar1=1.0 / (D * INNER), scalar2=1e-5,
                        op0=ALU.mult, op1=ALU.max)
        v.drain()
        v.tensor_copy(cwg[:], colt[:, 0:1])
        v.tensor_copy(cwu[:], colt[:, 1:2])
        v.tensor_copy(cwd[:], colt[:, 2:3])
        v.drain()
        v.reciprocal(swg[:], cwg[:])
        v.reciprocal(swu[:], cwu[:])
        return v.reciprocal(swd[:], cwd[:])
    kb.emit("vector", stat_fin, wait=((sT, t_s2),), inc=((sV, sV.tick() and 1),))
    v_stats = sV.n

    p2.close()
    p3 = contextlib.ExitStack()
    sb3 = mk(p3)
    wstg = [sb3("wstg0", [P, INNER]), sb3("wstg1", [P, INNER])]
    wgq = sb3("wgq", [P, DH * INNER], BF16)
    wuq = sb3("wuq", [P, DH * INNER], BF16)
    f16b = sb3("f16b", [P, INNER], F16)
    gu = sb3("gu", [P, INNER])
    gu1 = sb3("gu1", [P, TT])
    guq = sb3("guq", [P, INNER], BF16)
    xqc = sb3("xqc", [P, DH * P], BF16)

    PHW = ((sV, p2_V), (sS, p2_S), (sT, p2_T), (sX2S, p2_X2S), (sXQS, p2_XQS))

    s_wprev = 0
    first = True
    for (wdst, src, scol) in [(wgq, wgT_h, swg), (wuq, wuT_h, swu)]:
        for c in range(DH):
            stg = wstg[c % 2]
            sws = sW0 if c % 2 == 0 else sW1
            w_l = dma(sws, "sync", stg[:], src[c * P:(c + 1) * P, :],
                      wait=(PHW if first else ((sS, s_wprev),)))
            first = False
            kb.emit("scalar", lambda sc, stg=stg, scol=scol: sc.activation(
                out=gu[:], in_=stg[:], func=AF.Identity, scale=scol[:]),
                wait=((sws, w_l), (sV, max(v_stats, sV.n))),
                inc=((sS, sS.tick() and 1),))
            s_wq = sS.n
            s_wprev = s_wq

            def w_fin(v, wdst=wdst, c=c):
                v.tensor_scalar(out=f16b[:], in0=gu[:], scalar1=-0.5, scalar2=None,
                                op0=ALU.is_lt)
                return v.scalar_tensor_tensor(
                    out=wdst[:, c * INNER:(c + 1) * INNER], in0=gu[:], scalar=0.5,
                    in1=f16b[:], op0=ALU.is_gt, op1=ALU.subtract)
            kb.emit("vector", w_fin, wait=((sS, s_wq),), inc=((sV, sV.tick() and 1),))
    v_wq = sV.n

    if debug_scratch:
        dma(sX2S, "gpsimd", dbg_wgq, wgq[:], wait=((sV, v_wq),))
        def dbg_cp(v):
            v.tensor_copy(sqs[:, 0:24], wstat[:])
            v.tensor_copy(sqs[:, 24:32], colt[:])
            v.tensor_copy(sqs[:, 32:33], swg[:])
            v.tensor_copy(sqs[:, 33:34], swu[:])
            v.tensor_copy(sqs[:, 34:35], swd[:])
            v.tensor_copy(sqs[:, 35:36], cwg[:])
            v.tensor_copy(sqs[:, 36:37], cwu[:])
            return v.tensor_copy(sqs[:, 37:38], cwd[:])
        kb.emit("vector", dbg_cp, wait=((sV, v_wq),), inc=((sV, sV.tick() and 1),))
        dma(sX2S, "gpsimd", dbg_cols, sqs[:, 0:64], wait=((sV, sV.n),))
    xstg = sb3("xstg", [P, D], BF16)
    idb = sb3("idb", [P, P], BF16)
    kb.emit("vector", lambda v: v.tensor_copy(idb[:], idf[:]),
            wait=((sV, v_wq),), inc=((sV, sV.tick() and 1),))
    v_idb = sV.n
    sp_gq = 0
    s_xq_evacs = []
    for i in range(NTOK):
        trow = i * P
        wv0 = dma(sXQ, "sync", xstg[:], xq_d[trow:trow + P, :],
                  wait=(((sS, s_xq_evacs[-1] if s_xq_evacs else 0),
                         (sXQS, p2_XQS))))
        wv = wv0
        for c in range(DH):
            pt = ptr0 if c % 2 == 0 else ptr1
            bankw = ((sS, s_xq_evacs[-2],) if len(s_xq_evacs) >= 2 else (sS, 0))
            kb.emit("tensor", lambda t, c=c, pt=pt: t.transpose(
                pt[:, 0:64].bitcast(BF16), xstg[:, c * P:(c + 1) * P], idb[:]),
                wait=((sXQ, wv0), (sV, v_idb), bankw, (sT, sT.n)),
                inc=((sT, sT.tick() and 1),))
            t_tr = sT.n
            kb.emit("scalar", lambda sc, c=c, pt=pt: sc.activation(
                out=xqc[:, c * P:(c + 1) * P], in_=pt[:, 0:64].bitcast(BF16),
                func=AF.Copy), wait=((sT, t_tr),), inc=((sS, sS.tick() and 1),))
            s_xq_evacs.append(sS.n)

        def mk_desc(v, i=i):
            v.tensor_mul(lncol[:, 0:1], cwg[:], dsx[:, i:i + 1])
            v.tensor_mul(lncol[:, 1:2], cwu[:], dsx[:, i:i + 1])
            return v.drain()
        kb.emit("vector", mk_desc, wait=((sV, v_wq), (sS, sS.n)),
                inc=((sV, sV.tick() and 1),))
        v_ds = sV.n
        s_up_evacs = []
        for f in range(DH):
            pg = pA if (f % 2 == 0) else pB
            pu = pC if (f % 2 == 0) else pD
            bank_free = ((sS, s_up_evacs[f - 2]),) if f >= 2 else ()
            mm_group([(pg[:], xqc[:, c * P:(c + 1) * P],
                       wgq[:, c * INNER + f * TT:c * INNER + (f + 1) * TT])
                      for c in range(DH)],
                     wait=((sS, s_xq_evacs[-1]), (sV, v_ds)) + bank_free,
                     inc=((sT, sT.tick() and 1),))
            t_g = sT.n
            mm_group([(pu[:], xqc[:, c * P:(c + 1) * P],
                       wuq[:, c * INNER + f * TT:c * INNER + (f + 1) * TT])
                      for c in range(DH)],
                     wait=(), inc=((sT, sT.tick() and 1),))
            t_u = sT.n
            kb.emit("scalar", lambda sc, pg=pg: sc.activation(
                out=gu1[:], in_=pg[:], func=AF.Silu, scale=lncol[:, 0:1]),
                wait=((sT, t_g), (sV, sV.n)), inc=((sS, sS.tick() and 1),))
            kb.emit("scalar", lambda sc, pu=pu: sc.activation(
                out=sqs[:, 0:TT], in_=pu[:], func=AF.Copy, scale=lncol[:, 1:2]),
                wait=((sT, t_u), (sV, sV.n)), inc=((sS, sS.tick() and 1),))
            s_up = sS.n
            s_up_evacs.append(s_up)

            def gu_mul(v, f=f, s_up=s_up):
                v.wait_ge(sS.h, s_up)
                v.tensor_tensor(gu[:, f * TT:(f + 1) * TT], gu1[:], sqs[:, 0:TT],
                                ALU.mult)
                v.tensor_reduce(lncol[:, 2:3], gu[:, f * TT:(f + 1) * TT],
                                axis=AX.X, op=ALU.max, apply_absolute_value=True)
                v.drain()
                if f == 0:
                    return v.tensor_copy(lncol[:, 3:4], lncol[:, 2:3])
                return v.tensor_max(lncol[:, 3:4], lncol[:, 3:4], lncol[:, 2:3])
            kb.emit("vector", gu_mul, inc=((sV, sV.tick() and 1),))

        def gq_cols(v, i=i):
            v.tensor_scalar(out=lncol[:, 3:4], in0=lncol[:, 3:4], scalar1=1e-5,
                            scalar2=None, op0=ALU.max)
            v.drain()
            v.reciprocal(lncol[:, 4:5], lncol[:, 3:4])
            v.drain()
            v.tensor_scalar(out=lncol[:, 5:6], in0=lncol[:, 4:5], scalar1=127.0,
                            scalar2=None, op0=ALU.mult)
            return v.tensor_scalar(out=dsg[:, i:i + 1], in0=lncol[:, 3:4],
                                   scalar1=1.0 / 127.0, scalar2=None, op0=ALU.mult)
        kb.emit("vector", gq_cols, wait=((sS, sS.n),),
                inc=((sV, sV.tick() and 1),))
        v_gqc = sV.n
        if debug_scratch and i == 0:
            dma(sX2S, "gpsimd", dbg_xqc, xqc[:], wait=((sV, v_gqc),))
            dma(sX2S, "gpsimd", dbg_gu, gu[:], wait=((sV, v_gqc),))
        kb.emit("scalar", lambda sc: sc.activation(
            out=f16b[:], in_=gu[:], func=AF.Copy, scale=lncol[:, 5:6], bias=RB),
            wait=((sV, v_gqc),), inc=((sS, sS.tick() and 1),))
        s_gq = sS.n

        def g_fin(v, s_gq=s_gq, sp_gq=sp_gq):
            v.wait_ge(sS.h, s_gq)
            v.wait_ge(sGQS.h, sp_gq)
            return v.tensor_scalar(out=guq[:], in0=f16b[:], scalar1=-RB,
                                   scalar2=None, op0=ALU.add)
        kb.emit("vector", g_fin, inc=((sV, sV.tick() and 1),))
        v_gq = sV.n
        sp_gq = dma(sGQS, "gpsimd", gq_d[trow:trow + P, :], guq[:],
                    wait=((sV, v_gq),))
    p3_V, p3_S, p3_T, p3_GQS = sV.n, sS.n, sT.n, sGQS.n

    p3.close()
    p4 = contextlib.ExitStack()
    sb4 = mk(p4)
    wdq = sb4("wdq", [P, IH * D], BF16)
    gqT = sb4("gqT", [P, IH * P], BF16)
    gqT2 = sb4("gqT2", [P, IH * P], BF16)
    x2b = sb4("x2b", [P, SUB * D])
    o2b = sb4("o2b", [P, SUB * D])
    gstg = sb4("gstg", [P, INNER], BF16)
    idb4 = sb4("idb4", [P, P], BF16)
    wd16 = sb4("wd16", [P, D], F16)
    wdf = sb4("wdf", [P, D])
    wst4 = [sb4("wst4a", [P, D]), sb4("wst4b", [P, D])]

    PHW3 = ((sV, p3_V), (sS, p3_S), (sT, p3_T), (sGQS, p3_GQS))
    kb.emit("vector", lambda v: v.tensor_copy(idb4[:], idf[:]),
            wait=((sV, sV.n),), inc=((sV, sV.tick() and 1),))
    v_idb4 = sV.n
    s_gq_evacs = []
    s_wprev = 0
    first = True
    for c in range(IH):
        stg = wst4[c % 2]
        sws = sW0 if c % 2 == 0 else sW1
        w_l = dma(sws, "sync", stg[:, 0:D], wdT_h[c * P:(c + 1) * P, :],
                  wait=(PHW3 if first else ((sS, s_wprev),)))
        first = False
        kb.emit("scalar", lambda sc, stg=stg: sc.activation(
            out=wdf[:], in_=stg[:, 0:D], func=AF.Identity, scale=swd[:]),
            wait=((sws, w_l), (sV, sV.n)), inc=((sS, sS.tick() and 1),))
        s_wq = sS.n
        s_wprev = s_wq

        def wd_fin(v, c=c):
            v.tensor_scalar(out=wd16[:], in0=wdf[:], scalar1=-0.5, scalar2=None,
                            op0=ALU.is_lt)
            return v.scalar_tensor_tensor(
                out=wdq[:, c * D:(c + 1) * D], in0=wdf[:], scalar=0.5,
                in1=wd16[:], op0=ALU.is_gt, op1=ALU.subtract)
        kb.emit("vector", wd_fin, wait=((sS, s_wq),), inc=((sV, sV.tick() and 1),))
    v_wd = sV.n
    if debug_scratch:
        dma(sX2S, "gpsimd", dbg_wdq, wdq[:], wait=((sV, v_wd),))

    v_o_done = 0
    dm_out_prev = 0
    o2b4 = o2b[:].rearrange("p (s d) -> p s d", s=SUB)
    x2b4 = x2b[:].rearrange("p (s d) -> p s d", s=SUB)
    for gi in range(NTOK // SUB):
        rows = slice(gi * SUB * P, (gi + 1) * SUB * P)
        w_x2 = dma(sX2, "sync", x2b4,
                   x2_d[rows, :].rearrange("(s p) d -> p s d", p=P),
                   wait=((sV, v_o_done), (sX2S, p2_X2S)))
        def mk_dd(v, gi=gi):
            v.wait_ge(sV.h, v_wd)
            v.tensor_tensor(colt[:, 0:SUB], cwd[:].broadcast_to([P, SUB]),
                            dsg[:, gi * SUB:(gi + 1) * SUB], ALU.mult)
            return v.drain()
        kb.emit("vector", mk_dd, wait=((sS, sS.n),), inc=((sV, sV.tick() and 1),))
        v_dd = sV.n
        for il in range(SUB):
            i = gi * SUB + il
            trow = i * P
            gq_t = gqT if il % 2 == 0 else gqT2
            wvg = dma(sGQT, "sync", gstg[:], gq_d[trow:trow + P, :],
                      wait=(((sS, s_gq_evacs[-1] if s_gq_evacs else 0),
                             (sGQS, p3_GQS))))
            for c in range(IH):
                pt = pC if c % 2 == 0 else pD
                bankw = ((sS, s_gq_evacs[-2]) if len(s_gq_evacs) >= 2 else (sS, 0))
                kb.emit("tensor", lambda t, c=c, pt=pt, gq_t=gq_t: t.transpose(
                    pt[:, 0:64].bitcast(BF16), gstg[:, c * P:(c + 1) * P], idb4[:]),
                    wait=((sGQT, wvg), (sV, v_idb4), bankw, (sT, sT.n)),
                    inc=((sT, sT.tick() and 1),))
                t_tr = sT.n
                kb.emit("scalar", lambda sc, c=c, pt=pt, gq_t=gq_t: sc.activation(
                    out=gq_t[:, c * P:(c + 1) * P], in_=pt[:, 0:64].bitcast(BF16),
                    func=AF.Copy), wait=((sT, t_tr),), inc=((sS, sS.tick() and 1),))
                s_gq_evacs.append(sS.n)
            for g2 in range(2):
                pq = pA if (g2 % 2 == 0) else pB
                mm_group([(pq[:], gq_t[:, c * P:(c + 1) * P],
                           wdq[:, c * D + g2 * TT:c * D + (g2 + 1) * TT])
                          for c in range(IH)],
                         wait=((sS, s_gq_evacs[-1]),), inc=((sT, sT.tick() and 1),))
                t_d = sT.n
                kb.emit("scalar", lambda sc, il=il, g2=g2, pq=pq: sc.activation(
                    out=o2b[:, il * D + g2 * TT:il * D + (g2 + 1) * TT], in_=pq[:],
                    func=AF.Copy, scale=colt[:, il:il + 1]),
                    wait=((sT, t_d), (sV, v_dd), (sOUT, dm_out_prev)),
                    inc=((sS, sS.tick() and 1),))
        s_o2 = sS.n
        for s_ in range(SUB):
            kb.emit("scalar", lambda sc, s_=s_: sc.activation(
                out=sqs[:], in_=o2b4[:, s_, :], func=AF.Square,
                accum_out=lncol[:, s_:s_ + 1]),
                wait=((sV, sV.n),) if s_ == 0 else (), inc=((sS, sS.tick() and 1),))
        v_cols = ln_cols4(o2b4, sS.n)
        for s_ in range(SUB):
            kb.emit("scalar", lambda sc, s_=s_: sc.activation(
                out=o2b4[:, s_, :], in_=o2b4[:, s_, :], func=AF.Identity,
                scale=lncol[:, s_:s_ + 1], bias=lncol[:, 4 + s_:5 + s_]),
                wait=((sV, v_cols),) if s_ == 0 else (), inc=((sS, sS.tick() and 1),))
        s_ln4 = sS.n
        kb.emit("vector", lambda v: v.tensor_add(o2b[:], o2b[:], x2b[:]),
                wait=((sS, s_ln4), (sX2, w_x2)), inc=((sV, sV.tick() and 1),))
        v_o_done = sV.n
        dm_out_prev = dma(sOUT, "gpsimd",
                          out_h[rows, :].rearrange("(s p) d -> p s d", p=P),
                          o2b4, wait=((sV, v_o_done),))

    kb.emit("gpsimd", lambda g: None, wait=((sOUT, sOUT.n),))
    p4.close()

    with top:
        with nc.Block() as block:
            kb.run(block)
    return nc


_NC_CACHE = None


def kernel(**inputs):
    global _NC_CACHE
    f32 = np.float32
    T = lambda a: np.ascontiguousarray(np.asarray(a, dtype=f32).T)
    C = lambda a: np.ascontiguousarray(np.asarray(a, dtype=f32))
    x = C(inputs["L0_attn_x"])
    common = {
        "conv_w": C(np.asarray(inputs["L0_attn_conv_w"], dtype=f32)[:, 0, :]),
        "conv_b": C(inputs["L0_attn_conv_b"]),
        "A": C(inputs["L0_attn_A"]),
        "BpT": T(inputs["L0_attn_Bp_w"]),
        "Bpb": C(inputs["L0_attn_Bp_b"]),
        "CpT": T(inputs["L0_attn_Cp_w"]),
        "Cpb": C(inputs["L0_attn_Cp_b"]),
        "Dd": C(inputs["L0_attn_D"]),
        "gwT": T(inputs["L0_attn_gate_w"]),
        "gb": C(inputs["L0_attn_gate_b"]),
        "owT": T(inputs["L0_attn_out_w"]),
        "ob": C(inputs["L0_attn_out_b"]),
        "lng": C(inputs["L0_attn_ln_g"]),
        "lnb": C(inputs["L0_attn_ln_b"]),
        "wgT": T(inputs["L0_ffn_w_gate"]),
        "wuT": T(inputs["L0_ffn_w_up"]),
        "wdT": T(inputs["L0_ffn_w_down"]),
    }
    in_maps = [dict(common, x=x[c]) for c in range(8)]
    if _NC_CACHE is None:
        _NC_CACHE = build_core()
    res = run_bass_kernel_spmd(_NC_CACHE, in_maps, core_ids=list(range(8)))
    return np.stack([np.asarray(r["out"], dtype=f32) for r in res.results])


# version 16
# speedup vs baseline: 1.0168x; 1.0168x over previous
import contextlib
import numpy as np
import concourse.bass as bass
import concourse.mybir as mybir
from concourse.bass_utils import run_bass_kernel_spmd

F32 = mybir.dt.float32
F16 = mybir.dt.float16
BF16 = mybir.dt.bfloat16
AF = mybir.ActivationFunctionType
ALU = mybir.AluOpType
AX = mybir.AxisListType

P = 128
S = 2048
D = 1024
DH = D // P
NST = 16
DCONV = 4
INNER = 4096
IH = INNER // P
TT = 512
NCH = S // TT
SUB = TT // P
NTOK = S // P
LN_EPS = 1e-5
RB = 1536.0


class Sem:
    def __init__(self, h):
        self.h = h
        self.n = 0

    def tick(self, k=1):
        self.n += k
        return self.n


class KB:
    ENGS = ("sync", "tensor", "vector", "scalar", "gpsimd")

    def __init__(self, nc):
        self.nc = nc
        self.q = {e: [] for e in self.ENGS}

    def emit(self, eng, fn, wait=(), inc=()):
        wait = tuple((s.h, v) for s, v in wait)
        inc = tuple((s.h, k) for s, k in inc)

        def g(h):
            for sh, v in wait:
                h.wait_ge(sh, v)
            inst = fn(h)
            for sh, k in inc:
                inst.then_inc(sh, k)
        self.q[eng].append(g)

    def run(self, block):
        for eng in self.ENGS:
            fns = self.q[eng]
            getattr(block, eng)(lambda h, fns=fns: [f(h) for f in fns])


def build_core(debug_scratch=False):
    nc = bass.Bass("TRN2", target_bir_lowering=False, detect_race_conditions=False)

    x_h = nc.dram_tensor("x", [S, D], F32, kind="ExternalInput").ap()
    conv_w_h = nc.dram_tensor("conv_w", [D, DCONV], F32, kind="ExternalInput").ap()
    conv_b_h = nc.dram_tensor("conv_b", [D], F32, kind="ExternalInput").ap()
    A_h = nc.dram_tensor("A", [D, NST], F32, kind="ExternalInput").ap()
    BpT_h = nc.dram_tensor("BpT", [D, NST], F32, kind="ExternalInput").ap()
    Bpb_h = nc.dram_tensor("Bpb", [NST], F32, kind="ExternalInput").ap()
    CpT_h = nc.dram_tensor("CpT", [D, NST], F32, kind="ExternalInput").ap()
    Cpb_h = nc.dram_tensor("Cpb", [NST], F32, kind="ExternalInput").ap()
    Dd_h = nc.dram_tensor("Dd", [D], F32, kind="ExternalInput").ap()
    gwT_h = nc.dram_tensor("gwT", [D, D], F32, kind="ExternalInput").ap()
    gb_h = nc.dram_tensor("gb", [D], F32, kind="ExternalInput").ap()
    owT_h = nc.dram_tensor("owT", [D, D], F32, kind="ExternalInput").ap()
    ob_h = nc.dram_tensor("ob", [D], F32, kind="ExternalInput").ap()
    lng_h = nc.dram_tensor("lng", [D], F32, kind="ExternalInput").ap()
    lnb_h = nc.dram_tensor("lnb", [D], F32, kind="ExternalInput").ap()
    wgT_h = nc.dram_tensor("wgT", [D, INNER], F32, kind="ExternalInput").ap()
    wuT_h = nc.dram_tensor("wuT", [D, INNER], F32, kind="ExternalInput").ap()
    wdT_h = nc.dram_tensor("wdT", [INNER, D], F32, kind="ExternalInput").ap()
    out_h = nc.dram_tensor("out", [S, D], F32, kind="ExternalOutput").ap()
    _scr = "ExternalOutput" if debug_scratch else "Internal"
    x2_d = nc.dram_tensor("x2_scr", [S, D], F32, kind=_scr).ap()
    xq_d = nc.dram_tensor("xq_scr", [S, D], BF16, kind=_scr).ap()
    gq_d = nc.dram_tensor("gq_scr", [S, INNER], BF16, kind=_scr).ap()
    if debug_scratch:
        dbg_st = nc.dram_tensor("dbg_st", [P, 40], F32, kind=_scr).ap()
        dbg_wgq = nc.dram_tensor("dbg_wgq", [P, DH * INNER], BF16, kind=_scr).ap()
        dbg_xqc = nc.dram_tensor("dbg_xqc", [P, DH * P], BF16, kind=_scr).ap()
        dbg_gu = nc.dram_tensor("dbg_gu", [P, INNER], F32, kind=_scr).ap()
        dbg_cols = nc.dram_tensor("dbg_cols", [P, 64], F32, kind=_scr).ap()
        dbg_wdq = nc.dram_tensor("dbg_wdq", [P, IH * D], BF16, kind=_scr).ap()
        dbg_u = nc.dram_tensor("dbg_u", [P, DH * TT], F32, kind=_scr).ap()
        dbg_bt = nc.dram_tensor("dbg_bt", [NST, 2 * TT], F32, kind=_scr).ap()
        dbg_y = nc.dram_tensor("dbg_y", [P, DH * TT], F32, kind=_scr).ap()
        dbg_g = nc.dram_tensor("dbg_g", [P, DH * TT], F32, kind=_scr).ap()
        dbg_z = nc.dram_tensor("dbg_z", [P, DH * TT], F32, kind=_scr).ap()
        dbg_e = nc.dram_tensor("dbg_e", [P, 40], F32, kind=_scr).ap()

    kb = KB(nc)
    top = contextlib.ExitStack()

    def mk(es_):
        sbf = lambda n, shp, dt=F32: es_.enter_context(nc.sbuf_tensor(n, shp, dt))
        return sbf
    sb = mk(top)
    sem = lambda n: Sem(top.enter_context(nc.semaphore(n)))

    sIN = sem("sIN")
    sSP = sem("sSP")
    sXA = sem("sXA")
    sXB = sem("sXB")
    sW0 = sem("sW0")
    sW1 = sem("sW1")
    sWG = sem("sWG")
    sPRM = sem("sPRM")
    sXQ = sem("sXQ")
    sGQT = sem("sGQT")
    sX2 = sem("sX2")
    sX2S = sem("sX2S")
    sXQS = sem("sXQS")
    sGQS = sem("sGQS")
    sOUT = sem("sOUT")
    sT = sem("sT")
    sV = sem("sV")
    sS = sem("sS")
    sG = sem("sG")
    dmo = sem("dmo")

    pes = contextlib.ExitStack()
    top.enter_context(pes)
    ptr0 = pes.enter_context(nc.psum_tensor("ptr0", [P, TT], F32))
    ptr1 = pes.enter_context(nc.psum_tensor("ptr1", [P, TT], F32))
    pA = pes.enter_context(nc.psum_tensor("pA", [P, TT], F32))
    pB = pes.enter_context(nc.psum_tensor("pB", [P, TT], F32))
    pC = pes.enter_context(nc.psum_tensor("pC", [P, TT], F32))
    pD = pes.enter_context(nc.psum_tensor("pD", [P, TT], F32))
    pbt = pes.enter_context(nc.psum_tensor("pbt", [NST, 2 * TT], F32))

    def mm_group(plist, wait, inc, mm_waits=None):
        n = len(plist)

        def fn(t):
            inst = None
            for i, (o, l, r) in enumerate(plist):
                if mm_waits is not None and mm_waits[i]:
                    for sh, val in mm_waits[i]:
                        t.wait_ge(sh, val)
                inst = t.matmul(o, l, r, start=(i == 0),
                                stop=(i % 4 == 3 or i == n - 1),
                                skip_group_check=(i >= 4))
            return inst
        kb.emit("tensor", fn, wait=wait, inc=inc)

    def vfill(v):
        v.tensor_scalar(out=jnk[:, 0:1], in0=onec[:], scalar1=1.0,
                        scalar2=None, op0=ALU.mult)
        v.tensor_scalar(out=jnk[:, 1:2], in0=onec[:], scalar1=1.0,
                        scalar2=None, op0=ALU.mult)

    def dma(s_, eng, out_ap, in_ap, wait=(), **kw):
        v = s_.tick(16)
        kb.emit(eng, lambda e, o=out_ap, i=in_ap, kw=kw: e.dma_start(out=o, in_=i, **kw),
                wait=wait, inc=((s_, 16),))
        return v

    sqs = sb("sqs", [P, D])
    lncol = sb("lncol", [P, 16])
    colt = sb("colt", [P, 8])
    onec = sb("onec", [P, 1])
    jnk = sb("jnk", [P, 2])
    onerow = sb("onerow", [1, P])
    epsc = sb("epsc", [P, 1])
    idf = sb("idf", [P, P])
    wstat = sb("wstat", [P, 96])
    swg, swu, swd = sb("swg", [P, 1]), sb("swu", [P, 1]), sb("swd", [P, 1])
    cwg, cwu, cwd = sb("cwg", [P, 1]), sb("cwu", [P, 1]), sb("cwd", [P, 1])
    dsx = sb("dsx", [P, NTOK])
    dsg = sb("dsg", [P, NTOK])
    h16 = sb("h16", [P, D], F16)
    dbgc = sb("dbgc", [P, 40]) if debug_scratch else None

    kb.emit("gpsimd", lambda g: g.iota(idf[:], [[1, P]], channel_multiplier=-1,
                                       allow_small_or_imprecise_dtypes=True))
    kb.emit("gpsimd", lambda g: g.memset(onec[:], 1.0))
    kb.emit("gpsimd", lambda g: g.memset(onerow[:], 1.0))
    kb.emit("gpsimd", lambda g: g.memset(epsc[:], LN_EPS), inc=((sG, sG.tick() and 1),))
    g_tab0 = sG.n
    kb.emit("vector", lambda v: v.tensor_scalar(out=idf[:], in0=idf[:], scalar1=0.0,
                                                scalar2=None, op0=ALU.is_equal),
            wait=((sG, g_tab0),), inc=((sV, sV.tick() and 1),))
    v_tab = sV.n


    p2 = contextlib.ExitStack()
    sb2 = mk(p2)
    aexp = sb2("aexp", [P, DH * NST])
    cw = sb2("cw", [P, DH * DCONV])
    cb = sb2("cb", [P, DH])
    dD = sb2("dD", [P, DH])
    gbv = sb2("gbv", [P, DH])
    obv = sb2("obv", [P, DH])
    bpb = sb2("bpb", [NST, 1])
    cpb = sb2("cpb", [NST, 1])
    lng_bc = sb2("lng_bc", [P, D])
    lnb_bc = sb2("lnb_bc", [P, D])
    oneh = sb2("oneh", [NST, NST * P])
    cmb = sb2("cmb", [P, 6 * TT])
    hstate = sb2("hstate", [P, NST * DH])
    gwT = sb2("gwT_s", [P, DH * D])
    owT = sb2("owT_s", [P, DH * D])
    bpT = sb2("bpT", [P, DH * NST])
    cpT = sb2("cpT", [P, DH * NST])
    LF = TT + 8
    ln1F = sb2("ln1F", [P, DH * LF])
    ut = sb2("ut", [P, DH * TT])
    ub = sb2("ub", [P, DH * TT])
    hh = sb2("hh", [P, DH * TT])
    yy = sb2("yy", [P, DH * TT])
    gzF = sb2("gzF", [P, DH * TT])
    btT = sb2("btT", [NST, TT])
    ctT = sb2("ctT", [NST, TT])
    etok = sb2("etok", [P, D])
    lncol2 = sb2("lncol2", [P, 16])

    ln1F3 = ln1F[:].rearrange("p (h t) -> p h t", h=DH)
    ut3 = ut[:].rearrange("p (h t) -> p h t", h=DH)
    ub3 = ub[:].rearrange("p (h t) -> p h t", h=DH)
    hh3 = hh[:].rearrange("p (h t) -> p h t", h=DH)
    yy3 = yy[:].rearrange("p (h t) -> p h t", h=DH)
    cmb3 = cmb[:].rearrange("p (h t) -> p h t", h=6)
    g_yb = [0]
    gsl = [0]
    gzF3 = gzF[:].rearrange("p (h t) -> p h t", h=DH)
    bpT3 = bpT[:].rearrange("p (h n) -> p h n", h=DH)
    cpT3 = cpT[:].rearrange("p (h n) -> p h n", h=DH)

    wdT_blk = wdT_h.rearrange("(b p q) d -> b p (q d)", b=8, p=P)
    stat_blocks = []
    for mi, wsrc in enumerate([wgT_h, wuT_h, wdT_h]):
        for b in range(8):
            for q in range(4):
                if mi < 2:
                    ap_ = wsrc[b * P:(b + 1) * P, q * D:(q + 1) * D]
                else:
                    ap_ = wdT_blk[b][:, q * D:(q + 1) * D]
                stat_blocks.append((ap_, mi * 32 + b * 4 + q))
    stat_abs_ticks = []
    stats_gate = [0]

    def emit_stats_boundary(tc, g_free):
        slot_sems = (sW0, sW1, sWG)
        for k in range(24):
            blk = tc * 24 + k
            ap_, col = stat_blocks[blk]
            wt = ((sG, g_free),) if k == 0 else ()
            if blk >= 3:
                wt = wt + ((sS, stat_abs_ticks[blk - 3]),)
            sws = slot_sems[blk % 3]
            w_l = dma(sws, "sync", cmb[:, (blk % 3) * D:(blk % 3 + 1) * D],
                      ap_, wait=wt)
            kb.emit("scalar", lambda sc, blk=blk, col=col: sc.activation(
                out=cmb[:, (blk % 3) * D:(blk % 3 + 1) * D],
                in_=cmb[:, (blk % 3) * D:(blk % 3 + 1) * D],
                func=AF.Abs, accum_out=wstat[:, col:col + 1]),
                wait=((sws, w_l),), inc=((sS, sS.tick() and 1),))
            stat_abs_ticks.append(sS.n)
        stats_gate[0] = sS.n
    v_stats = sV.n
    if False and debug_scratch:
        def dbg_stats(v):
            v.tensor_copy(dbgc[:, 0:24], wstat[:])
            v.tensor_copy(dbgc[:, 24:32], colt[:])
            v.tensor_copy(dbgc[:, 32:33], swg[:])
            v.tensor_copy(dbgc[:, 33:34], cwg[:])
            v.tensor_copy(dbgc[:, 34:37], ptr0[:, 0:3])
            return v.tensor_copy(dbgc[:, 37:38], onec[:])
        kb.emit("vector", dbg_stats, wait=((sV, v_stats),), inc=((sV, sV.tick() and 1),))
        dma(sX2S, "gpsimd", dbg_st, dbgc[:], wait=((sV, sV.n),))

    dma(sPRM, "sync", aexp[:].rearrange("p (h n) -> p h n", h=DH),
        A_h.rearrange("(h p) n -> p h n", p=P))
    dma(sPRM, "sync", cw[:].rearrange("p (h j) -> p h j", h=DH),
        conv_w_h.rearrange("(h p) j -> p h j", p=P))
    dma(sPRM, "sync", cb[:], conv_b_h.rearrange("(h p) -> p h", p=P), allow_slow_non_contiguous=True)
    dma(sPRM, "sync", dD[:], Dd_h.rearrange("(h p) -> p h", p=P), allow_slow_non_contiguous=True)
    dma(sPRM, "sync", gbv[:], gb_h.rearrange("(h p) -> p h", p=P), allow_slow_non_contiguous=True)
    dma(sPRM, "sync", obv[:], ob_h.rearrange("(h p) -> p h", p=P), allow_slow_non_contiguous=True)
    dma(sPRM, "sync", bpb[:], Bpb_h[:, None])
    dma(sPRM, "sync", cpb[:], Cpb_h[:, None])
    dma(sPRM, "sync", lng_bc[:], lng_h[None, :].broadcast_to([P, D]))
    dma(sPRM, "sync", lnb_bc[:], lnb_h[None, :].broadcast_to([P, D]))
    in_params = sPRM.n
    dma(sPRM, "sync", bpT3, BpT_h.rearrange("(h p) n -> p h n", p=P))
    dma(sPRM, "sync", cpT3, CpT_h.rearrange("(h p) n -> p h n", p=P))
    in_bc = sPRM.n
    for c in range(DH):
        dma(sPRM, "sync", gwT[:, c * D:(c + 1) * D], gwT_h[c * P:(c + 1) * P, :])
    for c in range(DH):
        dma(sPRM, "sync", owT[:, c * D:(c + 1) * D], owT_h[c * P:(c + 1) * P, :])
    in_w = sPRM.n
    PRM_ALL = sPRM.n

    kb.emit("gpsimd", lambda g: g.iota(oneh[:], [[1, NST], [0, P]],
                                       channel_multiplier=-1,
                                       allow_small_or_imprecise_dtypes=True))
    kb.emit("gpsimd", lambda g: g.memset(hstate[:], 0.0), inc=((sG, sG.tick() and 1),))
    g_tab = sG.n
    kb.emit("vector", lambda v: v.tensor_scalar(out=oneh[:], in0=oneh[:], scalar1=0.0,
                                                scalar2=None, op0=ALU.is_equal),
            wait=((sG, g_tab),), inc=((sV, sV.tick() and 1),))
    v_oneh = sV.n
    def aexp1(sc):
        inst = sc.activation(out=aexp[:], in_=aexp[:], func=AF.Exp)
        sc.drain()
        return inst
    kb.emit("scalar", aexp1,
            wait=((sPRM, PRM_ALL),), inc=((sS, sS.tick() and 1),))
    kb.emit("scalar", lambda sc: sc.activation(out=aexp[:], in_=aexp[:], func=AF.Exp,
                                               scale=-1.0), inc=((sS, sS.tick() and 1),))
    s_aexp = sS.n

    def ln_pipeline(src_ap, dst_ap, affine):
        kb.emit("scalar", lambda sc: sc.activation(
            out=sqs[:], in_=src_ap, func=AF.Square, accum_out=lncol[:, 8:9]),
            wait=((sV, sV.n),), inc=((sS, sS.tick() and 1),))
        s_a = sS.n

        def cols1(v):
            v.reduce_sum(lncol[:, 9:10], src_ap, axis=AX.X)
            vfill(v)
            v.tensor_scalar(out=lncol[:, 10:11], in0=lncol[:, 9:10],
                            scalar1=1.0 / D, scalar2=None, op0=ALU.mult)
            v.tensor_scalar(out=lncol[:, 11:12], in0=lncol[:, 8:9],
                            scalar1=1.0 / D, scalar2=None, op0=ALU.mult)
            vfill(v)
            v.tensor_mul(lncol[:, 12:13], lncol[:, 10:11], lncol[:, 10:11])
            vfill(v)
            v.tensor_sub(lncol[:, 11:12], lncol[:, 11:12], lncol[:, 12:13])
            vfill(v)
            return v.tensor_scalar(out=lncol[:, 11:12], in0=lncol[:, 11:12],
                                   scalar1=LN_EPS, scalar2=LN_EPS,
                                   op0=ALU.add, op1=ALU.max)
        kb.emit("vector", cols1, wait=((sS, s_a),), inc=((sV, sV.tick() and 1),))
        v_a = sV.n
        kb.emit("scalar", lambda sc: sc.activation(
            out=lncol[:, 12:13], in_=lncol[:, 11:12], func=AF.Sqrt),
            wait=((sV, v_a),), inc=((sS, sS.tick() and 1),))
        s_b = sS.n

        def cols2(v):
            v.reciprocal(lncol[:, 13:14], lncol[:, 12:13])
            vfill(v)
            return v.scalar_tensor_tensor(out=lncol[:, 14:15], in0=lncol[:, 10:11],
                                          scalar=-1.0, in1=lncol[:, 13:14],
                                          op0=ALU.mult, op1=ALU.mult)
        kb.emit("vector", cols2, wait=((sS, s_b),), inc=((sV, sV.tick() and 1),))
        v_b = sV.n
        kb.emit("scalar", lambda sc: sc.activation(
            out=dst_ap, in_=src_ap, func=AF.Identity, scale=lncol[:, 13:14],
            bias=lncol[:, 14:15]), wait=((sV, v_b),), inc=((sS, sS.tick() and 1),))
        s_c = sS.n
        if affine is not None:
            gt, btl = affine

            def aff(v):
                v.tensor_mul(dst_ap, dst_ap, gt[:])
                return v.tensor_add(dst_ap, dst_ap, btl[:])
            kb.emit("vector", aff, wait=((sS, s_c),), inc=((sV, sV.tick() and 1),))


    def ln_cols4(src4, sq_wait, ct=None):
        if ct is None:
            ct = lncol

        def chain1(v, sq_wait=sq_wait, src4=src4, ct=ct):
            v.wait_ge(sS.h, sq_wait)
            v.reduce_sum(ct[:, 4:8], src4, axis=AX.X)
            vfill(v)
            v.tensor_scalar(out=ct[:, 8:12], in0=ct[:, 4:8],
                            scalar1=1.0 / D, scalar2=None, op0=ALU.mult)
            v.tensor_scalar(out=ct[:, 4:8], in0=ct[:, 0:4],
                            scalar1=1.0 / D, scalar2=None, op0=ALU.mult)
            v.tensor_mul(ct[:, 12:16], ct[:, 8:12], ct[:, 8:12])
            vfill(v)
            return v.scalar_tensor_tensor(out=ct[:, 4:8], in0=ct[:, 12:16],
                                          scalar=-1.0, in1=ct[:, 4:8],
                                          op0=ALU.mult, op1=ALU.add)
        kb.emit("vector", chain1, inc=((sV, sV.tick() and 1),))
        v_a = sV.n
        kb.emit("scalar", lambda sc, ct=ct: sc.activation(
            out=ct[:, 12:16], in_=ct[:, 4:8], func=AF.Sqrt, bias=epsc[:]),
            wait=((sV, v_a), (sG, g_tab0)), inc=((sS, sS.tick() and 1),))
        s_b = sS.n

        def chain2(v, s_b=s_b, ct=ct):
            v.wait_ge(sS.h, s_b)
            v.reciprocal(ct[:, 0:4], ct[:, 12:16])
            vfill(v)
            return v.scalar_tensor_tensor(out=ct[:, 4:8], in0=ct[:, 8:12],
                                          scalar=-1.0, in1=ct[:, 0:4],
                                          op0=ALU.mult, op1=ALU.mult)
        kb.emit("vector", chain2, inc=((sV, sV.tick() and 1),))
        return sV.n

    sp_x2_last = 0
    sp_xq_last = 0
    x4n = ut[:].rearrange("p (s d) -> p s d", s=SUB)

    def ln1_stats(t_chunk, extra_wait):
        w = dma(sXA, "sync", x4n,
                x_h[t_chunk * TT:(t_chunk + 1) * TT, :].rearrange(
                    "(s p) d -> p s d", p=P), wait=extra_wait)
        for s_ in range(SUB):
            kb.emit("scalar", lambda sc, s_=s_: sc.activation(
                out=sqs[:], in_=x4n[:, s_, :], func=AF.Square,
                accum_out=lncol2[:, s_:s_ + 1]),
                wait=((sXA, w),) if s_ == 0 else (), inc=((sS, sS.tick() and 1),))
        return ln_cols4(x4n, sS.n, ct=lncol2)

    v_ln1stats = ln1_stats(0, ())

    for tc in range(NCH):
        if tc == 0:
            kb.emit("vector", lambda v: v.memset(ln1F3[:, :, 0:8], 0.0),
                    wait=((sG, g_tab),), inc=((sV, sV.tick() and 1),))
        else:
            kb.emit("vector", lambda v: v.tensor_copy(ln1F3[:, :, 0:8],
                                                      ln1F3[:, :, TT:TT + 8]),
                    inc=((sV, sV.tick() and 1),))
        v_lead = sV.n

        t_tr_last = 0
        for s_ in range(SUB):
            kb.emit("scalar", lambda sc, s_=s_: sc.activation(
                out=etok[:], in_=x4n[:, s_, :], func=AF.Identity,
                scale=lncol2[:, s_:s_ + 1], bias=lncol2[:, 4 + s_:5 + s_]),
                wait=((sV, v_ln1stats), (sT, t_tr_last)), inc=((sS, sS.tick() and 1),))
            s_ln1 = sS.n
            for c in range(DH):
                bank = ptr0 if (c % 2 == 0) else ptr1
                kb.emit("tensor", lambda t, c=c, bank=bank: t.transpose(
                    bank[:, 0:P], etok[:, c * P:(c + 1) * P], idf[:]),
                    wait=((sS, s_ln1), (sV, max(v_tab, v_lead)), (sV, sV.n)),
                    inc=((sT, sT.tick() and 1),))
                t_tr = sT.n
                kb.emit("vector", lambda v, c=c, s_=s_, bank=bank: v.tensor_copy(
                    ln1F3[:, c, 8 + s_ * P:8 + (s_ + 1) * P], bank[:, 0:P]),
                    wait=((sT, t_tr),), inc=((sV, sV.tick() and 1),))
            t_tr_last = sT.n
        v_ln1f = sV.n

        def conv_emit(v):
            inst = None
            for c in range(DH):
                v.tensor_scalar(out=ut3[:, c, :], in0=ln1F3[:, c, 5:5 + TT],
                                scalar1=cw[:, c * DCONV:c * DCONV + 1],
                                scalar2=None, op0=ALU.mult)
                for j in (1, 2, 3):
                    inst = v.scalar_tensor_tensor(
                        out=ut3[:, c, :], in0=ln1F3[:, c, 5 + j:5 + j + TT],
                        scalar=cw[:, c * DCONV + j:c * DCONV + j + 1],
                        in1=ut3[:, c, :], op0=ALU.mult, op1=ALU.add)
            return inst
        kb.emit("vector", conv_emit,
                wait=((sV, v_ln1f), (sS, sS.n)), inc=((sV, sV.tick() and 1),))
        v_conv = sV.n

        def conv_silu(sc):
            inst = None
            for c in range(DH):
                inst = sc.activation(out=ut3[:, c, :], in_=ut3[:, c, :],
                                     func=AF.Silu, bias=cb[:, c:c + 1])
            return inst
        kb.emit("scalar", conv_silu, wait=((sV, v_conv),), inc=((sS, sS.tick() and 1),))
        s_u = sS.n
        if debug_scratch and tc == 1:
            spu = dma(sX2S, "gpsimd", dbg_u, ut[:], wait=((sS, s_u),))

        mm_group([(pbt[:, 0:TT], bpT3[:, c, :], ut3[:, c, :]) for c in range(DH)],
                 wait=((sS, s_u), (sPRM, PRM_ALL), (sS, sS.n)), inc=((sT, sT.tick() and 1),))
        t_bt = sT.n
        mm_group([(pbt[:, TT:2 * TT], cpT3[:, c, :], ut3[:, c, :]) for c in range(DH)],
                 wait=(), inc=((sT, sT.tick() and 1),))
        t_ct = sT.n
        kb.emit("scalar", lambda sc: sc.activation(out=btT[:], in_=pbt[:, 0:TT],
                                                   func=AF.Identity, bias=bpb[:]),
                wait=((sT, t_bt),), inc=((sS, sS.tick() and 1),))
        kb.emit("scalar", lambda sc: sc.activation(out=ctT[:], in_=pbt[:, TT:2 * TT],
                                                   func=AF.Identity, bias=cpb[:]),
                wait=((sT, t_ct),), inc=((sS, sS.tick() and 1),))
        s_btct = sS.n
        if debug_scratch and tc == 1:
            dma(sX2S, "gpsimd", dbg_bt[0:NST, 0:TT], btT[:], wait=((sS, s_btct),))
            dma(sX2S, "gpsimd", dbg_bt[0:NST, TT:2 * TT], ctT[:], wait=((sS, s_btct),))

        kb.emit("vector", lambda v: v.tensor_tensor(
            yy3, ut3, dD[:, :, None].broadcast_to([P, DH, TT]), ALU.mult),
            wait=((sS, s_u),), inc=((sV, sV.tick() and 1),))

        s_gate_evacs = []
        for n in range(NST):
            pb_ = pA if (n % 2 == 0) else pB
            pc_ = pC if (n % 2 == 0) else pD
            kb.emit("tensor", lambda t, n=n, pb_=pb_: t.matmul(
                pb_[:], oneh[:, n * P:(n + 1) * P], btT[:], start=True, stop=True),
                wait=((sS, s_btct), (sV, max(v_oneh, sV.n)), (sS, s_aexp)),
                inc=((sT, sT.tick() and 1),))
            t_b = sT.n
            kb.emit("tensor", lambda t, n=n, pc_=pc_: t.matmul(
                pc_[:], oneh[:, n * P:(n + 1) * P], ctT[:], start=True, stop=True),
                inc=((sT, sT.tick() and 1),))
            t_c = sT.n
            if 1 <= n <= DH:
                g = n - 1
                pg = ptr0 if (g % 2 == 0) else ptr1
                bank_free = () if g < 2 else ((sS, s_gate_evacs[g - 2]),)
                mm_group([(pg[:], gwT[:, c * D + g * P:c * D + (g + 1) * P],
                           ln1F3[:, c, 8:8 + TT]) for c in range(DH)],
                         wait=((sV, v_ln1f), (sPRM, PRM_ALL)) + bank_free,
                         inc=((sT, sT.tick() and 1),))
                t_g = sT.n
                kb.emit("scalar", lambda sc, g=g, pg=pg: sc.activation(
                    out=gzF3[:, g, :], in_=pg[:], func=AF.Silu, bias=gbv[:, g:g + 1]),
                    wait=((sT, t_g),), inc=((sS, sS.tick() and 1),))
                s_gate_evacs.append(sS.n)

            if gsl[0] == 0:
                g_yb[0] = sG.n
            base = gsl[0]
            sgate = stats_gate[0] if n == 0 else 0
            slice_ticks = [sV.tick() for _ in range(DH)]
            slots = [(base + c) % 6 for c in range(DH)]
            rws = [g_yb[0] + (base + c - 6) + 1 if base + c >= 6 else 0
                   for c in range(DH)]

            def scan_n(v, n=n, pb_=pb_, pc_=pc_, t_b=t_b, t_c=t_c,
                       slots=slots, rws=rws, sgate=sgate):
                v.wait_ge(sT.h, t_b)
                v.tensor_tensor(ub3, ut3, pb_[:, None, :].broadcast_to([P, DH, TT]),
                                ALU.mult)
                for c in range(DH):
                    v.tensor_tensor_scan(
                        out=hh3[:, c, :],
                        data0=aexp[:, c * NST + n:c * NST + n + 1].broadcast_to([P, TT]),
                        data1=ub3[:, c, :],
                        initial=hstate[:, n * DH + c:n * DH + c + 1],
                        op0=ALU.mult, op1=ALU.add)
                v.wait_ge(sT.h, t_c)
                if sgate:
                    v.wait_ge(sS.h, sgate)
                for c in range(DH):
                    if rws[c]:
                        v.wait_ge(sG.h, rws[c])
                    v.tensor_tensor(cmb3[:, slots[c], :], hh3[:, c, :], pc_[:],
                                    ALU.mult).then_inc(sV.h, 1)
                return v.tensor_copy(
                    hstate[:].rearrange("p (n h) -> p n h", n=NST)[:, n, :],
                    hh3[:, :, TT - 1])
            kb.emit("vector", scan_n, inc=((sV, sV.tick() and 1),))
            for c in range(DH):
                kb.emit("gpsimd", lambda g, c=c, sl=slots[c]: g.tensor_tensor(
                    yy3[:, c, :], yy3[:, c, :], cmb3[:, sl, :], ALU.add),
                    wait=((sV, slice_ticks[c]),), inc=((sG, sG.tick() and 1),))
                assert sG.n == g_yb[0] + base + c + 1
            gsl[0] = base + DH
        v_scan = sV.n
        g_yacc = sG.n
        dbg_w_y = 0
        if debug_scratch and tc == 1:
            dbg_w_y = dma(sX2S, "gpsimd", dbg_y, yy[:], wait=((sV, v_scan),))

        s_gate = s_gate_evacs[-1]
        dbg_w_g = 0
        if debug_scratch and tc == 1:
            dbg_w_g = dma(sX2S, "gpsimd", dbg_g, gzF[:], wait=((sS, s_gate),))

        kb.emit("vector", lambda v: v.tensor_mul(yy[:], yy[:], gzF[:]),
                wait=((sS, s_gate), (sX2S, dbg_w_y), (sG, g_yacc)),
                inc=((sV, sV.tick() and 1),))
        v_yg = sV.n

        z_evacs = []
        for g in range(DH):
            po = pC if (g % 2 == 0) else pD
            bank_free = ((sV, z_evacs[g - 2]),) if g >= 2 else ()
            mm_group([(po[:], owT[:, c * D + g * P:c * D + (g + 1) * P],
                       yy3[:, c, :]) for c in range(DH)],
                     wait=((sV, v_yg),) + bank_free, inc=((sT, sT.tick() and 1),))
            t_o = sT.n

            def z_evac(v, g=g, po=po, t_o=t_o, dbg_w_g=dbg_w_g):
                v.wait_ge(sT.h, t_o)
                if dbg_w_g:
                    v.wait_ge(sX2S.h, dbg_w_g)
                return v.scalar_tensor_tensor(out=gzF3[:, g, :], in0=po[:],
                                              scalar=obv[:, g:g + 1],
                                              in1=ln1F3[:, g, 8:8 + TT],
                                              op0=ALU.add, op1=ALU.add)
            kb.emit("vector", z_evac, inc=((sV, sV.tick() and 1),))
            z_evacs.append(sV.n)
        v_z = sV.n
        if debug_scratch and tc == 1:
            dbg_w_z = dma(sX2S, "gpsimd", dbg_z, gzF[:], wait=((sV, v_z),))
            sp_x2_last = max(sp_x2_last, dbg_w_z)

        if tc + 1 < NCH:
            v_ln1stats = ln1_stats(tc + 1, ((sV, v_yg),))
        emit_stats_boundary(tc, g_yacc)

        zt4 = hh[:].rearrange("p (s d) -> p s d", s=SUB)
        et4 = ub[:].rearrange("p (s d) -> p s d", s=SUB)
        xr4 = yy[:].rearrange("p (s d) -> p s d", s=SUB)
        qt4 = yy[:].bitcast(BF16)[:, 0:SUB * D].rearrange("p (s d) -> p s d", s=SUB)
        rows = slice(tc * TT, (tc + 1) * TT)
        for s_ in range(SUB):
            for c in range(DH):
                bank = ptr0 if (c % 2 == 0) else ptr1
                kb.emit("tensor", lambda t, c=c, s_=s_, bank=bank: t.transpose(
                    bank[:, 0:P], gzF3[:, c, s_ * P:(s_ + 1) * P], idf[:]),
                    wait=((sV, v_z), (sS, sS.n)), inc=((sT, sT.tick() and 1),))
                t_tr = sT.n
                kb.emit("scalar", lambda sc, c=c, s_=s_, bank=bank: sc.activation(
                    out=hh[:, s_ * D + c * P:s_ * D + (c + 1) * P],
                    in_=bank[:, 0:P], func=AF.Copy),
                    wait=((sT, t_tr), (sX2S, sp_x2_last), (sV, sV.n)),
                    inc=((sS, sS.tick() and 1),))
        for s_ in range(SUB):
            kb.emit("scalar", lambda sc, s_=s_: sc.activation(
                out=sqs[:], in_=zt4[:, s_, :], func=AF.Square,
                accum_out=lncol[:, s_:s_ + 1]), inc=((sS, sS.tick() and 1),))
        v_cols = ln_cols4(zt4, sS.n)
        for s_ in range(SUB):
            kb.emit("scalar", lambda sc, s_=s_: sc.activation(
                out=et4[:, s_, :], in_=zt4[:, s_, :], func=AF.Identity,
                scale=lncol[:, s_:s_ + 1], bias=lncol[:, 4 + s_:5 + s_]),
                wait=((sV, v_cols),) if s_ == 0 else (), inc=((sS, sS.tick() and 1),))
        s_innorm = sS.n

        def aff4(v, s_innorm=s_innorm):
            v.wait_ge(sS.h, s_innorm)
            v.tensor_tensor(et4, et4, lng_bc[:, None, :].broadcast_to([P, SUB, D]),
                            ALU.mult)
            return v.tensor_tensor(et4, et4,
                                   lnb_bc[:, None, :].broadcast_to([P, SUB, D]),
                                   ALU.add)
        kb.emit("vector", aff4, inc=((sV, sV.tick() and 1),))
        for s_ in range(SUB):
            kb.emit("scalar", lambda sc, s_=s_: sc.activation(
                out=sqs[:], in_=et4[:, s_, :], func=AF.Square,
                accum_out=lncol[:, s_:s_ + 1]),
                wait=((sV, sV.n),) if s_ == 0 else (), inc=((sS, sS.tick() and 1),))
        v_cols = ln_cols4(et4, sS.n)
        for s_ in range(SUB):
            kb.emit("scalar", lambda sc, s_=s_: sc.activation(
                out=zt4[:, s_, :], in_=et4[:, s_, :], func=AF.Identity,
                scale=lncol[:, s_:s_ + 1], bias=lncol[:, 4 + s_:5 + s_]),
                wait=((sV, v_cols),) if s_ == 0 else (), inc=((sS, sS.tick() and 1),))
        s_onorm = sS.n
        w_xr = dma(sXB, "sync", xr4,
                   x_h[rows, :].rearrange("(s p) d -> p s d", p=P),
                   wait=((sV, v_z),))
        kb.emit("vector", lambda v: v.tensor_add(hh[:], hh[:], yy[:]),
                wait=((sS, s_onorm), (sXB, w_xr)), inc=((sV, sV.tick() and 1),))
        v_x2 = sV.n
        sp_x2_last = dma(sX2S, "gpsimd",
                         x2_d[rows, :].rearrange("(s p) d -> p s d", p=P),
                         zt4, wait=((sV, v_x2),))
        for s_ in range(SUB):
            kb.emit("scalar", lambda sc, s_=s_: sc.activation(
                out=sqs[:], in_=zt4[:, s_, :], func=AF.Square,
                accum_out=lncol[:, s_:s_ + 1]),
                wait=((sV, v_x2),) if s_ == 0 else (), inc=((sS, sS.tick() and 1),))
        v_cols = ln_cols4(zt4, sS.n)
        for s_ in range(SUB):
            kb.emit("scalar", lambda sc, s_=s_: sc.activation(
                out=et4[:, s_, :], in_=zt4[:, s_, :], func=AF.Identity,
                scale=lncol[:, s_:s_ + 1], bias=lncol[:, 4 + s_:5 + s_]),
                wait=((sV, v_cols),) if s_ == 0 else (), inc=((sS, sS.tick() and 1),))
        s_ln3 = sS.n

        def q_cols4(v, s_ln3=s_ln3, tc=tc):
            v.wait_ge(sS.h, s_ln3)
            v.tensor_reduce(lncol[:, 0:4], et4, axis=AX.X, op=ALU.max,
                            apply_absolute_value=True)
            v.drain()
            v.tensor_scalar(out=lncol[:, 0:4], in0=lncol[:, 0:4],
                            scalar1=1e-5, scalar2=None, op0=ALU.max)
            v.drain()
            v.reciprocal(lncol[:, 4:8], lncol[:, 0:4])
            v.drain()
            v.tensor_scalar(out=lncol[:, 8:12], in0=lncol[:, 4:8],
                            scalar1=127.0, scalar2=None, op0=ALU.mult)
            return v.tensor_scalar(out=dsx[:, tc * SUB:(tc + 1) * SUB],
                                   in0=lncol[:, 0:4], scalar1=1.0 / 127.0,
                                   scalar2=None, op0=ALU.mult)
        kb.emit("vector", q_cols4, inc=((sV, sV.tick() and 1),))
        v_qc = sV.n
        for s_ in range(SUB):
            kb.emit("scalar", lambda sc, s_=s_: sc.activation(
                out=zt4[:, s_, :], in_=et4[:, s_, :], func=AF.Copy,
                scale=lncol[:, 8 + s_:9 + s_], bias=RB),
                wait=(((sV, v_qc), (sX2S, sp_x2_last)) if s_ == 0 else ()),
                inc=((sS, sS.tick() and 1),))
            s_q1 = sS.n

            def q_fin(v, s_=s_, s_q1=s_q1):
                v.wait_ge(sS.h, s_q1)
                v.tensor_copy(h16[:], zt4[:, s_, :])
                return v.tensor_scalar(out=qt4[:, s_, :], in0=h16[:], scalar1=-RB,
                                       scalar2=None, op0=ALU.add)
            kb.emit("vector", q_fin, inc=((sV, sV.tick() and 1),))
        v_q = sV.n
        sp_xq_last = dma(sXQS, "gpsimd",
                         xq_d[rows, :].rearrange("(s p) d -> p s d", p=P),
                         qt4, wait=((sV, v_q),))

    p2_V, p2_S, p2_T = sV.n, sS.n, sT.n
    p2_X2S, p2_XQS = sX2S.n, sXQS.n

    s_stats = stat_abs_ticks[-1]
    for mi in range(3):
        kb.emit("vector",
                lambda v, mi=mi: v.reduce_sum(colt[:, mi:mi + 1],
                                              wstat[:, mi * 32:(mi + 1) * 32],
                                              axis=AX.X),
                wait=((sS, s_stats),), inc=((sV, sV.tick() and 1),))
    v_sr = sV.n
    kb.emit("tensor", lambda t: t.matmul(pbt[0:1, 0:3], onec[:], colt[:, 0:3],
                                         start=True, stop=True),
            wait=((sV, v_sr),), inc=((sT, sT.tick() and 1),))
    t_s1 = sT.n
    kb.emit("vector", lambda v: v.tensor_copy(colt[0:1, 4:7], pbt[0:1, 0:3]),
            wait=((sT, t_s1),), inc=((sV, sV.tick() and 1),))
    v_s2 = sV.n
    kb.emit("tensor", lambda t: t.matmul(ptr0[:, 0:3], onerow[:],
                                         colt[0:1, 4:7], start=True, stop=True),
            wait=((sV, v_s2),), inc=((sT, sT.tick() and 1),))
    t_s2 = sT.n

    def stat_fin(v):
        v.tensor_scalar(out=colt[:, 0:3], in0=ptr0[:, 0:3],
                        scalar1=1.0 / (D * INNER), scalar2=1e-5,
                        op0=ALU.mult, op1=ALU.max)
        v.drain()
        v.tensor_copy(cwg[:], colt[:, 0:1])
        v.tensor_copy(cwu[:], colt[:, 1:2])
        v.tensor_copy(cwd[:], colt[:, 2:3])
        v.drain()
        v.reciprocal(swg[:], cwg[:])
        v.reciprocal(swu[:], cwu[:])
        return v.reciprocal(swd[:], cwd[:])
    kb.emit("vector", stat_fin, wait=((sT, t_s2),), inc=((sV, sV.tick() and 1),))
    v_stats = sV.n

    p2.close()
    p3 = contextlib.ExitStack()
    sb3 = mk(p3)
    wstg = [sb3("wstg0", [P, INNER]), sb3("wstg1", [P, INNER])]
    wgq = sb3("wgq", [P, DH * INNER], BF16)
    wuq = sb3("wuq", [P, DH * INNER], BF16)
    f16b = sb3("f16b", [P, INNER], F16)
    gu = sb3("gu", [P, INNER])
    gu1 = sb3("gu1", [P, TT])
    guq = sb3("guq", [P, INNER], BF16)
    xqc = sb3("xqc", [P, DH * P], BF16)

    PHW = ((sV, p2_V), (sS, p2_S), (sT, p2_T), (sX2S, p2_X2S), (sXQS, p2_XQS))

    s_wprev = 0
    first = True
    for (wdst, src, scol) in [(wgq, wgT_h, swg), (wuq, wuT_h, swu)]:
        for c in range(DH):
            stg = wstg[c % 2]
            sws = sW0 if c % 2 == 0 else sW1
            w_l = dma(sws, "sync", stg[:], src[c * P:(c + 1) * P, :],
                      wait=(PHW if first else ((sS, s_wprev),)))
            first = False
            kb.emit("scalar", lambda sc, stg=stg, scol=scol: sc.activation(
                out=gu[:], in_=stg[:], func=AF.Identity, scale=scol[:]),
                wait=((sws, w_l), (sV, max(v_stats, sV.n))),
                inc=((sS, sS.tick() and 1),))
            s_wq = sS.n
            s_wprev = s_wq

            def w_fin(v, wdst=wdst, c=c):
                v.tensor_scalar(out=f16b[:], in0=gu[:], scalar1=-0.5, scalar2=None,
                                op0=ALU.is_lt)
                return v.scalar_tensor_tensor(
                    out=wdst[:, c * INNER:(c + 1) * INNER], in0=gu[:], scalar=0.5,
                    in1=f16b[:], op0=ALU.is_gt, op1=ALU.subtract)
            kb.emit("vector", w_fin, wait=((sS, s_wq),), inc=((sV, sV.tick() and 1),))
    v_wq = sV.n

    if debug_scratch:
        dma(sX2S, "gpsimd", dbg_wgq, wgq[:], wait=((sV, v_wq),))
        def dbg_cp(v):
            v.tensor_copy(sqs[:, 0:24], wstat[:])
            v.tensor_copy(sqs[:, 24:32], colt[:])
            v.tensor_copy(sqs[:, 32:33], swg[:])
            v.tensor_copy(sqs[:, 33:34], swu[:])
            v.tensor_copy(sqs[:, 34:35], swd[:])
            v.tensor_copy(sqs[:, 35:36], cwg[:])
            v.tensor_copy(sqs[:, 36:37], cwu[:])
            return v.tensor_copy(sqs[:, 37:38], cwd[:])
        kb.emit("vector", dbg_cp, wait=((sV, v_wq),), inc=((sV, sV.tick() and 1),))
        dma(sX2S, "gpsimd", dbg_cols, sqs[:, 0:64], wait=((sV, sV.n),))
    xstg = sb3("xstg", [P, D], BF16)
    idb = sb3("idb", [P, P], BF16)
    kb.emit("vector", lambda v: v.tensor_copy(idb[:], idf[:]),
            wait=((sV, v_wq),), inc=((sV, sV.tick() and 1),))
    v_idb = sV.n
    sp_gq = 0
    s_xq_evacs = []
    for i in range(NTOK):
        trow = i * P
        wv0 = dma(sXQ, "sync", xstg[:], xq_d[trow:trow + P, :],
                  wait=(((sS, s_xq_evacs[-1] if s_xq_evacs else 0),
                         (sXQS, p2_XQS))))
        wv = wv0
        for c in range(DH):
            pt = ptr0 if c % 2 == 0 else ptr1
            bankw = ((sS, s_xq_evacs[-2],) if len(s_xq_evacs) >= 2 else (sS, 0))
            kb.emit("tensor", lambda t, c=c, pt=pt: t.transpose(
                pt[:, 0:64].bitcast(BF16), xstg[:, c * P:(c + 1) * P], idb[:]),
                wait=((sXQ, wv0), (sV, v_idb), bankw, (sT, sT.n)),
                inc=((sT, sT.tick() and 1),))
            t_tr = sT.n
            kb.emit("scalar", lambda sc, c=c, pt=pt: sc.activation(
                out=xqc[:, c * P:(c + 1) * P], in_=pt[:, 0:64].bitcast(BF16),
                func=AF.Copy), wait=((sT, t_tr),), inc=((sS, sS.tick() and 1),))
            s_xq_evacs.append(sS.n)

        def mk_desc(v, i=i):
            v.tensor_mul(lncol[:, 0:1], cwg[:], dsx[:, i:i + 1])
            v.tensor_mul(lncol[:, 1:2], cwu[:], dsx[:, i:i + 1])
            return v.drain()
        kb.emit("vector", mk_desc, wait=((sV, v_wq), (sS, sS.n)),
                inc=((sV, sV.tick() and 1),))
        v_ds = sV.n
        s_up_evacs = []
        for f in range(DH):
            pg = pA if (f % 2 == 0) else pB
            pu = pC if (f % 2 == 0) else pD
            bank_free = ((sS, s_up_evacs[f - 2]),) if f >= 2 else ()
            mm_group([(pg[:], xqc[:, c * P:(c + 1) * P],
                       wgq[:, c * INNER + f * TT:c * INNER + (f + 1) * TT])
                      for c in range(DH)],
                     wait=((sS, s_xq_evacs[-1]), (sV, v_ds)) + bank_free,
                     inc=((sT, sT.tick() and 1),))
            t_g = sT.n
            mm_group([(pu[:], xqc[:, c * P:(c + 1) * P],
                       wuq[:, c * INNER + f * TT:c * INNER + (f + 1) * TT])
                      for c in range(DH)],
                     wait=(), inc=((sT, sT.tick() and 1),))
            t_u = sT.n
            kb.emit("scalar", lambda sc, pg=pg: sc.activation(
                out=gu1[:], in_=pg[:], func=AF.Silu, scale=lncol[:, 0:1]),
                wait=((sT, t_g), (sV, sV.n)), inc=((sS, sS.tick() and 1),))
            kb.emit("scalar", lambda sc, pu=pu: sc.activation(
                out=sqs[:, 0:TT], in_=pu[:], func=AF.Copy, scale=lncol[:, 1:2]),
                wait=((sT, t_u), (sV, sV.n)), inc=((sS, sS.tick() and 1),))
            s_up = sS.n
            s_up_evacs.append(s_up)

            def gu_mul(v, f=f, s_up=s_up):
                v.wait_ge(sS.h, s_up)
                v.tensor_tensor(gu[:, f * TT:(f + 1) * TT], gu1[:], sqs[:, 0:TT],
                                ALU.mult)
                v.tensor_reduce(lncol[:, 2:3], gu[:, f * TT:(f + 1) * TT],
                                axis=AX.X, op=ALU.max, apply_absolute_value=True)
                v.drain()
                if f == 0:
                    return v.tensor_copy(lncol[:, 3:4], lncol[:, 2:3])
                return v.tensor_max(lncol[:, 3:4], lncol[:, 3:4], lncol[:, 2:3])
            kb.emit("vector", gu_mul, inc=((sV, sV.tick() and 1),))

        def gq_cols(v, i=i):
            v.tensor_scalar(out=lncol[:, 3:4], in0=lncol[:, 3:4], scalar1=1e-5,
                            scalar2=None, op0=ALU.max)
            v.drain()
            v.reciprocal(lncol[:, 4:5], lncol[:, 3:4])
            v.drain()
            v.tensor_scalar(out=lncol[:, 5:6], in0=lncol[:, 4:5], scalar1=127.0,
                            scalar2=None, op0=ALU.mult)
            return v.tensor_scalar(out=dsg[:, i:i + 1], in0=lncol[:, 3:4],
                                   scalar1=1.0 / 127.0, scalar2=None, op0=ALU.mult)
        kb.emit("vector", gq_cols, wait=((sS, sS.n),),
                inc=((sV, sV.tick() and 1),))
        v_gqc = sV.n
        if debug_scratch and i == 0:
            dma(sX2S, "gpsimd", dbg_xqc, xqc[:], wait=((sV, v_gqc),))
            dma(sX2S, "gpsimd", dbg_gu, gu[:], wait=((sV, v_gqc),))
        kb.emit("scalar", lambda sc: sc.activation(
            out=f16b[:], in_=gu[:], func=AF.Copy, scale=lncol[:, 5:6], bias=RB),
            wait=((sV, v_gqc),), inc=((sS, sS.tick() and 1),))
        s_gq = sS.n

        def g_fin(v, s_gq=s_gq, sp_gq=sp_gq):
            v.wait_ge(sS.h, s_gq)
            v.wait_ge(sGQS.h, sp_gq)
            return v.tensor_scalar(out=guq[:], in0=f16b[:], scalar1=-RB,
                                   scalar2=None, op0=ALU.add)
        kb.emit("vector", g_fin, inc=((sV, sV.tick() and 1),))
        v_gq = sV.n
        sp_gq = dma(sGQS, "gpsimd", gq_d[trow:trow + P, :], guq[:],
                    wait=((sV, v_gq),))
    p3_V, p3_S, p3_T, p3_GQS = sV.n, sS.n, sT.n, sGQS.n

    p3.close()
    p4 = contextlib.ExitStack()
    sb4 = mk(p4)
    wdq = sb4("wdq", [P, IH * D], BF16)
    gqT = sb4("gqT", [P, IH * P], BF16)
    gqT2 = sb4("gqT2", [P, IH * P], BF16)
    x2b = sb4("x2b", [P, SUB * D])
    o2b = sb4("o2b", [P, SUB * D])
    gstg = sb4("gstg", [P, INNER], BF16)
    idb4 = sb4("idb4", [P, P], BF16)
    wd16 = sb4("wd16", [P, D], F16)
    wdf = sb4("wdf", [P, D])
    wst4 = [sb4("wst4a", [P, D]), sb4("wst4b", [P, D])]

    PHW3 = ((sV, p3_V), (sS, p3_S), (sT, p3_T), (sGQS, p3_GQS))
    kb.emit("vector", lambda v: v.tensor_copy(idb4[:], idf[:]),
            wait=((sV, sV.n),), inc=((sV, sV.tick() and 1),))
    v_idb4 = sV.n
    s_gq_evacs = []
    v_wd_ticks = []
    s_wprev = 0
    first = True
    for c in range(IH):
        stg = wst4[c % 2]
        sws = sW0 if c % 2 == 0 else sW1
        w_l = dma(sws, "sync", stg[:, 0:D], wdT_h[c * P:(c + 1) * P, :],
                  wait=(PHW3 if first else ((sS, s_wprev),)))
        first = False
        kb.emit("scalar", lambda sc, stg=stg: sc.activation(
            out=wdf[:], in_=stg[:, 0:D], func=AF.Identity, scale=swd[:]),
            wait=((sws, w_l), (sV, sV.n)), inc=((sS, sS.tick() and 1),))
        s_wq = sS.n
        s_wprev = s_wq

        def wd_fin(v, c=c):
            v.tensor_scalar(out=wd16[:], in0=wdf[:], scalar1=-0.5, scalar2=None,
                            op0=ALU.is_lt)
            return v.scalar_tensor_tensor(
                out=wdq[:, c * D:(c + 1) * D], in0=wdf[:], scalar=0.5,
                in1=wd16[:], op0=ALU.is_gt, op1=ALU.subtract)
        kb.emit("vector", wd_fin, wait=((sS, s_wq),), inc=((sV, sV.tick() and 1),))
        v_wd_ticks.append(sV.n)
    v_wd = sV.n
    if debug_scratch:
        dma(sX2S, "gpsimd", dbg_wdq, wdq[:], wait=((sV, v_wd),))

    v_o_done = 0
    dm_out_prev = 0
    o2b4 = o2b[:].rearrange("p (s d) -> p s d", s=SUB)
    x2b4 = x2b[:].rearrange("p (s d) -> p s d", s=SUB)
    for gi in range(NTOK // SUB):
        rows = slice(gi * SUB * P, (gi + 1) * SUB * P)
        w_x2 = dma(sX2, "sync", x2b4,
                   x2_d[rows, :].rearrange("(s p) d -> p s d", p=P),
                   wait=((sV, v_o_done), (sX2S, p2_X2S)))
        def mk_dd(v, gi=gi):
            v.wait_ge(sV.h, v_wd)
            v.tensor_tensor(colt[:, 0:SUB], cwd[:].broadcast_to([P, SUB]),
                            dsg[:, gi * SUB:(gi + 1) * SUB], ALU.mult)
            return v.drain()
        kb.emit("vector", mk_dd, wait=((sS, sS.n),), inc=((sV, sV.tick() and 1),))
        v_dd = sV.n
        for il in range(SUB):
            i = gi * SUB + il
            trow = i * P
            gq_t = gqT if il % 2 == 0 else gqT2
            wvg = dma(sGQT, "sync", gstg[:], gq_d[trow:trow + P, :],
                      wait=(((sS, s_gq_evacs[-1] if s_gq_evacs else 0),
                             (sGQS, p3_GQS))))
            for c in range(IH):
                pt = pC if c % 2 == 0 else pD
                bankw = ((sS, s_gq_evacs[-2]) if len(s_gq_evacs) >= 2 else (sS, 0))
                kb.emit("tensor", lambda t, c=c, pt=pt, gq_t=gq_t: t.transpose(
                    pt[:, 0:64].bitcast(BF16), gstg[:, c * P:(c + 1) * P], idb4[:]),
                    wait=((sGQT, wvg), (sV, v_idb4), bankw, (sT, sT.n)),
                    inc=((sT, sT.tick() and 1),))
                t_tr = sT.n
                kb.emit("scalar", lambda sc, c=c, pt=pt, gq_t=gq_t: sc.activation(
                    out=gq_t[:, c * P:(c + 1) * P], in_=pt[:, 0:64].bitcast(BF16),
                    func=AF.Copy), wait=((sT, t_tr),), inc=((sS, sS.tick() and 1),))
                s_gq_evacs.append(sS.n)
            for g2 in range(2):
                pq = pA if (g2 % 2 == 0) else pB
                mmw = None
                wdw = ((sV, v_wd),) if (gi == 0 and il == 0) else ()
                mm_group([(pq[:], gq_t[:, c * P:(c + 1) * P],
                           wdq[:, c * D + g2 * TT:c * D + (g2 + 1) * TT])
                          for c in range(IH)],
                         wait=((sS, s_gq_evacs[-1]),) + wdw,
                         inc=((sT, sT.tick() and 1),), mm_waits=mmw)
                t_d = sT.n
                kb.emit("scalar", lambda sc, il=il, g2=g2, pq=pq: sc.activation(
                    out=o2b[:, il * D + g2 * TT:il * D + (g2 + 1) * TT], in_=pq[:],
                    func=AF.Copy, scale=colt[:, il:il + 1]),
                    wait=((sT, t_d), (sV, v_dd), (sOUT, dm_out_prev)),
                    inc=((sS, sS.tick() and 1),))
        s_o2 = sS.n
        for s_ in range(SUB):
            kb.emit("scalar", lambda sc, s_=s_: sc.activation(
                out=sqs[:], in_=o2b4[:, s_, :], func=AF.Square,
                accum_out=lncol[:, s_:s_ + 1]),
                wait=((sV, sV.n),) if s_ == 0 else (), inc=((sS, sS.tick() and 1),))
        v_cols = ln_cols4(o2b4, sS.n)
        for s_ in range(SUB):
            kb.emit("scalar", lambda sc, s_=s_: sc.activation(
                out=o2b4[:, s_, :], in_=o2b4[:, s_, :], func=AF.Identity,
                scale=lncol[:, s_:s_ + 1], bias=lncol[:, 4 + s_:5 + s_]),
                wait=((sV, v_cols),) if s_ == 0 else (), inc=((sS, sS.tick() and 1),))
        s_ln4 = sS.n
        kb.emit("vector", lambda v: v.tensor_add(o2b[:], o2b[:], x2b[:]),
                wait=((sS, s_ln4), (sX2, w_x2)), inc=((sV, sV.tick() and 1),))
        v_o_done = sV.n
        dm_out_prev = dma(sOUT, "gpsimd",
                          out_h[rows, :].rearrange("(s p) d -> p s d", p=P),
                          o2b4, wait=((sV, v_o_done),))

    kb.emit("gpsimd", lambda g: None, wait=((sOUT, sOUT.n),))
    p4.close()

    with top:
        with nc.Block() as block:
            kb.run(block)
    return nc


_NC_CACHE = None


def kernel(**inputs):
    global _NC_CACHE
    f32 = np.float32
    T = lambda a: np.ascontiguousarray(np.asarray(a, dtype=f32).T)
    C = lambda a: np.ascontiguousarray(np.asarray(a, dtype=f32))
    x = C(inputs["L0_attn_x"])
    common = {
        "conv_w": C(np.asarray(inputs["L0_attn_conv_w"], dtype=f32)[:, 0, :]),
        "conv_b": C(inputs["L0_attn_conv_b"]),
        "A": C(inputs["L0_attn_A"]),
        "BpT": T(inputs["L0_attn_Bp_w"]),
        "Bpb": C(inputs["L0_attn_Bp_b"]),
        "CpT": T(inputs["L0_attn_Cp_w"]),
        "Cpb": C(inputs["L0_attn_Cp_b"]),
        "Dd": C(inputs["L0_attn_D"]),
        "gwT": T(inputs["L0_attn_gate_w"]),
        "gb": C(inputs["L0_attn_gate_b"]),
        "owT": T(inputs["L0_attn_out_w"]),
        "ob": C(inputs["L0_attn_out_b"]),
        "lng": C(inputs["L0_attn_ln_g"]),
        "lnb": C(inputs["L0_attn_ln_b"]),
        "wgT": T(inputs["L0_ffn_w_gate"]),
        "wuT": T(inputs["L0_ffn_w_up"]),
        "wdT": T(inputs["L0_ffn_w_down"]),
    }
    in_maps = [dict(common, x=x[c]) for c in range(8)]
    if _NC_CACHE is None:
        _NC_CACHE = build_core()
    res = run_bass_kernel_spmd(_NC_CACHE, in_maps, core_ids=list(range(8)))
    return np.stack([np.asarray(r["out"], dtype=f32) for r in res.results])


# version 17
# speedup vs baseline: 1.0229x; 1.0060x over previous
import contextlib
import numpy as np
import concourse.bass as bass
import concourse.mybir as mybir
from concourse.bass_utils import run_bass_kernel_spmd

F32 = mybir.dt.float32
F16 = mybir.dt.float16
BF16 = mybir.dt.bfloat16
AF = mybir.ActivationFunctionType
ALU = mybir.AluOpType
AX = mybir.AxisListType

P = 128
S = 2048
D = 1024
DH = D // P
NST = 16
DCONV = 4
INNER = 4096
IH = INNER // P
TT = 512
NCH = S // TT
SUB = TT // P
NTOK = S // P
LN_EPS = 1e-5
RB = 1536.0


class Sem:
    def __init__(self, h):
        self.h = h
        self.n = 0

    def tick(self, k=1):
        self.n += k
        return self.n


class KB:
    ENGS = ("sync", "tensor", "vector", "scalar", "gpsimd")

    def __init__(self, nc):
        self.nc = nc
        self.q = {e: [] for e in self.ENGS}

    def emit(self, eng, fn, wait=(), inc=()):
        wait = tuple((s.h, v) for s, v in wait)
        inc = tuple((s.h, k) for s, k in inc)

        def g(h):
            for sh, v in wait:
                h.wait_ge(sh, v)
            inst = fn(h)
            for sh, k in inc:
                inst.then_inc(sh, k)
        self.q[eng].append(g)

    def run(self, block):
        for eng in self.ENGS:
            fns = self.q[eng]
            getattr(block, eng)(lambda h, fns=fns: [f(h) for f in fns])


def build_core(debug_scratch=False):
    nc = bass.Bass("TRN2", target_bir_lowering=False, detect_race_conditions=False)

    x_h = nc.dram_tensor("x", [S, D], F32, kind="ExternalInput").ap()
    conv_w_h = nc.dram_tensor("conv_w", [D, DCONV], F32, kind="ExternalInput").ap()
    conv_b_h = nc.dram_tensor("conv_b", [D], F32, kind="ExternalInput").ap()
    A_h = nc.dram_tensor("A", [D, NST], F32, kind="ExternalInput").ap()
    BpT_h = nc.dram_tensor("BpT", [D, NST], F32, kind="ExternalInput").ap()
    Bpb_h = nc.dram_tensor("Bpb", [NST], F32, kind="ExternalInput").ap()
    CpT_h = nc.dram_tensor("CpT", [D, NST], F32, kind="ExternalInput").ap()
    Cpb_h = nc.dram_tensor("Cpb", [NST], F32, kind="ExternalInput").ap()
    Dd_h = nc.dram_tensor("Dd", [D], F32, kind="ExternalInput").ap()
    gwT_h = nc.dram_tensor("gwT", [D, D], F32, kind="ExternalInput").ap()
    gb_h = nc.dram_tensor("gb", [D], F32, kind="ExternalInput").ap()
    owT_h = nc.dram_tensor("owT", [D, D], F32, kind="ExternalInput").ap()
    ob_h = nc.dram_tensor("ob", [D], F32, kind="ExternalInput").ap()
    lng_h = nc.dram_tensor("lng", [D], F32, kind="ExternalInput").ap()
    lnb_h = nc.dram_tensor("lnb", [D], F32, kind="ExternalInput").ap()
    wgT_h = nc.dram_tensor("wgT", [D, INNER], F32, kind="ExternalInput").ap()
    wuT_h = nc.dram_tensor("wuT", [D, INNER], F32, kind="ExternalInput").ap()
    wdT_h = nc.dram_tensor("wdT", [INNER, D], F32, kind="ExternalInput").ap()
    out_h = nc.dram_tensor("out", [S, D], F32, kind="ExternalOutput").ap()
    _scr = "ExternalOutput" if debug_scratch else "Internal"
    x2_d = nc.dram_tensor("x2_scr", [S, D], F32, kind=_scr).ap()
    xq_d = nc.dram_tensor("xq_scr", [S, D], BF16, kind=_scr).ap()
    gq_d = nc.dram_tensor("gq_scr", [S, INNER], BF16, kind=_scr).ap()
    if debug_scratch:
        dbg_st = nc.dram_tensor("dbg_st", [P, 40], F32, kind=_scr).ap()
        dbg_wgq = nc.dram_tensor("dbg_wgq", [P, DH * INNER], BF16, kind=_scr).ap()
        dbg_xqc = nc.dram_tensor("dbg_xqc", [P, DH * P], BF16, kind=_scr).ap()
        dbg_gu = nc.dram_tensor("dbg_gu", [P, INNER], F32, kind=_scr).ap()
        dbg_cols = nc.dram_tensor("dbg_cols", [P, 64], F32, kind=_scr).ap()
        dbg_wdq = nc.dram_tensor("dbg_wdq", [P, IH * D], BF16, kind=_scr).ap()
        dbg_u = nc.dram_tensor("dbg_u", [P, DH * TT], F32, kind=_scr).ap()
        dbg_bt = nc.dram_tensor("dbg_bt", [NST, 2 * TT], F32, kind=_scr).ap()
        dbg_y = nc.dram_tensor("dbg_y", [P, DH * TT], F32, kind=_scr).ap()
        dbg_g = nc.dram_tensor("dbg_g", [P, DH * TT], F32, kind=_scr).ap()
        dbg_z = nc.dram_tensor("dbg_z", [P, DH * TT], F32, kind=_scr).ap()
        dbg_e = nc.dram_tensor("dbg_e", [P, 40], F32, kind=_scr).ap()

    kb = KB(nc)
    top = contextlib.ExitStack()

    def mk(es_):
        sbf = lambda n, shp, dt=F32: es_.enter_context(nc.sbuf_tensor(n, shp, dt))
        return sbf
    sb = mk(top)
    sem = lambda n: Sem(top.enter_context(nc.semaphore(n)))

    sIN = sem("sIN")
    sSP = sem("sSP")
    sXA = sem("sXA")
    sXB = sem("sXB")
    sW0 = sem("sW0")
    sW1 = sem("sW1")
    sWG = sem("sWG")
    sPRM = sem("sPRM")
    sXQ = sem("sXQ")
    sGQT = sem("sGQT")
    sX2 = sem("sX2")
    sX2S = sem("sX2S")
    sXQS = sem("sXQS")
    sGQS = sem("sGQS")
    sOUT = sem("sOUT")
    sT = sem("sT")
    sV = sem("sV")
    sS = sem("sS")
    sG = sem("sG")
    dmo = sem("dmo")

    pes = contextlib.ExitStack()
    top.enter_context(pes)
    ptr0 = pes.enter_context(nc.psum_tensor("ptr0", [P, TT], F32))
    ptr1 = pes.enter_context(nc.psum_tensor("ptr1", [P, TT], F32))
    pA = pes.enter_context(nc.psum_tensor("pA", [P, TT], F32))
    pB = pes.enter_context(nc.psum_tensor("pB", [P, TT], F32))
    pC = pes.enter_context(nc.psum_tensor("pC", [P, TT], F32))
    pD = pes.enter_context(nc.psum_tensor("pD", [P, TT], F32))
    pbt = pes.enter_context(nc.psum_tensor("pbt", [NST, 2 * TT], F32))

    def mm_group(plist, wait, inc, mm_waits=None):
        n = len(plist)

        def fn(t):
            inst = None
            for i, (o, l, r) in enumerate(plist):
                if mm_waits is not None and mm_waits[i]:
                    for sh, val in mm_waits[i]:
                        t.wait_ge(sh, val)
                inst = t.matmul(o, l, r, start=(i == 0),
                                stop=(i % 4 == 3 or i == n - 1),
                                skip_group_check=(i >= 4))
            return inst
        kb.emit("tensor", fn, wait=wait, inc=inc)

    def vfill(v):
        v.tensor_scalar(out=jnk[:, 0:1], in0=onec[:], scalar1=1.0,
                        scalar2=None, op0=ALU.mult)
        v.tensor_scalar(out=jnk[:, 1:2], in0=onec[:], scalar1=1.0,
                        scalar2=None, op0=ALU.mult)

    def dma(s_, eng, out_ap, in_ap, wait=(), **kw):
        v = s_.tick(16)
        kb.emit(eng, lambda e, o=out_ap, i=in_ap, kw=kw: e.dma_start(out=o, in_=i, **kw),
                wait=wait, inc=((s_, 16),))
        return v

    sqs = sb("sqs", [P, D])
    lncol = sb("lncol", [P, 16])
    colt = sb("colt", [P, 8])
    onec = sb("onec", [P, 1])
    jnk = sb("jnk", [P, 2])
    onerow = sb("onerow", [1, P])
    epsc = sb("epsc", [P, 1])
    idf = sb("idf", [P, P])
    wstat = sb("wstat", [P, 96])
    swg, swu, swd = sb("swg", [P, 1]), sb("swu", [P, 1]), sb("swd", [P, 1])
    cwg, cwu, cwd = sb("cwg", [P, 1]), sb("cwu", [P, 1]), sb("cwd", [P, 1])
    dsx = sb("dsx", [P, NTOK])
    dsg = sb("dsg", [P, NTOK])
    h16 = sb("h16", [P, D], F16)
    dbgc = sb("dbgc", [P, 40]) if debug_scratch else None

    kb.emit("gpsimd", lambda g: g.iota(idf[:], [[1, P]], channel_multiplier=-1,
                                       allow_small_or_imprecise_dtypes=True))
    kb.emit("gpsimd", lambda g: g.memset(onec[:], 1.0))
    kb.emit("gpsimd", lambda g: g.memset(onerow[:], 1.0))
    kb.emit("gpsimd", lambda g: g.memset(epsc[:], LN_EPS), inc=((sG, sG.tick() and 1),))
    g_tab0 = sG.n
    kb.emit("vector", lambda v: v.tensor_scalar(out=idf[:], in0=idf[:], scalar1=0.0,
                                                scalar2=None, op0=ALU.is_equal),
            wait=((sG, g_tab0),), inc=((sV, sV.tick() and 1),))
    v_tab = sV.n


    p2 = contextlib.ExitStack()
    sb2 = mk(p2)
    aexp = sb2("aexp", [P, DH * NST])
    cw = sb2("cw", [P, DH * DCONV])
    cb = sb2("cb", [P, DH])
    dD = sb2("dD", [P, DH])
    gbv = sb2("gbv", [P, DH])
    obv = sb2("obv", [P, DH])
    bpb = sb2("bpb", [NST, 1])
    cpb = sb2("cpb", [NST, 1])
    lng_bc = sb2("lng_bc", [P, D])
    lnb_bc = sb2("lnb_bc", [P, D])
    oneh = sb2("oneh", [NST, NST * P])
    cmb = sb2("cmb", [P, 6 * TT])
    hstate = sb2("hstate", [P, NST * DH])
    gwT = sb2("gwT_s", [P, DH * D])
    owT = sb2("owT_s", [P, DH * D])
    bpT = sb2("bpT", [P, DH * NST])
    cpT = sb2("cpT", [P, DH * NST])
    LF = TT + 8
    ln1F = sb2("ln1F", [P, DH * LF])
    ut = sb2("ut", [P, DH * TT])
    ub = sb2("ub", [P, DH * TT])
    hh = sb2("hh", [P, DH * TT])
    yy = sb2("yy", [P, DH * TT])
    gzF = sb2("gzF", [P, DH * TT])
    btT = sb2("btT", [NST, TT])
    ctT = sb2("ctT", [NST, TT])
    etok = sb2("etok", [P, D])
    lncol2 = sb2("lncol2", [P, 16])

    ln1F3 = ln1F[:].rearrange("p (h t) -> p h t", h=DH)
    ut3 = ut[:].rearrange("p (h t) -> p h t", h=DH)
    ub3 = ub[:].rearrange("p (h t) -> p h t", h=DH)
    hh3 = hh[:].rearrange("p (h t) -> p h t", h=DH)
    yy3 = yy[:].rearrange("p (h t) -> p h t", h=DH)
    cmb3 = cmb[:].rearrange("p (h t) -> p h t", h=6)
    g_yb = [0]
    gsl = [0]
    gzF3 = gzF[:].rearrange("p (h t) -> p h t", h=DH)
    bpT3 = bpT[:].rearrange("p (h n) -> p h n", h=DH)
    cpT3 = cpT[:].rearrange("p (h n) -> p h n", h=DH)

    wdT_blk = wdT_h.rearrange("(b p q) d -> b p (q d)", b=8, p=P)
    stat_blocks = []
    for mi, wsrc in enumerate([wgT_h, wuT_h, wdT_h]):
        for b in range(8):
            for q in range(4):
                if mi < 2:
                    ap_ = wsrc[b * P:(b + 1) * P, q * D:(q + 1) * D]
                else:
                    ap_ = wdT_blk[b][:, q * D:(q + 1) * D]
                stat_blocks.append((ap_, mi * 32 + b * 4 + q))
    stat_abs_ticks = []
    stats_gate = [0]

    def emit_stats_boundary(tc, g_free):
        slot_sems = (sW0, sW1, sWG)
        for k in range(24):
            blk = tc * 24 + k
            ap_, col = stat_blocks[blk]
            wt = ((sG, g_free),) if k == 0 else ()
            if blk >= 3:
                wt = wt + ((sS, stat_abs_ticks[blk - 3]),)
            sws = slot_sems[blk % 3]
            w_l = dma(sws, "sync", cmb[:, (blk % 3) * D:(blk % 3 + 1) * D],
                      ap_, wait=wt)
            kb.emit("scalar", lambda sc, blk=blk, col=col: sc.activation(
                out=cmb[:, (blk % 3) * D:(blk % 3 + 1) * D],
                in_=cmb[:, (blk % 3) * D:(blk % 3 + 1) * D],
                func=AF.Abs, accum_out=wstat[:, col:col + 1]),
                wait=((sws, w_l),), inc=((sS, sS.tick() and 1),))
            stat_abs_ticks.append(sS.n)
        stats_gate[0] = sS.n
    v_stats = sV.n
    if False and debug_scratch:
        def dbg_stats(v):
            v.tensor_copy(dbgc[:, 0:24], wstat[:])
            v.tensor_copy(dbgc[:, 24:32], colt[:])
            v.tensor_copy(dbgc[:, 32:33], swg[:])
            v.tensor_copy(dbgc[:, 33:34], cwg[:])
            v.tensor_copy(dbgc[:, 34:37], ptr0[:, 0:3])
            return v.tensor_copy(dbgc[:, 37:38], onec[:])
        kb.emit("vector", dbg_stats, wait=((sV, v_stats),), inc=((sV, sV.tick() and 1),))
        dma(sX2S, "gpsimd", dbg_st, dbgc[:], wait=((sV, sV.n),))

    dma(sPRM, "sync", aexp[:].rearrange("p (h n) -> p h n", h=DH),
        A_h.rearrange("(h p) n -> p h n", p=P))
    dma(sPRM, "sync", cw[:].rearrange("p (h j) -> p h j", h=DH),
        conv_w_h.rearrange("(h p) j -> p h j", p=P))
    dma(sPRM, "sync", cb[:], conv_b_h.rearrange("(h p) -> p h", p=P), allow_slow_non_contiguous=True)
    dma(sPRM, "sync", dD[:], Dd_h.rearrange("(h p) -> p h", p=P), allow_slow_non_contiguous=True)
    dma(sPRM, "sync", gbv[:], gb_h.rearrange("(h p) -> p h", p=P), allow_slow_non_contiguous=True)
    dma(sPRM, "sync", obv[:], ob_h.rearrange("(h p) -> p h", p=P), allow_slow_non_contiguous=True)
    dma(sPRM, "sync", bpb[:], Bpb_h[:, None])
    dma(sPRM, "sync", cpb[:], Cpb_h[:, None])
    dma(sPRM, "sync", lng_bc[:], lng_h[None, :].broadcast_to([P, D]))
    dma(sPRM, "sync", lnb_bc[:], lnb_h[None, :].broadcast_to([P, D]))
    in_params = sPRM.n
    dma(sPRM, "sync", bpT3, BpT_h.rearrange("(h p) n -> p h n", p=P))
    dma(sPRM, "sync", cpT3, CpT_h.rearrange("(h p) n -> p h n", p=P))
    in_bc = sPRM.n
    for c in range(DH):
        dma(sPRM, "sync", gwT[:, c * D:(c + 1) * D], gwT_h[c * P:(c + 1) * P, :])
    for c in range(DH):
        dma(sPRM, "sync", owT[:, c * D:(c + 1) * D], owT_h[c * P:(c + 1) * P, :])
    in_w = sPRM.n
    PRM_ALL = sPRM.n

    kb.emit("gpsimd", lambda g: g.iota(oneh[:], [[1, NST], [0, P]],
                                       channel_multiplier=-1,
                                       allow_small_or_imprecise_dtypes=True))
    kb.emit("gpsimd", lambda g: g.memset(hstate[:], 0.0), inc=((sG, sG.tick() and 1),))
    g_tab = sG.n
    kb.emit("vector", lambda v: v.tensor_scalar(out=oneh[:], in0=oneh[:], scalar1=0.0,
                                                scalar2=None, op0=ALU.is_equal),
            wait=((sG, g_tab),), inc=((sV, sV.tick() and 1),))
    v_oneh = sV.n
    def aexp1(sc):
        inst = sc.activation(out=aexp[:], in_=aexp[:], func=AF.Exp)
        sc.drain()
        return inst
    kb.emit("scalar", aexp1,
            wait=((sPRM, PRM_ALL),), inc=((sS, sS.tick() and 1),))
    kb.emit("scalar", lambda sc: sc.activation(out=aexp[:], in_=aexp[:], func=AF.Exp,
                                               scale=-1.0), inc=((sS, sS.tick() and 1),))
    s_aexp = sS.n

    def ln_pipeline(src_ap, dst_ap, affine):
        kb.emit("scalar", lambda sc: sc.activation(
            out=sqs[:], in_=src_ap, func=AF.Square, accum_out=lncol[:, 8:9]),
            wait=((sV, sV.n),), inc=((sS, sS.tick() and 1),))
        s_a = sS.n

        def cols1(v):
            v.reduce_sum(lncol[:, 9:10], src_ap, axis=AX.X)
            vfill(v)
            v.tensor_scalar(out=lncol[:, 10:11], in0=lncol[:, 9:10],
                            scalar1=1.0 / D, scalar2=None, op0=ALU.mult)
            v.tensor_scalar(out=lncol[:, 11:12], in0=lncol[:, 8:9],
                            scalar1=1.0 / D, scalar2=None, op0=ALU.mult)
            vfill(v)
            v.tensor_mul(lncol[:, 12:13], lncol[:, 10:11], lncol[:, 10:11])
            vfill(v)
            v.tensor_sub(lncol[:, 11:12], lncol[:, 11:12], lncol[:, 12:13])
            vfill(v)
            return v.tensor_scalar(out=lncol[:, 11:12], in0=lncol[:, 11:12],
                                   scalar1=LN_EPS, scalar2=LN_EPS,
                                   op0=ALU.add, op1=ALU.max)
        kb.emit("vector", cols1, wait=((sS, s_a),), inc=((sV, sV.tick() and 1),))
        v_a = sV.n
        kb.emit("scalar", lambda sc: sc.activation(
            out=lncol[:, 12:13], in_=lncol[:, 11:12], func=AF.Sqrt),
            wait=((sV, v_a),), inc=((sS, sS.tick() and 1),))
        s_b = sS.n

        def cols2(v):
            v.reciprocal(lncol[:, 13:14], lncol[:, 12:13])
            vfill(v)
            return v.scalar_tensor_tensor(out=lncol[:, 14:15], in0=lncol[:, 10:11],
                                          scalar=-1.0, in1=lncol[:, 13:14],
                                          op0=ALU.mult, op1=ALU.mult)
        kb.emit("vector", cols2, wait=((sS, s_b),), inc=((sV, sV.tick() and 1),))
        v_b = sV.n
        kb.emit("scalar", lambda sc: sc.activation(
            out=dst_ap, in_=src_ap, func=AF.Identity, scale=lncol[:, 13:14],
            bias=lncol[:, 14:15]), wait=((sV, v_b),), inc=((sS, sS.tick() and 1),))
        s_c = sS.n
        if affine is not None:
            gt, btl = affine

            def aff(v):
                v.tensor_mul(dst_ap, dst_ap, gt[:])
                return v.tensor_add(dst_ap, dst_ap, btl[:])
            kb.emit("vector", aff, wait=((sS, s_c),), inc=((sV, sV.tick() and 1),))


    def ln_cols4(src4, sq_wait, ct=None):
        if ct is None:
            ct = lncol

        def chain1(v, sq_wait=sq_wait, src4=src4, ct=ct):
            v.wait_ge(sS.h, sq_wait)
            v.reduce_sum(ct[:, 4:8], src4, axis=AX.X)
            vfill(v)
            v.tensor_scalar(out=ct[:, 8:12], in0=ct[:, 4:8],
                            scalar1=1.0 / D, scalar2=None, op0=ALU.mult)
            v.tensor_scalar(out=ct[:, 4:8], in0=ct[:, 0:4],
                            scalar1=1.0 / D, scalar2=None, op0=ALU.mult)
            v.tensor_mul(ct[:, 12:16], ct[:, 8:12], ct[:, 8:12])
            vfill(v)
            return v.scalar_tensor_tensor(out=ct[:, 4:8], in0=ct[:, 12:16],
                                          scalar=-1.0, in1=ct[:, 4:8],
                                          op0=ALU.mult, op1=ALU.add)
        kb.emit("vector", chain1, inc=((sV, sV.tick() and 1),))
        v_a = sV.n
        kb.emit("scalar", lambda sc, ct=ct: sc.activation(
            out=ct[:, 12:16], in_=ct[:, 4:8], func=AF.Sqrt, bias=epsc[:]),
            wait=((sV, v_a), (sG, g_tab0)), inc=((sS, sS.tick() and 1),))
        s_b = sS.n

        def chain2(v, s_b=s_b, ct=ct):
            v.wait_ge(sS.h, s_b)
            v.reciprocal(ct[:, 0:4], ct[:, 12:16])
            vfill(v)
            return v.scalar_tensor_tensor(out=ct[:, 4:8], in0=ct[:, 8:12],
                                          scalar=-1.0, in1=ct[:, 0:4],
                                          op0=ALU.mult, op1=ALU.mult)
        kb.emit("vector", chain2, inc=((sV, sV.tick() and 1),))
        return sV.n

    sp_x2_last = 0
    sp_xq_last = 0
    x4n = ut[:].rearrange("p (s d) -> p s d", s=SUB)

    def ln1_stats(t_chunk, extra_wait):
        w = dma(sXA, "sync", x4n,
                x_h[t_chunk * TT:(t_chunk + 1) * TT, :].rearrange(
                    "(s p) d -> p s d", p=P), wait=extra_wait)
        for s_ in range(SUB):
            kb.emit("scalar", lambda sc, s_=s_: sc.activation(
                out=sqs[:], in_=x4n[:, s_, :], func=AF.Square,
                accum_out=lncol2[:, s_:s_ + 1]),
                wait=((sXA, w),) if s_ == 0 else (), inc=((sS, sS.tick() and 1),))
        return ln_cols4(x4n, sS.n, ct=lncol2)

    v_ln1stats = ln1_stats(0, ())

    def emit_prologue(tc, v_stats_ln1):
        if tc == 0:
            kb.emit("vector", lambda v: v.memset(ln1F3[:, :, 0:8], 0.0),
                    wait=((sG, g_tab),), inc=((sV, sV.tick() and 1),))
        else:
            kb.emit("vector", lambda v: v.tensor_copy(ln1F3[:, :, 0:8],
                                                      ln1F3[:, :, TT:TT + 8]),
                    inc=((sV, sV.tick() and 1),))
        v_lead = sV.n
        t_tr_last = 0
        for s_ in range(SUB):
            kb.emit("scalar", lambda sc, s_=s_: sc.activation(
                out=etok[:], in_=x4n[:, s_, :], func=AF.Identity,
                scale=lncol2[:, s_:s_ + 1], bias=lncol2[:, 4 + s_:5 + s_]),
                wait=((sV, v_stats_ln1), (sT, t_tr_last)),
                inc=((sS, sS.tick() and 1),))
            s_ln1 = sS.n
            for c in range(DH):
                bank = pC if (c % 2 == 0) else pD
                kb.emit("tensor", lambda t, c=c, bank=bank: t.transpose(
                    bank[:, 0:P], etok[:, c * P:(c + 1) * P], idf[:]),
                    wait=((sS, s_ln1), (sV, max(v_tab, v_lead)), (sV, sV.n)),
                    inc=((sT, sT.tick() and 1),))
                t_tr = sT.n
                kb.emit("vector", lambda v, c=c, s_=s_, bank=bank: v.tensor_copy(
                    ln1F3[:, c, 8 + s_ * P:8 + (s_ + 1) * P], bank[:, 0:P]),
                    wait=((sT, t_tr),), inc=((sV, sV.tick() and 1),))
            t_tr_last = sT.n
        return sV.n

    v_ln1f_next = emit_prologue(0, v_ln1stats)

    for tc in range(NCH):
        v_ln1f = v_ln1f_next

        def conv_emit(v):
            inst = None
            for c in range(DH):
                v.tensor_scalar(out=ut3[:, c, :], in0=ln1F3[:, c, 5:5 + TT],
                                scalar1=cw[:, c * DCONV:c * DCONV + 1],
                                scalar2=None, op0=ALU.mult)
                for j in (1, 2, 3):
                    inst = v.scalar_tensor_tensor(
                        out=ut3[:, c, :], in0=ln1F3[:, c, 5 + j:5 + j + TT],
                        scalar=cw[:, c * DCONV + j:c * DCONV + j + 1],
                        in1=ut3[:, c, :], op0=ALU.mult, op1=ALU.add)
            return inst
        kb.emit("vector", conv_emit,
                wait=((sV, v_ln1f), (sS, sS.n)), inc=((sV, sV.tick() and 1),))
        v_conv = sV.n

        def conv_silu(sc):
            inst = None
            for c in range(DH):
                inst = sc.activation(out=ut3[:, c, :], in_=ut3[:, c, :],
                                     func=AF.Silu, bias=cb[:, c:c + 1])
            return inst
        kb.emit("scalar", conv_silu, wait=((sV, v_conv),), inc=((sS, sS.tick() and 1),))
        s_u = sS.n
        if debug_scratch and tc == 1:
            spu = dma(sX2S, "gpsimd", dbg_u, ut[:], wait=((sS, s_u),))

        mm_group([(pbt[:, 0:TT], bpT3[:, c, :], ut3[:, c, :]) for c in range(DH)],
                 wait=((sS, s_u), (sPRM, PRM_ALL), (sS, sS.n)), inc=((sT, sT.tick() and 1),))
        t_bt = sT.n
        mm_group([(pbt[:, TT:2 * TT], cpT3[:, c, :], ut3[:, c, :]) for c in range(DH)],
                 wait=(), inc=((sT, sT.tick() and 1),))
        t_ct = sT.n
        kb.emit("scalar", lambda sc: sc.activation(out=btT[:], in_=pbt[:, 0:TT],
                                                   func=AF.Identity, bias=bpb[:]),
                wait=((sT, t_bt),), inc=((sS, sS.tick() and 1),))
        kb.emit("scalar", lambda sc: sc.activation(out=ctT[:], in_=pbt[:, TT:2 * TT],
                                                   func=AF.Identity, bias=cpb[:]),
                wait=((sT, t_ct),), inc=((sS, sS.tick() and 1),))
        s_btct = sS.n
        if debug_scratch and tc == 1:
            dma(sX2S, "gpsimd", dbg_bt[0:NST, 0:TT], btT[:], wait=((sS, s_btct),))
            dma(sX2S, "gpsimd", dbg_bt[0:NST, TT:2 * TT], ctT[:], wait=((sS, s_btct),))

        kb.emit("vector", lambda v: v.tensor_tensor(
            yy3, ut3, dD[:, :, None].broadcast_to([P, DH, TT]), ALU.mult),
            wait=((sS, s_u),), inc=((sV, sV.tick() and 1),))

        s_gate_evacs = []
        for n in range(NST):
            pb_ = pA if (n % 2 == 0) else pB
            pc_ = pC if (n % 2 == 0) else pD
            kb.emit("tensor", lambda t, n=n, pb_=pb_: t.matmul(
                pb_[:], oneh[:, n * P:(n + 1) * P], btT[:], start=True, stop=True),
                wait=((sS, s_btct), (sV, max(v_oneh, sV.n)), (sS, s_aexp)),
                inc=((sT, sT.tick() and 1),))
            t_b = sT.n
            kb.emit("tensor", lambda t, n=n, pc_=pc_: t.matmul(
                pc_[:], oneh[:, n * P:(n + 1) * P], ctT[:], start=True, stop=True),
                inc=((sT, sT.tick() and 1),))
            t_c = sT.n
            if 1 <= n <= DH:
                g = n - 1
                pg = ptr0 if (g % 2 == 0) else ptr1
                bank_free = () if g < 2 else ((sS, s_gate_evacs[g - 2]),)
                mm_group([(pg[:], gwT[:, c * D + g * P:c * D + (g + 1) * P],
                           ln1F3[:, c, 8:8 + TT]) for c in range(DH)],
                         wait=((sV, v_ln1f), (sPRM, PRM_ALL)) + bank_free,
                         inc=((sT, sT.tick() and 1),))
                t_g = sT.n
                kb.emit("scalar", lambda sc, g=g, pg=pg: sc.activation(
                    out=gzF3[:, g, :], in_=pg[:], func=AF.Silu, bias=gbv[:, g:g + 1]),
                    wait=((sT, t_g),), inc=((sS, sS.tick() and 1),))
                s_gate_evacs.append(sS.n)

            if gsl[0] == 0:
                g_yb[0] = sG.n
            base = gsl[0]
            sgate = stats_gate[0] if n == 0 else 0
            slice_ticks = [sV.tick() for _ in range(DH)]
            slots = [(base + c) % 6 for c in range(DH)]
            rws = [g_yb[0] + (base + c - 6) + 1 if base + c >= 6 else 0
                   for c in range(DH)]

            def scan_n(v, n=n, pb_=pb_, pc_=pc_, t_b=t_b, t_c=t_c,
                       slots=slots, rws=rws, sgate=sgate):
                v.wait_ge(sT.h, t_b)
                v.tensor_tensor(ub3, ut3, pb_[:, None, :].broadcast_to([P, DH, TT]),
                                ALU.mult)
                for c in range(DH):
                    v.tensor_tensor_scan(
                        out=hh3[:, c, :],
                        data0=aexp[:, c * NST + n:c * NST + n + 1].broadcast_to([P, TT]),
                        data1=ub3[:, c, :],
                        initial=hstate[:, n * DH + c:n * DH + c + 1],
                        op0=ALU.mult, op1=ALU.add)
                v.wait_ge(sT.h, t_c)
                if sgate:
                    v.wait_ge(sS.h, sgate)
                for c in range(DH):
                    if rws[c]:
                        v.wait_ge(sG.h, rws[c])
                    v.tensor_tensor(cmb3[:, slots[c], :], hh3[:, c, :], pc_[:],
                                    ALU.mult).then_inc(sV.h, 1)
                return v.tensor_copy(
                    hstate[:].rearrange("p (n h) -> p n h", n=NST)[:, n, :],
                    hh3[:, :, TT - 1])
            kb.emit("vector", scan_n, inc=((sV, sV.tick() and 1),))
            for c in range(DH):
                kb.emit("gpsimd", lambda g, c=c, sl=slots[c]: g.tensor_tensor(
                    yy3[:, c, :], yy3[:, c, :], cmb3[:, sl, :], ALU.add),
                    wait=((sV, slice_ticks[c]),), inc=((sG, sG.tick() and 1),))
                assert sG.n == g_yb[0] + base + c + 1
            gsl[0] = base + DH
        v_scan = sV.n
        g_yacc = sG.n
        dbg_w_y = 0
        if debug_scratch and tc == 1:
            dbg_w_y = dma(sX2S, "gpsimd", dbg_y, yy[:], wait=((sV, v_scan),))

        s_gate = s_gate_evacs[-1]
        dbg_w_g = 0
        if debug_scratch and tc == 1:
            dbg_w_g = dma(sX2S, "gpsimd", dbg_g, gzF[:], wait=((sS, s_gate),))

        kb.emit("vector", lambda v: v.tensor_mul(yy[:], yy[:], gzF[:]),
                wait=((sS, s_gate), (sX2S, dbg_w_y), (sG, g_yacc)),
                inc=((sV, sV.tick() and 1),))
        v_yg = sV.n

        z_evacs = []
        for g in range(DH):
            po = pC if (g % 2 == 0) else pD
            bank_free = ((sV, z_evacs[g - 2]),) if g >= 2 else ()
            mm_group([(po[:], owT[:, c * D + g * P:c * D + (g + 1) * P],
                       yy3[:, c, :]) for c in range(DH)],
                     wait=((sV, v_yg),) + bank_free, inc=((sT, sT.tick() and 1),))
            t_o = sT.n

            def z_evac(v, g=g, po=po, t_o=t_o, dbg_w_g=dbg_w_g):
                v.wait_ge(sT.h, t_o)
                if dbg_w_g:
                    v.wait_ge(sX2S.h, dbg_w_g)
                return v.scalar_tensor_tensor(out=gzF3[:, g, :], in0=po[:],
                                              scalar=obv[:, g:g + 1],
                                              in1=ln1F3[:, g, 8:8 + TT],
                                              op0=ALU.add, op1=ALU.add)
            kb.emit("vector", z_evac, inc=((sV, sV.tick() and 1),))
            z_evacs.append(sV.n)
        v_z = sV.n
        if debug_scratch and tc == 1:
            dbg_w_z = dma(sX2S, "gpsimd", dbg_z, gzF[:], wait=((sV, v_z),))
            sp_x2_last = max(sp_x2_last, dbg_w_z)

        if tc + 1 < NCH:
            v_ln1stats = ln1_stats(tc + 1, ((sV, v_yg),))
        emit_stats_boundary(tc, g_yacc)

        zt4 = hh[:].rearrange("p (s d) -> p s d", s=SUB)
        et4 = ub[:].rearrange("p (s d) -> p s d", s=SUB)
        xr4 = yy[:].rearrange("p (s d) -> p s d", s=SUB)
        qt4 = yy[:].bitcast(BF16)[:, 0:SUB * D].rearrange("p (s d) -> p s d", s=SUB)
        rows = slice(tc * TT, (tc + 1) * TT)
        for s_ in range(SUB):
            for c in range(DH):
                bank = ptr0 if (c % 2 == 0) else ptr1
                kb.emit("tensor", lambda t, c=c, s_=s_, bank=bank: t.transpose(
                    bank[:, 0:P], gzF3[:, c, s_ * P:(s_ + 1) * P], idf[:]),
                    wait=((sV, v_z), (sS, sS.n)), inc=((sT, sT.tick() and 1),))
                t_tr = sT.n
                kb.emit("scalar", lambda sc, c=c, s_=s_, bank=bank: sc.activation(
                    out=hh[:, s_ * D + c * P:s_ * D + (c + 1) * P],
                    in_=bank[:, 0:P], func=AF.Copy),
                    wait=((sT, t_tr), (sX2S, sp_x2_last), (sV, sV.n)),
                    inc=((sS, sS.tick() and 1),))
        if tc + 1 < NCH:
            v_ln1f_next = emit_prologue(tc + 1, v_ln1stats)

        for s_ in range(SUB):
            kb.emit("scalar", lambda sc, s_=s_: sc.activation(
                out=sqs[:], in_=zt4[:, s_, :], func=AF.Square,
                accum_out=lncol[:, s_:s_ + 1]), inc=((sS, sS.tick() and 1),))
        v_cols = ln_cols4(zt4, sS.n)
        for s_ in range(SUB):
            kb.emit("scalar", lambda sc, s_=s_: sc.activation(
                out=et4[:, s_, :], in_=zt4[:, s_, :], func=AF.Identity,
                scale=lncol[:, s_:s_ + 1], bias=lncol[:, 4 + s_:5 + s_]),
                wait=((sV, v_cols),) if s_ == 0 else (), inc=((sS, sS.tick() and 1),))
        s_innorm = sS.n

        def aff4(v, s_innorm=s_innorm):
            v.wait_ge(sS.h, s_innorm)
            v.tensor_tensor(et4, et4, lng_bc[:, None, :].broadcast_to([P, SUB, D]),
                            ALU.mult)
            return v.tensor_tensor(et4, et4,
                                   lnb_bc[:, None, :].broadcast_to([P, SUB, D]),
                                   ALU.add)
        kb.emit("vector", aff4, inc=((sV, sV.tick() and 1),))
        for s_ in range(SUB):
            kb.emit("scalar", lambda sc, s_=s_: sc.activation(
                out=sqs[:], in_=et4[:, s_, :], func=AF.Square,
                accum_out=lncol[:, s_:s_ + 1]),
                wait=((sV, sV.n),) if s_ == 0 else (), inc=((sS, sS.tick() and 1),))
        v_cols = ln_cols4(et4, sS.n)
        for s_ in range(SUB):
            kb.emit("scalar", lambda sc, s_=s_: sc.activation(
                out=zt4[:, s_, :], in_=et4[:, s_, :], func=AF.Identity,
                scale=lncol[:, s_:s_ + 1], bias=lncol[:, 4 + s_:5 + s_]),
                wait=((sV, v_cols),) if s_ == 0 else (), inc=((sS, sS.tick() and 1),))
        s_onorm = sS.n
        w_xr = dma(sXB, "sync", xr4,
                   x_h[rows, :].rearrange("(s p) d -> p s d", p=P),
                   wait=((sV, v_z),))
        kb.emit("vector", lambda v: v.tensor_add(hh[:], hh[:], yy[:]),
                wait=((sS, s_onorm), (sXB, w_xr)), inc=((sV, sV.tick() and 1),))
        v_x2 = sV.n
        sp_x2_last = dma(sX2S, "gpsimd",
                         x2_d[rows, :].rearrange("(s p) d -> p s d", p=P),
                         zt4, wait=((sV, v_x2),))
        for s_ in range(SUB):
            kb.emit("scalar", lambda sc, s_=s_: sc.activation(
                out=sqs[:], in_=zt4[:, s_, :], func=AF.Square,
                accum_out=lncol[:, s_:s_ + 1]),
                wait=((sV, v_x2),) if s_ == 0 else (), inc=((sS, sS.tick() and 1),))
        v_cols = ln_cols4(zt4, sS.n)
        for s_ in range(SUB):
            kb.emit("scalar", lambda sc, s_=s_: sc.activation(
                out=et4[:, s_, :], in_=zt4[:, s_, :], func=AF.Identity,
                scale=lncol[:, s_:s_ + 1], bias=lncol[:, 4 + s_:5 + s_]),
                wait=((sV, v_cols),) if s_ == 0 else (), inc=((sS, sS.tick() and 1),))
        s_ln3 = sS.n

        def q_cols4(v, s_ln3=s_ln3, tc=tc):
            v.wait_ge(sS.h, s_ln3)
            v.tensor_reduce(lncol[:, 0:4], et4, axis=AX.X, op=ALU.max,
                            apply_absolute_value=True)
            v.drain()
            v.tensor_scalar(out=lncol[:, 0:4], in0=lncol[:, 0:4],
                            scalar1=1e-5, scalar2=None, op0=ALU.max)
            v.drain()
            v.reciprocal(lncol[:, 4:8], lncol[:, 0:4])
            v.drain()
            v.tensor_scalar(out=lncol[:, 8:12], in0=lncol[:, 4:8],
                            scalar1=127.0, scalar2=None, op0=ALU.mult)
            return v.tensor_scalar(out=dsx[:, tc * SUB:(tc + 1) * SUB],
                                   in0=lncol[:, 0:4], scalar1=1.0 / 127.0,
                                   scalar2=None, op0=ALU.mult)
        kb.emit("vector", q_cols4, inc=((sV, sV.tick() and 1),))
        v_qc = sV.n
        for s_ in range(SUB):
            kb.emit("scalar", lambda sc, s_=s_: sc.activation(
                out=zt4[:, s_, :], in_=et4[:, s_, :], func=AF.Copy,
                scale=lncol[:, 8 + s_:9 + s_], bias=RB),
                wait=(((sV, v_qc), (sX2S, sp_x2_last)) if s_ == 0 else ()),
                inc=((sS, sS.tick() and 1),))
            s_q1 = sS.n

            def q_fin(v, s_=s_, s_q1=s_q1):
                v.wait_ge(sS.h, s_q1)
                v.tensor_copy(h16[:], zt4[:, s_, :])
                return v.tensor_scalar(out=qt4[:, s_, :], in0=h16[:], scalar1=-RB,
                                       scalar2=None, op0=ALU.add)
            kb.emit("vector", q_fin, inc=((sV, sV.tick() and 1),))
        v_q = sV.n
        sp_xq_last = dma(sXQS, "gpsimd",
                         xq_d[rows, :].rearrange("(s p) d -> p s d", p=P),
                         qt4, wait=((sV, v_q),))

    p2_V, p2_S, p2_T = sV.n, sS.n, sT.n
    p2_X2S, p2_XQS = sX2S.n, sXQS.n

    s_stats = stat_abs_ticks[-1]
    for mi in range(3):
        kb.emit("vector",
                lambda v, mi=mi: v.reduce_sum(colt[:, mi:mi + 1],
                                              wstat[:, mi * 32:(mi + 1) * 32],
                                              axis=AX.X),
                wait=((sS, s_stats),), inc=((sV, sV.tick() and 1),))
    v_sr = sV.n
    kb.emit("tensor", lambda t: t.matmul(pbt[0:1, 0:3], onec[:], colt[:, 0:3],
                                         start=True, stop=True),
            wait=((sV, v_sr),), inc=((sT, sT.tick() and 1),))
    t_s1 = sT.n
    kb.emit("vector", lambda v: v.tensor_copy(colt[0:1, 4:7], pbt[0:1, 0:3]),
            wait=((sT, t_s1),), inc=((sV, sV.tick() and 1),))
    v_s2 = sV.n
    kb.emit("tensor", lambda t: t.matmul(ptr0[:, 0:3], onerow[:],
                                         colt[0:1, 4:7], start=True, stop=True),
            wait=((sV, v_s2),), inc=((sT, sT.tick() and 1),))
    t_s2 = sT.n

    def stat_fin(v):
        v.tensor_scalar(out=colt[:, 0:3], in0=ptr0[:, 0:3],
                        scalar1=1.0 / (D * INNER), scalar2=1e-5,
                        op0=ALU.mult, op1=ALU.max)
        v.drain()
        v.tensor_copy(cwg[:], colt[:, 0:1])
        v.tensor_copy(cwu[:], colt[:, 1:2])
        v.tensor_copy(cwd[:], colt[:, 2:3])
        v.drain()
        v.reciprocal(swg[:], cwg[:])
        v.reciprocal(swu[:], cwu[:])
        return v.reciprocal(swd[:], cwd[:])
    kb.emit("vector", stat_fin, wait=((sT, t_s2),), inc=((sV, sV.tick() and 1),))
    v_stats = sV.n

    p2.close()
    p3 = contextlib.ExitStack()
    sb3 = mk(p3)
    wstg = [sb3("wstg0", [P, INNER]), sb3("wstg1", [P, INNER])]
    wgq = sb3("wgq", [P, DH * INNER], BF16)
    wuq = sb3("wuq", [P, DH * INNER], BF16)
    f16b = sb3("f16b", [P, INNER], F16)
    gu = sb3("gu", [P, INNER])
    gu1 = sb3("gu1", [P, TT])
    guq = sb3("guq", [P, INNER], BF16)
    xqc = sb3("xqc", [P, DH * P], BF16)

    PHW = ((sV, p2_V), (sS, p2_S), (sT, p2_T), (sX2S, p2_X2S), (sXQS, p2_XQS))

    s_wprev = 0
    first = True
    for (wdst, src, scol) in [(wgq, wgT_h, swg), (wuq, wuT_h, swu)]:
        for c in range(DH):
            stg = wstg[c % 2]
            sws = sW0 if c % 2 == 0 else sW1
            w_l = dma(sws, "sync", stg[:], src[c * P:(c + 1) * P, :],
                      wait=(PHW if first else ((sS, s_wprev),)))
            first = False
            kb.emit("scalar", lambda sc, stg=stg, scol=scol: sc.activation(
                out=gu[:], in_=stg[:], func=AF.Identity, scale=scol[:]),
                wait=((sws, w_l), (sV, max(v_stats, sV.n))),
                inc=((sS, sS.tick() and 1),))
            s_wq = sS.n
            s_wprev = s_wq

            def w_fin(v, wdst=wdst, c=c):
                v.tensor_scalar(out=f16b[:], in0=gu[:], scalar1=-0.5, scalar2=None,
                                op0=ALU.is_lt)
                return v.scalar_tensor_tensor(
                    out=wdst[:, c * INNER:(c + 1) * INNER], in0=gu[:], scalar=0.5,
                    in1=f16b[:], op0=ALU.is_gt, op1=ALU.subtract)
            kb.emit("vector", w_fin, wait=((sS, s_wq),), inc=((sV, sV.tick() and 1),))
    v_wq = sV.n

    if debug_scratch:
        dma(sX2S, "gpsimd", dbg_wgq, wgq[:], wait=((sV, v_wq),))
        def dbg_cp(v):
            v.tensor_copy(sqs[:, 0:24], wstat[:])
            v.tensor_copy(sqs[:, 24:32], colt[:])
            v.tensor_copy(sqs[:, 32:33], swg[:])
            v.tensor_copy(sqs[:, 33:34], swu[:])
            v.tensor_copy(sqs[:, 34:35], swd[:])
            v.tensor_copy(sqs[:, 35:36], cwg[:])
            v.tensor_copy(sqs[:, 36:37], cwu[:])
            return v.tensor_copy(sqs[:, 37:38], cwd[:])
        kb.emit("vector", dbg_cp, wait=((sV, v_wq),), inc=((sV, sV.tick() and 1),))
        dma(sX2S, "gpsimd", dbg_cols, sqs[:, 0:64], wait=((sV, sV.n),))
    xstg = sb3("xstg", [P, D], BF16)
    idb = sb3("idb", [P, P], BF16)
    kb.emit("vector", lambda v: v.tensor_copy(idb[:], idf[:]),
            wait=((sV, v_wq),), inc=((sV, sV.tick() and 1),))
    v_idb = sV.n
    sp_gq = 0
    s_xq_evacs = []
    for i in range(NTOK):
        trow = i * P
        wv0 = dma(sXQ, "sync", xstg[:], xq_d[trow:trow + P, :],
                  wait=(((sS, s_xq_evacs[-1] if s_xq_evacs else 0),
                         (sXQS, p2_XQS))))
        wv = wv0
        for c in range(DH):
            pt = ptr0 if c % 2 == 0 else ptr1
            bankw = ((sS, s_xq_evacs[-2],) if len(s_xq_evacs) >= 2 else (sS, 0))
            kb.emit("tensor", lambda t, c=c, pt=pt: t.transpose(
                pt[:, 0:64].bitcast(BF16), xstg[:, c * P:(c + 1) * P], idb[:]),
                wait=((sXQ, wv0), (sV, v_idb), bankw, (sT, sT.n)),
                inc=((sT, sT.tick() and 1),))
            t_tr = sT.n
            kb.emit("scalar", lambda sc, c=c, pt=pt: sc.activation(
                out=xqc[:, c * P:(c + 1) * P], in_=pt[:, 0:64].bitcast(BF16),
                func=AF.Copy), wait=((sT, t_tr),), inc=((sS, sS.tick() and 1),))
            s_xq_evacs.append(sS.n)

        def mk_desc(v, i=i):
            v.tensor_mul(lncol[:, 0:1], cwg[:], dsx[:, i:i + 1])
            v.tensor_mul(lncol[:, 1:2], cwu[:], dsx[:, i:i + 1])
            return v.drain()
        kb.emit("vector", mk_desc, wait=((sV, v_wq), (sS, sS.n)),
                inc=((sV, sV.tick() and 1),))
        v_ds = sV.n
        s_up_evacs = []
        for f in range(DH):
            pg = pA if (f % 2 == 0) else pB
            pu = pC if (f % 2 == 0) else pD
            bank_free = ((sS, s_up_evacs[f - 2]),) if f >= 2 else ()
            mm_group([(pg[:], xqc[:, c * P:(c + 1) * P],
                       wgq[:, c * INNER + f * TT:c * INNER + (f + 1) * TT])
                      for c in range(DH)],
                     wait=((sS, s_xq_evacs[-1]), (sV, v_ds)) + bank_free,
                     inc=((sT, sT.tick() and 1),))
            t_g = sT.n
            mm_group([(pu[:], xqc[:, c * P:(c + 1) * P],
                       wuq[:, c * INNER + f * TT:c * INNER + (f + 1) * TT])
                      for c in range(DH)],
                     wait=(), inc=((sT, sT.tick() and 1),))
            t_u = sT.n
            kb.emit("scalar", lambda sc, pg=pg: sc.activation(
                out=gu1[:], in_=pg[:], func=AF.Silu, scale=lncol[:, 0:1]),
                wait=((sT, t_g), (sV, sV.n)), inc=((sS, sS.tick() and 1),))
            kb.emit("scalar", lambda sc, pu=pu: sc.activation(
                out=sqs[:, 0:TT], in_=pu[:], func=AF.Copy, scale=lncol[:, 1:2]),
                wait=((sT, t_u), (sV, sV.n)), inc=((sS, sS.tick() and 1),))
            s_up = sS.n
            s_up_evacs.append(s_up)

            def gu_mul(v, f=f, s_up=s_up):
                v.wait_ge(sS.h, s_up)
                v.tensor_tensor(gu[:, f * TT:(f + 1) * TT], gu1[:], sqs[:, 0:TT],
                                ALU.mult)
                v.tensor_reduce(lncol[:, 2:3], gu[:, f * TT:(f + 1) * TT],
                                axis=AX.X, op=ALU.max, apply_absolute_value=True)
                v.drain()
                if f == 0:
                    return v.tensor_copy(lncol[:, 3:4], lncol[:, 2:3])
                return v.tensor_max(lncol[:, 3:4], lncol[:, 3:4], lncol[:, 2:3])
            kb.emit("vector", gu_mul, inc=((sV, sV.tick() and 1),))

        def gq_cols(v, i=i):
            v.tensor_scalar(out=lncol[:, 3:4], in0=lncol[:, 3:4], scalar1=1e-5,
                            scalar2=None, op0=ALU.max)
            v.drain()
            v.reciprocal(lncol[:, 4:5], lncol[:, 3:4])
            v.drain()
            v.tensor_scalar(out=lncol[:, 5:6], in0=lncol[:, 4:5], scalar1=127.0,
                            scalar2=None, op0=ALU.mult)
            return v.tensor_scalar(out=dsg[:, i:i + 1], in0=lncol[:, 3:4],
                                   scalar1=1.0 / 127.0, scalar2=None, op0=ALU.mult)
        kb.emit("vector", gq_cols, wait=((sS, sS.n),),
                inc=((sV, sV.tick() and 1),))
        v_gqc = sV.n
        if debug_scratch and i == 0:
            dma(sX2S, "gpsimd", dbg_xqc, xqc[:], wait=((sV, v_gqc),))
            dma(sX2S, "gpsimd", dbg_gu, gu[:], wait=((sV, v_gqc),))
        kb.emit("scalar", lambda sc: sc.activation(
            out=f16b[:], in_=gu[:], func=AF.Copy, scale=lncol[:, 5:6], bias=RB),
            wait=((sV, v_gqc),), inc=((sS, sS.tick() and 1),))
        s_gq = sS.n

        def g_fin(v, s_gq=s_gq, sp_gq=sp_gq):
            v.wait_ge(sS.h, s_gq)
            v.wait_ge(sGQS.h, sp_gq)
            return v.tensor_scalar(out=guq[:], in0=f16b[:], scalar1=-RB,
                                   scalar2=None, op0=ALU.add)
        kb.emit("vector", g_fin, inc=((sV, sV.tick() and 1),))
        v_gq = sV.n
        sp_gq = dma(sGQS, "gpsimd", gq_d[trow:trow + P, :], guq[:],
                    wait=((sV, v_gq),))
    p3_V, p3_S, p3_T, p3_GQS = sV.n, sS.n, sT.n, sGQS.n

    p3.close()
    p4 = contextlib.ExitStack()
    sb4 = mk(p4)
    wdq = sb4("wdq", [P, IH * D], BF16)
    gqT = sb4("gqT", [P, IH * P], BF16)
    gqT2 = sb4("gqT2", [P, IH * P], BF16)
    x2b = sb4("x2b", [P, SUB * D])
    o2b = sb4("o2b", [P, SUB * D])
    gstg = sb4("gstg", [P, INNER], BF16)
    idb4 = sb4("idb4", [P, P], BF16)
    wd16 = sb4("wd16", [P, D], F16)
    wdf = sb4("wdf", [P, D])
    wst4 = [sb4("wst4a", [P, D]), sb4("wst4b", [P, D])]

    PHW3 = ((sV, p3_V), (sS, p3_S), (sT, p3_T), (sGQS, p3_GQS))
    kb.emit("vector", lambda v: v.tensor_copy(idb4[:], idf[:]),
            wait=((sV, sV.n),), inc=((sV, sV.tick() and 1),))
    v_idb4 = sV.n
    s_gq_evacs = []
    v_wd_ticks = []
    s_wprev = 0
    first = True
    for c in range(IH):
        stg = wst4[c % 2]
        sws = sW0 if c % 2 == 0 else sW1
        w_l = dma(sws, "sync", stg[:, 0:D], wdT_h[c * P:(c + 1) * P, :],
                  wait=(PHW3 if first else ((sS, s_wprev),)))
        first = False
        kb.emit("scalar", lambda sc, stg=stg: sc.activation(
            out=wdf[:], in_=stg[:, 0:D], func=AF.Identity, scale=swd[:]),
            wait=((sws, w_l), (sV, sV.n)), inc=((sS, sS.tick() and 1),))
        s_wq = sS.n
        s_wprev = s_wq

        def wd_fin(v, c=c):
            v.tensor_scalar(out=wd16[:], in0=wdf[:], scalar1=-0.5, scalar2=None,
                            op0=ALU.is_lt)
            return v.scalar_tensor_tensor(
                out=wdq[:, c * D:(c + 1) * D], in0=wdf[:], scalar=0.5,
                in1=wd16[:], op0=ALU.is_gt, op1=ALU.subtract)
        kb.emit("vector", wd_fin, wait=((sS, s_wq),), inc=((sV, sV.tick() and 1),))
        v_wd_ticks.append(sV.n)
    v_wd = sV.n
    if debug_scratch:
        dma(sX2S, "gpsimd", dbg_wdq, wdq[:], wait=((sV, v_wd),))

    v_o_done = 0
    dm_out_prev = 0
    o2b4 = o2b[:].rearrange("p (s d) -> p s d", s=SUB)
    x2b4 = x2b[:].rearrange("p (s d) -> p s d", s=SUB)
    for gi in range(NTOK // SUB):
        rows = slice(gi * SUB * P, (gi + 1) * SUB * P)
        w_x2 = dma(sX2, "sync", x2b4,
                   x2_d[rows, :].rearrange("(s p) d -> p s d", p=P),
                   wait=((sV, v_o_done), (sX2S, p2_X2S)))
        def mk_dd(v, gi=gi):
            v.wait_ge(sV.h, v_wd)
            v.tensor_tensor(colt[:, 0:SUB], cwd[:].broadcast_to([P, SUB]),
                            dsg[:, gi * SUB:(gi + 1) * SUB], ALU.mult)
            return v.drain()
        kb.emit("vector", mk_dd, wait=((sS, sS.n),), inc=((sV, sV.tick() and 1),))
        v_dd = sV.n
        for il in range(SUB):
            i = gi * SUB + il
            trow = i * P
            gq_t = gqT if il % 2 == 0 else gqT2
            wvg = dma(sGQT, "sync", gstg[:], gq_d[trow:trow + P, :],
                      wait=(((sS, s_gq_evacs[-1] if s_gq_evacs else 0),
                             (sGQS, p3_GQS))))
            for c in range(IH):
                pt = pC if c % 2 == 0 else pD
                bankw = ((sS, s_gq_evacs[-2]) if len(s_gq_evacs) >= 2 else (sS, 0))
                kb.emit("tensor", lambda t, c=c, pt=pt, gq_t=gq_t: t.transpose(
                    pt[:, 0:64].bitcast(BF16), gstg[:, c * P:(c + 1) * P], idb4[:]),
                    wait=((sGQT, wvg), (sV, v_idb4), bankw, (sT, sT.n)),
                    inc=((sT, sT.tick() and 1),))
                t_tr = sT.n
                kb.emit("scalar", lambda sc, c=c, pt=pt, gq_t=gq_t: sc.activation(
                    out=gq_t[:, c * P:(c + 1) * P], in_=pt[:, 0:64].bitcast(BF16),
                    func=AF.Copy), wait=((sT, t_tr),), inc=((sS, sS.tick() and 1),))
                s_gq_evacs.append(sS.n)
            for g2 in range(2):
                pq = pA if (g2 % 2 == 0) else pB
                mmw = None
                wdw = ((sV, v_wd),) if (gi == 0 and il == 0) else ()
                mm_group([(pq[:], gq_t[:, c * P:(c + 1) * P],
                           wdq[:, c * D + g2 * TT:c * D + (g2 + 1) * TT])
                          for c in range(IH)],
                         wait=((sS, s_gq_evacs[-1]),) + wdw,
                         inc=((sT, sT.tick() and 1),), mm_waits=mmw)
                t_d = sT.n
                kb.emit("scalar", lambda sc, il=il, g2=g2, pq=pq: sc.activation(
                    out=o2b[:, il * D + g2 * TT:il * D + (g2 + 1) * TT], in_=pq[:],
                    func=AF.Copy, scale=colt[:, il:il + 1]),
                    wait=((sT, t_d), (sV, v_dd), (sOUT, dm_out_prev)),
                    inc=((sS, sS.tick() and 1),))
        s_o2 = sS.n
        for s_ in range(SUB):
            kb.emit("scalar", lambda sc, s_=s_: sc.activation(
                out=sqs[:], in_=o2b4[:, s_, :], func=AF.Square,
                accum_out=lncol[:, s_:s_ + 1]),
                wait=((sV, sV.n),) if s_ == 0 else (), inc=((sS, sS.tick() and 1),))
        v_cols = ln_cols4(o2b4, sS.n)
        for s_ in range(SUB):
            kb.emit("scalar", lambda sc, s_=s_: sc.activation(
                out=o2b4[:, s_, :], in_=o2b4[:, s_, :], func=AF.Identity,
                scale=lncol[:, s_:s_ + 1], bias=lncol[:, 4 + s_:5 + s_]),
                wait=((sV, v_cols),) if s_ == 0 else (), inc=((sS, sS.tick() and 1),))
        s_ln4 = sS.n
        kb.emit("vector", lambda v: v.tensor_add(o2b[:], o2b[:], x2b[:]),
                wait=((sS, s_ln4), (sX2, w_x2)), inc=((sV, sV.tick() and 1),))
        v_o_done = sV.n
        dm_out_prev = dma(sOUT, "gpsimd",
                          out_h[rows, :].rearrange("(s p) d -> p s d", p=P),
                          o2b4, wait=((sV, v_o_done),))

    kb.emit("gpsimd", lambda g: None, wait=((sOUT, sOUT.n),))
    p4.close()

    with top:
        with nc.Block() as block:
            kb.run(block)
    return nc


_NC_CACHE = None


def kernel(**inputs):
    global _NC_CACHE
    f32 = np.float32
    T = lambda a: np.ascontiguousarray(np.asarray(a, dtype=f32).T)
    C = lambda a: np.ascontiguousarray(np.asarray(a, dtype=f32))
    x = C(inputs["L0_attn_x"])
    common = {
        "conv_w": C(np.asarray(inputs["L0_attn_conv_w"], dtype=f32)[:, 0, :]),
        "conv_b": C(inputs["L0_attn_conv_b"]),
        "A": C(inputs["L0_attn_A"]),
        "BpT": T(inputs["L0_attn_Bp_w"]),
        "Bpb": C(inputs["L0_attn_Bp_b"]),
        "CpT": T(inputs["L0_attn_Cp_w"]),
        "Cpb": C(inputs["L0_attn_Cp_b"]),
        "Dd": C(inputs["L0_attn_D"]),
        "gwT": T(inputs["L0_attn_gate_w"]),
        "gb": C(inputs["L0_attn_gate_b"]),
        "owT": T(inputs["L0_attn_out_w"]),
        "ob": C(inputs["L0_attn_out_b"]),
        "lng": C(inputs["L0_attn_ln_g"]),
        "lnb": C(inputs["L0_attn_ln_b"]),
        "wgT": T(inputs["L0_ffn_w_gate"]),
        "wuT": T(inputs["L0_ffn_w_up"]),
        "wdT": T(inputs["L0_ffn_w_down"]),
    }
    in_maps = [dict(common, x=x[c]) for c in range(8)]
    if _NC_CACHE is None:
        _NC_CACHE = build_core()
    res = run_bass_kernel_spmd(_NC_CACHE, in_maps, core_ids=list(range(8)))
    return np.stack([np.asarray(r["out"], dtype=f32) for r in res.results])


# version 18
# speedup vs baseline: 1.0372x; 1.0140x over previous
import contextlib
import numpy as np
import concourse.bass as bass
import concourse.mybir as mybir
from concourse.bass_utils import run_bass_kernel_spmd

F32 = mybir.dt.float32
F16 = mybir.dt.float16
BF16 = mybir.dt.bfloat16
AF = mybir.ActivationFunctionType
ALU = mybir.AluOpType
AX = mybir.AxisListType

P = 128
S = 2048
D = 1024
DH = D // P
NST = 16
DCONV = 4
INNER = 4096
IH = INNER // P
TT = 512
NCH = S // TT
SUB = TT // P
NTOK = S // P
LN_EPS = 1e-5
RB = 1536.0


class Sem:
    def __init__(self, h):
        self.h = h
        self.n = 0

    def tick(self, k=1):
        self.n += k
        return self.n


class KB:
    ENGS = ("sync", "tensor", "vector", "scalar", "gpsimd")

    def __init__(self, nc):
        self.nc = nc
        self.q = {e: [] for e in self.ENGS}

    def emit(self, eng, fn, wait=(), inc=()):
        wait = tuple((s.h, v) for s, v in wait)
        inc = tuple((s.h, k) for s, k in inc)

        def g(h):
            for sh, v in wait:
                h.wait_ge(sh, v)
            inst = fn(h)
            for sh, k in inc:
                inst.then_inc(sh, k)
        self.q[eng].append(g)

    def run(self, block):
        for eng in self.ENGS:
            fns = self.q[eng]
            getattr(block, eng)(lambda h, fns=fns: [f(h) for f in fns])


def build_core(debug_scratch=False):
    nc = bass.Bass("TRN2", target_bir_lowering=False, detect_race_conditions=False)

    x_h = nc.dram_tensor("x", [S, D], F32, kind="ExternalInput").ap()
    conv_w_h = nc.dram_tensor("conv_w", [D, DCONV], F32, kind="ExternalInput").ap()
    conv_b_h = nc.dram_tensor("conv_b", [D], F32, kind="ExternalInput").ap()
    A_h = nc.dram_tensor("A", [D, NST], F32, kind="ExternalInput").ap()
    BpT_h = nc.dram_tensor("BpT", [D, NST], F32, kind="ExternalInput").ap()
    Bpb_h = nc.dram_tensor("Bpb", [NST], F32, kind="ExternalInput").ap()
    CpT_h = nc.dram_tensor("CpT", [D, NST], F32, kind="ExternalInput").ap()
    Cpb_h = nc.dram_tensor("Cpb", [NST], F32, kind="ExternalInput").ap()
    Dd_h = nc.dram_tensor("Dd", [D], F32, kind="ExternalInput").ap()
    gwT_h = nc.dram_tensor("gwT", [D, D], F32, kind="ExternalInput").ap()
    gb_h = nc.dram_tensor("gb", [D], F32, kind="ExternalInput").ap()
    owT_h = nc.dram_tensor("owT", [D, D], F32, kind="ExternalInput").ap()
    ob_h = nc.dram_tensor("ob", [D], F32, kind="ExternalInput").ap()
    lng_h = nc.dram_tensor("lng", [D], F32, kind="ExternalInput").ap()
    lnb_h = nc.dram_tensor("lnb", [D], F32, kind="ExternalInput").ap()
    wgT_h = nc.dram_tensor("wgT", [D, INNER], F32, kind="ExternalInput").ap()
    wuT_h = nc.dram_tensor("wuT", [D, INNER], F32, kind="ExternalInput").ap()
    wdT_h = nc.dram_tensor("wdT", [INNER, D], F32, kind="ExternalInput").ap()
    out_h = nc.dram_tensor("out", [S, D], F32, kind="ExternalOutput").ap()
    _scr = "ExternalOutput" if debug_scratch else "Internal"
    x2_d = nc.dram_tensor("x2_scr", [S, D], F32, kind=_scr).ap()
    xq_d = nc.dram_tensor("xq_scr", [S, D], BF16, kind=_scr).ap()
    gq_d = nc.dram_tensor("gq_scr", [S, INNER], BF16, kind=_scr).ap()
    if debug_scratch:
        dbg_st = nc.dram_tensor("dbg_st", [P, 40], F32, kind=_scr).ap()
        dbg_wgq = nc.dram_tensor("dbg_wgq", [P, DH * INNER], BF16, kind=_scr).ap()
        dbg_xqc = nc.dram_tensor("dbg_xqc", [P, DH * P], BF16, kind=_scr).ap()
        dbg_gu = nc.dram_tensor("dbg_gu", [P, INNER], F32, kind=_scr).ap()
        dbg_cols = nc.dram_tensor("dbg_cols", [P, 64], F32, kind=_scr).ap()
        dbg_wdq = nc.dram_tensor("dbg_wdq", [P, IH * D], BF16, kind=_scr).ap()
        dbg_u = nc.dram_tensor("dbg_u", [P, DH * TT], F32, kind=_scr).ap()
        dbg_bt = nc.dram_tensor("dbg_bt", [NST, 2 * TT], F32, kind=_scr).ap()
        dbg_y = nc.dram_tensor("dbg_y", [P, DH * TT], F32, kind=_scr).ap()
        dbg_g = nc.dram_tensor("dbg_g", [P, DH * TT], F32, kind=_scr).ap()
        dbg_z = nc.dram_tensor("dbg_z", [P, DH * TT], F32, kind=_scr).ap()
        dbg_e = nc.dram_tensor("dbg_e", [P, 40], F32, kind=_scr).ap()

    kb = KB(nc)
    top = contextlib.ExitStack()

    def mk(es_):
        sbf = lambda n, shp, dt=F32: es_.enter_context(nc.sbuf_tensor(n, shp, dt))
        return sbf
    sb = mk(top)
    sem = lambda n: Sem(top.enter_context(nc.semaphore(n)))

    sIN = sem("sIN")
    sSP = sem("sSP")
    sXA = sem("sXA")
    sXB = sem("sXB")
    sW0 = sem("sW0")
    sW1 = sem("sW1")
    sWG = sem("sWG")
    sPRM = sem("sPRM")
    sXQ = sem("sXQ")
    sGQT = sem("sGQT")
    sX2 = sem("sX2")
    sX2S = sem("sX2S")
    sXQS = sem("sXQS")
    sGQS = sem("sGQS")
    sOUT = sem("sOUT")
    sT = sem("sT")
    sV = sem("sV")
    sS = sem("sS")
    sG = sem("sG")
    dmo = sem("dmo")

    pes = contextlib.ExitStack()
    top.enter_context(pes)
    ptr0 = pes.enter_context(nc.psum_tensor("ptr0", [P, TT], F32))
    ptr1 = pes.enter_context(nc.psum_tensor("ptr1", [P, TT], F32))
    pA = pes.enter_context(nc.psum_tensor("pA", [P, TT], F32))
    pB = pes.enter_context(nc.psum_tensor("pB", [P, TT], F32))
    pC = pes.enter_context(nc.psum_tensor("pC", [P, TT], F32))
    pD = pes.enter_context(nc.psum_tensor("pD", [P, TT], F32))
    pbt = pes.enter_context(nc.psum_tensor("pbt", [NST, 2 * TT], F32))

    def mm_group(plist, wait, inc, mm_waits=None):
        n = len(plist)

        def fn(t):
            inst = None
            for i, (o, l, r) in enumerate(plist):
                if mm_waits is not None and mm_waits[i]:
                    for sh, val in mm_waits[i]:
                        t.wait_ge(sh, val)
                inst = t.matmul(o, l, r, start=(i == 0),
                                stop=(i % 4 == 3 or i == n - 1),
                                skip_group_check=(i >= 4))
            return inst
        kb.emit("tensor", fn, wait=wait, inc=inc)

    def vfill(v):
        v.tensor_scalar(out=jnk[:, 0:1], in0=onec[:], scalar1=1.0,
                        scalar2=None, op0=ALU.mult)
        v.tensor_scalar(out=jnk[:, 1:2], in0=onec[:], scalar1=1.0,
                        scalar2=None, op0=ALU.mult)

    def dma(s_, eng, out_ap, in_ap, wait=(), **kw):
        v = s_.tick(16)
        kb.emit(eng, lambda e, o=out_ap, i=in_ap, kw=kw: e.dma_start(out=o, in_=i, **kw),
                wait=wait, inc=((s_, 16),))
        return v

    sqs = sb("sqs", [P, D])
    lncol = sb("lncol", [P, 16])
    colt = sb("colt", [P, 8])
    onec = sb("onec", [P, 1])
    jnk = sb("jnk", [P, 2])
    onerow = sb("onerow", [1, P])
    epsc = sb("epsc", [P, 1])
    idf = sb("idf", [P, P])
    wstat = sb("wstat", [P, 96])
    swg, swu, swd = sb("swg", [P, 1]), sb("swu", [P, 1]), sb("swd", [P, 1])
    cwg, cwu, cwd = sb("cwg", [P, 1]), sb("cwu", [P, 1]), sb("cwd", [P, 1])
    dsx = sb("dsx", [P, NTOK])
    dsg = sb("dsg", [P, NTOK])
    h16 = sb("h16", [P, D], F16)
    dbgc = sb("dbgc", [P, 40]) if debug_scratch else None

    kb.emit("gpsimd", lambda g: g.iota(idf[:], [[1, P]], channel_multiplier=-1,
                                       allow_small_or_imprecise_dtypes=True))
    kb.emit("gpsimd", lambda g: g.memset(onec[:], 1.0))
    kb.emit("gpsimd", lambda g: g.memset(onerow[:], 1.0))
    kb.emit("gpsimd", lambda g: g.memset(epsc[:], LN_EPS), inc=((sG, sG.tick() and 1),))
    g_tab0 = sG.n
    kb.emit("vector", lambda v: v.tensor_scalar(out=idf[:], in0=idf[:], scalar1=0.0,
                                                scalar2=None, op0=ALU.is_equal),
            wait=((sG, g_tab0),), inc=((sV, sV.tick() and 1),))
    v_tab = sV.n


    p2 = contextlib.ExitStack()
    sb2 = mk(p2)
    aexp = sb2("aexp", [P, DH * NST])
    cw = sb2("cw", [P, DH * DCONV])
    cb = sb2("cb", [P, DH])
    dD = sb2("dD", [P, DH])
    gbv = sb2("gbv", [P, DH])
    obv = sb2("obv", [P, DH])
    bpb = sb2("bpb", [NST, 1])
    cpb = sb2("cpb", [NST, 1])
    lng_bc = sb2("lng_bc", [P, D])
    lnb_bc = sb2("lnb_bc", [P, D])
    oneh = sb2("oneh", [NST, NST * P])
    cmb = sb2("cmb", [P, 6 * TT])
    hstate = sb2("hstate", [P, NST * DH])
    gwT = sb2("gwT_s", [P, DH * D])
    owT = sb2("owT_s", [P, DH * D])
    bpT = sb2("bpT", [P, DH * NST])
    cpT = sb2("cpT", [P, DH * NST])
    LF = TT + 8
    ln1F = sb2("ln1F", [P, DH * LF])
    ut = sb2("ut", [P, DH * TT])
    ub = sb2("ub", [P, DH * TT])
    hh = sb2("hh", [P, DH * TT])
    yy = sb2("yy", [P, DH * TT])
    gzF = sb2("gzF", [P, DH * TT])
    btT = sb2("btT", [NST, TT])
    ctT = sb2("ctT", [NST, TT])
    etok = sb2("etok", [P, D])
    lncol2 = sb2("lncol2", [P, 16])

    ln1F3 = ln1F[:].rearrange("p (h t) -> p h t", h=DH)
    ut3 = ut[:].rearrange("p (h t) -> p h t", h=DH)
    ub3 = ub[:].rearrange("p (h t) -> p h t", h=DH)
    hh3 = hh[:].rearrange("p (h t) -> p h t", h=DH)
    yy3 = yy[:].rearrange("p (h t) -> p h t", h=DH)
    cmb3 = cmb[:].rearrange("p (h t) -> p h t", h=6)
    g_yb = [0]
    gsl = [0]
    gzF3 = gzF[:].rearrange("p (h t) -> p h t", h=DH)
    bpT3 = bpT[:].rearrange("p (h n) -> p h n", h=DH)
    cpT3 = cpT[:].rearrange("p (h n) -> p h n", h=DH)

    wdT_blk = wdT_h.rearrange("(b p q) d -> b p (q d)", b=8, p=P)
    stat_blocks = []
    for mi, wsrc in enumerate([wgT_h, wuT_h, wdT_h]):
        for b in range(8):
            for q in range(4):
                if mi < 2:
                    ap_ = wsrc[b * P:(b + 1) * P, q * D:(q + 1) * D]
                else:
                    ap_ = wdT_blk[b][:, q * D:(q + 1) * D]
                stat_blocks.append((ap_, mi * 32 + b * 4 + q))
    stat_abs_ticks = []
    stats_gate = [0]

    def emit_stats_boundary(tc, g_free):
        slot_sems = (sW0, sW1, sWG)
        for k in range(24):
            blk = tc * 24 + k
            ap_, col = stat_blocks[blk]
            wt = ((sG, g_free),) if k == 0 else ()
            if blk >= 3:
                wt = wt + ((sS, stat_abs_ticks[blk - 3]),)
            sws = slot_sems[blk % 3]
            w_l = dma(sws, "sync", cmb[:, (blk % 3) * D:(blk % 3 + 1) * D],
                      ap_, wait=wt)
            kb.emit("scalar", lambda sc, blk=blk, col=col: sc.activation(
                out=cmb[:, (blk % 3) * D:(blk % 3 + 1) * D],
                in_=cmb[:, (blk % 3) * D:(blk % 3 + 1) * D],
                func=AF.Abs, accum_out=wstat[:, col:col + 1]),
                wait=((sws, w_l),), inc=((sS, sS.tick() and 1),))
            stat_abs_ticks.append(sS.n)
        stats_gate[0] = sS.n
    v_stats = sV.n
    if False and debug_scratch:
        def dbg_stats(v):
            v.tensor_copy(dbgc[:, 0:24], wstat[:])
            v.tensor_copy(dbgc[:, 24:32], colt[:])
            v.tensor_copy(dbgc[:, 32:33], swg[:])
            v.tensor_copy(dbgc[:, 33:34], cwg[:])
            v.tensor_copy(dbgc[:, 34:37], ptr0[:, 0:3])
            return v.tensor_copy(dbgc[:, 37:38], onec[:])
        kb.emit("vector", dbg_stats, wait=((sV, v_stats),), inc=((sV, sV.tick() and 1),))
        dma(sX2S, "gpsimd", dbg_st, dbgc[:], wait=((sV, sV.n),))

    dma(sPRM, "sync", aexp[:].rearrange("p (h n) -> p h n", h=DH),
        A_h.rearrange("(h p) n -> p h n", p=P))
    dma(sPRM, "sync", cw[:].rearrange("p (h j) -> p h j", h=DH),
        conv_w_h.rearrange("(h p) j -> p h j", p=P))
    dma(sPRM, "sync", cb[:], conv_b_h.rearrange("(h p) -> p h", p=P), allow_slow_non_contiguous=True)
    dma(sPRM, "sync", dD[:], Dd_h.rearrange("(h p) -> p h", p=P), allow_slow_non_contiguous=True)
    dma(sPRM, "sync", gbv[:], gb_h.rearrange("(h p) -> p h", p=P), allow_slow_non_contiguous=True)
    dma(sPRM, "sync", obv[:], ob_h.rearrange("(h p) -> p h", p=P), allow_slow_non_contiguous=True)
    dma(sPRM, "sync", bpb[:], Bpb_h[:, None])
    dma(sPRM, "sync", cpb[:], Cpb_h[:, None])
    dma(sPRM, "sync", lng_bc[:], lng_h[None, :].broadcast_to([P, D]))
    dma(sPRM, "sync", lnb_bc[:], lnb_h[None, :].broadcast_to([P, D]))
    in_params = sPRM.n
    dma(sPRM, "sync", bpT3, BpT_h.rearrange("(h p) n -> p h n", p=P))
    dma(sPRM, "sync", cpT3, CpT_h.rearrange("(h p) n -> p h n", p=P))
    in_bc = sPRM.n
    for c in range(DH):
        dma(sPRM, "sync", gwT[:, c * D:(c + 1) * D], gwT_h[c * P:(c + 1) * P, :])
    for c in range(DH):
        dma(sPRM, "sync", owT[:, c * D:(c + 1) * D], owT_h[c * P:(c + 1) * P, :])
    in_w = sPRM.n
    PRM_ALL = sPRM.n

    kb.emit("gpsimd", lambda g: g.iota(oneh[:], [[1, NST], [0, P]],
                                       channel_multiplier=-1,
                                       allow_small_or_imprecise_dtypes=True))
    kb.emit("gpsimd", lambda g: g.memset(hstate[:], 0.0), inc=((sG, sG.tick() and 1),))
    g_tab = sG.n
    kb.emit("vector", lambda v: v.tensor_scalar(out=oneh[:], in0=oneh[:], scalar1=0.0,
                                                scalar2=None, op0=ALU.is_equal),
            wait=((sG, g_tab),), inc=((sV, sV.tick() and 1),))
    v_oneh = sV.n
    def aexp1(sc):
        inst = sc.activation(out=aexp[:], in_=aexp[:], func=AF.Exp)
        sc.drain()
        return inst
    kb.emit("scalar", aexp1,
            wait=((sPRM, PRM_ALL),), inc=((sS, sS.tick() and 1),))
    kb.emit("scalar", lambda sc: sc.activation(out=aexp[:], in_=aexp[:], func=AF.Exp,
                                               scale=-1.0), inc=((sS, sS.tick() and 1),))
    s_aexp = sS.n

    def ln_pipeline(src_ap, dst_ap, affine):
        kb.emit("scalar", lambda sc: sc.activation(
            out=sqs[:], in_=src_ap, func=AF.Square, accum_out=lncol[:, 8:9]),
            wait=((sV, sV.n),), inc=((sS, sS.tick() and 1),))
        s_a = sS.n

        def cols1(v):
            v.reduce_sum(lncol[:, 9:10], src_ap, axis=AX.X)
            vfill(v)
            v.tensor_scalar(out=lncol[:, 10:11], in0=lncol[:, 9:10],
                            scalar1=1.0 / D, scalar2=None, op0=ALU.mult)
            v.tensor_scalar(out=lncol[:, 11:12], in0=lncol[:, 8:9],
                            scalar1=1.0 / D, scalar2=None, op0=ALU.mult)
            vfill(v)
            v.tensor_mul(lncol[:, 12:13], lncol[:, 10:11], lncol[:, 10:11])
            vfill(v)
            v.tensor_sub(lncol[:, 11:12], lncol[:, 11:12], lncol[:, 12:13])
            vfill(v)
            return v.tensor_scalar(out=lncol[:, 11:12], in0=lncol[:, 11:12],
                                   scalar1=LN_EPS, scalar2=LN_EPS,
                                   op0=ALU.add, op1=ALU.max)
        kb.emit("vector", cols1, wait=((sS, s_a),), inc=((sV, sV.tick() and 1),))
        v_a = sV.n
        kb.emit("scalar", lambda sc: sc.activation(
            out=lncol[:, 12:13], in_=lncol[:, 11:12], func=AF.Sqrt),
            wait=((sV, v_a),), inc=((sS, sS.tick() and 1),))
        s_b = sS.n

        def cols2(v):
            v.reciprocal(lncol[:, 13:14], lncol[:, 12:13])
            vfill(v)
            return v.scalar_tensor_tensor(out=lncol[:, 14:15], in0=lncol[:, 10:11],
                                          scalar=-1.0, in1=lncol[:, 13:14],
                                          op0=ALU.mult, op1=ALU.mult)
        kb.emit("vector", cols2, wait=((sS, s_b),), inc=((sV, sV.tick() and 1),))
        v_b = sV.n
        kb.emit("scalar", lambda sc: sc.activation(
            out=dst_ap, in_=src_ap, func=AF.Identity, scale=lncol[:, 13:14],
            bias=lncol[:, 14:15]), wait=((sV, v_b),), inc=((sS, sS.tick() and 1),))
        s_c = sS.n
        if affine is not None:
            gt, btl = affine

            def aff(v):
                v.tensor_mul(dst_ap, dst_ap, gt[:])
                return v.tensor_add(dst_ap, dst_ap, btl[:])
            kb.emit("vector", aff, wait=((sS, s_c),), inc=((sV, sV.tick() and 1),))


    def ln_cols4(src4, sq_wait, ct=None):
        if ct is None:
            ct = lncol

        def chain1(v, sq_wait=sq_wait, src4=src4, ct=ct):
            v.wait_ge(sS.h, sq_wait)
            v.reduce_sum(ct[:, 4:8], src4, axis=AX.X)
            vfill(v)
            v.tensor_scalar(out=ct[:, 8:12], in0=ct[:, 4:8],
                            scalar1=1.0 / D, scalar2=None, op0=ALU.mult)
            v.tensor_scalar(out=ct[:, 4:8], in0=ct[:, 0:4],
                            scalar1=1.0 / D, scalar2=None, op0=ALU.mult)
            v.tensor_mul(ct[:, 12:16], ct[:, 8:12], ct[:, 8:12])
            vfill(v)
            return v.scalar_tensor_tensor(out=ct[:, 4:8], in0=ct[:, 12:16],
                                          scalar=-1.0, in1=ct[:, 4:8],
                                          op0=ALU.mult, op1=ALU.add)
        kb.emit("vector", chain1, inc=((sV, sV.tick() and 1),))
        v_a = sV.n
        kb.emit("scalar", lambda sc, ct=ct: sc.activation(
            out=ct[:, 12:16], in_=ct[:, 4:8], func=AF.Sqrt, bias=epsc[:]),
            wait=((sV, v_a), (sG, g_tab0)), inc=((sS, sS.tick() and 1),))
        s_b = sS.n

        def chain2(v, s_b=s_b, ct=ct):
            v.wait_ge(sS.h, s_b)
            v.reciprocal(ct[:, 0:4], ct[:, 12:16])
            vfill(v)
            return v.scalar_tensor_tensor(out=ct[:, 4:8], in0=ct[:, 8:12],
                                          scalar=-1.0, in1=ct[:, 0:4],
                                          op0=ALU.mult, op1=ALU.mult)
        kb.emit("vector", chain2, inc=((sV, sV.tick() and 1),))
        return sV.n

    sp_x2_last = 0
    sp_xq_last = 0
    x4n = ut[:].rearrange("p (s d) -> p s d", s=SUB)

    def ln1_stats(t_chunk, extra_wait):
        w = dma(sXA, "sync", x4n,
                x_h[t_chunk * TT:(t_chunk + 1) * TT, :].rearrange(
                    "(s p) d -> p s d", p=P), wait=extra_wait)
        for s_ in range(SUB):
            kb.emit("scalar", lambda sc, s_=s_: sc.activation(
                out=sqs[:], in_=x4n[:, s_, :], func=AF.Square,
                accum_out=lncol2[:, s_:s_ + 1]),
                wait=((sXA, w),) if s_ == 0 else (), inc=((sS, sS.tick() and 1),))
        return ln_cols4(x4n, sS.n, ct=lncol2)

    v_ln1stats = ln1_stats(0, ())

    def emit_prologue(tc, v_stats_ln1):
        if tc == 0:
            kb.emit("vector", lambda v: v.memset(ln1F3[:, :, 0:8], 0.0),
                    wait=((sG, g_tab),), inc=((sV, sV.tick() and 1),))
        else:
            kb.emit("vector", lambda v: v.tensor_copy(ln1F3[:, :, 0:8],
                                                      ln1F3[:, :, TT:TT + 8]),
                    inc=((sV, sV.tick() and 1),))
        v_lead = sV.n
        t_tr_last = 0
        for s_ in range(SUB):
            kb.emit("scalar", lambda sc, s_=s_: sc.activation(
                out=etok[:], in_=x4n[:, s_, :], func=AF.Identity,
                scale=lncol2[:, s_:s_ + 1], bias=lncol2[:, 4 + s_:5 + s_]),
                wait=((sV, v_stats_ln1), (sT, t_tr_last)),
                inc=((sS, sS.tick() and 1),))
            s_ln1 = sS.n
            for c in range(DH):
                bank = pC if (c % 2 == 0) else pD
                kb.emit("tensor", lambda t, c=c, bank=bank: t.transpose(
                    bank[:, 0:P], etok[:, c * P:(c + 1) * P], idf[:]),
                    wait=((sS, s_ln1), (sV, max(v_tab, v_lead)), (sV, sV.n)),
                    inc=((sT, sT.tick() and 1),))
                t_tr = sT.n
                kb.emit("vector", lambda v, c=c, s_=s_, bank=bank: v.tensor_copy(
                    ln1F3[:, c, 8 + s_ * P:8 + (s_ + 1) * P], bank[:, 0:P]),
                    wait=((sT, t_tr),), inc=((sV, sV.tick() and 1),))
            t_tr_last = sT.n
        return sV.n

    v_ln1f_next = emit_prologue(0, v_ln1stats)

    for tc in range(NCH):
        v_ln1f = v_ln1f_next

        def conv_emit(v):
            inst = None
            for c in range(DH):
                v.tensor_scalar(out=ut3[:, c, :], in0=ln1F3[:, c, 5:5 + TT],
                                scalar1=cw[:, c * DCONV:c * DCONV + 1],
                                scalar2=None, op0=ALU.mult)
                for j in (1, 2, 3):
                    inst = v.scalar_tensor_tensor(
                        out=ut3[:, c, :], in0=ln1F3[:, c, 5 + j:5 + j + TT],
                        scalar=cw[:, c * DCONV + j:c * DCONV + j + 1],
                        in1=ut3[:, c, :], op0=ALU.mult, op1=ALU.add)
            return inst
        kb.emit("vector", conv_emit,
                wait=((sV, v_ln1f), (sS, sS.n)), inc=((sV, sV.tick() and 1),))
        v_conv = sV.n

        def conv_silu(sc):
            inst = None
            for c in range(DH):
                inst = sc.activation(out=ut3[:, c, :], in_=ut3[:, c, :],
                                     func=AF.Silu, bias=cb[:, c:c + 1])
            return inst
        kb.emit("scalar", conv_silu, wait=((sV, v_conv),), inc=((sS, sS.tick() and 1),))
        s_u = sS.n
        if debug_scratch and tc == 1:
            spu = dma(sX2S, "gpsimd", dbg_u, ut[:], wait=((sS, s_u),))

        mm_group([(pbt[:, 0:TT], bpT3[:, c, :], ut3[:, c, :]) for c in range(DH)],
                 wait=((sS, s_u), (sPRM, PRM_ALL), (sS, sS.n)), inc=((sT, sT.tick() and 1),))
        t_bt = sT.n
        mm_group([(pbt[:, TT:2 * TT], cpT3[:, c, :], ut3[:, c, :]) for c in range(DH)],
                 wait=(), inc=((sT, sT.tick() and 1),))
        t_ct = sT.n
        kb.emit("scalar", lambda sc: sc.activation(out=btT[:], in_=pbt[:, 0:TT],
                                                   func=AF.Identity, bias=bpb[:]),
                wait=((sT, t_bt),), inc=((sS, sS.tick() and 1),))
        kb.emit("scalar", lambda sc: sc.activation(out=ctT[:], in_=pbt[:, TT:2 * TT],
                                                   func=AF.Identity, bias=cpb[:]),
                wait=((sT, t_ct),), inc=((sS, sS.tick() and 1),))
        s_btct = sS.n
        if debug_scratch and tc == 1:
            dma(sX2S, "gpsimd", dbg_bt[0:NST, 0:TT], btT[:], wait=((sS, s_btct),))
            dma(sX2S, "gpsimd", dbg_bt[0:NST, TT:2 * TT], ctT[:], wait=((sS, s_btct),))

        kb.emit("vector", lambda v: v.tensor_tensor(
            yy3, ut3, dD[:, :, None].broadcast_to([P, DH, TT]), ALU.mult),
            wait=((sS, s_u),), inc=((sV, sV.tick() and 1),))

        s_gate_evacs = []
        for n in range(NST):
            pb_ = pA if (n % 2 == 0) else pB
            pc_ = pC if (n % 2 == 0) else pD
            kb.emit("tensor", lambda t, n=n, pb_=pb_: t.matmul(
                pb_[:], oneh[:, n * P:(n + 1) * P], btT[:], start=True, stop=True),
                wait=((sS, s_btct), (sV, max(v_oneh, sV.n)), (sS, s_aexp)),
                inc=((sT, sT.tick() and 1),))
            t_b = sT.n
            kb.emit("tensor", lambda t, n=n, pc_=pc_: t.matmul(
                pc_[:], oneh[:, n * P:(n + 1) * P], ctT[:], start=True, stop=True),
                inc=((sT, sT.tick() and 1),))
            t_c = sT.n
            if 1 <= n <= DH:
                g = n - 1
                pg = ptr0 if (g % 2 == 0) else ptr1
                bank_free = () if g < 2 else ((sS, s_gate_evacs[g - 2]),)
                mm_group([(pg[:], gwT[:, c * D + g * P:c * D + (g + 1) * P],
                           ln1F3[:, c, 8:8 + TT]) for c in range(DH)],
                         wait=((sV, v_ln1f), (sPRM, PRM_ALL)) + bank_free,
                         inc=((sT, sT.tick() and 1),))
                t_g = sT.n
                kb.emit("scalar", lambda sc, g=g, pg=pg: sc.activation(
                    out=gzF3[:, g, :], in_=pg[:], func=AF.Silu, bias=gbv[:, g:g + 1]),
                    wait=((sT, t_g),), inc=((sS, sS.tick() and 1),))
                s_gate_evacs.append(sS.n)

            if gsl[0] == 0:
                g_yb[0] = sG.n
            base = gsl[0]
            sgate = stats_gate[0] if n == 0 else 0
            slice_ticks = [sV.tick() for _ in range(DH)]
            slots = [(base + c) % 6 for c in range(DH)]
            rws = [g_yb[0] + (base + c - 6) + 1 if base + c >= 6 else 0
                   for c in range(DH)]

            def scan_n(v, n=n, pb_=pb_, pc_=pc_, t_b=t_b, t_c=t_c,
                       slots=slots, rws=rws, sgate=sgate):
                v.wait_ge(sT.h, t_b)
                v.tensor_tensor(ub3, ut3, pb_[:, None, :].broadcast_to([P, DH, TT]),
                                ALU.mult)
                for c in range(DH):
                    v.tensor_tensor_scan(
                        out=hh3[:, c, :],
                        data0=aexp[:, c * NST + n:c * NST + n + 1].broadcast_to([P, TT]),
                        data1=ub3[:, c, :],
                        initial=hstate[:, n * DH + c:n * DH + c + 1],
                        op0=ALU.mult, op1=ALU.add)
                v.wait_ge(sT.h, t_c)
                if sgate:
                    v.wait_ge(sS.h, sgate)
                for c in range(DH):
                    if rws[c]:
                        v.wait_ge(sG.h, rws[c])
                    v.tensor_tensor(cmb3[:, slots[c], :], hh3[:, c, :], pc_[:],
                                    ALU.mult).then_inc(sV.h, 1)
                return v.tensor_copy(
                    hstate[:].rearrange("p (n h) -> p n h", n=NST)[:, n, :],
                    hh3[:, :, TT - 1])
            kb.emit("vector", scan_n, inc=((sV, sV.tick() and 1),))
            for c in range(DH):
                kb.emit("gpsimd", lambda g, c=c, sl=slots[c]: g.tensor_tensor(
                    yy3[:, c, :], yy3[:, c, :], cmb3[:, sl, :], ALU.add),
                    wait=((sV, slice_ticks[c]),), inc=((sG, sG.tick() and 1),))
                assert sG.n == g_yb[0] + base + c + 1
            gsl[0] = base + DH
        v_scan = sV.n
        g_yacc = sG.n
        dbg_w_y = 0
        if debug_scratch and tc == 1:
            dbg_w_y = dma(sX2S, "gpsimd", dbg_y, yy[:], wait=((sV, v_scan),))

        s_gate = s_gate_evacs[-1]
        dbg_w_g = 0
        if debug_scratch and tc == 1:
            dbg_w_g = dma(sX2S, "gpsimd", dbg_g, gzF[:], wait=((sS, s_gate),))

        kb.emit("vector", lambda v: v.tensor_mul(yy[:], yy[:], gzF[:]),
                wait=((sS, s_gate), (sX2S, dbg_w_y), (sG, g_yacc)),
                inc=((sV, sV.tick() and 1),))
        v_yg = sV.n

        z_evacs = []
        for g in range(DH):
            po = pC if (g % 2 == 0) else pD
            bank_free = ((sV, z_evacs[g - 2]),) if g >= 2 else ()
            mm_group([(po[:], owT[:, c * D + g * P:c * D + (g + 1) * P],
                       yy3[:, c, :]) for c in range(DH)],
                     wait=((sV, v_yg),) + bank_free, inc=((sT, sT.tick() and 1),))
            t_o = sT.n

            def z_evac(v, g=g, po=po, t_o=t_o, dbg_w_g=dbg_w_g):
                v.wait_ge(sT.h, t_o)
                if dbg_w_g:
                    v.wait_ge(sX2S.h, dbg_w_g)
                return v.scalar_tensor_tensor(out=gzF3[:, g, :], in0=po[:],
                                              scalar=obv[:, g:g + 1],
                                              in1=ln1F3[:, g, 8:8 + TT],
                                              op0=ALU.add, op1=ALU.add)
            kb.emit("vector", z_evac, inc=((sV, sV.tick() and 1),))
            z_evacs.append(sV.n)
        v_z = sV.n
        if debug_scratch and tc == 1:
            dbg_w_z = dma(sX2S, "gpsimd", dbg_z, gzF[:], wait=((sV, v_z),))
            sp_x2_last = max(sp_x2_last, dbg_w_z)

        if tc + 1 < NCH:
            v_ln1stats = ln1_stats(tc + 1, ((sV, v_yg),))
        emit_stats_boundary(tc, g_yacc)

        zt4 = hh[:].rearrange("p (s d) -> p s d", s=SUB)
        et4 = ub[:].rearrange("p (s d) -> p s d", s=SUB)
        xr4 = yy[:].rearrange("p (s d) -> p s d", s=SUB)
        qt4 = yy[:].bitcast(BF16)[:, 0:SUB * D].rearrange("p (s d) -> p s d", s=SUB)
        rows = slice(tc * TT, (tc + 1) * TT)
        for s_ in range(SUB):
            for c in range(DH):
                bank = ptr0 if (c % 2 == 0) else ptr1
                kb.emit("tensor", lambda t, c=c, s_=s_, bank=bank: t.transpose(
                    bank[:, 0:P], gzF3[:, c, s_ * P:(s_ + 1) * P], idf[:]),
                    wait=((sV, v_z), (sS, sS.n)), inc=((sT, sT.tick() and 1),))
                t_tr = sT.n
                kb.emit("scalar", lambda sc, c=c, s_=s_, bank=bank: sc.activation(
                    out=hh[:, s_ * D + c * P:s_ * D + (c + 1) * P],
                    in_=bank[:, 0:P], func=AF.Copy),
                    wait=((sT, t_tr), (sX2S, sp_x2_last), (sV, sV.n)),
                    inc=((sS, sS.tick() and 1),))
        if tc + 1 < NCH:
            v_ln1f_next = emit_prologue(tc + 1, v_ln1stats)

        for s_ in range(SUB):
            kb.emit("scalar", lambda sc, s_=s_: sc.activation(
                out=sqs[:], in_=zt4[:, s_, :], func=AF.Square,
                accum_out=lncol[:, s_:s_ + 1]), inc=((sS, sS.tick() and 1),))
        v_cols = ln_cols4(zt4, sS.n)
        for s_ in range(SUB):
            kb.emit("scalar", lambda sc, s_=s_: sc.activation(
                out=et4[:, s_, :], in_=zt4[:, s_, :], func=AF.Identity,
                scale=lncol[:, s_:s_ + 1], bias=lncol[:, 4 + s_:5 + s_]),
                wait=((sV, v_cols),) if s_ == 0 else (), inc=((sS, sS.tick() and 1),))
        s_innorm = sS.n

        def aff4(v, s_innorm=s_innorm):
            v.wait_ge(sS.h, s_innorm)
            v.tensor_tensor(et4, et4, lng_bc[:, None, :].broadcast_to([P, SUB, D]),
                            ALU.mult)
            return v.tensor_tensor(et4, et4,
                                   lnb_bc[:, None, :].broadcast_to([P, SUB, D]),
                                   ALU.add)
        kb.emit("vector", aff4, inc=((sV, sV.tick() and 1),))
        for s_ in range(SUB):
            kb.emit("scalar", lambda sc, s_=s_: sc.activation(
                out=sqs[:], in_=et4[:, s_, :], func=AF.Square,
                accum_out=lncol[:, s_:s_ + 1]),
                wait=((sV, sV.n),) if s_ == 0 else (), inc=((sS, sS.tick() and 1),))
        v_cols = ln_cols4(et4, sS.n)
        for s_ in range(SUB):
            kb.emit("scalar", lambda sc, s_=s_: sc.activation(
                out=zt4[:, s_, :], in_=et4[:, s_, :], func=AF.Identity,
                scale=lncol[:, s_:s_ + 1], bias=lncol[:, 4 + s_:5 + s_]),
                wait=((sV, v_cols),) if s_ == 0 else (), inc=((sS, sS.tick() and 1),))
        s_onorm = sS.n
        w_xr = dma(sXB, "sync", xr4,
                   x_h[rows, :].rearrange("(s p) d -> p s d", p=P),
                   wait=((sV, v_z),))
        kb.emit("vector", lambda v: v.tensor_add(hh[:], hh[:], yy[:]),
                wait=((sS, s_onorm), (sXB, w_xr)), inc=((sV, sV.tick() and 1),))
        v_x2 = sV.n
        sp_x2_last = dma(sX2S, "gpsimd",
                         x2_d[rows, :].rearrange("(s p) d -> p s d", p=P),
                         zt4, wait=((sV, v_x2),))
        for s_ in range(SUB):
            kb.emit("scalar", lambda sc, s_=s_: sc.activation(
                out=sqs[:], in_=zt4[:, s_, :], func=AF.Square,
                accum_out=lncol[:, s_:s_ + 1]),
                wait=((sV, v_x2),) if s_ == 0 else (), inc=((sS, sS.tick() and 1),))
        v_cols = ln_cols4(zt4, sS.n)
        for s_ in range(SUB):
            kb.emit("scalar", lambda sc, s_=s_: sc.activation(
                out=et4[:, s_, :], in_=zt4[:, s_, :], func=AF.Identity,
                scale=lncol[:, s_:s_ + 1], bias=lncol[:, 4 + s_:5 + s_]),
                wait=((sV, v_cols),) if s_ == 0 else (), inc=((sS, sS.tick() and 1),))
        s_ln3 = sS.n

        def q_cols4(v, s_ln3=s_ln3, tc=tc):
            v.wait_ge(sS.h, s_ln3)
            v.tensor_reduce(lncol[:, 0:4], et4, axis=AX.X, op=ALU.max,
                            apply_absolute_value=True)
            v.drain()
            v.tensor_scalar(out=lncol[:, 0:4], in0=lncol[:, 0:4],
                            scalar1=1e-5, scalar2=None, op0=ALU.max)
            v.drain()
            v.reciprocal(lncol[:, 4:8], lncol[:, 0:4])
            v.drain()
            v.tensor_scalar(out=lncol[:, 8:12], in0=lncol[:, 4:8],
                            scalar1=127.0, scalar2=None, op0=ALU.mult)
            return v.tensor_scalar(out=dsx[:, tc * SUB:(tc + 1) * SUB],
                                   in0=lncol[:, 0:4], scalar1=1.0 / 127.0,
                                   scalar2=None, op0=ALU.mult)
        kb.emit("vector", q_cols4, inc=((sV, sV.tick() and 1),))
        v_qc = sV.n
        for s_ in range(SUB):
            kb.emit("scalar", lambda sc, s_=s_: sc.activation(
                out=zt4[:, s_, :], in_=et4[:, s_, :], func=AF.Copy,
                scale=lncol[:, 8 + s_:9 + s_], bias=RB),
                wait=(((sV, v_qc), (sX2S, sp_x2_last)) if s_ == 0 else ()),
                inc=((sS, sS.tick() and 1),))
            s_q1 = sS.n

            def q_fin(v, s_=s_, s_q1=s_q1):
                v.wait_ge(sS.h, s_q1)
                v.tensor_copy(h16[:], zt4[:, s_, :])
                return v.tensor_scalar(out=qt4[:, s_, :], in0=h16[:], scalar1=-RB,
                                       scalar2=None, op0=ALU.add)
            kb.emit("vector", q_fin, inc=((sV, sV.tick() and 1),))
        v_q = sV.n
        sp_xq_last = dma(sXQS, "gpsimd",
                         xq_d[rows, :].rearrange("(s p) d -> p s d", p=P),
                         qt4, wait=((sV, v_q),))

    p2_V, p2_S, p2_T = sV.n, sS.n, sT.n
    p2_X2S, p2_XQS = sX2S.n, sXQS.n

    s_stats = stat_abs_ticks[-1]
    for mi in range(3):
        kb.emit("vector",
                lambda v, mi=mi: v.reduce_sum(colt[:, mi:mi + 1],
                                              wstat[:, mi * 32:(mi + 1) * 32],
                                              axis=AX.X),
                wait=((sS, s_stats),), inc=((sV, sV.tick() and 1),))
    v_sr = sV.n
    kb.emit("tensor", lambda t: t.matmul(pbt[0:1, 0:3], onec[:], colt[:, 0:3],
                                         start=True, stop=True),
            wait=((sV, v_sr),), inc=((sT, sT.tick() and 1),))
    t_s1 = sT.n
    kb.emit("vector", lambda v: v.tensor_copy(colt[0:1, 4:7], pbt[0:1, 0:3]),
            wait=((sT, t_s1),), inc=((sV, sV.tick() and 1),))
    v_s2 = sV.n
    kb.emit("tensor", lambda t: t.matmul(ptr0[:, 0:3], onerow[:],
                                         colt[0:1, 4:7], start=True, stop=True),
            wait=((sV, v_s2),), inc=((sT, sT.tick() and 1),))
    t_s2 = sT.n

    def stat_fin(v):
        v.tensor_scalar(out=colt[:, 0:3], in0=ptr0[:, 0:3],
                        scalar1=1.0 / (D * INNER), scalar2=1e-5,
                        op0=ALU.mult, op1=ALU.max)
        v.drain()
        v.tensor_copy(cwg[:], colt[:, 0:1])
        v.tensor_copy(cwu[:], colt[:, 1:2])
        v.tensor_copy(cwd[:], colt[:, 2:3])
        v.drain()
        v.reciprocal(swg[:], cwg[:])
        v.reciprocal(swu[:], cwu[:])
        return v.reciprocal(swd[:], cwd[:])
    kb.emit("vector", stat_fin, wait=((sT, t_s2),), inc=((sV, sV.tick() and 1),))
    v_stats = sV.n

    p2.close()
    p3 = contextlib.ExitStack()
    sb3 = mk(p3)
    wstg = [sb3("wstg0", [P, INNER]), sb3("wstg1", [P, INNER])]
    wgq = sb3("wgq", [P, DH * INNER], BF16)
    wuq = sb3("wuq", [P, DH * INNER], BF16)
    f16b = sb3("f16b", [P, INNER], F16)
    gu = sb3("gu", [P, INNER])
    gu1 = sb3("gu1", [P, TT])
    guq = sb3("guq", [P, INNER], BF16)
    xqc = sb3("xqc", [P, DH * P], BF16)

    PHW = ((sV, p2_V), (sS, p2_S), (sT, p2_T), (sX2S, p2_X2S), (sXQS, p2_XQS))

    s_wprev = 0
    first = True
    for (wdst, src, scol) in [(wgq, wgT_h, swg), (wuq, wuT_h, swu)]:
        for c in range(DH):
            stg = wstg[c % 2]
            sws = sW0 if c % 2 == 0 else sW1
            w_l = dma(sws, "sync", stg[:], src[c * P:(c + 1) * P, :],
                      wait=(PHW if first else ((sS, s_wprev),)))
            first = False
            kb.emit("scalar", lambda sc, stg=stg, scol=scol: sc.activation(
                out=gu[:], in_=stg[:], func=AF.Identity, scale=scol[:]),
                wait=((sws, w_l), (sV, max(v_stats, sV.n))),
                inc=((sS, sS.tick() and 1),))
            s_wq = sS.n
            s_wprev = s_wq

            def w_fin(v, wdst=wdst, c=c):
                v.tensor_scalar(out=f16b[:], in0=gu[:], scalar1=-0.5, scalar2=None,
                                op0=ALU.is_lt)
                return v.scalar_tensor_tensor(
                    out=wdst[:, c * INNER:(c + 1) * INNER], in0=gu[:], scalar=0.5,
                    in1=f16b[:], op0=ALU.is_gt, op1=ALU.subtract)
            kb.emit("vector", w_fin, wait=((sS, s_wq),), inc=((sV, sV.tick() and 1),))
    v_wq = sV.n

    if debug_scratch:
        dma(sX2S, "gpsimd", dbg_wgq, wgq[:], wait=((sV, v_wq),))
        def dbg_cp(v):
            v.tensor_copy(sqs[:, 0:24], wstat[:])
            v.tensor_copy(sqs[:, 24:32], colt[:])
            v.tensor_copy(sqs[:, 32:33], swg[:])
            v.tensor_copy(sqs[:, 33:34], swu[:])
            v.tensor_copy(sqs[:, 34:35], swd[:])
            v.tensor_copy(sqs[:, 35:36], cwg[:])
            v.tensor_copy(sqs[:, 36:37], cwu[:])
            return v.tensor_copy(sqs[:, 37:38], cwd[:])
        kb.emit("vector", dbg_cp, wait=((sV, v_wq),), inc=((sV, sV.tick() and 1),))
        dma(sX2S, "gpsimd", dbg_cols, sqs[:, 0:64], wait=((sV, sV.n),))
    xstg = sb3("xstg", [P, D], BF16)
    idb = sb3("idb", [P, P], BF16)
    kb.emit("vector", lambda v: v.tensor_copy(idb[:], idf[:]),
            wait=((sV, v_wq),), inc=((sV, sV.tick() and 1),))
    v_idb = sV.n
    sp_gq = 0
    s_xq_evacs = []
    for i in range(NTOK):
        trow = i * P
        wv0 = dma(sXQ, "sync", xstg[:], xq_d[trow:trow + P, :],
                  wait=(((sS, s_xq_evacs[-1] if s_xq_evacs else 0),
                         (sXQS, p2_XQS))))
        wv = wv0
        for c in range(DH):
            pt = ptr0 if c % 2 == 0 else ptr1
            bankw = ((sS, s_xq_evacs[-2],) if len(s_xq_evacs) >= 2 else (sS, 0))
            kb.emit("tensor", lambda t, c=c, pt=pt: t.transpose(
                pt[:, 0:64].bitcast(BF16), xstg[:, c * P:(c + 1) * P], idb[:]),
                wait=((sXQ, wv0), (sV, v_idb), bankw, (sT, sT.n)),
                inc=((sT, sT.tick() and 1),))
            t_tr = sT.n
            kb.emit("scalar", lambda sc, c=c, pt=pt: sc.activation(
                out=xqc[:, c * P:(c + 1) * P], in_=pt[:, 0:64].bitcast(BF16),
                func=AF.Copy), wait=((sT, t_tr),), inc=((sS, sS.tick() and 1),))
            s_xq_evacs.append(sS.n)

        def mk_desc(v, i=i):
            v.tensor_mul(lncol[:, 0:1], cwg[:], dsx[:, i:i + 1])
            v.tensor_mul(lncol[:, 1:2], cwu[:], dsx[:, i:i + 1])
            return v.drain()
        kb.emit("vector", mk_desc, wait=((sV, v_wq), (sS, sS.n)),
                inc=((sV, sV.tick() and 1),))
        v_ds = sV.n
        s_up_evacs = []
        for f in range(DH):
            pg = pA if (f % 2 == 0) else pB
            pu = pC if (f % 2 == 0) else pD
            bank_free = ((sS, s_up_evacs[f - 2]),) if f >= 2 else ()
            mm_group([(pg[:], xqc[:, c * P:(c + 1) * P],
                       wgq[:, c * INNER + f * TT:c * INNER + (f + 1) * TT])
                      for c in range(DH)],
                     wait=((sS, s_xq_evacs[-1]), (sV, v_ds)) + bank_free,
                     inc=((sT, sT.tick() and 1),))
            t_g = sT.n
            mm_group([(pu[:], xqc[:, c * P:(c + 1) * P],
                       wuq[:, c * INNER + f * TT:c * INNER + (f + 1) * TT])
                      for c in range(DH)],
                     wait=(), inc=((sT, sT.tick() and 1),))
            t_u = sT.n
            kb.emit("scalar", lambda sc, pg=pg: sc.activation(
                out=gu1[:], in_=pg[:], func=AF.Silu, scale=lncol[:, 0:1]),
                wait=((sT, t_g), (sV, sV.n)), inc=((sS, sS.tick() and 1),))
            kb.emit("scalar", lambda sc, pu=pu: sc.activation(
                out=sqs[:, 0:TT], in_=pu[:], func=AF.Copy, scale=lncol[:, 1:2]),
                wait=((sT, t_u), (sV, sV.n)), inc=((sS, sS.tick() and 1),))
            s_up = sS.n
            s_up_evacs.append(s_up)

            def gu_mul(v, f=f, s_up=s_up):
                v.wait_ge(sS.h, s_up)
                v.tensor_tensor(gu[:, f * TT:(f + 1) * TT], gu1[:], sqs[:, 0:TT],
                                ALU.mult)
                v.tensor_reduce(lncol[:, 2:3], gu[:, f * TT:(f + 1) * TT],
                                axis=AX.X, op=ALU.max, apply_absolute_value=True)
                v.drain()
                if f == 0:
                    return v.tensor_copy(lncol[:, 3:4], lncol[:, 2:3])
                return v.tensor_max(lncol[:, 3:4], lncol[:, 3:4], lncol[:, 2:3])
            kb.emit("vector", gu_mul, inc=((sV, sV.tick() and 1),))

        def gq_cols(v, i=i):
            v.tensor_scalar(out=lncol[:, 3:4], in0=lncol[:, 3:4], scalar1=1e-5,
                            scalar2=None, op0=ALU.max)
            v.drain()
            v.reciprocal(lncol[:, 4:5], lncol[:, 3:4])
            v.drain()
            v.tensor_scalar(out=lncol[:, 5:6], in0=lncol[:, 4:5], scalar1=127.0,
                            scalar2=None, op0=ALU.mult)
            return v.tensor_scalar(out=dsg[:, i:i + 1], in0=lncol[:, 3:4],
                                   scalar1=1.0 / 127.0, scalar2=None, op0=ALU.mult)
        kb.emit("vector", gq_cols, wait=((sS, sS.n),),
                inc=((sV, sV.tick() and 1),))
        v_gqc = sV.n
        if debug_scratch and i == 0:
            dma(sX2S, "gpsimd", dbg_xqc, xqc[:], wait=((sV, v_gqc),))
            dma(sX2S, "gpsimd", dbg_gu, gu[:], wait=((sV, v_gqc),))
        kb.emit("scalar", lambda sc: sc.activation(
            out=f16b[:], in_=gu[:], func=AF.Copy, scale=lncol[:, 5:6], bias=RB),
            wait=((sV, v_gqc),), inc=((sS, sS.tick() and 1),))
        s_gq = sS.n

        def g_fin(v, s_gq=s_gq, sp_gq=sp_gq):
            v.wait_ge(sS.h, s_gq)
            v.wait_ge(sGQS.h, sp_gq)
            return v.tensor_scalar(out=guq[:], in0=f16b[:], scalar1=-RB,
                                   scalar2=None, op0=ALU.add)
        kb.emit("vector", g_fin, inc=((sV, sV.tick() and 1),))
        v_gq = sV.n
        sp_gq = dma(sGQS, "gpsimd", gq_d[trow:trow + P, :], guq[:],
                    wait=((sV, v_gq),))
    p3_V, p3_S, p3_T, p3_GQS = sV.n, sS.n, sT.n, sGQS.n

    p3.close()
    p4 = contextlib.ExitStack()
    sb4 = mk(p4)
    wdq = sb4("wdq", [P, IH * D], BF16)
    gqT = sb4("gqT", [P, IH * P], BF16)
    gqT2 = sb4("gqT2", [P, IH * P], BF16)
    x2b_ = [sb4("x2b", [P, SUB * D]), sb4("x2c", [P, SUB * D])]
    o2b_ = [sb4("o2b", [P, SUB * D]), sb4("o2c", [P, SUB * D])]
    gstg_ = [sb4("gstg", [P, INNER], BF16), sb4("gsth", [P, INNER], BF16)]
    idb4 = sb4("idb4", [P, P], BF16)
    wd16 = sb4("wd16", [P, D], F16)
    wdf = sb4("wdf", [P, D])
    wst4 = [sb4("wst4a", [P, D]), sb4("wst4b", [P, D])]

    PHW3 = ((sV, p3_V), (sS, p3_S), (sT, p3_T), (sGQS, p3_GQS))
    kb.emit("vector", lambda v: v.tensor_copy(idb4[:], idf[:]),
            wait=((sV, sV.n),), inc=((sV, sV.tick() and 1),))
    v_idb4 = sV.n
    s_gq_evacs = []
    v_wd_ticks = []
    s_wprev = 0
    first = True
    for c in range(IH):
        stg = wst4[c % 2]
        sws = sW0 if c % 2 == 0 else sW1
        w_l = dma(sws, "sync", stg[:, 0:D], wdT_h[c * P:(c + 1) * P, :],
                  wait=(PHW3 if first else ((sS, s_wprev),)))
        first = False
        kb.emit("scalar", lambda sc, stg=stg: sc.activation(
            out=wdf[:], in_=stg[:, 0:D], func=AF.Identity, scale=swd[:]),
            wait=((sws, w_l), (sV, sV.n)), inc=((sS, sS.tick() and 1),))
        s_wq = sS.n
        s_wprev = s_wq

        def wd_fin(v, c=c):
            v.tensor_scalar(out=wd16[:], in0=wdf[:], scalar1=-0.5, scalar2=None,
                            op0=ALU.is_lt)
            return v.scalar_tensor_tensor(
                out=wdq[:, c * D:(c + 1) * D], in0=wdf[:], scalar=0.5,
                in1=wd16[:], op0=ALU.is_gt, op1=ALU.subtract)
        kb.emit("vector", wd_fin, wait=((sS, s_wq),), inc=((sV, sV.tick() and 1),))
        v_wd_ticks.append(sV.n)
    v_wd = sV.n
    if debug_scratch:
        dma(sX2S, "gpsimd", dbg_wdq, wdq[:], wait=((sV, v_wd),))

    v_o_dones = [0, 0]
    dm_outs = [0, 0]
    tr_lasts = [0, 0]
    for gi in range(NTOK // SUB):
        x2b = x2b_[gi % 2]
        o2b = o2b_[gi % 2]
        o2b4 = o2b[:].rearrange("p (s d) -> p s d", s=SUB)
        x2b4 = x2b[:].rearrange("p (s d) -> p s d", s=SUB)
        rows = slice(gi * SUB * P, (gi + 1) * SUB * P)
        w_x2 = dma(sX2, "sync", x2b4,
                   x2_d[rows, :].rearrange("(s p) d -> p s d", p=P),
                   wait=((sV, v_o_dones[gi % 2]), (sX2S, p2_X2S)))
        def mk_dd(v, gi=gi):
            v.wait_ge(sV.h, v_wd)
            v.tensor_tensor(colt[:, 0:SUB], cwd[:].broadcast_to([P, SUB]),
                            dsg[:, gi * SUB:(gi + 1) * SUB], ALU.mult)
            return v.drain()
        kb.emit("vector", mk_dd, wait=((sS, sS.n),), inc=((sV, sV.tick() and 1),))
        v_dd = sV.n
        for il in range(SUB):
            i = gi * SUB + il
            trow = i * P
            gq_t = gqT if il % 2 == 0 else gqT2
            gstg = gstg_[i % 2]
            wvg = dma(sGQT, "sync", gstg[:], gq_d[trow:trow + P, :],
                      wait=(((sT, tr_lasts[i % 2]), (sGQS, p3_GQS))))
            for c in range(IH):
                pt = pC if c % 2 == 0 else pD
                bankw = ((sS, s_gq_evacs[-2]) if len(s_gq_evacs) >= 2 else (sS, 0))
                kb.emit("tensor", lambda t, c=c, pt=pt, gq_t=gq_t, gstg=gstg:
                    t.transpose(
                        pt[:, 0:64].bitcast(BF16), gstg[:, c * P:(c + 1) * P],
                        idb4[:]),
                    wait=((sGQT, wvg), (sV, v_idb4), bankw, (sT, sT.n)),
                    inc=((sT, sT.tick() and 1),))
                t_tr = sT.n
                kb.emit("scalar", lambda sc, c=c, pt=pt, gq_t=gq_t: sc.activation(
                    out=gq_t[:, c * P:(c + 1) * P], in_=pt[:, 0:64].bitcast(BF16),
                    func=AF.Copy), wait=((sT, t_tr),), inc=((sS, sS.tick() and 1),))
                s_gq_evacs.append(sS.n)
            tr_lasts[i % 2] = sT.n
            for g2 in range(2):
                pq = pA if (g2 % 2 == 0) else pB
                mmw = None
                wdw = ((sV, v_wd),) if (gi == 0 and il == 0) else ()
                mm_group([(pq[:], gq_t[:, c * P:(c + 1) * P],
                           wdq[:, c * D + g2 * TT:c * D + (g2 + 1) * TT])
                          for c in range(IH)],
                         wait=((sS, s_gq_evacs[-1]),) + wdw,
                         inc=((sT, sT.tick() and 1),), mm_waits=mmw)
                t_d = sT.n
                kb.emit("scalar", lambda sc, il=il, g2=g2, pq=pq, o2b=o2b:
                    sc.activation(
                        out=o2b[:, il * D + g2 * TT:il * D + (g2 + 1) * TT],
                        in_=pq[:], func=AF.Copy, scale=colt[:, il:il + 1]),
                    wait=((sT, t_d), (sV, v_dd), (sOUT, dm_outs[gi % 2])),
                    inc=((sS, sS.tick() and 1),))
        s_o2 = sS.n
        for s_ in range(SUB):
            kb.emit("scalar", lambda sc, s_=s_, o2b4=o2b4: sc.activation(
                out=sqs[:], in_=o2b4[:, s_, :], func=AF.Square,
                accum_out=lncol[:, s_:s_ + 1]),
                wait=((sV, sV.n),) if s_ == 0 else (), inc=((sS, sS.tick() and 1),))
        v_cols = ln_cols4(o2b4, sS.n)
        for s_ in range(SUB):
            kb.emit("scalar", lambda sc, s_=s_, o2b4=o2b4: sc.activation(
                out=o2b4[:, s_, :], in_=o2b4[:, s_, :], func=AF.Identity,
                scale=lncol[:, s_:s_ + 1], bias=lncol[:, 4 + s_:5 + s_]),
                wait=((sV, v_cols),) if s_ == 0 else (), inc=((sS, sS.tick() and 1),))
        s_ln4 = sS.n
        kb.emit("vector", lambda v, o2b=o2b, x2b=x2b: v.tensor_add(
            o2b[:], o2b[:], x2b[:]),
            wait=((sS, s_ln4), (sX2, w_x2)), inc=((sV, sV.tick() and 1),))
        v_o_dones[gi % 2] = sV.n
        dm_outs[gi % 2] = dma(sOUT, "gpsimd",
                              out_h[rows, :].rearrange("(s p) d -> p s d", p=P),
                              o2b4, wait=((sV, sV.n),))

    kb.emit("gpsimd", lambda g: None, wait=((sOUT, sOUT.n),))
    p4.close()

    with top:
        with nc.Block() as block:
            kb.run(block)
    return nc


_NC_CACHE = None


def kernel(**inputs):
    global _NC_CACHE
    f32 = np.float32
    T = lambda a: np.ascontiguousarray(np.asarray(a, dtype=f32).T)
    C = lambda a: np.ascontiguousarray(np.asarray(a, dtype=f32))
    x = C(inputs["L0_attn_x"])
    common = {
        "conv_w": C(np.asarray(inputs["L0_attn_conv_w"], dtype=f32)[:, 0, :]),
        "conv_b": C(inputs["L0_attn_conv_b"]),
        "A": C(inputs["L0_attn_A"]),
        "BpT": T(inputs["L0_attn_Bp_w"]),
        "Bpb": C(inputs["L0_attn_Bp_b"]),
        "CpT": T(inputs["L0_attn_Cp_w"]),
        "Cpb": C(inputs["L0_attn_Cp_b"]),
        "Dd": C(inputs["L0_attn_D"]),
        "gwT": T(inputs["L0_attn_gate_w"]),
        "gb": C(inputs["L0_attn_gate_b"]),
        "owT": T(inputs["L0_attn_out_w"]),
        "ob": C(inputs["L0_attn_out_b"]),
        "lng": C(inputs["L0_attn_ln_g"]),
        "lnb": C(inputs["L0_attn_ln_b"]),
        "wgT": T(inputs["L0_ffn_w_gate"]),
        "wuT": T(inputs["L0_ffn_w_up"]),
        "wdT": T(inputs["L0_ffn_w_down"]),
    }
    in_maps = [dict(common, x=x[c]) for c in range(8)]
    if _NC_CACHE is None:
        _NC_CACHE = build_core()
    res = run_bass_kernel_spmd(_NC_CACHE, in_maps, core_ids=list(range(8)))
    return np.stack([np.asarray(r["out"], dtype=f32) for r in res.results])


# version 19
# speedup vs baseline: 1.0423x; 1.0048x over previous
import contextlib
import numpy as np
import concourse.bass as bass
import concourse.mybir as mybir
from concourse.bass_utils import run_bass_kernel_spmd

F32 = mybir.dt.float32
F16 = mybir.dt.float16
BF16 = mybir.dt.bfloat16
AF = mybir.ActivationFunctionType
ALU = mybir.AluOpType
AX = mybir.AxisListType

P = 128
S = 2048
D = 1024
DH = D // P
NST = 16
DCONV = 4
INNER = 4096
IH = INNER // P
TT = 512
NCH = S // TT
SUB = TT // P
NTOK = S // P
LN_EPS = 1e-5
RB = 1536.0


class Sem:
    def __init__(self, h):
        self.h = h
        self.n = 0

    def tick(self, k=1):
        self.n += k
        return self.n


class KB:
    ENGS = ("sync", "tensor", "vector", "scalar", "gpsimd")

    def __init__(self, nc):
        self.nc = nc
        self.q = {e: [] for e in self.ENGS}

    def emit(self, eng, fn, wait=(), inc=()):
        wait = tuple((s.h, v) for s, v in wait)
        inc = tuple((s.h, k) for s, k in inc)

        def g(h):
            for sh, v in wait:
                h.wait_ge(sh, v)
            inst = fn(h)
            for sh, k in inc:
                inst.then_inc(sh, k)
        self.q[eng].append(g)

    def run(self, block):
        for eng in self.ENGS:
            fns = self.q[eng]
            getattr(block, eng)(lambda h, fns=fns: [f(h) for f in fns])


def build_core(debug_scratch=False):
    nc = bass.Bass("TRN2", target_bir_lowering=False, detect_race_conditions=False)

    x_h = nc.dram_tensor("x", [S, D], F32, kind="ExternalInput").ap()
    conv_w_h = nc.dram_tensor("conv_w", [D, DCONV], F32, kind="ExternalInput").ap()
    conv_b_h = nc.dram_tensor("conv_b", [D], F32, kind="ExternalInput").ap()
    A_h = nc.dram_tensor("A", [D, NST], F32, kind="ExternalInput").ap()
    BpT_h = nc.dram_tensor("BpT", [D, NST], F32, kind="ExternalInput").ap()
    Bpb_h = nc.dram_tensor("Bpb", [NST], F32, kind="ExternalInput").ap()
    CpT_h = nc.dram_tensor("CpT", [D, NST], F32, kind="ExternalInput").ap()
    Cpb_h = nc.dram_tensor("Cpb", [NST], F32, kind="ExternalInput").ap()
    Dd_h = nc.dram_tensor("Dd", [D], F32, kind="ExternalInput").ap()
    gwT_h = nc.dram_tensor("gwT", [D, D], F32, kind="ExternalInput").ap()
    gb_h = nc.dram_tensor("gb", [D], F32, kind="ExternalInput").ap()
    owT_h = nc.dram_tensor("owT", [D, D], F32, kind="ExternalInput").ap()
    ob_h = nc.dram_tensor("ob", [D], F32, kind="ExternalInput").ap()
    lng_h = nc.dram_tensor("lng", [D], F32, kind="ExternalInput").ap()
    lnb_h = nc.dram_tensor("lnb", [D], F32, kind="ExternalInput").ap()
    wgT_h = nc.dram_tensor("wgT", [D, INNER], F32, kind="ExternalInput").ap()
    wuT_h = nc.dram_tensor("wuT", [D, INNER], F32, kind="ExternalInput").ap()
    wdT_h = nc.dram_tensor("wdT", [INNER, D], F32, kind="ExternalInput").ap()
    out_h = nc.dram_tensor("out", [S, D], F32, kind="ExternalOutput").ap()
    _scr = "ExternalOutput" if debug_scratch else "Internal"
    x2_d = nc.dram_tensor("x2_scr", [S, D], F32, kind=_scr).ap()
    xq_d = nc.dram_tensor("xq_scr", [S, D], BF16, kind=_scr).ap()
    gq_d = nc.dram_tensor("gq_scr", [S, INNER], BF16, kind=_scr).ap()
    if debug_scratch:
        dbg_st = nc.dram_tensor("dbg_st", [P, 40], F32, kind=_scr).ap()
        dbg_wgq = nc.dram_tensor("dbg_wgq", [P, DH * INNER], BF16, kind=_scr).ap()
        dbg_xqc = nc.dram_tensor("dbg_xqc", [P, DH * P], BF16, kind=_scr).ap()
        dbg_gu = nc.dram_tensor("dbg_gu", [P, INNER], F32, kind=_scr).ap()
        dbg_cols = nc.dram_tensor("dbg_cols", [P, 64], F32, kind=_scr).ap()
        dbg_wdq = nc.dram_tensor("dbg_wdq", [P, IH * D], BF16, kind=_scr).ap()
        dbg_u = nc.dram_tensor("dbg_u", [P, DH * TT], F32, kind=_scr).ap()
        dbg_bt = nc.dram_tensor("dbg_bt", [NST, 2 * TT], F32, kind=_scr).ap()
        dbg_y = nc.dram_tensor("dbg_y", [P, DH * TT], F32, kind=_scr).ap()
        dbg_g = nc.dram_tensor("dbg_g", [P, DH * TT], F32, kind=_scr).ap()
        dbg_z = nc.dram_tensor("dbg_z", [P, DH * TT], F32, kind=_scr).ap()
        dbg_e = nc.dram_tensor("dbg_e", [P, 40], F32, kind=_scr).ap()

    kb = KB(nc)
    top = contextlib.ExitStack()

    def mk(es_):
        sbf = lambda n, shp, dt=F32: es_.enter_context(nc.sbuf_tensor(n, shp, dt))
        return sbf
    sb = mk(top)
    sem = lambda n: Sem(top.enter_context(nc.semaphore(n)))

    sIN = sem("sIN")
    sSP = sem("sSP")
    sXA = sem("sXA")
    sXB = sem("sXB")
    sW0 = sem("sW0")
    sW1 = sem("sW1")
    sWG = sem("sWG")
    sPRM = sem("sPRM")
    sXQ = sem("sXQ")
    sGQT = sem("sGQT")
    sX2 = sem("sX2")
    sX2S = sem("sX2S")
    sXQS = sem("sXQS")
    sGQS = sem("sGQS")
    sOUT = sem("sOUT")
    sT = sem("sT")
    sV = sem("sV")
    sS = sem("sS")
    sG = sem("sG")
    dmo = sem("dmo")

    pes = contextlib.ExitStack()
    top.enter_context(pes)
    ptr0 = pes.enter_context(nc.psum_tensor("ptr0", [P, TT], F32))
    ptr1 = pes.enter_context(nc.psum_tensor("ptr1", [P, TT], F32))
    pA = pes.enter_context(nc.psum_tensor("pA", [P, TT], F32))
    pB = pes.enter_context(nc.psum_tensor("pB", [P, TT], F32))
    pC = pes.enter_context(nc.psum_tensor("pC", [P, TT], F32))
    pD = pes.enter_context(nc.psum_tensor("pD", [P, TT], F32))
    pbt = pes.enter_context(nc.psum_tensor("pbt", [NST, 2 * TT], F32))

    def mm_group(plist, wait, inc, mm_waits=None):
        n = len(plist)

        def fn(t):
            inst = None
            for i, (o, l, r) in enumerate(plist):
                if mm_waits is not None and mm_waits[i]:
                    for sh, val in mm_waits[i]:
                        t.wait_ge(sh, val)
                inst = t.matmul(o, l, r, start=(i == 0),
                                stop=(i % 4 == 3 or i == n - 1),
                                skip_group_check=(i >= 4))
            return inst
        kb.emit("tensor", fn, wait=wait, inc=inc)

    def vfill(v):
        v.tensor_scalar(out=jnk[:, 0:1], in0=onec[:], scalar1=1.0,
                        scalar2=None, op0=ALU.mult)
        v.tensor_scalar(out=jnk[:, 1:2], in0=onec[:], scalar1=1.0,
                        scalar2=None, op0=ALU.mult)

    def dma(s_, eng, out_ap, in_ap, wait=(), **kw):
        v = s_.tick(16)
        kb.emit(eng, lambda e, o=out_ap, i=in_ap, kw=kw: e.dma_start(out=o, in_=i, **kw),
                wait=wait, inc=((s_, 16),))
        return v

    sqs = sb("sqs", [P, D])
    lncol = sb("lncol", [P, 16])
    colt = sb("colt", [P, 8])
    onec = sb("onec", [P, 1])
    jnk = sb("jnk", [P, 2])
    onerow = sb("onerow", [1, P])
    epsc = sb("epsc", [P, 1])
    idf = sb("idf", [P, P])
    wstat = sb("wstat", [P, 96])
    swg, swu, swd = sb("swg", [P, 1]), sb("swu", [P, 1]), sb("swd", [P, 1])
    cwg, cwu, cwd = sb("cwg", [P, 1]), sb("cwu", [P, 1]), sb("cwd", [P, 1])
    dsx = sb("dsx", [P, NTOK])
    dsg = sb("dsg", [P, NTOK])
    h16 = sb("h16", [P, D], F16)
    dbgc = sb("dbgc", [P, 40]) if debug_scratch else None

    kb.emit("gpsimd", lambda g: g.iota(idf[:], [[1, P]], channel_multiplier=-1,
                                       allow_small_or_imprecise_dtypes=True))
    kb.emit("gpsimd", lambda g: g.memset(onec[:], 1.0))
    kb.emit("gpsimd", lambda g: g.memset(onerow[:], 1.0))
    kb.emit("gpsimd", lambda g: g.memset(epsc[:], LN_EPS), inc=((sG, sG.tick() and 1),))
    g_tab0 = sG.n
    kb.emit("vector", lambda v: v.tensor_scalar(out=idf[:], in0=idf[:], scalar1=0.0,
                                                scalar2=None, op0=ALU.is_equal),
            wait=((sG, g_tab0),), inc=((sV, sV.tick() and 1),))
    v_tab = sV.n


    p2 = contextlib.ExitStack()
    sb2 = mk(p2)
    aexp = sb2("aexp", [P, DH * NST])
    cw = sb2("cw", [P, DH * DCONV])
    cb = sb2("cb", [P, DH])
    dD = sb2("dD", [P, DH])
    gbv = sb2("gbv", [P, DH])
    obv = sb2("obv", [P, DH])
    bpb = sb2("bpb", [NST, 1])
    cpb = sb2("cpb", [NST, 1])
    lng_bc = sb2("lng_bc", [P, D])
    lnb_bc = sb2("lnb_bc", [P, D])
    oneh = sb2("oneh", [NST, NST * P])
    cmb = sb2("cmb", [P, 6 * TT])
    hstate = sb2("hstate", [P, NST * DH])
    gwT = sb2("gwT_s", [P, DH * D])
    owT = sb2("owT_s", [P, DH * D])
    bpT = sb2("bpT", [P, DH * NST])
    cpT = sb2("cpT", [P, DH * NST])
    LF = TT + 8
    ln1F = sb2("ln1F", [P, DH * LF])
    ut = sb2("ut", [P, DH * TT])
    ub = sb2("ub", [P, DH * TT])
    hh = sb2("hh", [P, DH * TT])
    yy = sb2("yy", [P, DH * TT])
    gzF = sb2("gzF", [P, DH * TT])
    btT = sb2("btT", [NST, TT])
    ctT = sb2("ctT", [NST, TT])
    etok = sb2("etok", [P, D])
    lncol2 = sb2("lncol2", [P, 16])

    ln1F3 = ln1F[:].rearrange("p (h t) -> p h t", h=DH)
    ut3 = ut[:].rearrange("p (h t) -> p h t", h=DH)
    ub3 = ub[:].rearrange("p (h t) -> p h t", h=DH)
    hh3 = hh[:].rearrange("p (h t) -> p h t", h=DH)
    yy3 = yy[:].rearrange("p (h t) -> p h t", h=DH)
    cmb3 = cmb[:].rearrange("p (h t) -> p h t", h=6)
    g_yb = [0]
    gsl = [0]
    gzF3 = gzF[:].rearrange("p (h t) -> p h t", h=DH)
    bpT3 = bpT[:].rearrange("p (h n) -> p h n", h=DH)
    cpT3 = cpT[:].rearrange("p (h n) -> p h n", h=DH)

    wdT_blk = wdT_h.rearrange("(b p q) d -> b p (q d)", b=8, p=P)
    stat_blocks = []
    for mi, wsrc in enumerate([wgT_h, wuT_h, wdT_h]):
        for b in range(8):
            for q in range(4):
                if mi < 2:
                    ap_ = wsrc[b * P:(b + 1) * P, q * D:(q + 1) * D]
                else:
                    ap_ = wdT_blk[b][:, q * D:(q + 1) * D]
                stat_blocks.append((ap_, mi * 32 + b * 4 + q))
    stat_abs_ticks = []
    stats_gate = [0]

    def emit_stats_boundary(tc, g_free):
        slot_sems = (sW0, sW1, sWG)
        for k in range(24):
            blk = tc * 24 + k
            ap_, col = stat_blocks[blk]
            wt = ((sG, g_free),) if k == 0 else ()
            if blk >= 3:
                wt = wt + ((sS, stat_abs_ticks[blk - 3]),)
            sws = slot_sems[blk % 3]
            w_l = dma(sws, "sync", cmb[:, (blk % 3) * D:(blk % 3 + 1) * D],
                      ap_, wait=wt)
            kb.emit("scalar", lambda sc, blk=blk, col=col: sc.activation(
                out=cmb[:, (blk % 3) * D:(blk % 3 + 1) * D],
                in_=cmb[:, (blk % 3) * D:(blk % 3 + 1) * D],
                func=AF.Abs, accum_out=wstat[:, col:col + 1]),
                wait=((sws, w_l),), inc=((sS, sS.tick() and 1),))
            stat_abs_ticks.append(sS.n)
        stats_gate[0] = sS.n
    v_stats = sV.n
    if False and debug_scratch:
        def dbg_stats(v):
            v.tensor_copy(dbgc[:, 0:24], wstat[:])
            v.tensor_copy(dbgc[:, 24:32], colt[:])
            v.tensor_copy(dbgc[:, 32:33], swg[:])
            v.tensor_copy(dbgc[:, 33:34], cwg[:])
            v.tensor_copy(dbgc[:, 34:37], ptr0[:, 0:3])
            return v.tensor_copy(dbgc[:, 37:38], onec[:])
        kb.emit("vector", dbg_stats, wait=((sV, v_stats),), inc=((sV, sV.tick() and 1),))
        dma(sX2S, "gpsimd", dbg_st, dbgc[:], wait=((sV, sV.n),))

    dma(sPRM, "sync", aexp[:].rearrange("p (h n) -> p h n", h=DH),
        A_h.rearrange("(h p) n -> p h n", p=P))
    dma(sPRM, "sync", cw[:].rearrange("p (h j) -> p h j", h=DH),
        conv_w_h.rearrange("(h p) j -> p h j", p=P))
    dma(sPRM, "sync", cb[:], conv_b_h.rearrange("(h p) -> p h", p=P), allow_slow_non_contiguous=True)
    dma(sPRM, "sync", dD[:], Dd_h.rearrange("(h p) -> p h", p=P), allow_slow_non_contiguous=True)
    dma(sPRM, "sync", gbv[:], gb_h.rearrange("(h p) -> p h", p=P), allow_slow_non_contiguous=True)
    dma(sPRM, "sync", obv[:], ob_h.rearrange("(h p) -> p h", p=P), allow_slow_non_contiguous=True)
    dma(sPRM, "sync", bpb[:], Bpb_h[:, None])
    dma(sPRM, "sync", cpb[:], Cpb_h[:, None])
    dma(sPRM, "sync", lng_bc[:], lng_h[None, :].broadcast_to([P, D]))
    dma(sPRM, "sync", lnb_bc[:], lnb_h[None, :].broadcast_to([P, D]))
    in_params = sPRM.n
    dma(sPRM, "sync", bpT3, BpT_h.rearrange("(h p) n -> p h n", p=P))
    dma(sPRM, "sync", cpT3, CpT_h.rearrange("(h p) n -> p h n", p=P))
    in_bc = sPRM.n
    for c in range(DH):
        dma(sPRM, "sync", gwT[:, c * D:(c + 1) * D], gwT_h[c * P:(c + 1) * P, :])
    for c in range(DH):
        dma(sPRM, "sync", owT[:, c * D:(c + 1) * D], owT_h[c * P:(c + 1) * P, :])
    in_w = sPRM.n
    PRM_ALL = sPRM.n

    kb.emit("gpsimd", lambda g: g.iota(oneh[:], [[1, NST], [0, P]],
                                       channel_multiplier=-1,
                                       allow_small_or_imprecise_dtypes=True))
    kb.emit("gpsimd", lambda g: g.memset(hstate[:], 0.0), inc=((sG, sG.tick() and 1),))
    g_tab = sG.n
    kb.emit("vector", lambda v: v.tensor_scalar(out=oneh[:], in0=oneh[:], scalar1=0.0,
                                                scalar2=None, op0=ALU.is_equal),
            wait=((sG, g_tab),), inc=((sV, sV.tick() and 1),))
    v_oneh = sV.n
    def aexp1(sc):
        inst = sc.activation(out=aexp[:], in_=aexp[:], func=AF.Exp)
        sc.drain()
        return inst
    kb.emit("scalar", aexp1,
            wait=((sPRM, PRM_ALL),), inc=((sS, sS.tick() and 1),))
    kb.emit("scalar", lambda sc: sc.activation(out=aexp[:], in_=aexp[:], func=AF.Exp,
                                               scale=-1.0), inc=((sS, sS.tick() and 1),))
    s_aexp = sS.n

    def ln_pipeline(src_ap, dst_ap, affine):
        kb.emit("scalar", lambda sc: sc.activation(
            out=sqs[:], in_=src_ap, func=AF.Square, accum_out=lncol[:, 8:9]),
            wait=((sV, sV.n),), inc=((sS, sS.tick() and 1),))
        s_a = sS.n

        def cols1(v):
            v.reduce_sum(lncol[:, 9:10], src_ap, axis=AX.X)
            vfill(v)
            v.tensor_scalar(out=lncol[:, 10:11], in0=lncol[:, 9:10],
                            scalar1=1.0 / D, scalar2=None, op0=ALU.mult)
            v.tensor_scalar(out=lncol[:, 11:12], in0=lncol[:, 8:9],
                            scalar1=1.0 / D, scalar2=None, op0=ALU.mult)
            vfill(v)
            v.tensor_mul(lncol[:, 12:13], lncol[:, 10:11], lncol[:, 10:11])
            vfill(v)
            v.tensor_sub(lncol[:, 11:12], lncol[:, 11:12], lncol[:, 12:13])
            vfill(v)
            return v.tensor_scalar(out=lncol[:, 11:12], in0=lncol[:, 11:12],
                                   scalar1=LN_EPS, scalar2=LN_EPS,
                                   op0=ALU.add, op1=ALU.max)
        kb.emit("vector", cols1, wait=((sS, s_a),), inc=((sV, sV.tick() and 1),))
        v_a = sV.n
        kb.emit("scalar", lambda sc: sc.activation(
            out=lncol[:, 12:13], in_=lncol[:, 11:12], func=AF.Sqrt),
            wait=((sV, v_a),), inc=((sS, sS.tick() and 1),))
        s_b = sS.n

        def cols2(v):
            v.reciprocal(lncol[:, 13:14], lncol[:, 12:13])
            vfill(v)
            return v.scalar_tensor_tensor(out=lncol[:, 14:15], in0=lncol[:, 10:11],
                                          scalar=-1.0, in1=lncol[:, 13:14],
                                          op0=ALU.mult, op1=ALU.mult)
        kb.emit("vector", cols2, wait=((sS, s_b),), inc=((sV, sV.tick() and 1),))
        v_b = sV.n
        kb.emit("scalar", lambda sc: sc.activation(
            out=dst_ap, in_=src_ap, func=AF.Identity, scale=lncol[:, 13:14],
            bias=lncol[:, 14:15]), wait=((sV, v_b),), inc=((sS, sS.tick() and 1),))
        s_c = sS.n
        if affine is not None:
            gt, btl = affine

            def aff(v):
                v.tensor_mul(dst_ap, dst_ap, gt[:])
                return v.tensor_add(dst_ap, dst_ap, btl[:])
            kb.emit("vector", aff, wait=((sS, s_c),), inc=((sV, sV.tick() and 1),))


    def ln_cols4(src4, sq_wait, ct=None):
        if ct is None:
            ct = lncol

        def chain1(v, sq_wait=sq_wait, src4=src4, ct=ct):
            v.wait_ge(sS.h, sq_wait)
            v.reduce_sum(ct[:, 4:8], src4, axis=AX.X)
            vfill(v)
            v.tensor_scalar(out=ct[:, 8:12], in0=ct[:, 4:8],
                            scalar1=1.0 / D, scalar2=None, op0=ALU.mult)
            v.tensor_scalar(out=ct[:, 4:8], in0=ct[:, 0:4],
                            scalar1=1.0 / D, scalar2=None, op0=ALU.mult)
            v.tensor_mul(ct[:, 12:16], ct[:, 8:12], ct[:, 8:12])
            vfill(v)
            return v.scalar_tensor_tensor(out=ct[:, 4:8], in0=ct[:, 12:16],
                                          scalar=-1.0, in1=ct[:, 4:8],
                                          op0=ALU.mult, op1=ALU.add)
        kb.emit("vector", chain1, inc=((sV, sV.tick() and 1),))
        v_a = sV.n
        kb.emit("scalar", lambda sc, ct=ct: sc.activation(
            out=ct[:, 12:16], in_=ct[:, 4:8], func=AF.Sqrt, bias=epsc[:]),
            wait=((sV, v_a), (sG, g_tab0)), inc=((sS, sS.tick() and 1),))
        s_b = sS.n

        def chain2(v, s_b=s_b, ct=ct):
            v.wait_ge(sS.h, s_b)
            v.reciprocal(ct[:, 0:4], ct[:, 12:16])
            vfill(v)
            return v.scalar_tensor_tensor(out=ct[:, 4:8], in0=ct[:, 8:12],
                                          scalar=-1.0, in1=ct[:, 0:4],
                                          op0=ALU.mult, op1=ALU.mult)
        kb.emit("vector", chain2, inc=((sV, sV.tick() and 1),))
        return sV.n

    sp_x2_last = 0
    sp_xq_last = 0
    x4n = ut[:].rearrange("p (s d) -> p s d", s=SUB)

    def ln1_stats(t_chunk, extra_wait):
        w = dma(sXA, "sync", x4n,
                x_h[t_chunk * TT:(t_chunk + 1) * TT, :].rearrange(
                    "(s p) d -> p s d", p=P), wait=extra_wait)
        for s_ in range(SUB):
            kb.emit("scalar", lambda sc, s_=s_: sc.activation(
                out=sqs[:], in_=x4n[:, s_, :], func=AF.Square,
                accum_out=lncol2[:, s_:s_ + 1]),
                wait=((sXA, w),) if s_ == 0 else (), inc=((sS, sS.tick() and 1),))
        return ln_cols4(x4n, sS.n, ct=lncol2)

    v_ln1stats = ln1_stats(0, ())

    def emit_prologue(tc, v_stats_ln1):
        if tc == 0:
            kb.emit("vector", lambda v: v.memset(ln1F3[:, :, 0:8], 0.0),
                    wait=((sG, g_tab),), inc=((sV, sV.tick() and 1),))
        else:
            kb.emit("vector", lambda v: v.tensor_copy(ln1F3[:, :, 0:8],
                                                      ln1F3[:, :, TT:TT + 8]),
                    inc=((sV, sV.tick() and 1),))
        v_lead = sV.n
        t_tr_last = 0
        for s_ in range(SUB):
            kb.emit("scalar", lambda sc, s_=s_: sc.activation(
                out=etok[:], in_=x4n[:, s_, :], func=AF.Identity,
                scale=lncol2[:, s_:s_ + 1], bias=lncol2[:, 4 + s_:5 + s_]),
                wait=((sV, v_stats_ln1), (sT, t_tr_last)),
                inc=((sS, sS.tick() and 1),))
            s_ln1 = sS.n
            for c in range(DH):
                bank = pC if (c % 2 == 0) else pD
                kb.emit("tensor", lambda t, c=c, bank=bank: t.transpose(
                    bank[:, 0:P], etok[:, c * P:(c + 1) * P], idf[:]),
                    wait=((sS, s_ln1), (sV, max(v_tab, v_lead)), (sV, sV.n)),
                    inc=((sT, sT.tick() and 1),))
                t_tr = sT.n
                kb.emit("vector", lambda v, c=c, s_=s_, bank=bank: v.tensor_copy(
                    ln1F3[:, c, 8 + s_ * P:8 + (s_ + 1) * P], bank[:, 0:P]),
                    wait=((sT, t_tr),), inc=((sV, sV.tick() and 1),))
            t_tr_last = sT.n
        return sV.n

    v_ln1f_next = emit_prologue(0, v_ln1stats)

    for tc in range(NCH):
        v_ln1f = v_ln1f_next

        def conv_emit(v):
            inst = None
            for c in range(DH):
                v.tensor_scalar(out=ut3[:, c, :], in0=ln1F3[:, c, 5:5 + TT],
                                scalar1=cw[:, c * DCONV:c * DCONV + 1],
                                scalar2=None, op0=ALU.mult)
                for j in (1, 2, 3):
                    inst = v.scalar_tensor_tensor(
                        out=ut3[:, c, :], in0=ln1F3[:, c, 5 + j:5 + j + TT],
                        scalar=cw[:, c * DCONV + j:c * DCONV + j + 1],
                        in1=ut3[:, c, :], op0=ALU.mult, op1=ALU.add)
            return inst
        kb.emit("vector", conv_emit,
                wait=((sV, v_ln1f), (sS, sS.n)), inc=((sV, sV.tick() and 1),))
        v_conv = sV.n

        def conv_silu(sc):
            inst = None
            for c in range(DH):
                inst = sc.activation(out=ut3[:, c, :], in_=ut3[:, c, :],
                                     func=AF.Silu, bias=cb[:, c:c + 1])
            return inst
        kb.emit("scalar", conv_silu, wait=((sV, v_conv),), inc=((sS, sS.tick() and 1),))
        s_u = sS.n
        if debug_scratch and tc == 1:
            spu = dma(sX2S, "gpsimd", dbg_u, ut[:], wait=((sS, s_u),))

        mm_group([(pbt[:, 0:TT], bpT3[:, c, :], ut3[:, c, :]) for c in range(DH)],
                 wait=((sS, s_u), (sPRM, PRM_ALL), (sS, sS.n)), inc=((sT, sT.tick() and 1),))
        t_bt = sT.n
        mm_group([(pbt[:, TT:2 * TT], cpT3[:, c, :], ut3[:, c, :]) for c in range(DH)],
                 wait=(), inc=((sT, sT.tick() and 1),))
        t_ct = sT.n
        kb.emit("scalar", lambda sc: sc.activation(out=btT[:], in_=pbt[:, 0:TT],
                                                   func=AF.Identity, bias=bpb[:]),
                wait=((sT, t_bt),), inc=((sS, sS.tick() and 1),))
        kb.emit("scalar", lambda sc: sc.activation(out=ctT[:], in_=pbt[:, TT:2 * TT],
                                                   func=AF.Identity, bias=cpb[:]),
                wait=((sT, t_ct),), inc=((sS, sS.tick() and 1),))
        s_btct = sS.n
        if debug_scratch and tc == 1:
            dma(sX2S, "gpsimd", dbg_bt[0:NST, 0:TT], btT[:], wait=((sS, s_btct),))
            dma(sX2S, "gpsimd", dbg_bt[0:NST, TT:2 * TT], ctT[:], wait=((sS, s_btct),))

        kb.emit("vector", lambda v: v.tensor_tensor(
            yy3, ut3, dD[:, :, None].broadcast_to([P, DH, TT]), ALU.mult),
            wait=((sS, s_u),), inc=((sV, sV.tick() and 1),))

        s_gate_evacs = []
        for n in range(NST):
            pb_ = pA if (n % 2 == 0) else pB
            pc_ = pC if (n % 2 == 0) else pD
            kb.emit("tensor", lambda t, n=n, pb_=pb_: t.matmul(
                pb_[:], oneh[:, n * P:(n + 1) * P], btT[:], start=True, stop=True),
                wait=((sS, s_btct), (sV, max(v_oneh, sV.n)), (sS, s_aexp)),
                inc=((sT, sT.tick() and 1),))
            t_b = sT.n
            kb.emit("tensor", lambda t, n=n, pc_=pc_: t.matmul(
                pc_[:], oneh[:, n * P:(n + 1) * P], ctT[:], start=True, stop=True),
                inc=((sT, sT.tick() and 1),))
            t_c = sT.n
            if 1 <= n <= DH:
                g = n - 1
                pg = ptr0 if (g % 2 == 0) else ptr1
                bank_free = () if g < 2 else ((sS, s_gate_evacs[g - 2]),)
                mm_group([(pg[:], gwT[:, c * D + g * P:c * D + (g + 1) * P],
                           ln1F3[:, c, 8:8 + TT]) for c in range(DH)],
                         wait=((sV, v_ln1f), (sPRM, PRM_ALL)) + bank_free,
                         inc=((sT, sT.tick() and 1),))
                t_g = sT.n
                kb.emit("scalar", lambda sc, g=g, pg=pg: sc.activation(
                    out=gzF3[:, g, :], in_=pg[:], func=AF.Silu, bias=gbv[:, g:g + 1]),
                    wait=((sT, t_g),), inc=((sS, sS.tick() and 1),))
                s_gate_evacs.append(sS.n)

            if gsl[0] == 0:
                g_yb[0] = sG.n
            base = gsl[0]
            sgate = stats_gate[0] if n == 0 else 0
            slice_ticks = [sV.tick() for _ in range(DH)]
            slots = [(base + c) % 6 for c in range(DH)]
            rws = [g_yb[0] + (base + c - 6) + 1 if base + c >= 6 else 0
                   for c in range(DH)]

            def scan_n(v, n=n, pb_=pb_, pc_=pc_, t_b=t_b, t_c=t_c,
                       slots=slots, rws=rws, sgate=sgate):
                v.wait_ge(sT.h, t_b)
                v.tensor_tensor(ub3, ut3, pb_[:, None, :].broadcast_to([P, DH, TT]),
                                ALU.mult)
                for c in range(DH):
                    v.tensor_tensor_scan(
                        out=hh3[:, c, :],
                        data0=aexp[:, c * NST + n:c * NST + n + 1].broadcast_to([P, TT]),
                        data1=ub3[:, c, :],
                        initial=hstate[:, n * DH + c:n * DH + c + 1],
                        op0=ALU.mult, op1=ALU.add)
                v.wait_ge(sT.h, t_c)
                if sgate:
                    v.wait_ge(sS.h, sgate)
                for c in range(DH):
                    if rws[c]:
                        v.wait_ge(sG.h, rws[c])
                    v.tensor_tensor(cmb3[:, slots[c], :], hh3[:, c, :], pc_[:],
                                    ALU.mult).then_inc(sV.h, 1)
                return v.tensor_copy(
                    hstate[:].rearrange("p (n h) -> p n h", n=NST)[:, n, :],
                    hh3[:, :, TT - 1])
            kb.emit("vector", scan_n, inc=((sV, sV.tick() and 1),))
            for c in range(DH):
                kb.emit("gpsimd", lambda g, c=c, sl=slots[c]: g.tensor_tensor(
                    yy3[:, c, :], yy3[:, c, :], cmb3[:, sl, :], ALU.add),
                    wait=((sV, slice_ticks[c]),), inc=((sG, sG.tick() and 1),))
                assert sG.n == g_yb[0] + base + c + 1
            gsl[0] = base + DH
        v_scan = sV.n
        g_yacc = sG.n
        dbg_w_y = 0
        if debug_scratch and tc == 1:
            dbg_w_y = dma(sX2S, "gpsimd", dbg_y, yy[:], wait=((sV, v_scan),))

        s_gate = s_gate_evacs[-1]
        dbg_w_g = 0
        if debug_scratch and tc == 1:
            dbg_w_g = dma(sX2S, "gpsimd", dbg_g, gzF[:], wait=((sS, s_gate),))

        kb.emit("vector", lambda v: v.tensor_mul(yy[:], yy[:], gzF[:]),
                wait=((sS, s_gate), (sX2S, dbg_w_y), (sG, g_yacc)),
                inc=((sV, sV.tick() and 1),))
        v_yg = sV.n

        z_evacs = []
        for g in range(DH):
            po = pC if (g % 2 == 0) else pD
            bank_free = ((sV, z_evacs[g - 2]),) if g >= 2 else ()
            mm_group([(po[:], owT[:, c * D + g * P:c * D + (g + 1) * P],
                       yy3[:, c, :]) for c in range(DH)],
                     wait=((sV, v_yg),) + bank_free, inc=((sT, sT.tick() and 1),))
            t_o = sT.n

            def z_evac(v, g=g, po=po, t_o=t_o, dbg_w_g=dbg_w_g):
                v.wait_ge(sT.h, t_o)
                if dbg_w_g:
                    v.wait_ge(sX2S.h, dbg_w_g)
                return v.scalar_tensor_tensor(out=gzF3[:, g, :], in0=po[:],
                                              scalar=obv[:, g:g + 1],
                                              in1=ln1F3[:, g, 8:8 + TT],
                                              op0=ALU.add, op1=ALU.add)
            kb.emit("vector", z_evac, inc=((sV, sV.tick() and 1),))
            z_evacs.append(sV.n)
        v_z = sV.n
        if debug_scratch and tc == 1:
            dbg_w_z = dma(sX2S, "gpsimd", dbg_z, gzF[:], wait=((sV, v_z),))
            sp_x2_last = max(sp_x2_last, dbg_w_z)

        if tc + 1 < NCH:
            v_ln1stats = ln1_stats(tc + 1, ((sV, v_yg),))
        emit_stats_boundary(tc, g_yacc)

        zt4 = hh[:].rearrange("p (s d) -> p s d", s=SUB)
        et4 = ub[:].rearrange("p (s d) -> p s d", s=SUB)
        xr4 = yy[:].rearrange("p (s d) -> p s d", s=SUB)
        qt4 = yy[:].bitcast(BF16)[:, 0:SUB * D].rearrange("p (s d) -> p s d", s=SUB)
        rows = slice(tc * TT, (tc + 1) * TT)
        for s_ in range(SUB):
            for c in range(DH):
                bank = ptr0 if (c % 2 == 0) else ptr1
                kb.emit("tensor", lambda t, c=c, s_=s_, bank=bank: t.transpose(
                    bank[:, 0:P], gzF3[:, c, s_ * P:(s_ + 1) * P], idf[:]),
                    wait=((sV, v_z), (sS, sS.n)), inc=((sT, sT.tick() and 1),))
                t_tr = sT.n
                kb.emit("scalar", lambda sc, c=c, s_=s_, bank=bank: sc.activation(
                    out=hh[:, s_ * D + c * P:s_ * D + (c + 1) * P],
                    in_=bank[:, 0:P], func=AF.Copy),
                    wait=((sT, t_tr), (sX2S, sp_x2_last), (sV, sV.n)),
                    inc=((sS, sS.tick() and 1),))
        if tc + 1 < NCH:
            v_ln1f_next = emit_prologue(tc + 1, v_ln1stats)

        for s_ in range(SUB):
            kb.emit("scalar", lambda sc, s_=s_: sc.activation(
                out=sqs[:], in_=zt4[:, s_, :], func=AF.Square,
                accum_out=lncol[:, s_:s_ + 1]), inc=((sS, sS.tick() and 1),))
        v_cols = ln_cols4(zt4, sS.n)
        for s_ in range(SUB):
            kb.emit("scalar", lambda sc, s_=s_: sc.activation(
                out=et4[:, s_, :], in_=zt4[:, s_, :], func=AF.Identity,
                scale=lncol[:, s_:s_ + 1], bias=lncol[:, 4 + s_:5 + s_]),
                wait=((sV, v_cols),) if s_ == 0 else (), inc=((sS, sS.tick() and 1),))
        s_innorm = sS.n

        def aff4(v, s_innorm=s_innorm):
            v.wait_ge(sS.h, s_innorm)
            v.tensor_tensor(et4, et4, lng_bc[:, None, :].broadcast_to([P, SUB, D]),
                            ALU.mult)
            return v.tensor_tensor(et4, et4,
                                   lnb_bc[:, None, :].broadcast_to([P, SUB, D]),
                                   ALU.add)
        kb.emit("vector", aff4, inc=((sV, sV.tick() and 1),))
        for s_ in range(SUB):
            kb.emit("scalar", lambda sc, s_=s_: sc.activation(
                out=sqs[:], in_=et4[:, s_, :], func=AF.Square,
                accum_out=lncol[:, s_:s_ + 1]),
                wait=((sV, sV.n),) if s_ == 0 else (), inc=((sS, sS.tick() and 1),))
        v_cols = ln_cols4(et4, sS.n)
        for s_ in range(SUB):
            kb.emit("scalar", lambda sc, s_=s_: sc.activation(
                out=zt4[:, s_, :], in_=et4[:, s_, :], func=AF.Identity,
                scale=lncol[:, s_:s_ + 1], bias=lncol[:, 4 + s_:5 + s_]),
                wait=((sV, v_cols),) if s_ == 0 else (), inc=((sS, sS.tick() and 1),))
        s_onorm = sS.n
        w_xr = dma(sXB, "sync", xr4,
                   x_h[rows, :].rearrange("(s p) d -> p s d", p=P),
                   wait=((sV, v_z),))
        kb.emit("vector", lambda v: v.tensor_add(hh[:], hh[:], yy[:]),
                wait=((sS, s_onorm), (sXB, w_xr)), inc=((sV, sV.tick() and 1),))
        v_x2 = sV.n
        sp_x2_last = dma(sX2S, "gpsimd",
                         x2_d[rows, :].rearrange("(s p) d -> p s d", p=P),
                         zt4, wait=((sV, v_x2),))
        for s_ in range(SUB):
            kb.emit("scalar", lambda sc, s_=s_: sc.activation(
                out=sqs[:], in_=zt4[:, s_, :], func=AF.Square,
                accum_out=lncol[:, s_:s_ + 1]),
                wait=((sV, v_x2),) if s_ == 0 else (), inc=((sS, sS.tick() and 1),))
        v_cols = ln_cols4(zt4, sS.n)
        for s_ in range(SUB):
            kb.emit("scalar", lambda sc, s_=s_: sc.activation(
                out=et4[:, s_, :], in_=zt4[:, s_, :], func=AF.Identity,
                scale=lncol[:, s_:s_ + 1], bias=lncol[:, 4 + s_:5 + s_]),
                wait=((sV, v_cols),) if s_ == 0 else (), inc=((sS, sS.tick() and 1),))
        s_ln3 = sS.n

        def q_cols4(v, s_ln3=s_ln3, tc=tc):
            v.wait_ge(sS.h, s_ln3)
            v.tensor_reduce(lncol[:, 0:4], et4, axis=AX.X, op=ALU.max,
                            apply_absolute_value=True)
            v.drain()
            v.tensor_scalar(out=lncol[:, 0:4], in0=lncol[:, 0:4],
                            scalar1=1e-5, scalar2=None, op0=ALU.max)
            v.drain()
            v.reciprocal(lncol[:, 4:8], lncol[:, 0:4])
            v.drain()
            v.tensor_scalar(out=lncol[:, 8:12], in0=lncol[:, 4:8],
                            scalar1=127.0, scalar2=None, op0=ALU.mult)
            return v.tensor_scalar(out=dsx[:, tc * SUB:(tc + 1) * SUB],
                                   in0=lncol[:, 0:4], scalar1=1.0 / 127.0,
                                   scalar2=None, op0=ALU.mult)
        kb.emit("vector", q_cols4, inc=((sV, sV.tick() and 1),))
        v_qc = sV.n
        for s_ in range(SUB):
            kb.emit("scalar", lambda sc, s_=s_: sc.activation(
                out=zt4[:, s_, :], in_=et4[:, s_, :], func=AF.Copy,
                scale=lncol[:, 8 + s_:9 + s_], bias=RB),
                wait=(((sV, v_qc), (sX2S, sp_x2_last)) if s_ == 0 else ()),
                inc=((sS, sS.tick() and 1),))
            s_q1 = sS.n

            def q_fin(v, s_=s_, s_q1=s_q1):
                v.wait_ge(sS.h, s_q1)
                v.tensor_copy(h16[:], zt4[:, s_, :])
                return v.tensor_scalar(out=qt4[:, s_, :], in0=h16[:], scalar1=-RB,
                                       scalar2=None, op0=ALU.add)
            kb.emit("vector", q_fin, inc=((sV, sV.tick() and 1),))
        v_q = sV.n
        sp_xq_last = dma(sXQS, "gpsimd",
                         xq_d[rows, :].rearrange("(s p) d -> p s d", p=P),
                         qt4, wait=((sV, v_q),))

    p2_V, p2_S, p2_T = sV.n, sS.n, sT.n
    p2_X2S, p2_XQS = sX2S.n, sXQS.n

    s_stats = stat_abs_ticks[-1]
    for mi in range(3):
        kb.emit("vector",
                lambda v, mi=mi: v.reduce_sum(colt[:, mi:mi + 1],
                                              wstat[:, mi * 32:(mi + 1) * 32],
                                              axis=AX.X),
                wait=((sS, s_stats),), inc=((sV, sV.tick() and 1),))
    v_sr = sV.n
    kb.emit("tensor", lambda t: t.matmul(pbt[0:1, 0:3], onec[:], colt[:, 0:3],
                                         start=True, stop=True),
            wait=((sV, v_sr),), inc=((sT, sT.tick() and 1),))
    t_s1 = sT.n
    kb.emit("vector", lambda v: v.tensor_copy(colt[0:1, 4:7], pbt[0:1, 0:3]),
            wait=((sT, t_s1),), inc=((sV, sV.tick() and 1),))
    v_s2 = sV.n
    kb.emit("tensor", lambda t: t.matmul(ptr0[:, 0:3], onerow[:],
                                         colt[0:1, 4:7], start=True, stop=True),
            wait=((sV, v_s2),), inc=((sT, sT.tick() and 1),))
    t_s2 = sT.n

    def stat_fin(v):
        v.tensor_scalar(out=colt[:, 0:3], in0=ptr0[:, 0:3],
                        scalar1=1.0 / (D * INNER), scalar2=1e-5,
                        op0=ALU.mult, op1=ALU.max)
        v.drain()
        v.tensor_copy(cwg[:], colt[:, 0:1])
        v.tensor_copy(cwu[:], colt[:, 1:2])
        v.tensor_copy(cwd[:], colt[:, 2:3])
        v.drain()
        v.reciprocal(swg[:], cwg[:])
        v.reciprocal(swu[:], cwu[:])
        return v.reciprocal(swd[:], cwd[:])
    kb.emit("vector", stat_fin, wait=((sT, t_s2),), inc=((sV, sV.tick() and 1),))
    v_stats = sV.n

    p2.close()
    p3 = contextlib.ExitStack()
    sb3 = mk(p3)
    wstg = [sb3("wstg0", [P, INNER]), sb3("wstg1", [P, INNER])]
    wgq = sb3("wgq", [P, DH * INNER], BF16)
    wuq = sb3("wuq", [P, DH * INNER], BF16)
    f16b = sb3("f16b", [P, INNER], F16)
    gu = sb3("gu", [P, INNER])
    gu1 = sb3("gu1", [P, TT])
    guq = sb3("guq", [P, INNER], BF16)
    xqc = sb3("xqc", [P, DH * P], BF16)

    PHW = ((sV, p2_V), (sS, p2_S), (sT, p2_T), (sX2S, p2_X2S), (sXQS, p2_XQS))

    s_wprev = 0
    first = True
    for (wdst, src, scol) in [(wgq, wgT_h, swg), (wuq, wuT_h, swu)]:
        for c in range(DH):
            stg = wstg[c % 2]
            sws = sW0 if c % 2 == 0 else sW1
            w_l = dma(sws, "sync", stg[:], src[c * P:(c + 1) * P, :],
                      wait=(PHW if first else ((sS, s_wprev),)))
            first = False
            kb.emit("scalar", lambda sc, stg=stg, scol=scol: sc.activation(
                out=gu[:], in_=stg[:], func=AF.Identity, scale=scol[:]),
                wait=((sws, w_l), (sV, max(v_stats, sV.n))),
                inc=((sS, sS.tick() and 1),))
            s_wq = sS.n
            s_wprev = s_wq

            def w_fin(v, wdst=wdst, c=c):
                v.tensor_scalar(out=f16b[:], in0=gu[:], scalar1=-0.5, scalar2=None,
                                op0=ALU.is_lt)
                return v.scalar_tensor_tensor(
                    out=wdst[:, c * INNER:(c + 1) * INNER], in0=gu[:], scalar=0.5,
                    in1=f16b[:], op0=ALU.is_gt, op1=ALU.subtract)
            kb.emit("vector", w_fin, wait=((sS, s_wq),), inc=((sV, sV.tick() and 1),))
    v_wq = sV.n

    if debug_scratch:
        dma(sX2S, "gpsimd", dbg_wgq, wgq[:], wait=((sV, v_wq),))
        def dbg_cp(v):
            v.tensor_copy(sqs[:, 0:24], wstat[:])
            v.tensor_copy(sqs[:, 24:32], colt[:])
            v.tensor_copy(sqs[:, 32:33], swg[:])
            v.tensor_copy(sqs[:, 33:34], swu[:])
            v.tensor_copy(sqs[:, 34:35], swd[:])
            v.tensor_copy(sqs[:, 35:36], cwg[:])
            v.tensor_copy(sqs[:, 36:37], cwu[:])
            return v.tensor_copy(sqs[:, 37:38], cwd[:])
        kb.emit("vector", dbg_cp, wait=((sV, v_wq),), inc=((sV, sV.tick() and 1),))
        dma(sX2S, "gpsimd", dbg_cols, sqs[:, 0:64], wait=((sV, sV.n),))
    xstg = sb3("xstg", [P, D], BF16)
    idb = sb3("idb", [P, P], BF16)
    kb.emit("vector", lambda v: v.tensor_copy(idb[:], idf[:]),
            wait=((sV, v_wq),), inc=((sV, sV.tick() and 1),))
    v_idb = sV.n
    sp_gq = 0
    s_xq_evacs = []
    xq_tr_lasts = [0, 0]
    for i in range(NTOK):
        trow = i * P
        wv0 = dma(sXQ, "sync", xstg[:], xq_d[trow:trow + P, :],
                  wait=(((sT, xq_tr_lasts[(i + 1) % 2]), (sXQS, p2_XQS))))
        wv = wv0
        for c in range(DH):
            pt = ptr0 if c % 2 == 0 else ptr1
            bankw = ((sS, s_xq_evacs[-2],) if len(s_xq_evacs) >= 2 else (sS, 0))
            kb.emit("tensor", lambda t, c=c, pt=pt, xstg=xstg: t.transpose(
                pt[:, 0:64].bitcast(BF16), xstg[:, c * P:(c + 1) * P], idb[:]),
                wait=((sXQ, wv0), (sV, v_idb), bankw, (sT, sT.n)),
                inc=((sT, sT.tick() and 1),))
            t_tr = sT.n
            kb.emit("scalar", lambda sc, c=c, pt=pt: sc.activation(
                out=xqc[:, c * P:(c + 1) * P], in_=pt[:, 0:64].bitcast(BF16),
                func=AF.Copy), wait=((sT, t_tr),), inc=((sS, sS.tick() and 1),))
            s_xq_evacs.append(sS.n)
        xq_tr_lasts[i % 2] = sT.n

        def mk_desc(v, i=i):
            v.tensor_mul(lncol[:, 0:1], cwg[:], dsx[:, i:i + 1])
            v.tensor_mul(lncol[:, 1:2], cwu[:], dsx[:, i:i + 1])
            return v.drain()
        kb.emit("vector", mk_desc, wait=((sV, v_wq), (sS, sS.n)),
                inc=((sV, sV.tick() and 1),))
        v_ds = sV.n
        s_up_evacs = []
        for f in range(DH):
            pg = pA if (f % 2 == 0) else pB
            pu = pC if (f % 2 == 0) else pD
            bank_free = ((sS, s_up_evacs[f - 2]),) if f >= 2 else ()
            mm_group([(pg[:], xqc[:, c * P:(c + 1) * P],
                       wgq[:, c * INNER + f * TT:c * INNER + (f + 1) * TT])
                      for c in range(DH)],
                     wait=((sS, s_xq_evacs[-1]), (sV, v_ds)) + bank_free,
                     inc=((sT, sT.tick() and 1),))
            t_g = sT.n
            mm_group([(pu[:], xqc[:, c * P:(c + 1) * P],
                       wuq[:, c * INNER + f * TT:c * INNER + (f + 1) * TT])
                      for c in range(DH)],
                     wait=(), inc=((sT, sT.tick() and 1),))
            t_u = sT.n
            kb.emit("scalar", lambda sc, pg=pg: sc.activation(
                out=gu1[:], in_=pg[:], func=AF.Silu, scale=lncol[:, 0:1]),
                wait=((sT, t_g), (sV, sV.n)), inc=((sS, sS.tick() and 1),))
            kb.emit("scalar", lambda sc, pu=pu: sc.activation(
                out=sqs[:, 0:TT], in_=pu[:], func=AF.Copy, scale=lncol[:, 1:2]),
                wait=((sT, t_u), (sV, sV.n)), inc=((sS, sS.tick() and 1),))
            s_up = sS.n
            s_up_evacs.append(s_up)

            def gu_mul(v, f=f, s_up=s_up):
                v.wait_ge(sS.h, s_up)
                v.tensor_tensor(gu[:, f * TT:(f + 1) * TT], gu1[:], sqs[:, 0:TT],
                                ALU.mult)
                v.tensor_reduce(lncol[:, 2:3], gu[:, f * TT:(f + 1) * TT],
                                axis=AX.X, op=ALU.max, apply_absolute_value=True)
                v.drain()
                if f == 0:
                    return v.tensor_copy(lncol[:, 3:4], lncol[:, 2:3])
                return v.tensor_max(lncol[:, 3:4], lncol[:, 3:4], lncol[:, 2:3])
            kb.emit("vector", gu_mul, inc=((sV, sV.tick() and 1),))

        def gq_cols(v, i=i):
            v.tensor_scalar(out=lncol[:, 3:4], in0=lncol[:, 3:4], scalar1=1e-5,
                            scalar2=None, op0=ALU.max)
            v.drain()
            v.reciprocal(lncol[:, 4:5], lncol[:, 3:4])
            v.drain()
            v.tensor_scalar(out=lncol[:, 5:6], in0=lncol[:, 4:5], scalar1=127.0,
                            scalar2=None, op0=ALU.mult)
            return v.tensor_scalar(out=dsg[:, i:i + 1], in0=lncol[:, 3:4],
                                   scalar1=1.0 / 127.0, scalar2=None, op0=ALU.mult)
        kb.emit("vector", gq_cols, wait=((sS, sS.n),),
                inc=((sV, sV.tick() and 1),))
        v_gqc = sV.n
        if debug_scratch and i == 0:
            dma(sX2S, "gpsimd", dbg_xqc, xqc[:], wait=((sV, v_gqc),))
            dma(sX2S, "gpsimd", dbg_gu, gu[:], wait=((sV, v_gqc),))
        kb.emit("scalar", lambda sc: sc.activation(
            out=f16b[:], in_=gu[:], func=AF.Copy, scale=lncol[:, 5:6], bias=RB),
            wait=((sV, v_gqc),), inc=((sS, sS.tick() and 1),))
        s_gq = sS.n

        def g_fin(v, s_gq=s_gq, sp_gq=sp_gq):
            v.wait_ge(sS.h, s_gq)
            v.wait_ge(sGQS.h, sp_gq)
            return v.tensor_scalar(out=guq[:], in0=f16b[:], scalar1=-RB,
                                   scalar2=None, op0=ALU.add)
        kb.emit("vector", g_fin, inc=((sV, sV.tick() and 1),))
        v_gq = sV.n
        sp_gq = dma(sGQS, "gpsimd", gq_d[trow:trow + P, :], guq[:],
                    wait=((sV, v_gq),))
    p3_V, p3_S, p3_T, p3_GQS = sV.n, sS.n, sT.n, sGQS.n

    p3.close()
    p4 = contextlib.ExitStack()
    sb4 = mk(p4)
    wdq = sb4("wdq", [P, IH * D], BF16)
    gqT = sb4("gqT", [P, IH * P], BF16)
    gqT2 = sb4("gqT2", [P, IH * P], BF16)
    x2b_ = [sb4("x2b", [P, SUB * D]), sb4("x2c", [P, SUB * D])]
    o2b_ = [sb4("o2b", [P, SUB * D]), sb4("o2c", [P, SUB * D])]
    gstg_ = [sb4("gstg", [P, INNER], BF16), sb4("gsth", [P, INNER], BF16)]
    idb4 = sb4("idb4", [P, P], BF16)
    wd16 = sb4("wd16", [P, D], F16)
    wdf = sb4("wdf", [P, D])
    wst4 = [sb4("wst4a", [P, D]), sb4("wst4b", [P, D])]

    PHW3 = ((sV, p3_V), (sS, p3_S), (sT, p3_T), (sGQS, p3_GQS))
    kb.emit("vector", lambda v: v.tensor_copy(idb4[:], idf[:]),
            wait=((sV, sV.n),), inc=((sV, sV.tick() and 1),))
    v_idb4 = sV.n
    s_gq_evacs = []
    v_wd_ticks = []
    s_wprev = 0
    first = True
    for c in range(IH):
        stg = wst4[c % 2]
        sws = sW0 if c % 2 == 0 else sW1
        w_l = dma(sws, "sync", stg[:, 0:D], wdT_h[c * P:(c + 1) * P, :],
                  wait=(PHW3 if first else ((sS, s_wprev),)))
        first = False
        kb.emit("scalar", lambda sc, stg=stg: sc.activation(
            out=wdf[:], in_=stg[:, 0:D], func=AF.Identity, scale=swd[:]),
            wait=((sws, w_l), (sV, sV.n)), inc=((sS, sS.tick() and 1),))
        s_wq = sS.n
        s_wprev = s_wq

        def wd_fin(v, c=c):
            v.tensor_scalar(out=wd16[:], in0=wdf[:], scalar1=-0.5, scalar2=None,
                            op0=ALU.is_lt)
            return v.scalar_tensor_tensor(
                out=wdq[:, c * D:(c + 1) * D], in0=wdf[:], scalar=0.5,
                in1=wd16[:], op0=ALU.is_gt, op1=ALU.subtract)
        kb.emit("vector", wd_fin, wait=((sS, s_wq),), inc=((sV, sV.tick() and 1),))
        v_wd_ticks.append(sV.n)
    v_wd = sV.n
    if debug_scratch:
        dma(sX2S, "gpsimd", dbg_wdq, wdq[:], wait=((sV, v_wd),))

    v_o_dones = [0, 0]
    dm_outs = [0, 0]
    tr_lasts = [0, 0]
    for gi in range(NTOK // SUB):
        x2b = x2b_[gi % 2]
        o2b = o2b_[gi % 2]
        o2b4 = o2b[:].rearrange("p (s d) -> p s d", s=SUB)
        x2b4 = x2b[:].rearrange("p (s d) -> p s d", s=SUB)
        rows = slice(gi * SUB * P, (gi + 1) * SUB * P)
        w_x2 = dma(sX2, "sync", x2b4,
                   x2_d[rows, :].rearrange("(s p) d -> p s d", p=P),
                   wait=((sV, v_o_dones[gi % 2]), (sX2S, p2_X2S)))
        def mk_dd(v, gi=gi):
            v.wait_ge(sV.h, v_wd)
            v.tensor_tensor(colt[:, 0:SUB], cwd[:].broadcast_to([P, SUB]),
                            dsg[:, gi * SUB:(gi + 1) * SUB], ALU.mult)
            return v.drain()
        kb.emit("vector", mk_dd, wait=((sS, sS.n),), inc=((sV, sV.tick() and 1),))
        v_dd = sV.n
        for il in range(SUB):
            i = gi * SUB + il
            trow = i * P
            gq_t = gqT if il % 2 == 0 else gqT2
            gstg = gstg_[i % 2]
            wvg = dma(sGQT, "sync", gstg[:], gq_d[trow:trow + P, :],
                      wait=(((sT, tr_lasts[i % 2]), (sGQS, p3_GQS))))
            for c in range(IH):
                pt = pC if c % 2 == 0 else pD
                bankw = ((sS, s_gq_evacs[-2]) if len(s_gq_evacs) >= 2 else (sS, 0))
                kb.emit("tensor", lambda t, c=c, pt=pt, gq_t=gq_t, gstg=gstg:
                    t.transpose(
                        pt[:, 0:64].bitcast(BF16), gstg[:, c * P:(c + 1) * P],
                        idb4[:]),
                    wait=((sGQT, wvg), (sV, v_idb4), bankw, (sT, sT.n)),
                    inc=((sT, sT.tick() and 1),))
                t_tr = sT.n
                kb.emit("scalar", lambda sc, c=c, pt=pt, gq_t=gq_t: sc.activation(
                    out=gq_t[:, c * P:(c + 1) * P], in_=pt[:, 0:64].bitcast(BF16),
                    func=AF.Copy), wait=((sT, t_tr),), inc=((sS, sS.tick() and 1),))
                s_gq_evacs.append(sS.n)
            tr_lasts[i % 2] = sT.n
            for g2 in range(2):
                pq = pA if (g2 % 2 == 0) else pB
                mmw = None
                wdw = ((sV, v_wd),) if (gi == 0 and il == 0) else ()
                mm_group([(pq[:], gq_t[:, c * P:(c + 1) * P],
                           wdq[:, c * D + g2 * TT:c * D + (g2 + 1) * TT])
                          for c in range(IH)],
                         wait=((sS, s_gq_evacs[-1]),) + wdw,
                         inc=((sT, sT.tick() and 1),), mm_waits=mmw)
                t_d = sT.n
                kb.emit("scalar", lambda sc, il=il, g2=g2, pq=pq, o2b=o2b:
                    sc.activation(
                        out=o2b[:, il * D + g2 * TT:il * D + (g2 + 1) * TT],
                        in_=pq[:], func=AF.Copy, scale=colt[:, il:il + 1]),
                    wait=((sT, t_d), (sV, v_dd), (sOUT, dm_outs[gi % 2])),
                    inc=((sS, sS.tick() and 1),))
        s_o2 = sS.n
        for s_ in range(SUB):
            kb.emit("scalar", lambda sc, s_=s_, o2b4=o2b4: sc.activation(
                out=sqs[:], in_=o2b4[:, s_, :], func=AF.Square,
                accum_out=lncol[:, s_:s_ + 1]),
                wait=((sV, sV.n),) if s_ == 0 else (), inc=((sS, sS.tick() and 1),))
        v_cols = ln_cols4(o2b4, sS.n)
        for s_ in range(SUB):
            kb.emit("scalar", lambda sc, s_=s_, o2b4=o2b4: sc.activation(
                out=o2b4[:, s_, :], in_=o2b4[:, s_, :], func=AF.Identity,
                scale=lncol[:, s_:s_ + 1], bias=lncol[:, 4 + s_:5 + s_]),
                wait=((sV, v_cols),) if s_ == 0 else (), inc=((sS, sS.tick() and 1),))
        s_ln4 = sS.n
        kb.emit("vector", lambda v, o2b=o2b, x2b=x2b: v.tensor_add(
            o2b[:], o2b[:], x2b[:]),
            wait=((sS, s_ln4), (sX2, w_x2)), inc=((sV, sV.tick() and 1),))
        v_o_dones[gi % 2] = sV.n
        dm_outs[gi % 2] = dma(sOUT, "gpsimd",
                              out_h[rows, :].rearrange("(s p) d -> p s d", p=P),
                              o2b4, wait=((sV, sV.n),))

    kb.emit("gpsimd", lambda g: None, wait=((sOUT, sOUT.n),))
    p4.close()

    with top:
        with nc.Block() as block:
            kb.run(block)
    return nc


_NC_CACHE = None


def kernel(**inputs):
    global _NC_CACHE
    f32 = np.float32
    T = lambda a: np.ascontiguousarray(np.asarray(a, dtype=f32).T)
    C = lambda a: np.ascontiguousarray(np.asarray(a, dtype=f32))
    x = C(inputs["L0_attn_x"])
    common = {
        "conv_w": C(np.asarray(inputs["L0_attn_conv_w"], dtype=f32)[:, 0, :]),
        "conv_b": C(inputs["L0_attn_conv_b"]),
        "A": C(inputs["L0_attn_A"]),
        "BpT": T(inputs["L0_attn_Bp_w"]),
        "Bpb": C(inputs["L0_attn_Bp_b"]),
        "CpT": T(inputs["L0_attn_Cp_w"]),
        "Cpb": C(inputs["L0_attn_Cp_b"]),
        "Dd": C(inputs["L0_attn_D"]),
        "gwT": T(inputs["L0_attn_gate_w"]),
        "gb": C(inputs["L0_attn_gate_b"]),
        "owT": T(inputs["L0_attn_out_w"]),
        "ob": C(inputs["L0_attn_out_b"]),
        "lng": C(inputs["L0_attn_ln_g"]),
        "lnb": C(inputs["L0_attn_ln_b"]),
        "wgT": T(inputs["L0_ffn_w_gate"]),
        "wuT": T(inputs["L0_ffn_w_up"]),
        "wdT": T(inputs["L0_ffn_w_down"]),
    }
    in_maps = [dict(common, x=x[c]) for c in range(8)]
    if _NC_CACHE is None:
        _NC_CACHE = build_core()
    res = run_bass_kernel_spmd(_NC_CACHE, in_maps, core_ids=list(range(8)))
    return np.stack([np.asarray(r["out"], dtype=f32) for r in res.results])


# version 20
# speedup vs baseline: 1.0741x; 1.0305x over previous
import contextlib
import numpy as np
import concourse.bass as bass
import concourse.mybir as mybir
from concourse.bass_utils import run_bass_kernel_spmd

F32 = mybir.dt.float32
F16 = mybir.dt.float16
BF16 = mybir.dt.bfloat16
AF = mybir.ActivationFunctionType
ALU = mybir.AluOpType
AX = mybir.AxisListType

P = 128
S = 2048
D = 1024
DH = D // P
NST = 16
DCONV = 4
INNER = 4096
IH = INNER // P
TT = 512
NCH = S // TT
SUB = TT // P
NTOK = S // P
LN_EPS = 1e-5
RB = 1536.0


class Sem:
    def __init__(self, h):
        self.h = h
        self.n = 0

    def tick(self, k=1):
        self.n += k
        return self.n


class KB:
    ENGS = ("sync", "tensor", "vector", "scalar", "gpsimd")

    def __init__(self, nc):
        self.nc = nc
        self.q = {e: [] for e in self.ENGS}

    def emit(self, eng, fn, wait=(), inc=()):
        wait = tuple((s.h, v) for s, v in wait)
        inc = tuple((s.h, k) for s, k in inc)

        def g(h):
            for sh, v in wait:
                h.wait_ge(sh, v)
            inst = fn(h)
            for sh, k in inc:
                inst.then_inc(sh, k)
        self.q[eng].append(g)

    def run(self, block):
        for eng in self.ENGS:
            fns = self.q[eng]
            getattr(block, eng)(lambda h, fns=fns: [f(h) for f in fns])


def build_core(debug_scratch=False):
    nc = bass.Bass("TRN2", target_bir_lowering=False, detect_race_conditions=False)

    x_h = nc.dram_tensor("x", [S, D], F32, kind="ExternalInput").ap()
    conv_w_h = nc.dram_tensor("conv_w", [D, DCONV], F32, kind="ExternalInput").ap()
    conv_b_h = nc.dram_tensor("conv_b", [D], F32, kind="ExternalInput").ap()
    A_h = nc.dram_tensor("A", [D, NST], F32, kind="ExternalInput").ap()
    BpT_h = nc.dram_tensor("BpT", [D, NST], F32, kind="ExternalInput").ap()
    Bpb_h = nc.dram_tensor("Bpb", [NST], F32, kind="ExternalInput").ap()
    CpT_h = nc.dram_tensor("CpT", [D, NST], F32, kind="ExternalInput").ap()
    Cpb_h = nc.dram_tensor("Cpb", [NST], F32, kind="ExternalInput").ap()
    Dd_h = nc.dram_tensor("Dd", [D], F32, kind="ExternalInput").ap()
    gwT_h = nc.dram_tensor("gwT", [D, D], F32, kind="ExternalInput").ap()
    gb_h = nc.dram_tensor("gb", [D], F32, kind="ExternalInput").ap()
    owT_h = nc.dram_tensor("owT", [D, D], F32, kind="ExternalInput").ap()
    ob_h = nc.dram_tensor("ob", [D], F32, kind="ExternalInput").ap()
    lng_h = nc.dram_tensor("lng", [D], F32, kind="ExternalInput").ap()
    lnb_h = nc.dram_tensor("lnb", [D], F32, kind="ExternalInput").ap()
    wgT_h = nc.dram_tensor("wgT", [D, INNER], F32, kind="ExternalInput").ap()
    wuT_h = nc.dram_tensor("wuT", [D, INNER], F32, kind="ExternalInput").ap()
    wdT_h = nc.dram_tensor("wdT", [INNER, D], F32, kind="ExternalInput").ap()
    out_h = nc.dram_tensor("out", [S, D], F32, kind="ExternalOutput").ap()
    _scr = "ExternalOutput" if debug_scratch else "Internal"
    x2_d = nc.dram_tensor("x2_scr", [S, D], F32, kind=_scr).ap()
    xq_d = nc.dram_tensor("xq_scr", [S, D], BF16, kind=_scr).ap()
    gq_d = nc.dram_tensor("gq_scr", [S, INNER], BF16, kind=_scr).ap()
    if debug_scratch:
        dbg_st = nc.dram_tensor("dbg_st", [P, 40], F32, kind=_scr).ap()
        dbg_wgq = nc.dram_tensor("dbg_wgq", [P, DH * INNER], BF16, kind=_scr).ap()
        dbg_xqc = nc.dram_tensor("dbg_xqc", [P, DH * P], BF16, kind=_scr).ap()
        dbg_gu = nc.dram_tensor("dbg_gu", [P, INNER], F32, kind=_scr).ap()
        dbg_cols = nc.dram_tensor("dbg_cols", [P, 64], F32, kind=_scr).ap()
        dbg_wdq = nc.dram_tensor("dbg_wdq", [P, IH * D], BF16, kind=_scr).ap()
        dbg_u = nc.dram_tensor("dbg_u", [P, DH * TT], F32, kind=_scr).ap()
        dbg_bt = nc.dram_tensor("dbg_bt", [NST, 2 * TT], F32, kind=_scr).ap()
        dbg_y = nc.dram_tensor("dbg_y", [P, DH * TT], F32, kind=_scr).ap()
        dbg_g = nc.dram_tensor("dbg_g", [P, DH * TT], F32, kind=_scr).ap()
        dbg_z = nc.dram_tensor("dbg_z", [P, DH * TT], F32, kind=_scr).ap()
        dbg_e = nc.dram_tensor("dbg_e", [P, 40], F32, kind=_scr).ap()

    kb = KB(nc)
    top = contextlib.ExitStack()

    def mk(es_):
        sbf = lambda n, shp, dt=F32: es_.enter_context(nc.sbuf_tensor(n, shp, dt))
        return sbf
    sb = mk(top)
    sem = lambda n: Sem(top.enter_context(nc.semaphore(n)))

    sIN = sem("sIN")
    sSP = sem("sSP")
    sXA = sem("sXA")
    sXB = sem("sXB")
    sW0 = sem("sW0")
    sW1 = sem("sW1")
    sWG = sem("sWG")
    sPRM = sem("sPRM")
    sXQ = sem("sXQ")
    sGQT = sem("sGQT")
    sX2 = sem("sX2")
    sX2S = sem("sX2S")
    sXQS = sem("sXQS")
    sGQS = sem("sGQS")
    sOUT = sem("sOUT")
    sT = sem("sT")
    sV = sem("sV")
    sS = sem("sS")
    sG = sem("sG")
    dmo = sem("dmo")

    pes = contextlib.ExitStack()
    top.enter_context(pes)
    ptr0 = pes.enter_context(nc.psum_tensor("ptr0", [P, TT], F32))
    ptr1 = pes.enter_context(nc.psum_tensor("ptr1", [P, TT], F32))
    pA = pes.enter_context(nc.psum_tensor("pA", [P, TT], F32))
    pB = pes.enter_context(nc.psum_tensor("pB", [P, TT], F32))
    pC = pes.enter_context(nc.psum_tensor("pC", [P, TT], F32))
    pD = pes.enter_context(nc.psum_tensor("pD", [P, TT], F32))
    pbt = pes.enter_context(nc.psum_tensor("pbt", [NST, 2 * TT], F32))

    def mm_group(plist, wait, inc, mm_waits=None):
        n = len(plist)

        def fn(t):
            inst = None
            for i, (o, l, r) in enumerate(plist):
                if mm_waits is not None and mm_waits[i]:
                    for sh, val in mm_waits[i]:
                        t.wait_ge(sh, val)
                inst = t.matmul(o, l, r, start=(i == 0),
                                stop=(i % 4 == 3 or i == n - 1),
                                skip_group_check=(i >= 4))
            return inst
        kb.emit("tensor", fn, wait=wait, inc=inc)

    def vfill(v):
        v.tensor_scalar(out=jnk[:, 0:1], in0=onec[:], scalar1=1.0,
                        scalar2=None, op0=ALU.mult)
        v.tensor_scalar(out=jnk[:, 1:2], in0=onec[:], scalar1=1.0,
                        scalar2=None, op0=ALU.mult)

    def dma(s_, eng, out_ap, in_ap, wait=(), **kw):
        v = s_.tick(16)
        kb.emit(eng, lambda e, o=out_ap, i=in_ap, kw=kw: e.dma_start(out=o, in_=i, **kw),
                wait=wait, inc=((s_, 16),))
        return v

    sqs = sb("sqs", [P, D])
    lncol = sb("lncol", [P, 16])
    colt = sb("colt", [P, 8])
    onec = sb("onec", [P, 1])
    jnk = sb("jnk", [P, 2])
    onerow = sb("onerow", [1, P])
    epsc = sb("epsc", [P, 1])
    idf = sb("idf", [P, P])
    wstat = sb("wstat", [P, 96])
    swg, swu, swd = sb("swg", [P, 1]), sb("swu", [P, 1]), sb("swd", [P, 1])
    cwg, cwu, cwd = sb("cwg", [P, 1]), sb("cwu", [P, 1]), sb("cwd", [P, 1])
    dsx = sb("dsx", [P, NTOK])
    dsg = sb("dsg", [P, NTOK])
    h16 = sb("h16", [P, D], F16)
    dbgc = sb("dbgc", [P, 40]) if debug_scratch else None

    kb.emit("gpsimd", lambda g: g.iota(idf[:], [[1, P]], channel_multiplier=-1,
                                       allow_small_or_imprecise_dtypes=True))
    kb.emit("gpsimd", lambda g: g.memset(onec[:], 1.0))
    kb.emit("gpsimd", lambda g: g.memset(onerow[:], 1.0))
    kb.emit("gpsimd", lambda g: g.memset(epsc[:], LN_EPS), inc=((sG, sG.tick() and 1),))
    g_tab0 = sG.n
    kb.emit("vector", lambda v: v.tensor_scalar(out=idf[:], in0=idf[:], scalar1=0.0,
                                                scalar2=None, op0=ALU.is_equal),
            wait=((sG, g_tab0),), inc=((sV, sV.tick() and 1),))
    v_tab = sV.n


    p2 = contextlib.ExitStack()
    sb2 = mk(p2)
    aexp = sb2("aexp", [P, DH * NST])
    cw = sb2("cw", [P, DH * DCONV])
    cb = sb2("cb", [P, DH])
    dD = sb2("dD", [P, DH])
    gbv = sb2("gbv", [P, DH])
    obv = sb2("obv", [P, DH])
    bpb = sb2("bpb", [NST, 1])
    cpb = sb2("cpb", [NST, 1])
    lng_bc = sb2("lng_bc", [P, D])
    lnb_bc = sb2("lnb_bc", [P, D])
    oneh = sb2("oneh", [NST, NST * P])
    cmb = sb2("cmb", [P, 6 * TT])
    hstate = sb2("hstate", [P, NST * DH])
    gwT = sb2("gwT_s", [P, DH * D])
    owT = sb2("owT_s", [P, DH * D])
    bpT = sb2("bpT", [P, DH * NST])
    cpT = sb2("cpT", [P, DH * NST])
    LF = TT + 8
    ln1F = sb2("ln1F", [P, DH * LF])
    ut = sb2("ut", [P, DH * TT])
    ub = sb2("ub", [P, DH * TT])
    hh = sb2("hh", [P, DH * TT])
    yy = sb2("yy", [P, DH * TT])
    gzF = sb2("gzF", [P, DH * TT])
    btT = sb2("btT", [NST, TT])
    ctT = sb2("ctT", [NST, TT])
    etok = sb2("etok", [P, D])
    lncol2 = sb2("lncol2", [P, 16])

    ln1F3 = ln1F[:].rearrange("p (h t) -> p h t", h=DH)
    ut3 = ut[:].rearrange("p (h t) -> p h t", h=DH)
    ub3 = ub[:].rearrange("p (h t) -> p h t", h=DH)
    hh3 = hh[:].rearrange("p (h t) -> p h t", h=DH)
    yy3 = yy[:].rearrange("p (h t) -> p h t", h=DH)
    cmb3 = cmb[:].rearrange("p (h t) -> p h t", h=6)
    g_yb = [0]
    gsl = [0]
    gzF3 = gzF[:].rearrange("p (h t) -> p h t", h=DH)
    bpT3 = bpT[:].rearrange("p (h n) -> p h n", h=DH)
    cpT3 = cpT[:].rearrange("p (h n) -> p h n", h=DH)

    wdT_blk = wdT_h.rearrange("(b p q) d -> b p (q d)", b=8, p=P)
    stat_blocks = []
    for mi, wsrc in enumerate([wgT_h, wuT_h, wdT_h]):
        for b in range(8):
            for q in range(4):
                if mi < 2:
                    ap_ = wsrc[b * P:(b + 1) * P, q * D:(q + 1) * D]
                else:
                    ap_ = wdT_blk[b][:, q * D:(q + 1) * D]
                stat_blocks.append((ap_, mi * 32 + b * 4 + q))
    stat_abs_ticks = []
    stats_gate = [0]

    def emit_stats_boundary(tc, g_free):
        slot_sems = (sW0, sW1, sWG)
        for k in range(24):
            blk = tc * 24 + k
            ap_, col = stat_blocks[blk]
            wt = ((sG, g_free),) if k == 0 else ()
            if blk >= 3:
                wt = wt + ((sS, stat_abs_ticks[blk - 3]),)
            sws = slot_sems[blk % 3]
            w_l = dma(sws, "sync", cmb[:, (blk % 3) * D:(blk % 3 + 1) * D],
                      ap_, wait=wt)
            kb.emit("scalar", lambda sc, blk=blk, col=col: sc.activation(
                out=cmb[:, (blk % 3) * D:(blk % 3 + 1) * D],
                in_=cmb[:, (blk % 3) * D:(blk % 3 + 1) * D],
                func=AF.Abs, accum_out=wstat[:, col:col + 1]),
                wait=((sws, w_l),), inc=((sS, sS.tick() and 1),))
            stat_abs_ticks.append(sS.n)
        stats_gate[0] = sS.n
    v_stats = sV.n
    if False and debug_scratch:
        def dbg_stats(v):
            v.tensor_copy(dbgc[:, 0:24], wstat[:])
            v.tensor_copy(dbgc[:, 24:32], colt[:])
            v.tensor_copy(dbgc[:, 32:33], swg[:])
            v.tensor_copy(dbgc[:, 33:34], cwg[:])
            v.tensor_copy(dbgc[:, 34:37], ptr0[:, 0:3])
            return v.tensor_copy(dbgc[:, 37:38], onec[:])
        kb.emit("vector", dbg_stats, wait=((sV, v_stats),), inc=((sV, sV.tick() and 1),))
        dma(sX2S, "gpsimd", dbg_st, dbgc[:], wait=((sV, sV.n),))

    dma(sPRM, "sync", aexp[:].rearrange("p (h n) -> p h n", h=DH),
        A_h.rearrange("(h p) n -> p h n", p=P))
    dma(sPRM, "sync", cw[:].rearrange("p (h j) -> p h j", h=DH),
        conv_w_h.rearrange("(h p) j -> p h j", p=P))
    dma(sPRM, "sync", cb[:], conv_b_h.rearrange("(h p) -> p h", p=P), allow_slow_non_contiguous=True)
    dma(sPRM, "sync", dD[:], Dd_h.rearrange("(h p) -> p h", p=P), allow_slow_non_contiguous=True)
    dma(sPRM, "sync", gbv[:], gb_h.rearrange("(h p) -> p h", p=P), allow_slow_non_contiguous=True)
    dma(sPRM, "sync", obv[:], ob_h.rearrange("(h p) -> p h", p=P), allow_slow_non_contiguous=True)
    dma(sPRM, "sync", bpb[:], Bpb_h[:, None])
    dma(sPRM, "sync", cpb[:], Cpb_h[:, None])
    dma(sPRM, "sync", lng_bc[:], lng_h[None, :].broadcast_to([P, D]))
    dma(sPRM, "sync", lnb_bc[:], lnb_h[None, :].broadcast_to([P, D]))
    in_params = sPRM.n
    dma(sPRM, "sync", bpT3, BpT_h.rearrange("(h p) n -> p h n", p=P))
    dma(sPRM, "sync", cpT3, CpT_h.rearrange("(h p) n -> p h n", p=P))
    in_bc = sPRM.n
    for c in range(DH):
        dma(sPRM, "sync", gwT[:, c * D:(c + 1) * D], gwT_h[c * P:(c + 1) * P, :])
    for c in range(DH):
        dma(sPRM, "sync", owT[:, c * D:(c + 1) * D], owT_h[c * P:(c + 1) * P, :])
    in_w = sPRM.n
    PRM_ALL = sPRM.n

    kb.emit("gpsimd", lambda g: g.iota(oneh[:], [[1, NST], [0, P]],
                                       channel_multiplier=-1,
                                       allow_small_or_imprecise_dtypes=True))
    kb.emit("gpsimd", lambda g: g.memset(hstate[:], 0.0), inc=((sG, sG.tick() and 1),))
    g_tab = sG.n
    kb.emit("vector", lambda v: v.tensor_scalar(out=oneh[:], in0=oneh[:], scalar1=0.0,
                                                scalar2=None, op0=ALU.is_equal),
            wait=((sG, g_tab),), inc=((sV, sV.tick() and 1),))
    v_oneh = sV.n
    def aexp1(sc):
        inst = sc.activation(out=aexp[:], in_=aexp[:], func=AF.Exp)
        sc.drain()
        return inst
    kb.emit("scalar", aexp1,
            wait=((sPRM, PRM_ALL),), inc=((sS, sS.tick() and 1),))
    kb.emit("scalar", lambda sc: sc.activation(out=aexp[:], in_=aexp[:], func=AF.Exp,
                                               scale=-1.0), inc=((sS, sS.tick() and 1),))
    s_aexp = sS.n

    def ln_pipeline(src_ap, dst_ap, affine):
        kb.emit("scalar", lambda sc: sc.activation(
            out=sqs[:], in_=src_ap, func=AF.Square, accum_out=lncol[:, 8:9]),
            wait=((sV, sV.n),), inc=((sS, sS.tick() and 1),))
        s_a = sS.n

        def cols1(v):
            v.reduce_sum(lncol[:, 9:10], src_ap, axis=AX.X)
            vfill(v)
            v.tensor_scalar(out=lncol[:, 10:11], in0=lncol[:, 9:10],
                            scalar1=1.0 / D, scalar2=None, op0=ALU.mult)
            v.tensor_scalar(out=lncol[:, 11:12], in0=lncol[:, 8:9],
                            scalar1=1.0 / D, scalar2=None, op0=ALU.mult)
            vfill(v)
            v.tensor_mul(lncol[:, 12:13], lncol[:, 10:11], lncol[:, 10:11])
            vfill(v)
            v.tensor_sub(lncol[:, 11:12], lncol[:, 11:12], lncol[:, 12:13])
            vfill(v)
            return v.tensor_scalar(out=lncol[:, 11:12], in0=lncol[:, 11:12],
                                   scalar1=LN_EPS, scalar2=LN_EPS,
                                   op0=ALU.add, op1=ALU.max)
        kb.emit("vector", cols1, wait=((sS, s_a),), inc=((sV, sV.tick() and 1),))
        v_a = sV.n
        kb.emit("scalar", lambda sc: sc.activation(
            out=lncol[:, 12:13], in_=lncol[:, 11:12], func=AF.Sqrt),
            wait=((sV, v_a),), inc=((sS, sS.tick() and 1),))
        s_b = sS.n

        def cols2(v):
            v.reciprocal(lncol[:, 13:14], lncol[:, 12:13])
            vfill(v)
            return v.scalar_tensor_tensor(out=lncol[:, 14:15], in0=lncol[:, 10:11],
                                          scalar=-1.0, in1=lncol[:, 13:14],
                                          op0=ALU.mult, op1=ALU.mult)
        kb.emit("vector", cols2, wait=((sS, s_b),), inc=((sV, sV.tick() and 1),))
        v_b = sV.n
        kb.emit("scalar", lambda sc: sc.activation(
            out=dst_ap, in_=src_ap, func=AF.Identity, scale=lncol[:, 13:14],
            bias=lncol[:, 14:15]), wait=((sV, v_b),), inc=((sS, sS.tick() and 1),))
        s_c = sS.n
        if affine is not None:
            gt, btl = affine

            def aff(v):
                v.tensor_mul(dst_ap, dst_ap, gt[:])
                return v.tensor_add(dst_ap, dst_ap, btl[:])
            kb.emit("vector", aff, wait=((sS, s_c),), inc=((sV, sV.tick() and 1),))


    def ln_cols4(src4, sq_wait, ct=None):
        if ct is None:
            ct = lncol

        def chain1(v, sq_wait=sq_wait, src4=src4, ct=ct):
            v.wait_ge(sS.h, sq_wait)
            v.reduce_sum(ct[:, 4:8], src4, axis=AX.X)
            vfill(v)
            v.tensor_scalar(out=ct[:, 8:12], in0=ct[:, 4:8],
                            scalar1=1.0 / D, scalar2=None, op0=ALU.mult)
            v.tensor_scalar(out=ct[:, 4:8], in0=ct[:, 0:4],
                            scalar1=1.0 / D, scalar2=None, op0=ALU.mult)
            v.tensor_mul(ct[:, 12:16], ct[:, 8:12], ct[:, 8:12])
            vfill(v)
            return v.scalar_tensor_tensor(out=ct[:, 4:8], in0=ct[:, 12:16],
                                          scalar=-1.0, in1=ct[:, 4:8],
                                          op0=ALU.mult, op1=ALU.add)
        kb.emit("vector", chain1, inc=((sV, sV.tick() and 1),))
        v_a = sV.n
        kb.emit("scalar", lambda sc, ct=ct: sc.activation(
            out=ct[:, 12:16], in_=ct[:, 4:8], func=AF.Sqrt, bias=epsc[:]),
            wait=((sV, v_a), (sG, g_tab0)), inc=((sS, sS.tick() and 1),))
        s_b = sS.n

        def chain2(v, s_b=s_b, ct=ct):
            v.wait_ge(sS.h, s_b)
            v.reciprocal(ct[:, 0:4], ct[:, 12:16])
            vfill(v)
            return v.scalar_tensor_tensor(out=ct[:, 4:8], in0=ct[:, 8:12],
                                          scalar=-1.0, in1=ct[:, 0:4],
                                          op0=ALU.mult, op1=ALU.mult)
        kb.emit("vector", chain2, inc=((sV, sV.tick() and 1),))
        return sV.n

    sp_x2_last = 0
    sp_xq_last = 0
    x4n = ut[:].rearrange("p (s d) -> p s d", s=SUB)

    def ln1_stats(t_chunk, extra_wait):
        w = dma(sXA, "sync", x4n,
                x_h[t_chunk * TT:(t_chunk + 1) * TT, :].rearrange(
                    "(s p) d -> p s d", p=P), wait=extra_wait)
        for s_ in range(SUB):
            kb.emit("scalar", lambda sc, s_=s_: sc.activation(
                out=sqs[:], in_=x4n[:, s_, :], func=AF.Square,
                accum_out=lncol2[:, s_:s_ + 1]),
                wait=((sXA, w),) if s_ == 0 else (), inc=((sS, sS.tick() and 1),))
        return ln_cols4(x4n, sS.n, ct=lncol2)

    v_ln1stats = ln1_stats(0, ())

    def emit_prologue(tc, v_stats_ln1):
        if tc == 0:
            kb.emit("vector", lambda v: v.memset(ln1F3[:, :, 0:8], 0.0),
                    wait=((sG, g_tab),), inc=((sV, sV.tick() and 1),))
        else:
            kb.emit("vector", lambda v: v.tensor_copy(ln1F3[:, :, 0:8],
                                                      ln1F3[:, :, TT:TT + 8]),
                    inc=((sV, sV.tick() and 1),))
        v_lead = sV.n
        t_tr_last = 0
        pr_evacs = [sV.n, sV.n]
        for s_ in range(SUB):
            kb.emit("scalar", lambda sc, s_=s_: sc.activation(
                out=etok[:], in_=x4n[:, s_, :], func=AF.Identity,
                scale=lncol2[:, s_:s_ + 1], bias=lncol2[:, 4 + s_:5 + s_]),
                wait=((sV, v_stats_ln1), (sT, t_tr_last)),
                inc=((sS, sS.tick() and 1),))
            s_ln1 = sS.n
            for c in range(DH):
                k = s_ * DH + c
                bank = pC if (c % 2 == 0) else pD
                bw_ = max(pr_evacs[k % 2], v_lead, v_tab)
                kb.emit("tensor", lambda t, c=c, bank=bank: t.transpose(
                    bank[:, 0:P], etok[:, c * P:(c + 1) * P], idf[:]),
                    wait=((sS, s_ln1), (sV, bw_)),
                    inc=((sT, sT.tick() and 1),))
                t_tr = sT.n
                kb.emit("vector", lambda v, c=c, s_=s_, bank=bank: v.tensor_copy(
                    ln1F3[:, c, 8 + s_ * P:8 + (s_ + 1) * P], bank[:, 0:P]),
                    wait=((sT, t_tr),), inc=((sV, sV.tick() and 1),))
                pr_evacs[k % 2] = sV.n
            t_tr_last = sT.n
        return sV.n

    v_ln1f_next = emit_prologue(0, v_ln1stats)

    for tc in range(NCH):
        v_ln1f = v_ln1f_next

        def conv_emit(v):
            inst = None
            for c in range(DH):
                v.tensor_scalar(out=ut3[:, c, :], in0=ln1F3[:, c, 5:5 + TT],
                                scalar1=cw[:, c * DCONV:c * DCONV + 1],
                                scalar2=None, op0=ALU.mult)
                for j in (1, 2, 3):
                    inst = v.scalar_tensor_tensor(
                        out=ut3[:, c, :], in0=ln1F3[:, c, 5 + j:5 + j + TT],
                        scalar=cw[:, c * DCONV + j:c * DCONV + j + 1],
                        in1=ut3[:, c, :], op0=ALU.mult, op1=ALU.add)
            return inst
        kb.emit("vector", conv_emit,
                wait=((sV, v_ln1f), (sS, sS.n)), inc=((sV, sV.tick() and 1),))
        v_conv = sV.n

        def conv_silu(sc):
            inst = None
            for c in range(DH):
                inst = sc.activation(out=ut3[:, c, :], in_=ut3[:, c, :],
                                     func=AF.Silu, bias=cb[:, c:c + 1])
            return inst
        kb.emit("scalar", conv_silu, wait=((sV, v_conv),), inc=((sS, sS.tick() and 1),))
        s_u = sS.n
        if debug_scratch and tc == 1:
            spu = dma(sX2S, "gpsimd", dbg_u, ut[:], wait=((sS, s_u),))

        mm_group([(pbt[:, 0:TT], bpT3[:, c, :], ut3[:, c, :]) for c in range(DH)],
                 wait=((sS, s_u), (sPRM, PRM_ALL), (sS, sS.n)), inc=((sT, sT.tick() and 1),))
        t_bt = sT.n
        mm_group([(pbt[:, TT:2 * TT], cpT3[:, c, :], ut3[:, c, :]) for c in range(DH)],
                 wait=(), inc=((sT, sT.tick() and 1),))
        t_ct = sT.n
        kb.emit("scalar", lambda sc: sc.activation(out=btT[:], in_=pbt[:, 0:TT],
                                                   func=AF.Identity, bias=bpb[:]),
                wait=((sT, t_bt),), inc=((sS, sS.tick() and 1),))
        kb.emit("scalar", lambda sc: sc.activation(out=ctT[:], in_=pbt[:, TT:2 * TT],
                                                   func=AF.Identity, bias=cpb[:]),
                wait=((sT, t_ct),), inc=((sS, sS.tick() and 1),))
        s_btct = sS.n
        if debug_scratch and tc == 1:
            dma(sX2S, "gpsimd", dbg_bt[0:NST, 0:TT], btT[:], wait=((sS, s_btct),))
            dma(sX2S, "gpsimd", dbg_bt[0:NST, TT:2 * TT], ctT[:], wait=((sS, s_btct),))

        kb.emit("vector", lambda v: v.tensor_tensor(
            yy3, ut3, dD[:, :, None].broadcast_to([P, DH, TT]), ALU.mult),
            wait=((sS, s_u),), inc=((sV, sV.tick() and 1),))

        s_gate_evacs = []
        for n in range(NST):
            pb_ = pA if (n % 2 == 0) else pB
            pc_ = pC if (n % 2 == 0) else pD
            kb.emit("tensor", lambda t, n=n, pb_=pb_: t.matmul(
                pb_[:], oneh[:, n * P:(n + 1) * P], btT[:], start=True, stop=True),
                wait=((sS, s_btct), (sV, max(v_oneh, sV.n)), (sS, s_aexp)),
                inc=((sT, sT.tick() and 1),))
            t_b = sT.n
            kb.emit("tensor", lambda t, n=n, pc_=pc_: t.matmul(
                pc_[:], oneh[:, n * P:(n + 1) * P], ctT[:], start=True, stop=True),
                inc=((sT, sT.tick() and 1),))
            t_c = sT.n
            if 1 <= n <= DH:
                g = n - 1
                pg = ptr0 if (g % 2 == 0) else ptr1
                bank_free = () if g < 2 else ((sS, s_gate_evacs[g - 2]),)
                mm_group([(pg[:], gwT[:, c * D + g * P:c * D + (g + 1) * P],
                           ln1F3[:, c, 8:8 + TT]) for c in range(DH)],
                         wait=((sV, v_ln1f), (sPRM, PRM_ALL)) + bank_free,
                         inc=((sT, sT.tick() and 1),))
                t_g = sT.n
                kb.emit("scalar", lambda sc, g=g, pg=pg: sc.activation(
                    out=gzF3[:, g, :], in_=pg[:], func=AF.Silu, bias=gbv[:, g:g + 1]),
                    wait=((sT, t_g),), inc=((sS, sS.tick() and 1),))
                s_gate_evacs.append(sS.n)

            if gsl[0] == 0:
                g_yb[0] = sG.n
            base = gsl[0]
            sgate = stats_gate[0] if n == 0 else 0
            slice_ticks = [sV.tick() for _ in range(DH)]
            slots = [(base + c) % 6 for c in range(DH)]
            rws = [g_yb[0] + (base + c - 6) + 1 if base + c >= 6 else 0
                   for c in range(DH)]

            def scan_n(v, n=n, pb_=pb_, pc_=pc_, t_b=t_b, t_c=t_c,
                       slots=slots, rws=rws, sgate=sgate):
                v.wait_ge(sT.h, t_b)
                v.tensor_tensor(ub3, ut3, pb_[:, None, :].broadcast_to([P, DH, TT]),
                                ALU.mult)
                for c in range(DH):
                    v.tensor_tensor_scan(
                        out=hh3[:, c, :],
                        data0=aexp[:, c * NST + n:c * NST + n + 1].broadcast_to([P, TT]),
                        data1=ub3[:, c, :],
                        initial=hstate[:, n * DH + c:n * DH + c + 1],
                        op0=ALU.mult, op1=ALU.add)
                v.wait_ge(sT.h, t_c)
                if sgate:
                    v.wait_ge(sS.h, sgate)
                for c in range(DH):
                    if rws[c]:
                        v.wait_ge(sG.h, rws[c])
                    v.tensor_tensor(cmb3[:, slots[c], :], hh3[:, c, :], pc_[:],
                                    ALU.mult).then_inc(sV.h, 1)
                return v.tensor_copy(
                    hstate[:].rearrange("p (n h) -> p n h", n=NST)[:, n, :],
                    hh3[:, :, TT - 1])
            kb.emit("vector", scan_n, inc=((sV, sV.tick() and 1),))
            for c in range(DH):
                kb.emit("gpsimd", lambda g, c=c, sl=slots[c]: g.tensor_tensor(
                    yy3[:, c, :], yy3[:, c, :], cmb3[:, sl, :], ALU.add),
                    wait=((sV, slice_ticks[c]),), inc=((sG, sG.tick() and 1),))
                assert sG.n == g_yb[0] + base + c + 1
            gsl[0] = base + DH
        v_scan = sV.n
        g_yacc = sG.n
        dbg_w_y = 0
        if debug_scratch and tc == 1:
            dbg_w_y = dma(sX2S, "gpsimd", dbg_y, yy[:], wait=((sV, v_scan),))

        s_gate = s_gate_evacs[-1]
        dbg_w_g = 0
        if debug_scratch and tc == 1:
            dbg_w_g = dma(sX2S, "gpsimd", dbg_g, gzF[:], wait=((sS, s_gate),))

        kb.emit("vector", lambda v: v.tensor_mul(yy[:], yy[:], gzF[:]),
                wait=((sS, s_gate), (sX2S, dbg_w_y), (sG, g_yacc)),
                inc=((sV, sV.tick() and 1),))
        v_yg = sV.n

        z_evacs = []
        for g in range(DH):
            po = pC if (g % 2 == 0) else pD
            bank_free = ((sV, z_evacs[g - 2]),) if g >= 2 else ()
            mm_group([(po[:], owT[:, c * D + g * P:c * D + (g + 1) * P],
                       yy3[:, c, :]) for c in range(DH)],
                     wait=((sV, v_yg),) + bank_free, inc=((sT, sT.tick() and 1),))
            t_o = sT.n

            def z_evac(v, g=g, po=po, t_o=t_o, dbg_w_g=dbg_w_g):
                v.wait_ge(sT.h, t_o)
                if dbg_w_g:
                    v.wait_ge(sX2S.h, dbg_w_g)
                return v.scalar_tensor_tensor(out=gzF3[:, g, :], in0=po[:],
                                              scalar=obv[:, g:g + 1],
                                              in1=ln1F3[:, g, 8:8 + TT],
                                              op0=ALU.add, op1=ALU.add)
            kb.emit("vector", z_evac, inc=((sV, sV.tick() and 1),))
            z_evacs.append(sV.n)
        v_z = sV.n
        if debug_scratch and tc == 1:
            dbg_w_z = dma(sX2S, "gpsimd", dbg_z, gzF[:], wait=((sV, v_z),))
            sp_x2_last = max(sp_x2_last, dbg_w_z)

        if tc + 1 < NCH:
            v_ln1stats = ln1_stats(tc + 1, ((sV, v_yg),))
        emit_stats_boundary(tc, g_yacc)

        zt4 = hh[:].rearrange("p (s d) -> p s d", s=SUB)
        et4 = ub[:].rearrange("p (s d) -> p s d", s=SUB)
        xr4 = yy[:].rearrange("p (s d) -> p s d", s=SUB)
        qt4 = yy[:].bitcast(BF16)[:, 0:SUB * D].rearrange("p (s d) -> p s d", s=SUB)
        rows = slice(tc * TT, (tc + 1) * TT)
        ep_evacs = [sS.n, sS.n]
        for s_ in range(SUB):
            for c in range(DH):
                k = s_ * DH + c
                bank = ptr0 if (c % 2 == 0) else ptr1
                bw_ = ep_evacs[k % 2]
                kb.emit("tensor", lambda t, c=c, s_=s_, bank=bank: t.transpose(
                    bank[:, 0:P], gzF3[:, c, s_ * P:(s_ + 1) * P], idf[:]),
                    wait=((sV, v_z), (sS, bw_)), inc=((sT, sT.tick() and 1),))
                t_tr = sT.n
                kb.emit("scalar", lambda sc, c=c, s_=s_, bank=bank: sc.activation(
                    out=hh[:, s_ * D + c * P:s_ * D + (c + 1) * P],
                    in_=bank[:, 0:P], func=AF.Copy),
                    wait=((sT, t_tr), (sX2S, sp_x2_last), (sV, sV.n)),
                    inc=((sS, sS.tick() and 1),))
                ep_evacs[k % 2] = sS.n
        if tc + 1 < NCH:
            v_ln1f_next = emit_prologue(tc + 1, v_ln1stats)

        for s_ in range(SUB):
            kb.emit("scalar", lambda sc, s_=s_: sc.activation(
                out=sqs[:], in_=zt4[:, s_, :], func=AF.Square,
                accum_out=lncol[:, s_:s_ + 1]), inc=((sS, sS.tick() and 1),))
        v_cols = ln_cols4(zt4, sS.n)
        for s_ in range(SUB):
            kb.emit("scalar", lambda sc, s_=s_: sc.activation(
                out=et4[:, s_, :], in_=zt4[:, s_, :], func=AF.Identity,
                scale=lncol[:, s_:s_ + 1], bias=lncol[:, 4 + s_:5 + s_]),
                wait=((sV, v_cols),) if s_ == 0 else (), inc=((sS, sS.tick() and 1),))
        s_innorm = sS.n

        def aff4(v, s_innorm=s_innorm):
            v.wait_ge(sS.h, s_innorm)
            v.tensor_tensor(et4, et4, lng_bc[:, None, :].broadcast_to([P, SUB, D]),
                            ALU.mult)
            return v.tensor_tensor(et4, et4,
                                   lnb_bc[:, None, :].broadcast_to([P, SUB, D]),
                                   ALU.add)
        kb.emit("vector", aff4, inc=((sV, sV.tick() and 1),))
        for s_ in range(SUB):
            kb.emit("scalar", lambda sc, s_=s_: sc.activation(
                out=sqs[:], in_=et4[:, s_, :], func=AF.Square,
                accum_out=lncol[:, s_:s_ + 1]),
                wait=((sV, sV.n),) if s_ == 0 else (), inc=((sS, sS.tick() and 1),))
        v_cols = ln_cols4(et4, sS.n)
        for s_ in range(SUB):
            kb.emit("scalar", lambda sc, s_=s_: sc.activation(
                out=zt4[:, s_, :], in_=et4[:, s_, :], func=AF.Identity,
                scale=lncol[:, s_:s_ + 1], bias=lncol[:, 4 + s_:5 + s_]),
                wait=((sV, v_cols),) if s_ == 0 else (), inc=((sS, sS.tick() and 1),))
        s_onorm = sS.n
        w_xr = dma(sXB, "sync", xr4,
                   x_h[rows, :].rearrange("(s p) d -> p s d", p=P),
                   wait=((sV, v_z),))
        kb.emit("vector", lambda v: v.tensor_add(hh[:], hh[:], yy[:]),
                wait=((sS, s_onorm), (sXB, w_xr)), inc=((sV, sV.tick() and 1),))
        v_x2 = sV.n
        sp_x2_last = dma(sX2S, "gpsimd",
                         x2_d[rows, :].rearrange("(s p) d -> p s d", p=P),
                         zt4, wait=((sV, v_x2),))
        for s_ in range(SUB):
            kb.emit("scalar", lambda sc, s_=s_: sc.activation(
                out=sqs[:], in_=zt4[:, s_, :], func=AF.Square,
                accum_out=lncol[:, s_:s_ + 1]),
                wait=((sV, v_x2),) if s_ == 0 else (), inc=((sS, sS.tick() and 1),))
        v_cols = ln_cols4(zt4, sS.n)
        for s_ in range(SUB):
            kb.emit("scalar", lambda sc, s_=s_: sc.activation(
                out=et4[:, s_, :], in_=zt4[:, s_, :], func=AF.Identity,
                scale=lncol[:, s_:s_ + 1], bias=lncol[:, 4 + s_:5 + s_]),
                wait=((sV, v_cols),) if s_ == 0 else (), inc=((sS, sS.tick() and 1),))
        s_ln3 = sS.n

        def q_cols4(v, s_ln3=s_ln3, tc=tc):
            v.wait_ge(sS.h, s_ln3)
            v.tensor_reduce(lncol[:, 0:4], et4, axis=AX.X, op=ALU.max,
                            apply_absolute_value=True)
            v.drain()
            v.tensor_scalar(out=lncol[:, 0:4], in0=lncol[:, 0:4],
                            scalar1=1e-5, scalar2=None, op0=ALU.max)
            v.drain()
            v.reciprocal(lncol[:, 4:8], lncol[:, 0:4])
            v.drain()
            v.tensor_scalar(out=lncol[:, 8:12], in0=lncol[:, 4:8],
                            scalar1=127.0, scalar2=None, op0=ALU.mult)
            return v.tensor_scalar(out=dsx[:, tc * SUB:(tc + 1) * SUB],
                                   in0=lncol[:, 0:4], scalar1=1.0 / 127.0,
                                   scalar2=None, op0=ALU.mult)
        kb.emit("vector", q_cols4, inc=((sV, sV.tick() and 1),))
        v_qc = sV.n
        for s_ in range(SUB):
            kb.emit("scalar", lambda sc, s_=s_: sc.activation(
                out=zt4[:, s_, :], in_=et4[:, s_, :], func=AF.Copy,
                scale=lncol[:, 8 + s_:9 + s_], bias=RB),
                wait=(((sV, v_qc), (sX2S, sp_x2_last)) if s_ == 0 else ()),
                inc=((sS, sS.tick() and 1),))
            s_q1 = sS.n

            def q_fin(v, s_=s_, s_q1=s_q1):
                v.wait_ge(sS.h, s_q1)
                v.tensor_copy(h16[:], zt4[:, s_, :])
                return v.tensor_scalar(out=qt4[:, s_, :], in0=h16[:], scalar1=-RB,
                                       scalar2=None, op0=ALU.add)
            kb.emit("vector", q_fin, inc=((sV, sV.tick() and 1),))
        v_q = sV.n
        sp_xq_last = dma(sXQS, "gpsimd",
                         xq_d[rows, :].rearrange("(s p) d -> p s d", p=P),
                         qt4, wait=((sV, v_q),))

    p2_V, p2_S, p2_T = sV.n, sS.n, sT.n
    p2_X2S, p2_XQS = sX2S.n, sXQS.n

    s_stats = stat_abs_ticks[-1]
    for mi in range(3):
        kb.emit("vector",
                lambda v, mi=mi: v.reduce_sum(colt[:, mi:mi + 1],
                                              wstat[:, mi * 32:(mi + 1) * 32],
                                              axis=AX.X),
                wait=((sS, s_stats),), inc=((sV, sV.tick() and 1),))
    v_sr = sV.n
    kb.emit("tensor", lambda t: t.matmul(pbt[0:1, 0:3], onec[:], colt[:, 0:3],
                                         start=True, stop=True),
            wait=((sV, v_sr),), inc=((sT, sT.tick() and 1),))
    t_s1 = sT.n
    kb.emit("vector", lambda v: v.tensor_copy(colt[0:1, 4:7], pbt[0:1, 0:3]),
            wait=((sT, t_s1),), inc=((sV, sV.tick() and 1),))
    v_s2 = sV.n
    kb.emit("tensor", lambda t: t.matmul(ptr0[:, 0:3], onerow[:],
                                         colt[0:1, 4:7], start=True, stop=True),
            wait=((sV, v_s2),), inc=((sT, sT.tick() and 1),))
    t_s2 = sT.n

    def stat_fin(v):
        v.tensor_scalar(out=colt[:, 0:3], in0=ptr0[:, 0:3],
                        scalar1=1.0 / (D * INNER), scalar2=1e-5,
                        op0=ALU.mult, op1=ALU.max)
        v.drain()
        v.tensor_copy(cwg[:], colt[:, 0:1])
        v.tensor_copy(cwu[:], colt[:, 1:2])
        v.tensor_copy(cwd[:], colt[:, 2:3])
        v.drain()
        v.reciprocal(swg[:], cwg[:])
        v.reciprocal(swu[:], cwu[:])
        return v.reciprocal(swd[:], cwd[:])
    kb.emit("vector", stat_fin, wait=((sT, t_s2),), inc=((sV, sV.tick() and 1),))
    v_stats = sV.n

    p2.close()
    p3 = contextlib.ExitStack()
    sb3 = mk(p3)
    wstg = [sb3("wstg0", [P, INNER]), sb3("wstg1", [P, INNER])]
    wgq = sb3("wgq", [P, DH * INNER], BF16)
    wuq = sb3("wuq", [P, DH * INNER], BF16)
    f16b = sb3("f16b", [P, INNER], F16)
    gu = sb3("gu", [P, INNER])
    gu1 = sb3("gu1", [P, TT])
    guq = sb3("guq", [P, INNER], BF16)
    xqc = sb3("xqc", [P, DH * P], BF16)

    PHW = ((sV, p2_V), (sS, p2_S), (sT, p2_T), (sX2S, p2_X2S), (sXQS, p2_XQS))

    s_wprev = 0
    first = True
    for (wdst, src, scol) in [(wgq, wgT_h, swg), (wuq, wuT_h, swu)]:
        for c in range(DH):
            stg = wstg[c % 2]
            sws = sW0 if c % 2 == 0 else sW1
            w_l = dma(sws, "sync", stg[:], src[c * P:(c + 1) * P, :],
                      wait=(PHW if first else ((sS, s_wprev),)))
            first = False
            kb.emit("scalar", lambda sc, stg=stg, scol=scol: sc.activation(
                out=gu[:], in_=stg[:], func=AF.Identity, scale=scol[:]),
                wait=((sws, w_l), (sV, max(v_stats, sV.n))),
                inc=((sS, sS.tick() and 1),))
            s_wq = sS.n
            s_wprev = s_wq

            def w_fin(v, wdst=wdst, c=c):
                v.tensor_scalar(out=f16b[:], in0=gu[:], scalar1=-0.5, scalar2=None,
                                op0=ALU.is_lt)
                return v.scalar_tensor_tensor(
                    out=wdst[:, c * INNER:(c + 1) * INNER], in0=gu[:], scalar=0.5,
                    in1=f16b[:], op0=ALU.is_gt, op1=ALU.subtract)
            kb.emit("vector", w_fin, wait=((sS, s_wq),), inc=((sV, sV.tick() and 1),))
    v_wq = sV.n

    if debug_scratch:
        dma(sX2S, "gpsimd", dbg_wgq, wgq[:], wait=((sV, v_wq),))
        def dbg_cp(v):
            v.tensor_copy(sqs[:, 0:24], wstat[:])
            v.tensor_copy(sqs[:, 24:32], colt[:])
            v.tensor_copy(sqs[:, 32:33], swg[:])
            v.tensor_copy(sqs[:, 33:34], swu[:])
            v.tensor_copy(sqs[:, 34:35], swd[:])
            v.tensor_copy(sqs[:, 35:36], cwg[:])
            v.tensor_copy(sqs[:, 36:37], cwu[:])
            return v.tensor_copy(sqs[:, 37:38], cwd[:])
        kb.emit("vector", dbg_cp, wait=((sV, v_wq),), inc=((sV, sV.tick() and 1),))
        dma(sX2S, "gpsimd", dbg_cols, sqs[:, 0:64], wait=((sV, sV.n),))
    xstg = sb3("xstg", [P, D], BF16)
    idb = sb3("idb", [P, P], BF16)
    kb.emit("vector", lambda v: v.tensor_copy(idb[:], idf[:]),
            wait=((sV, v_wq),), inc=((sV, sV.tick() and 1),))
    v_idb = sV.n
    sp_gq = 0
    s_xq_evacs = []
    for i in range(NTOK):
        trow = i * P
        wv0 = dma(sXQ, "sync", xstg[:], xq_d[trow:trow + P, :],
                  wait=(((sS, s_xq_evacs[-1] if s_xq_evacs else 0),
                         (sXQS, p2_XQS))))
        wv = wv0
        for c in range(DH):
            pt = ptr0 if c % 2 == 0 else ptr1
            bankw = ((sS, s_xq_evacs[-2],) if len(s_xq_evacs) >= 2 else (sS, 0))
            kb.emit("tensor", lambda t, c=c, pt=pt: t.transpose(
                pt[:, 0:64].bitcast(BF16), xstg[:, c * P:(c + 1) * P], idb[:]),
                wait=((sXQ, wv0), (sV, v_idb), bankw, (sT, sT.n)),
                inc=((sT, sT.tick() and 1),))
            t_tr = sT.n
            kb.emit("scalar", lambda sc, c=c, pt=pt: sc.activation(
                out=xqc[:, c * P:(c + 1) * P], in_=pt[:, 0:64].bitcast(BF16),
                func=AF.Copy), wait=((sT, t_tr),), inc=((sS, sS.tick() and 1),))
            s_xq_evacs.append(sS.n)

        def mk_desc(v, i=i):
            v.tensor_mul(lncol[:, 0:1], cwg[:], dsx[:, i:i + 1])
            v.tensor_mul(lncol[:, 1:2], cwu[:], dsx[:, i:i + 1])
            return v.drain()
        kb.emit("vector", mk_desc, wait=((sV, v_wq), (sS, sS.n)),
                inc=((sV, sV.tick() and 1),))
        v_ds = sV.n
        s_up_evacs = []
        for f in range(DH):
            pg = pA if (f % 2 == 0) else pB
            pu = pC if (f % 2 == 0) else pD
            bank_free = ((sS, s_up_evacs[f - 2]),) if f >= 2 else ()
            mm_group([(pg[:], xqc[:, c * P:(c + 1) * P],
                       wgq[:, c * INNER + f * TT:c * INNER + (f + 1) * TT])
                      for c in range(DH)],
                     wait=((sS, s_xq_evacs[-1]), (sV, v_ds)) + bank_free,
                     inc=((sT, sT.tick() and 1),))
            t_g = sT.n
            mm_group([(pu[:], xqc[:, c * P:(c + 1) * P],
                       wuq[:, c * INNER + f * TT:c * INNER + (f + 1) * TT])
                      for c in range(DH)],
                     wait=(), inc=((sT, sT.tick() and 1),))
            t_u = sT.n
            kb.emit("scalar", lambda sc, pg=pg: sc.activation(
                out=gu1[:], in_=pg[:], func=AF.Silu, scale=lncol[:, 0:1]),
                wait=((sT, t_g), (sV, sV.n)), inc=((sS, sS.tick() and 1),))
            kb.emit("scalar", lambda sc, pu=pu: sc.activation(
                out=sqs[:, 0:TT], in_=pu[:], func=AF.Copy, scale=lncol[:, 1:2]),
                wait=((sT, t_u), (sV, sV.n)), inc=((sS, sS.tick() and 1),))
            s_up = sS.n
            s_up_evacs.append(s_up)

            def gu_mul(v, f=f, s_up=s_up):
                v.wait_ge(sS.h, s_up)
                v.tensor_tensor(gu[:, f * TT:(f + 1) * TT], gu1[:], sqs[:, 0:TT],
                                ALU.mult)
                v.tensor_reduce(lncol[:, 2:3], gu[:, f * TT:(f + 1) * TT],
                                axis=AX.X, op=ALU.max, apply_absolute_value=True)
                v.drain()
                if f == 0:
                    return v.tensor_copy(lncol[:, 3:4], lncol[:, 2:3])
                return v.tensor_max(lncol[:, 3:4], lncol[:, 3:4], lncol[:, 2:3])
            kb.emit("vector", gu_mul, inc=((sV, sV.tick() and 1),))

        def gq_cols(v, i=i):
            v.tensor_scalar(out=lncol[:, 3:4], in0=lncol[:, 3:4], scalar1=1e-5,
                            scalar2=None, op0=ALU.max)
            v.drain()
            v.reciprocal(lncol[:, 4:5], lncol[:, 3:4])
            v.drain()
            v.tensor_scalar(out=lncol[:, 5:6], in0=lncol[:, 4:5], scalar1=127.0,
                            scalar2=None, op0=ALU.mult)
            return v.tensor_scalar(out=dsg[:, i:i + 1], in0=lncol[:, 3:4],
                                   scalar1=1.0 / 127.0, scalar2=None, op0=ALU.mult)
        kb.emit("vector", gq_cols, wait=((sS, sS.n),),
                inc=((sV, sV.tick() and 1),))
        v_gqc = sV.n
        if debug_scratch and i == 0:
            dma(sX2S, "gpsimd", dbg_xqc, xqc[:], wait=((sV, v_gqc),))
            dma(sX2S, "gpsimd", dbg_gu, gu[:], wait=((sV, v_gqc),))
        kb.emit("scalar", lambda sc: sc.activation(
            out=f16b[:], in_=gu[:], func=AF.Copy, scale=lncol[:, 5:6], bias=RB),
            wait=((sV, v_gqc),), inc=((sS, sS.tick() and 1),))
        s_gq = sS.n

        def g_fin(v, s_gq=s_gq, sp_gq=sp_gq):
            v.wait_ge(sS.h, s_gq)
            v.wait_ge(sGQS.h, sp_gq)
            return v.tensor_scalar(out=guq[:], in0=f16b[:], scalar1=-RB,
                                   scalar2=None, op0=ALU.add)
        kb.emit("vector", g_fin, inc=((sV, sV.tick() and 1),))
        v_gq = sV.n
        sp_gq = dma(sGQS, "gpsimd", gq_d[trow:trow + P, :], guq[:],
                    wait=((sV, v_gq),))
    p3_V, p3_S, p3_T, p3_GQS = sV.n, sS.n, sT.n, sGQS.n

    p3.close()
    p4 = contextlib.ExitStack()
    sb4 = mk(p4)
    wdq = sb4("wdq", [P, IH * D], BF16)
    gqT = sb4("gqT", [P, IH * P], BF16)
    gqT2 = sb4("gqT2", [P, IH * P], BF16)
    x2b_ = [sb4("x2b", [P, SUB * D]), sb4("x2c", [P, SUB * D])]
    o2b_ = [sb4("o2b", [P, SUB * D]), sb4("o2c", [P, SUB * D])]
    gstg_ = [sb4("gstg", [P, INNER], BF16), sb4("gsth", [P, INNER], BF16)]
    idb4 = sb4("idb4", [P, P], BF16)
    wd16 = sb4("wd16", [P, D], F16)
    wdf = sb4("wdf", [P, D])
    wst4 = [sb4("wst4a", [P, D]), sb4("wst4b", [P, D])]

    PHW3 = ((sV, p3_V), (sS, p3_S), (sT, p3_T), (sGQS, p3_GQS))
    kb.emit("vector", lambda v: v.tensor_copy(idb4[:], idf[:]),
            wait=((sV, sV.n),), inc=((sV, sV.tick() and 1),))
    v_idb4 = sV.n
    s_gq_evacs = []
    v_wd_ticks = []
    s_wprev = 0
    first = True
    for c in range(IH):
        stg = wst4[c % 2]
        sws = sW0 if c % 2 == 0 else sW1
        w_l = dma(sws, "sync", stg[:, 0:D], wdT_h[c * P:(c + 1) * P, :],
                  wait=(PHW3 if first else ((sS, s_wprev),)))
        first = False
        kb.emit("scalar", lambda sc, stg=stg: sc.activation(
            out=wdf[:], in_=stg[:, 0:D], func=AF.Identity, scale=swd[:]),
            wait=((sws, w_l), (sV, sV.n)), inc=((sS, sS.tick() and 1),))
        s_wq = sS.n
        s_wprev = s_wq

        def wd_fin(v, c=c):
            v.tensor_scalar(out=wd16[:], in0=wdf[:], scalar1=-0.5, scalar2=None,
                            op0=ALU.is_lt)
            return v.scalar_tensor_tensor(
                out=wdq[:, c * D:(c + 1) * D], in0=wdf[:], scalar=0.5,
                in1=wd16[:], op0=ALU.is_gt, op1=ALU.subtract)
        kb.emit("vector", wd_fin, wait=((sS, s_wq),), inc=((sV, sV.tick() and 1),))
        v_wd_ticks.append(sV.n)
    v_wd = sV.n
    if debug_scratch:
        dma(sX2S, "gpsimd", dbg_wdq, wdq[:], wait=((sV, v_wd),))

    v_o_dones = [0, 0]
    dm_outs = [0, 0]
    tr_lasts = [0, 0]
    for gi in range(NTOK // SUB):
        x2b = x2b_[gi % 2]
        o2b = o2b_[gi % 2]
        o2b4 = o2b[:].rearrange("p (s d) -> p s d", s=SUB)
        x2b4 = x2b[:].rearrange("p (s d) -> p s d", s=SUB)
        rows = slice(gi * SUB * P, (gi + 1) * SUB * P)
        w_x2 = dma(sX2, "sync", x2b4,
                   x2_d[rows, :].rearrange("(s p) d -> p s d", p=P),
                   wait=((sV, v_o_dones[gi % 2]), (sX2S, p2_X2S)))
        def mk_dd(v, gi=gi):
            v.wait_ge(sV.h, v_wd)
            v.tensor_tensor(colt[:, 0:SUB], cwd[:].broadcast_to([P, SUB]),
                            dsg[:, gi * SUB:(gi + 1) * SUB], ALU.mult)
            return v.drain()
        kb.emit("vector", mk_dd, wait=((sS, sS.n),), inc=((sV, sV.tick() and 1),))
        v_dd = sV.n
        for il in range(SUB):
            i = gi * SUB + il
            trow = i * P
            gq_t = gqT if il % 2 == 0 else gqT2
            gstg = gstg_[i % 2]
            wvg = dma(sGQT, "sync", gstg[:], gq_d[trow:trow + P, :],
                      wait=(((sT, tr_lasts[i % 2]), (sGQS, p3_GQS))))
            for c in range(IH):
                pt = pC if c % 2 == 0 else pD
                bankw = ((sS, s_gq_evacs[-2]) if len(s_gq_evacs) >= 2 else (sS, 0))
                kb.emit("tensor", lambda t, c=c, pt=pt, gq_t=gq_t, gstg=gstg:
                    t.transpose(
                        pt[:, 0:64].bitcast(BF16), gstg[:, c * P:(c + 1) * P],
                        idb4[:]),
                    wait=((sGQT, wvg), (sV, v_idb4), bankw, (sT, sT.n)),
                    inc=((sT, sT.tick() and 1),))
                t_tr = sT.n
                kb.emit("scalar", lambda sc, c=c, pt=pt, gq_t=gq_t: sc.activation(
                    out=gq_t[:, c * P:(c + 1) * P], in_=pt[:, 0:64].bitcast(BF16),
                    func=AF.Copy), wait=((sT, t_tr),), inc=((sS, sS.tick() and 1),))
                s_gq_evacs.append(sS.n)
            tr_lasts[i % 2] = sT.n
            for g2 in range(2):
                pq = pA if (g2 % 2 == 0) else pB
                mmw = None
                wdw = ((sV, v_wd),) if (gi == 0 and il == 0) else ()
                mm_group([(pq[:], gq_t[:, c * P:(c + 1) * P],
                           wdq[:, c * D + g2 * TT:c * D + (g2 + 1) * TT])
                          for c in range(IH)],
                         wait=((sS, s_gq_evacs[-1]),) + wdw,
                         inc=((sT, sT.tick() and 1),), mm_waits=mmw)
                t_d = sT.n
                kb.emit("scalar", lambda sc, il=il, g2=g2, pq=pq, o2b=o2b:
                    sc.activation(
                        out=o2b[:, il * D + g2 * TT:il * D + (g2 + 1) * TT],
                        in_=pq[:], func=AF.Copy, scale=colt[:, il:il + 1]),
                    wait=((sT, t_d), (sV, v_dd), (sOUT, dm_outs[gi % 2])),
                    inc=((sS, sS.tick() and 1),))
        s_o2 = sS.n
        for s_ in range(SUB):
            kb.emit("scalar", lambda sc, s_=s_, o2b4=o2b4: sc.activation(
                out=sqs[:], in_=o2b4[:, s_, :], func=AF.Square,
                accum_out=lncol[:, s_:s_ + 1]),
                wait=((sV, sV.n),) if s_ == 0 else (), inc=((sS, sS.tick() and 1),))
        v_cols = ln_cols4(o2b4, sS.n)
        for s_ in range(SUB):
            kb.emit("scalar", lambda sc, s_=s_, o2b4=o2b4: sc.activation(
                out=o2b4[:, s_, :], in_=o2b4[:, s_, :], func=AF.Identity,
                scale=lncol[:, s_:s_ + 1], bias=lncol[:, 4 + s_:5 + s_]),
                wait=((sV, v_cols),) if s_ == 0 else (), inc=((sS, sS.tick() and 1),))
        s_ln4 = sS.n
        kb.emit("vector", lambda v, o2b=o2b, x2b=x2b: v.tensor_add(
            o2b[:], o2b[:], x2b[:]),
            wait=((sS, s_ln4), (sX2, w_x2)), inc=((sV, sV.tick() and 1),))
        v_o_dones[gi % 2] = sV.n
        dm_outs[gi % 2] = dma(sOUT, "gpsimd",
                              out_h[rows, :].rearrange("(s p) d -> p s d", p=P),
                              o2b4, wait=((sV, sV.n),))

    kb.emit("gpsimd", lambda g: None, wait=((sOUT, sOUT.n),))
    p4.close()

    with top:
        with nc.Block() as block:
            kb.run(block)
    return nc


_NC_CACHE = None


def kernel(**inputs):
    global _NC_CACHE
    f32 = np.float32
    T = lambda a: np.ascontiguousarray(np.asarray(a, dtype=f32).T)
    C = lambda a: np.ascontiguousarray(np.asarray(a, dtype=f32))
    x = C(inputs["L0_attn_x"])
    common = {
        "conv_w": C(np.asarray(inputs["L0_attn_conv_w"], dtype=f32)[:, 0, :]),
        "conv_b": C(inputs["L0_attn_conv_b"]),
        "A": C(inputs["L0_attn_A"]),
        "BpT": T(inputs["L0_attn_Bp_w"]),
        "Bpb": C(inputs["L0_attn_Bp_b"]),
        "CpT": T(inputs["L0_attn_Cp_w"]),
        "Cpb": C(inputs["L0_attn_Cp_b"]),
        "Dd": C(inputs["L0_attn_D"]),
        "gwT": T(inputs["L0_attn_gate_w"]),
        "gb": C(inputs["L0_attn_gate_b"]),
        "owT": T(inputs["L0_attn_out_w"]),
        "ob": C(inputs["L0_attn_out_b"]),
        "lng": C(inputs["L0_attn_ln_g"]),
        "lnb": C(inputs["L0_attn_ln_b"]),
        "wgT": T(inputs["L0_ffn_w_gate"]),
        "wuT": T(inputs["L0_ffn_w_up"]),
        "wdT": T(inputs["L0_ffn_w_down"]),
    }
    in_maps = [dict(common, x=x[c]) for c in range(8)]
    if _NC_CACHE is None:
        _NC_CACHE = build_core()
    res = run_bass_kernel_spmd(_NC_CACHE, in_maps, core_ids=list(range(8)))
    return np.stack([np.asarray(r["out"], dtype=f32) for r in res.results])


# version 21 (final)
# speedup vs baseline: 1.1038x; 1.0276x over previous
import contextlib
import numpy as np
import concourse.bass as bass
import concourse.mybir as mybir
from concourse.bass_utils import run_bass_kernel_spmd

F32 = mybir.dt.float32
F16 = mybir.dt.float16
BF16 = mybir.dt.bfloat16
AF = mybir.ActivationFunctionType
ALU = mybir.AluOpType
AX = mybir.AxisListType

P = 128
S = 2048
D = 1024
DH = D // P
NST = 16
DCONV = 4
INNER = 4096
IH = INNER // P
TT = 512
NCH = S // TT
SUB = TT // P
NTOK = S // P
LN_EPS = 1e-5
RB = 1536.0


class Sem:
    def __init__(self, h):
        self.h = h
        self.n = 0

    def tick(self, k=1):
        self.n += k
        return self.n


class KB:
    ENGS = ("sync", "tensor", "vector", "scalar", "gpsimd")

    def __init__(self, nc):
        self.nc = nc
        self.q = {e: [] for e in self.ENGS}

    def emit(self, eng, fn, wait=(), inc=()):
        wait = tuple((s.h, v) for s, v in wait)
        inc = tuple((s.h, k) for s, k in inc)

        def g(h):
            for sh, v in wait:
                h.wait_ge(sh, v)
            inst = fn(h)
            for sh, k in inc:
                inst.then_inc(sh, k)
        self.q[eng].append(g)

    def run(self, block):
        for eng in self.ENGS:
            fns = self.q[eng]
            getattr(block, eng)(lambda h, fns=fns: [f(h) for f in fns])


def build_core(debug_scratch=False):
    nc = bass.Bass("TRN2", target_bir_lowering=False, detect_race_conditions=False)

    x_h = nc.dram_tensor("x", [S, D], F32, kind="ExternalInput").ap()
    conv_w_h = nc.dram_tensor("conv_w", [D, DCONV], F32, kind="ExternalInput").ap()
    conv_b_h = nc.dram_tensor("conv_b", [D], F32, kind="ExternalInput").ap()
    A_h = nc.dram_tensor("A", [D, NST], F32, kind="ExternalInput").ap()
    BpT_h = nc.dram_tensor("BpT", [D, NST], F32, kind="ExternalInput").ap()
    Bpb_h = nc.dram_tensor("Bpb", [NST], F32, kind="ExternalInput").ap()
    CpT_h = nc.dram_tensor("CpT", [D, NST], F32, kind="ExternalInput").ap()
    Cpb_h = nc.dram_tensor("Cpb", [NST], F32, kind="ExternalInput").ap()
    Dd_h = nc.dram_tensor("Dd", [D], F32, kind="ExternalInput").ap()
    gwT_h = nc.dram_tensor("gwT", [D, D], F32, kind="ExternalInput").ap()
    gb_h = nc.dram_tensor("gb", [D], F32, kind="ExternalInput").ap()
    owT_h = nc.dram_tensor("owT", [D, D], F32, kind="ExternalInput").ap()
    ob_h = nc.dram_tensor("ob", [D], F32, kind="ExternalInput").ap()
    lng_h = nc.dram_tensor("lng", [D], F32, kind="ExternalInput").ap()
    lnb_h = nc.dram_tensor("lnb", [D], F32, kind="ExternalInput").ap()
    wgT_h = nc.dram_tensor("wgT", [D, INNER], F32, kind="ExternalInput").ap()
    wuT_h = nc.dram_tensor("wuT", [D, INNER], F32, kind="ExternalInput").ap()
    wdT_h = nc.dram_tensor("wdT", [INNER, D], F32, kind="ExternalInput").ap()
    out_h = nc.dram_tensor("out", [S, D], F32, kind="ExternalOutput").ap()
    _scr = "ExternalOutput" if debug_scratch else "Internal"
    x2_d = nc.dram_tensor("x2_scr", [S, D], F32, kind=_scr).ap()
    xq_d = nc.dram_tensor("xq_scr", [S, D], BF16, kind=_scr).ap()
    gq_d = nc.dram_tensor("gq_scr", [S, INNER], BF16, kind=_scr).ap()
    if debug_scratch:
        dbg_st = nc.dram_tensor("dbg_st", [P, 40], F32, kind=_scr).ap()
        dbg_wgq = nc.dram_tensor("dbg_wgq", [P, DH * INNER], BF16, kind=_scr).ap()
        dbg_xqc = nc.dram_tensor("dbg_xqc", [P, DH * P], BF16, kind=_scr).ap()
        dbg_gu = nc.dram_tensor("dbg_gu", [P, INNER], F32, kind=_scr).ap()
        dbg_cols = nc.dram_tensor("dbg_cols", [P, 64], F32, kind=_scr).ap()
        dbg_wdq = nc.dram_tensor("dbg_wdq", [P, IH * D], BF16, kind=_scr).ap()
        dbg_u = nc.dram_tensor("dbg_u", [P, DH * TT], F32, kind=_scr).ap()
        dbg_bt = nc.dram_tensor("dbg_bt", [NST, 2 * TT], F32, kind=_scr).ap()
        dbg_y = nc.dram_tensor("dbg_y", [P, DH * TT], F32, kind=_scr).ap()
        dbg_g = nc.dram_tensor("dbg_g", [P, DH * TT], F32, kind=_scr).ap()
        dbg_z = nc.dram_tensor("dbg_z", [P, DH * TT], F32, kind=_scr).ap()
        dbg_e = nc.dram_tensor("dbg_e", [P, 40], F32, kind=_scr).ap()

    kb = KB(nc)
    top = contextlib.ExitStack()

    def mk(es_):
        sbf = lambda n, shp, dt=F32: es_.enter_context(nc.sbuf_tensor(n, shp, dt))
        return sbf
    sb = mk(top)
    sem = lambda n: Sem(top.enter_context(nc.semaphore(n)))

    sIN = sem("sIN")
    sSP = sem("sSP")
    sXA = sem("sXA")
    sXB = sem("sXB")
    sW0 = sem("sW0")
    sW1 = sem("sW1")
    sWG = sem("sWG")
    sPRM = sem("sPRM")
    sXQ = sem("sXQ")
    sGQT = sem("sGQT")
    sX2 = sem("sX2")
    sX2S = sem("sX2S")
    sXQS = sem("sXQS")
    sGQS = sem("sGQS")
    sOUT = sem("sOUT")
    sT = sem("sT")
    sV = sem("sV")
    sS = sem("sS")
    sG = sem("sG")
    dmo = sem("dmo")

    pes = contextlib.ExitStack()
    top.enter_context(pes)
    ptr0 = pes.enter_context(nc.psum_tensor("ptr0", [P, TT], F32))
    ptr1 = pes.enter_context(nc.psum_tensor("ptr1", [P, TT], F32))
    pA = pes.enter_context(nc.psum_tensor("pA", [P, TT], F32))
    pB = pes.enter_context(nc.psum_tensor("pB", [P, TT], F32))
    pC = pes.enter_context(nc.psum_tensor("pC", [P, TT], F32))
    pD = pes.enter_context(nc.psum_tensor("pD", [P, TT], F32))
    pbt = pes.enter_context(nc.psum_tensor("pbt", [NST, 2 * TT], F32))

    def mm_group(plist, wait, inc, mm_waits=None):
        n = len(plist)

        def fn(t):
            inst = None
            for i, (o, l, r) in enumerate(plist):
                if mm_waits is not None and mm_waits[i]:
                    for sh, val in mm_waits[i]:
                        t.wait_ge(sh, val)
                inst = t.matmul(o, l, r, start=(i == 0),
                                stop=(i % 4 == 3 or i == n - 1),
                                skip_group_check=(i >= 4))
            return inst
        kb.emit("tensor", fn, wait=wait, inc=inc)

    def vfill(v):
        v.tensor_scalar(out=jnk[:, 0:1], in0=onec[:], scalar1=1.0,
                        scalar2=None, op0=ALU.mult)
        v.tensor_scalar(out=jnk[:, 1:2], in0=onec[:], scalar1=1.0,
                        scalar2=None, op0=ALU.mult)

    def dma(s_, eng, out_ap, in_ap, wait=(), **kw):
        v = s_.tick(16)
        kb.emit(eng, lambda e, o=out_ap, i=in_ap, kw=kw: e.dma_start(out=o, in_=i, **kw),
                wait=wait, inc=((s_, 16),))
        return v

    sqs = sb("sqs", [P, D])
    lncol = sb("lncol", [P, 16])
    colt = sb("colt", [P, 8])
    onec = sb("onec", [P, 1])
    jnk = sb("jnk", [P, 2])
    onerow = sb("onerow", [1, P])
    epsc = sb("epsc", [P, 1])
    idf = sb("idf", [P, P])
    wstat = sb("wstat", [P, 96])
    swg, swu, swd = sb("swg", [P, 1]), sb("swu", [P, 1]), sb("swd", [P, 1])
    cwg, cwu, cwd = sb("cwg", [P, 1]), sb("cwu", [P, 1]), sb("cwd", [P, 1])
    dsx = sb("dsx", [P, NTOK])
    dsg = sb("dsg", [P, NTOK])
    h16 = sb("h16", [P, D], F16)
    dbgc = sb("dbgc", [P, 40]) if debug_scratch else None

    kb.emit("gpsimd", lambda g: g.iota(idf[:], [[1, P]], channel_multiplier=-1,
                                       allow_small_or_imprecise_dtypes=True))
    kb.emit("gpsimd", lambda g: g.memset(onec[:], 1.0))
    kb.emit("gpsimd", lambda g: g.memset(onerow[:], 1.0))
    kb.emit("gpsimd", lambda g: g.memset(epsc[:], LN_EPS), inc=((sG, sG.tick() and 1),))
    g_tab0 = sG.n
    kb.emit("vector", lambda v: v.tensor_scalar(out=idf[:], in0=idf[:], scalar1=0.0,
                                                scalar2=None, op0=ALU.is_equal),
            wait=((sG, g_tab0),), inc=((sV, sV.tick() and 1),))
    v_tab = sV.n


    p2 = contextlib.ExitStack()
    sb2 = mk(p2)
    aexp = sb2("aexp", [P, DH * NST])
    cw = sb2("cw", [P, DH * DCONV])
    cb = sb2("cb", [P, DH])
    dD = sb2("dD", [P, DH])
    gbv = sb2("gbv", [P, DH])
    obv = sb2("obv", [P, DH])
    bpb = sb2("bpb", [NST, 1])
    cpb = sb2("cpb", [NST, 1])
    lng_bc = sb2("lng_bc", [P, D])
    lnb_bc = sb2("lnb_bc", [P, D])
    oneh = sb2("oneh", [NST, NST * P])
    cmb = sb2("cmb", [P, 6 * TT])
    hstate = sb2("hstate", [P, NST * DH])
    gwT = sb2("gwT_s", [P, DH * D])
    owT = sb2("owT_s", [P, DH * D])
    bpT = sb2("bpT", [P, DH * NST])
    cpT = sb2("cpT", [P, DH * NST])
    LF = TT + 8
    ln1F = sb2("ln1F", [P, DH * LF])
    ut = sb2("ut", [P, DH * TT])
    ub = sb2("ub", [P, DH * TT])
    hh = sb2("hh", [P, DH * TT])
    yy = sb2("yy", [P, DH * TT])
    gzF = sb2("gzF", [P, DH * TT])
    btT = sb2("btT", [NST, TT])
    ctT = sb2("ctT", [NST, TT])
    etok = sb2("etok", [P, D])
    lncol2 = sb2("lncol2", [P, 16])

    ln1F3 = ln1F[:].rearrange("p (h t) -> p h t", h=DH)
    ut3 = ut[:].rearrange("p (h t) -> p h t", h=DH)
    ub3 = ub[:].rearrange("p (h t) -> p h t", h=DH)
    hh3 = hh[:].rearrange("p (h t) -> p h t", h=DH)
    yy3 = yy[:].rearrange("p (h t) -> p h t", h=DH)
    cmb3 = cmb[:].rearrange("p (h t) -> p h t", h=6)
    g_yb = [0]
    gsl = [0]
    scan_ticks = []
    gzF3 = gzF[:].rearrange("p (h t) -> p h t", h=DH)
    bpT3 = bpT[:].rearrange("p (h n) -> p h n", h=DH)
    cpT3 = cpT[:].rearrange("p (h n) -> p h n", h=DH)

    wdT_blk = wdT_h.rearrange("(b p q) d -> b p (q d)", b=8, p=P)
    stat_blocks = []
    for mi, wsrc in enumerate([wgT_h, wuT_h, wdT_h]):
        for b in range(8):
            for q in range(4):
                if mi < 2:
                    ap_ = wsrc[b * P:(b + 1) * P, q * D:(q + 1) * D]
                else:
                    ap_ = wdT_blk[b][:, q * D:(q + 1) * D]
                stat_blocks.append((ap_, mi * 32 + b * 4 + q))
    stat_abs_ticks = []
    stats_gate = [0]

    def emit_stats_boundary(tc, g_free):
        slot_sems = (sW0, sW1, sWG)
        for k in range(24):
            blk = tc * 24 + k
            ap_, col = stat_blocks[blk]
            wt = ((sG, g_free),) if k == 0 else ()
            if blk >= 3:
                wt = wt + ((sS, stat_abs_ticks[blk - 3]),)
            sws = slot_sems[blk % 3]
            w_l = dma(sws, "sync", cmb[:, (blk % 3) * D:(blk % 3 + 1) * D],
                      ap_, wait=wt)
            kb.emit("scalar", lambda sc, blk=blk, col=col: sc.activation(
                out=cmb[:, (blk % 3) * D:(blk % 3 + 1) * D],
                in_=cmb[:, (blk % 3) * D:(blk % 3 + 1) * D],
                func=AF.Abs, accum_out=wstat[:, col:col + 1]),
                wait=((sws, w_l),), inc=((sS, sS.tick() and 1),))
            stat_abs_ticks.append(sS.n)
        stats_gate[0] = sS.n
    v_stats = sV.n
    if False and debug_scratch:
        def dbg_stats(v):
            v.tensor_copy(dbgc[:, 0:24], wstat[:])
            v.tensor_copy(dbgc[:, 24:32], colt[:])
            v.tensor_copy(dbgc[:, 32:33], swg[:])
            v.tensor_copy(dbgc[:, 33:34], cwg[:])
            v.tensor_copy(dbgc[:, 34:37], ptr0[:, 0:3])
            return v.tensor_copy(dbgc[:, 37:38], onec[:])
        kb.emit("vector", dbg_stats, wait=((sV, v_stats),), inc=((sV, sV.tick() and 1),))
        dma(sX2S, "gpsimd", dbg_st, dbgc[:], wait=((sV, sV.n),))

    dma(sPRM, "sync", aexp[:].rearrange("p (h n) -> p h n", h=DH),
        A_h.rearrange("(h p) n -> p h n", p=P))
    dma(sPRM, "sync", cw[:].rearrange("p (h j) -> p h j", h=DH),
        conv_w_h.rearrange("(h p) j -> p h j", p=P))
    dma(sPRM, "sync", cb[:], conv_b_h.rearrange("(h p) -> p h", p=P), allow_slow_non_contiguous=True)
    dma(sPRM, "sync", dD[:], Dd_h.rearrange("(h p) -> p h", p=P), allow_slow_non_contiguous=True)
    dma(sPRM, "sync", gbv[:], gb_h.rearrange("(h p) -> p h", p=P), allow_slow_non_contiguous=True)
    dma(sPRM, "sync", obv[:], ob_h.rearrange("(h p) -> p h", p=P), allow_slow_non_contiguous=True)
    dma(sPRM, "sync", bpb[:], Bpb_h[:, None])
    dma(sPRM, "sync", cpb[:], Cpb_h[:, None])
    dma(sPRM, "sync", lng_bc[:], lng_h[None, :].broadcast_to([P, D]))
    dma(sPRM, "sync", lnb_bc[:], lnb_h[None, :].broadcast_to([P, D]))
    in_params = sPRM.n
    dma(sPRM, "sync", bpT3, BpT_h.rearrange("(h p) n -> p h n", p=P))
    dma(sPRM, "sync", cpT3, CpT_h.rearrange("(h p) n -> p h n", p=P))
    in_bc = sPRM.n
    for c in range(DH):
        dma(sPRM, "sync", gwT[:, c * D:(c + 1) * D], gwT_h[c * P:(c + 1) * P, :])
    for c in range(DH):
        dma(sPRM, "sync", owT[:, c * D:(c + 1) * D], owT_h[c * P:(c + 1) * P, :])
    in_w = sPRM.n
    PRM_ALL = sPRM.n

    kb.emit("gpsimd", lambda g: g.iota(oneh[:], [[1, NST], [0, P]],
                                       channel_multiplier=-1,
                                       allow_small_or_imprecise_dtypes=True))
    kb.emit("gpsimd", lambda g: g.memset(hstate[:], 0.0), inc=((sG, sG.tick() and 1),))
    g_tab = sG.n
    kb.emit("vector", lambda v: v.tensor_scalar(out=oneh[:], in0=oneh[:], scalar1=0.0,
                                                scalar2=None, op0=ALU.is_equal),
            wait=((sG, g_tab),), inc=((sV, sV.tick() and 1),))
    v_oneh = sV.n
    def aexp1(sc):
        inst = sc.activation(out=aexp[:], in_=aexp[:], func=AF.Exp)
        sc.drain()
        return inst
    kb.emit("scalar", aexp1,
            wait=((sPRM, PRM_ALL),), inc=((sS, sS.tick() and 1),))
    kb.emit("scalar", lambda sc: sc.activation(out=aexp[:], in_=aexp[:], func=AF.Exp,
                                               scale=-1.0), inc=((sS, sS.tick() and 1),))
    s_aexp = sS.n

    def ln_pipeline(src_ap, dst_ap, affine):
        kb.emit("scalar", lambda sc: sc.activation(
            out=sqs[:], in_=src_ap, func=AF.Square, accum_out=lncol[:, 8:9]),
            wait=((sV, sV.n),), inc=((sS, sS.tick() and 1),))
        s_a = sS.n

        def cols1(v):
            v.reduce_sum(lncol[:, 9:10], src_ap, axis=AX.X)
            vfill(v)
            v.tensor_scalar(out=lncol[:, 10:11], in0=lncol[:, 9:10],
                            scalar1=1.0 / D, scalar2=None, op0=ALU.mult)
            v.tensor_scalar(out=lncol[:, 11:12], in0=lncol[:, 8:9],
                            scalar1=1.0 / D, scalar2=None, op0=ALU.mult)
            vfill(v)
            v.tensor_mul(lncol[:, 12:13], lncol[:, 10:11], lncol[:, 10:11])
            vfill(v)
            v.tensor_sub(lncol[:, 11:12], lncol[:, 11:12], lncol[:, 12:13])
            vfill(v)
            return v.tensor_scalar(out=lncol[:, 11:12], in0=lncol[:, 11:12],
                                   scalar1=LN_EPS, scalar2=LN_EPS,
                                   op0=ALU.add, op1=ALU.max)
        kb.emit("vector", cols1, wait=((sS, s_a),), inc=((sV, sV.tick() and 1),))
        v_a = sV.n
        kb.emit("scalar", lambda sc: sc.activation(
            out=lncol[:, 12:13], in_=lncol[:, 11:12], func=AF.Sqrt),
            wait=((sV, v_a),), inc=((sS, sS.tick() and 1),))
        s_b = sS.n

        def cols2(v):
            v.reciprocal(lncol[:, 13:14], lncol[:, 12:13])
            vfill(v)
            return v.scalar_tensor_tensor(out=lncol[:, 14:15], in0=lncol[:, 10:11],
                                          scalar=-1.0, in1=lncol[:, 13:14],
                                          op0=ALU.mult, op1=ALU.mult)
        kb.emit("vector", cols2, wait=((sS, s_b),), inc=((sV, sV.tick() and 1),))
        v_b = sV.n
        kb.emit("scalar", lambda sc: sc.activation(
            out=dst_ap, in_=src_ap, func=AF.Identity, scale=lncol[:, 13:14],
            bias=lncol[:, 14:15]), wait=((sV, v_b),), inc=((sS, sS.tick() and 1),))
        s_c = sS.n
        if affine is not None:
            gt, btl = affine

            def aff(v):
                v.tensor_mul(dst_ap, dst_ap, gt[:])
                return v.tensor_add(dst_ap, dst_ap, btl[:])
            kb.emit("vector", aff, wait=((sS, s_c),), inc=((sV, sV.tick() and 1),))


    def ln_cols4(src4, sq_wait, ct=None):
        if ct is None:
            ct = lncol

        def chain1(v, sq_wait=sq_wait, src4=src4, ct=ct):
            v.wait_ge(sS.h, sq_wait)
            v.reduce_sum(ct[:, 4:8], src4, axis=AX.X)
            vfill(v)
            v.tensor_scalar(out=ct[:, 8:12], in0=ct[:, 4:8],
                            scalar1=1.0 / D, scalar2=None, op0=ALU.mult)
            v.tensor_scalar(out=ct[:, 4:8], in0=ct[:, 0:4],
                            scalar1=1.0 / D, scalar2=None, op0=ALU.mult)
            v.tensor_mul(ct[:, 12:16], ct[:, 8:12], ct[:, 8:12])
            vfill(v)
            return v.scalar_tensor_tensor(out=ct[:, 4:8], in0=ct[:, 12:16],
                                          scalar=-1.0, in1=ct[:, 4:8],
                                          op0=ALU.mult, op1=ALU.add)
        kb.emit("vector", chain1, inc=((sV, sV.tick() and 1),))
        v_a = sV.n
        kb.emit("scalar", lambda sc, ct=ct: sc.activation(
            out=ct[:, 12:16], in_=ct[:, 4:8], func=AF.Sqrt, bias=epsc[:]),
            wait=((sV, v_a), (sG, g_tab0)), inc=((sS, sS.tick() and 1),))
        s_b = sS.n

        def chain2(v, s_b=s_b, ct=ct):
            v.wait_ge(sS.h, s_b)
            v.reciprocal(ct[:, 0:4], ct[:, 12:16])
            vfill(v)
            return v.scalar_tensor_tensor(out=ct[:, 4:8], in0=ct[:, 8:12],
                                          scalar=-1.0, in1=ct[:, 0:4],
                                          op0=ALU.mult, op1=ALU.mult)
        kb.emit("vector", chain2, inc=((sV, sV.tick() and 1),))
        return sV.n

    sp_x2_last = 0
    sp_xq_last = 0
    x4n = ut[:].rearrange("p (s d) -> p s d", s=SUB)

    def ln1_stats(t_chunk, extra_wait):
        w = dma(sXA, "sync", x4n,
                x_h[t_chunk * TT:(t_chunk + 1) * TT, :].rearrange(
                    "(s p) d -> p s d", p=P), wait=extra_wait)
        for s_ in range(SUB):
            kb.emit("scalar", lambda sc, s_=s_: sc.activation(
                out=sqs[:], in_=x4n[:, s_, :], func=AF.Square,
                accum_out=lncol2[:, s_:s_ + 1]),
                wait=((sXA, w),) if s_ == 0 else (), inc=((sS, sS.tick() and 1),))
        return ln_cols4(x4n, sS.n, ct=lncol2)

    v_ln1stats = ln1_stats(0, ())

    def emit_prologue(tc, v_stats_ln1):
        if tc == 0:
            kb.emit("vector", lambda v: v.memset(ln1F3[:, :, 0:8], 0.0),
                    wait=((sG, g_tab),), inc=((sV, sV.tick() and 1),))
        else:
            kb.emit("vector", lambda v: v.tensor_copy(ln1F3[:, :, 0:8],
                                                      ln1F3[:, :, TT:TT + 8]),
                    inc=((sV, sV.tick() and 1),))
        v_lead = sV.n
        t_tr_last = 0
        pr_evacs = [sV.n, sV.n]
        for s_ in range(SUB):
            kb.emit("scalar", lambda sc, s_=s_: sc.activation(
                out=etok[:], in_=x4n[:, s_, :], func=AF.Identity,
                scale=lncol2[:, s_:s_ + 1], bias=lncol2[:, 4 + s_:5 + s_]),
                wait=((sV, v_stats_ln1), (sT, t_tr_last)),
                inc=((sS, sS.tick() and 1),))
            s_ln1 = sS.n
            for c in range(DH):
                k = s_ * DH + c
                bank = pC if (c % 2 == 0) else pD
                bw_ = max(pr_evacs[k % 2], v_lead, v_tab)
                kb.emit("tensor", lambda t, c=c, bank=bank: t.transpose(
                    bank[:, 0:P], etok[:, c * P:(c + 1) * P], idf[:]),
                    wait=((sS, s_ln1), (sV, bw_)),
                    inc=((sT, sT.tick() and 1),))
                t_tr = sT.n
                kb.emit("vector", lambda v, c=c, s_=s_, bank=bank: v.tensor_copy(
                    ln1F3[:, c, 8 + s_ * P:8 + (s_ + 1) * P], bank[:, 0:P]),
                    wait=((sT, t_tr),), inc=((sV, sV.tick() and 1),))
                pr_evacs[k % 2] = sV.n
            t_tr_last = sT.n
        return sV.n

    v_ln1f_next = emit_prologue(0, v_ln1stats)

    for tc in range(NCH):
        v_ln1f = v_ln1f_next

        def conv_emit(v):
            inst = None
            for c in range(DH):
                v.tensor_scalar(out=ut3[:, c, :], in0=ln1F3[:, c, 5:5 + TT],
                                scalar1=cw[:, c * DCONV:c * DCONV + 1],
                                scalar2=None, op0=ALU.mult)
                for j in (1, 2, 3):
                    inst = v.scalar_tensor_tensor(
                        out=ut3[:, c, :], in0=ln1F3[:, c, 5 + j:5 + j + TT],
                        scalar=cw[:, c * DCONV + j:c * DCONV + j + 1],
                        in1=ut3[:, c, :], op0=ALU.mult, op1=ALU.add)
            return inst
        kb.emit("vector", conv_emit,
                wait=((sV, v_ln1f), (sS, sS.n)), inc=((sV, sV.tick() and 1),))
        v_conv = sV.n

        def conv_silu(sc):
            inst = None
            for c in range(DH):
                inst = sc.activation(out=ut3[:, c, :], in_=ut3[:, c, :],
                                     func=AF.Silu, bias=cb[:, c:c + 1])
            return inst
        kb.emit("scalar", conv_silu, wait=((sV, v_conv),), inc=((sS, sS.tick() and 1),))
        s_u = sS.n
        if debug_scratch and tc == 1:
            spu = dma(sX2S, "gpsimd", dbg_u, ut[:], wait=((sS, s_u),))

        mm_group([(pbt[:, 0:TT], bpT3[:, c, :], ut3[:, c, :]) for c in range(DH)],
                 wait=((sS, s_u), (sPRM, PRM_ALL), (sS, sS.n)), inc=((sT, sT.tick() and 1),))
        t_bt = sT.n
        mm_group([(pbt[:, TT:2 * TT], cpT3[:, c, :], ut3[:, c, :]) for c in range(DH)],
                 wait=(), inc=((sT, sT.tick() and 1),))
        t_ct = sT.n
        kb.emit("scalar", lambda sc: sc.activation(out=btT[:], in_=pbt[:, 0:TT],
                                                   func=AF.Identity, bias=bpb[:]),
                wait=((sT, t_bt),), inc=((sS, sS.tick() and 1),))
        kb.emit("scalar", lambda sc: sc.activation(out=ctT[:], in_=pbt[:, TT:2 * TT],
                                                   func=AF.Identity, bias=cpb[:]),
                wait=((sT, t_ct),), inc=((sS, sS.tick() and 1),))
        s_btct = sS.n
        if debug_scratch and tc == 1:
            dma(sX2S, "gpsimd", dbg_bt[0:NST, 0:TT], btT[:], wait=((sS, s_btct),))
            dma(sX2S, "gpsimd", dbg_bt[0:NST, TT:2 * TT], ctT[:], wait=((sS, s_btct),))

        kb.emit("vector", lambda v: v.tensor_tensor(
            yy3, ut3, dD[:, :, None].broadcast_to([P, DH, TT]), ALU.mult),
            wait=((sS, s_u),), inc=((sV, sV.tick() and 1),))

        s_gate_evacs = []
        for n in range(NST):
            pb_ = pA if (n % 2 == 0) else pB
            pc_ = pC if (n % 2 == 0) else pD
            if n < 2:
                bfree = max(v_oneh, v_ln1f)
            else:
                bfree = max(v_oneh, scan_ticks[-2])
            kb.emit("tensor", lambda t, n=n, pb_=pb_: t.matmul(
                pb_[:], oneh[:, n * P:(n + 1) * P], btT[:], start=True, stop=True),
                wait=((sS, s_btct), (sV, bfree), (sS, s_aexp)),
                inc=((sT, sT.tick() and 1),))
            t_b = sT.n
            kb.emit("tensor", lambda t, n=n, pc_=pc_: t.matmul(
                pc_[:], oneh[:, n * P:(n + 1) * P], ctT[:], start=True, stop=True),
                inc=((sT, sT.tick() and 1),))
            t_c = sT.n
            if 1 <= n <= DH:
                g = n - 1
                pg = ptr0 if (g % 2 == 0) else ptr1
                bank_free = () if g < 2 else ((sS, s_gate_evacs[g - 2]),)
                mm_group([(pg[:], gwT[:, c * D + g * P:c * D + (g + 1) * P],
                           ln1F3[:, c, 8:8 + TT]) for c in range(DH)],
                         wait=((sV, v_ln1f), (sPRM, PRM_ALL)) + bank_free,
                         inc=((sT, sT.tick() and 1),))
                t_g = sT.n
                kb.emit("scalar", lambda sc, g=g, pg=pg: sc.activation(
                    out=gzF3[:, g, :], in_=pg[:], func=AF.Silu, bias=gbv[:, g:g + 1]),
                    wait=((sT, t_g),), inc=((sS, sS.tick() and 1),))
                s_gate_evacs.append(sS.n)

            if gsl[0] == 0:
                g_yb[0] = sG.n
            base = gsl[0]
            sgate = stats_gate[0] if n == 0 else 0
            slice_ticks = [sV.tick() for _ in range(DH)]
            slots = [(base + c) % 6 for c in range(DH)]
            rws = [g_yb[0] + (base + c - 6) + 1 if base + c >= 6 else 0
                   for c in range(DH)]

            def scan_n(v, n=n, pb_=pb_, pc_=pc_, t_b=t_b, t_c=t_c,
                       slots=slots, rws=rws, sgate=sgate):
                v.wait_ge(sT.h, t_b)
                v.tensor_tensor(ub3, ut3, pb_[:, None, :].broadcast_to([P, DH, TT]),
                                ALU.mult)
                for c in range(DH):
                    v.tensor_tensor_scan(
                        out=hh3[:, c, :],
                        data0=aexp[:, c * NST + n:c * NST + n + 1].broadcast_to([P, TT]),
                        data1=ub3[:, c, :],
                        initial=hstate[:, n * DH + c:n * DH + c + 1],
                        op0=ALU.mult, op1=ALU.add)
                v.wait_ge(sT.h, t_c)
                if sgate:
                    v.wait_ge(sS.h, sgate)
                for c in range(DH):
                    if rws[c]:
                        v.wait_ge(sG.h, rws[c])
                    v.tensor_tensor(cmb3[:, slots[c], :], hh3[:, c, :], pc_[:],
                                    ALU.mult).then_inc(sV.h, 1)
                return v.tensor_copy(
                    hstate[:].rearrange("p (n h) -> p n h", n=NST)[:, n, :],
                    hh3[:, :, TT - 1])
            kb.emit("vector", scan_n, inc=((sV, sV.tick() and 1),))
            scan_ticks.append(sV.n)
            for c in range(DH):
                kb.emit("gpsimd", lambda g, c=c, sl=slots[c]: g.tensor_tensor(
                    yy3[:, c, :], yy3[:, c, :], cmb3[:, sl, :], ALU.add),
                    wait=((sV, slice_ticks[c]),), inc=((sG, sG.tick() and 1),))
                assert sG.n == g_yb[0] + base + c + 1
            gsl[0] = base + DH
        v_scan = sV.n
        g_yacc = sG.n
        dbg_w_y = 0
        if debug_scratch and tc == 1:
            dbg_w_y = dma(sX2S, "gpsimd", dbg_y, yy[:], wait=((sV, v_scan),))

        s_gate = s_gate_evacs[-1]
        dbg_w_g = 0
        if debug_scratch and tc == 1:
            dbg_w_g = dma(sX2S, "gpsimd", dbg_g, gzF[:], wait=((sS, s_gate),))

        kb.emit("vector", lambda v: v.tensor_mul(yy[:], yy[:], gzF[:]),
                wait=((sS, s_gate), (sX2S, dbg_w_y), (sG, g_yacc)),
                inc=((sV, sV.tick() and 1),))
        v_yg = sV.n

        z_evacs = []
        for g in range(DH):
            po = pC if (g % 2 == 0) else pD
            bank_free = ((sV, z_evacs[g - 2]),) if g >= 2 else ()
            mm_group([(po[:], owT[:, c * D + g * P:c * D + (g + 1) * P],
                       yy3[:, c, :]) for c in range(DH)],
                     wait=((sV, v_yg),) + bank_free, inc=((sT, sT.tick() and 1),))
            t_o = sT.n

            def z_evac(v, g=g, po=po, t_o=t_o, dbg_w_g=dbg_w_g):
                v.wait_ge(sT.h, t_o)
                if dbg_w_g:
                    v.wait_ge(sX2S.h, dbg_w_g)
                return v.scalar_tensor_tensor(out=gzF3[:, g, :], in0=po[:],
                                              scalar=obv[:, g:g + 1],
                                              in1=ln1F3[:, g, 8:8 + TT],
                                              op0=ALU.add, op1=ALU.add)
            kb.emit("vector", z_evac, inc=((sV, sV.tick() and 1),))
            z_evacs.append(sV.n)
        v_z = sV.n
        if debug_scratch and tc == 1:
            dbg_w_z = dma(sX2S, "gpsimd", dbg_z, gzF[:], wait=((sV, v_z),))
            sp_x2_last = max(sp_x2_last, dbg_w_z)

        if tc + 1 < NCH:
            v_ln1stats = ln1_stats(tc + 1, ((sV, v_yg),))
        emit_stats_boundary(tc, g_yacc)

        zt4 = hh[:].rearrange("p (s d) -> p s d", s=SUB)
        et4 = ub[:].rearrange("p (s d) -> p s d", s=SUB)
        xr4 = yy[:].rearrange("p (s d) -> p s d", s=SUB)
        qt4 = yy[:].bitcast(BF16)[:, 0:SUB * D].rearrange("p (s d) -> p s d", s=SUB)
        rows = slice(tc * TT, (tc + 1) * TT)
        ep_evacs = [sS.n, sS.n]
        for s_ in range(SUB):
            for c in range(DH):
                k = s_ * DH + c
                bank = ptr0 if (c % 2 == 0) else ptr1
                bw_ = ep_evacs[k % 2]
                kb.emit("tensor", lambda t, c=c, s_=s_, bank=bank: t.transpose(
                    bank[:, 0:P], gzF3[:, c, s_ * P:(s_ + 1) * P], idf[:]),
                    wait=((sV, v_z), (sS, bw_)), inc=((sT, sT.tick() and 1),))
                t_tr = sT.n
                kb.emit("scalar", lambda sc, c=c, s_=s_, bank=bank: sc.activation(
                    out=hh[:, s_ * D + c * P:s_ * D + (c + 1) * P],
                    in_=bank[:, 0:P], func=AF.Copy),
                    wait=((sT, t_tr), (sX2S, sp_x2_last), (sV, sV.n)),
                    inc=((sS, sS.tick() and 1),))
                ep_evacs[k % 2] = sS.n
        if tc + 1 < NCH:
            v_ln1f_next = emit_prologue(tc + 1, v_ln1stats)

        for s_ in range(SUB):
            kb.emit("scalar", lambda sc, s_=s_: sc.activation(
                out=sqs[:], in_=zt4[:, s_, :], func=AF.Square,
                accum_out=lncol[:, s_:s_ + 1]), inc=((sS, sS.tick() and 1),))
        v_cols = ln_cols4(zt4, sS.n)
        for s_ in range(SUB):
            kb.emit("scalar", lambda sc, s_=s_: sc.activation(
                out=et4[:, s_, :], in_=zt4[:, s_, :], func=AF.Identity,
                scale=lncol[:, s_:s_ + 1], bias=lncol[:, 4 + s_:5 + s_]),
                wait=((sV, v_cols),) if s_ == 0 else (), inc=((sS, sS.tick() and 1),))
        s_innorm = sS.n

        def aff4(v, s_innorm=s_innorm):
            v.wait_ge(sS.h, s_innorm)
            v.tensor_tensor(et4, et4, lng_bc[:, None, :].broadcast_to([P, SUB, D]),
                            ALU.mult)
            return v.tensor_tensor(et4, et4,
                                   lnb_bc[:, None, :].broadcast_to([P, SUB, D]),
                                   ALU.add)
        kb.emit("vector", aff4, inc=((sV, sV.tick() and 1),))
        for s_ in range(SUB):
            kb.emit("scalar", lambda sc, s_=s_: sc.activation(
                out=sqs[:], in_=et4[:, s_, :], func=AF.Square,
                accum_out=lncol[:, s_:s_ + 1]),
                wait=((sV, sV.n),) if s_ == 0 else (), inc=((sS, sS.tick() and 1),))
        v_cols = ln_cols4(et4, sS.n)
        for s_ in range(SUB):
            kb.emit("scalar", lambda sc, s_=s_: sc.activation(
                out=zt4[:, s_, :], in_=et4[:, s_, :], func=AF.Identity,
                scale=lncol[:, s_:s_ + 1], bias=lncol[:, 4 + s_:5 + s_]),
                wait=((sV, v_cols),) if s_ == 0 else (), inc=((sS, sS.tick() and 1),))
        s_onorm = sS.n
        w_xr = dma(sXB, "sync", xr4,
                   x_h[rows, :].rearrange("(s p) d -> p s d", p=P),
                   wait=((sV, v_z),))
        kb.emit("vector", lambda v: v.tensor_add(hh[:], hh[:], yy[:]),
                wait=((sS, s_onorm), (sXB, w_xr)), inc=((sV, sV.tick() and 1),))
        v_x2 = sV.n
        sp_x2_last = dma(sX2S, "gpsimd",
                         x2_d[rows, :].rearrange("(s p) d -> p s d", p=P),
                         zt4, wait=((sV, v_x2),))
        for s_ in range(SUB):
            kb.emit("scalar", lambda sc, s_=s_: sc.activation(
                out=sqs[:], in_=zt4[:, s_, :], func=AF.Square,
                accum_out=lncol[:, s_:s_ + 1]),
                wait=((sV, v_x2),) if s_ == 0 else (), inc=((sS, sS.tick() and 1),))
        v_cols = ln_cols4(zt4, sS.n)
        for s_ in range(SUB):
            kb.emit("scalar", lambda sc, s_=s_: sc.activation(
                out=et4[:, s_, :], in_=zt4[:, s_, :], func=AF.Identity,
                scale=lncol[:, s_:s_ + 1], bias=lncol[:, 4 + s_:5 + s_]),
                wait=((sV, v_cols),) if s_ == 0 else (), inc=((sS, sS.tick() and 1),))
        s_ln3 = sS.n

        def q_cols4(v, s_ln3=s_ln3, tc=tc):
            v.wait_ge(sS.h, s_ln3)
            v.tensor_reduce(lncol[:, 0:4], et4, axis=AX.X, op=ALU.max,
                            apply_absolute_value=True)
            v.drain()
            v.tensor_scalar(out=lncol[:, 0:4], in0=lncol[:, 0:4],
                            scalar1=1e-5, scalar2=None, op0=ALU.max)
            v.drain()
            v.reciprocal(lncol[:, 4:8], lncol[:, 0:4])
            v.drain()
            v.tensor_scalar(out=lncol[:, 8:12], in0=lncol[:, 4:8],
                            scalar1=127.0, scalar2=None, op0=ALU.mult)
            return v.tensor_scalar(out=dsx[:, tc * SUB:(tc + 1) * SUB],
                                   in0=lncol[:, 0:4], scalar1=1.0 / 127.0,
                                   scalar2=None, op0=ALU.mult)
        kb.emit("vector", q_cols4, inc=((sV, sV.tick() and 1),))
        v_qc = sV.n
        for s_ in range(SUB):
            kb.emit("scalar", lambda sc, s_=s_: sc.activation(
                out=zt4[:, s_, :], in_=et4[:, s_, :], func=AF.Copy,
                scale=lncol[:, 8 + s_:9 + s_], bias=RB),
                wait=(((sV, v_qc), (sX2S, sp_x2_last)) if s_ == 0 else ()),
                inc=((sS, sS.tick() and 1),))
            s_q1 = sS.n

            def q_fin(v, s_=s_, s_q1=s_q1):
                v.wait_ge(sS.h, s_q1)
                v.tensor_copy(h16[:], zt4[:, s_, :])
                return v.tensor_scalar(out=qt4[:, s_, :], in0=h16[:], scalar1=-RB,
                                       scalar2=None, op0=ALU.add)
            kb.emit("vector", q_fin, inc=((sV, sV.tick() and 1),))
        v_q = sV.n
        sp_xq_last = dma(sXQS, "gpsimd",
                         xq_d[rows, :].rearrange("(s p) d -> p s d", p=P),
                         qt4, wait=((sV, v_q),))

    p2_V, p2_S, p2_T = sV.n, sS.n, sT.n
    p2_X2S, p2_XQS = sX2S.n, sXQS.n

    s_stats = stat_abs_ticks[-1]
    for mi in range(3):
        kb.emit("vector",
                lambda v, mi=mi: v.reduce_sum(colt[:, mi:mi + 1],
                                              wstat[:, mi * 32:(mi + 1) * 32],
                                              axis=AX.X),
                wait=((sS, s_stats),), inc=((sV, sV.tick() and 1),))
    v_sr = sV.n
    kb.emit("tensor", lambda t: t.matmul(pbt[0:1, 0:3], onec[:], colt[:, 0:3],
                                         start=True, stop=True),
            wait=((sV, v_sr),), inc=((sT, sT.tick() and 1),))
    t_s1 = sT.n
    kb.emit("vector", lambda v: v.tensor_copy(colt[0:1, 4:7], pbt[0:1, 0:3]),
            wait=((sT, t_s1),), inc=((sV, sV.tick() and 1),))
    v_s2 = sV.n
    kb.emit("tensor", lambda t: t.matmul(ptr0[:, 0:3], onerow[:],
                                         colt[0:1, 4:7], start=True, stop=True),
            wait=((sV, v_s2),), inc=((sT, sT.tick() and 1),))
    t_s2 = sT.n

    def stat_fin(v):
        v.tensor_scalar(out=colt[:, 0:3], in0=ptr0[:, 0:3],
                        scalar1=1.0 / (D * INNER), scalar2=1e-5,
                        op0=ALU.mult, op1=ALU.max)
        v.drain()
        v.tensor_copy(cwg[:], colt[:, 0:1])
        v.tensor_copy(cwu[:], colt[:, 1:2])
        v.tensor_copy(cwd[:], colt[:, 2:3])
        v.drain()
        v.reciprocal(swg[:], cwg[:])
        v.reciprocal(swu[:], cwu[:])
        return v.reciprocal(swd[:], cwd[:])
    kb.emit("vector", stat_fin, wait=((sT, t_s2),), inc=((sV, sV.tick() and 1),))
    v_stats = sV.n

    p2.close()
    p3 = contextlib.ExitStack()
    sb3 = mk(p3)
    wstg = [sb3("wstg0", [P, INNER]), sb3("wstg1", [P, INNER])]
    wgq = sb3("wgq", [P, DH * INNER], BF16)
    wuq = sb3("wuq", [P, DH * INNER], BF16)
    f16b = sb3("f16b", [P, INNER], F16)
    gu = sb3("gu", [P, INNER])
    gu1 = sb3("gu1", [P, TT])
    guq = sb3("guq", [P, INNER], BF16)
    xqc = sb3("xqc", [P, DH * P], BF16)

    PHW = ((sV, p2_V), (sS, p2_S), (sT, p2_T), (sX2S, p2_X2S), (sXQS, p2_XQS))

    s_wprev = 0
    first = True
    for (wdst, src, scol) in [(wgq, wgT_h, swg), (wuq, wuT_h, swu)]:
        for c in range(DH):
            stg = wstg[c % 2]
            sws = sW0 if c % 2 == 0 else sW1
            w_l = dma(sws, "sync", stg[:], src[c * P:(c + 1) * P, :],
                      wait=(PHW if first else ((sS, s_wprev),)))
            first = False
            kb.emit("scalar", lambda sc, stg=stg, scol=scol: sc.activation(
                out=gu[:], in_=stg[:], func=AF.Identity, scale=scol[:]),
                wait=((sws, w_l), (sV, max(v_stats, sV.n))),
                inc=((sS, sS.tick() and 1),))
            s_wq = sS.n
            s_wprev = s_wq

            def w_fin(v, wdst=wdst, c=c):
                v.tensor_scalar(out=f16b[:], in0=gu[:], scalar1=-0.5, scalar2=None,
                                op0=ALU.is_lt)
                return v.scalar_tensor_tensor(
                    out=wdst[:, c * INNER:(c + 1) * INNER], in0=gu[:], scalar=0.5,
                    in1=f16b[:], op0=ALU.is_gt, op1=ALU.subtract)
            kb.emit("vector", w_fin, wait=((sS, s_wq),), inc=((sV, sV.tick() and 1),))
    v_wq = sV.n

    if debug_scratch:
        dma(sX2S, "gpsimd", dbg_wgq, wgq[:], wait=((sV, v_wq),))
        def dbg_cp(v):
            v.tensor_copy(sqs[:, 0:24], wstat[:])
            v.tensor_copy(sqs[:, 24:32], colt[:])
            v.tensor_copy(sqs[:, 32:33], swg[:])
            v.tensor_copy(sqs[:, 33:34], swu[:])
            v.tensor_copy(sqs[:, 34:35], swd[:])
            v.tensor_copy(sqs[:, 35:36], cwg[:])
            v.tensor_copy(sqs[:, 36:37], cwu[:])
            return v.tensor_copy(sqs[:, 37:38], cwd[:])
        kb.emit("vector", dbg_cp, wait=((sV, v_wq),), inc=((sV, sV.tick() and 1),))
        dma(sX2S, "gpsimd", dbg_cols, sqs[:, 0:64], wait=((sV, sV.n),))
    xstg = sb3("xstg", [P, D], BF16)
    idb = sb3("idb", [P, P], BF16)
    kb.emit("vector", lambda v: v.tensor_copy(idb[:], idf[:]),
            wait=((sV, v_wq),), inc=((sV, sV.tick() and 1),))
    v_idb = sV.n
    sp_gq = 0
    s_xq_evacs = []
    for i in range(NTOK):
        trow = i * P
        wv0 = dma(sXQ, "sync", xstg[:], xq_d[trow:trow + P, :],
                  wait=(((sS, s_xq_evacs[-1] if s_xq_evacs else 0),
                         (sXQS, p2_XQS))))
        wv = wv0
        for c in range(DH):
            pt = ptr0 if c % 2 == 0 else ptr1
            bankw = ((sS, s_xq_evacs[-2],) if len(s_xq_evacs) >= 2 else (sS, 0))
            kb.emit("tensor", lambda t, c=c, pt=pt: t.transpose(
                pt[:, 0:64].bitcast(BF16), xstg[:, c * P:(c + 1) * P], idb[:]),
                wait=((sXQ, wv0), (sV, v_idb), bankw, (sT, sT.n)),
                inc=((sT, sT.tick() and 1),))
            t_tr = sT.n
            kb.emit("scalar", lambda sc, c=c, pt=pt: sc.activation(
                out=xqc[:, c * P:(c + 1) * P], in_=pt[:, 0:64].bitcast(BF16),
                func=AF.Copy), wait=((sT, t_tr),), inc=((sS, sS.tick() and 1),))
            s_xq_evacs.append(sS.n)

        def mk_desc(v, i=i):
            v.tensor_mul(lncol[:, 0:1], cwg[:], dsx[:, i:i + 1])
            v.tensor_mul(lncol[:, 1:2], cwu[:], dsx[:, i:i + 1])
            return v.drain()
        kb.emit("vector", mk_desc, wait=((sV, v_wq), (sS, sS.n)),
                inc=((sV, sV.tick() and 1),))
        v_ds = sV.n
        s_up_evacs = []
        for f in range(DH):
            pg = pA if (f % 2 == 0) else pB
            pu = pC if (f % 2 == 0) else pD
            bank_free = ((sS, s_up_evacs[f - 2]),) if f >= 2 else ()
            mm_group([(pg[:], xqc[:, c * P:(c + 1) * P],
                       wgq[:, c * INNER + f * TT:c * INNER + (f + 1) * TT])
                      for c in range(DH)],
                     wait=((sS, s_xq_evacs[-1]), (sV, v_ds)) + bank_free,
                     inc=((sT, sT.tick() and 1),))
            t_g = sT.n
            mm_group([(pu[:], xqc[:, c * P:(c + 1) * P],
                       wuq[:, c * INNER + f * TT:c * INNER + (f + 1) * TT])
                      for c in range(DH)],
                     wait=(), inc=((sT, sT.tick() and 1),))
            t_u = sT.n
            kb.emit("scalar", lambda sc, pg=pg: sc.activation(
                out=gu1[:], in_=pg[:], func=AF.Silu, scale=lncol[:, 0:1]),
                wait=((sT, t_g), (sV, sV.n)), inc=((sS, sS.tick() and 1),))
            kb.emit("scalar", lambda sc, pu=pu: sc.activation(
                out=sqs[:, 0:TT], in_=pu[:], func=AF.Copy, scale=lncol[:, 1:2]),
                wait=((sT, t_u), (sV, sV.n)), inc=((sS, sS.tick() and 1),))
            s_up = sS.n
            s_up_evacs.append(s_up)

            def gu_mul(v, f=f, s_up=s_up):
                v.wait_ge(sS.h, s_up)
                v.tensor_tensor(gu[:, f * TT:(f + 1) * TT], gu1[:], sqs[:, 0:TT],
                                ALU.mult)
                v.tensor_reduce(lncol[:, 2:3], gu[:, f * TT:(f + 1) * TT],
                                axis=AX.X, op=ALU.max, apply_absolute_value=True)
                v.drain()
                if f == 0:
                    return v.tensor_copy(lncol[:, 3:4], lncol[:, 2:3])
                return v.tensor_max(lncol[:, 3:4], lncol[:, 3:4], lncol[:, 2:3])
            kb.emit("vector", gu_mul, inc=((sV, sV.tick() and 1),))

        def gq_cols(v, i=i):
            v.tensor_scalar(out=lncol[:, 3:4], in0=lncol[:, 3:4], scalar1=1e-5,
                            scalar2=None, op0=ALU.max)
            v.drain()
            v.reciprocal(lncol[:, 4:5], lncol[:, 3:4])
            v.drain()
            v.tensor_scalar(out=lncol[:, 5:6], in0=lncol[:, 4:5], scalar1=127.0,
                            scalar2=None, op0=ALU.mult)
            return v.tensor_scalar(out=dsg[:, i:i + 1], in0=lncol[:, 3:4],
                                   scalar1=1.0 / 127.0, scalar2=None, op0=ALU.mult)
        kb.emit("vector", gq_cols, wait=((sS, sS.n),),
                inc=((sV, sV.tick() and 1),))
        v_gqc = sV.n
        if debug_scratch and i == 0:
            dma(sX2S, "gpsimd", dbg_xqc, xqc[:], wait=((sV, v_gqc),))
            dma(sX2S, "gpsimd", dbg_gu, gu[:], wait=((sV, v_gqc),))
        kb.emit("scalar", lambda sc: sc.activation(
            out=f16b[:], in_=gu[:], func=AF.Copy, scale=lncol[:, 5:6], bias=RB),
            wait=((sV, v_gqc),), inc=((sS, sS.tick() and 1),))
        s_gq = sS.n

        def g_fin(v, s_gq=s_gq, sp_gq=sp_gq):
            v.wait_ge(sS.h, s_gq)
            v.wait_ge(sGQS.h, sp_gq)
            return v.tensor_scalar(out=guq[:], in0=f16b[:], scalar1=-RB,
                                   scalar2=None, op0=ALU.add)
        kb.emit("vector", g_fin, inc=((sV, sV.tick() and 1),))
        v_gq = sV.n
        sp_gq = dma(sGQS, "gpsimd", gq_d[trow:trow + P, :], guq[:],
                    wait=((sV, v_gq),))
    p3_V, p3_S, p3_T, p3_GQS = sV.n, sS.n, sT.n, sGQS.n

    p3.close()
    p4 = contextlib.ExitStack()
    sb4 = mk(p4)
    wdq = sb4("wdq", [P, IH * D], BF16)
    gqT = sb4("gqT", [P, IH * P], BF16)
    gqT2 = sb4("gqT2", [P, IH * P], BF16)
    x2b_ = [sb4("x2b", [P, SUB * D]), sb4("x2c", [P, SUB * D])]
    o2b_ = [sb4("o2b", [P, SUB * D]), sb4("o2c", [P, SUB * D])]
    gstg_ = [sb4("gstg", [P, INNER], BF16), sb4("gsth", [P, INNER], BF16)]
    idb4 = sb4("idb4", [P, P], BF16)
    wd16 = sb4("wd16", [P, D], F16)
    wdf = sb4("wdf", [P, D])
    wst4 = [sb4("wst4a", [P, D]), sb4("wst4b", [P, D])]

    PHW3 = ((sV, p3_V), (sS, p3_S), (sT, p3_T), (sGQS, p3_GQS))
    kb.emit("vector", lambda v: v.tensor_copy(idb4[:], idf[:]),
            wait=((sV, sV.n),), inc=((sV, sV.tick() and 1),))
    v_idb4 = sV.n
    s_gq_evacs = []
    v_wd_ticks = []
    s_wprev = 0
    first = True
    for c in range(IH):
        stg = wst4[c % 2]
        sws = sW0 if c % 2 == 0 else sW1
        w_l = dma(sws, "sync", stg[:, 0:D], wdT_h[c * P:(c + 1) * P, :],
                  wait=(PHW3 if first else ((sS, s_wprev),)))
        first = False
        kb.emit("scalar", lambda sc, stg=stg: sc.activation(
            out=wdf[:], in_=stg[:, 0:D], func=AF.Identity, scale=swd[:]),
            wait=((sws, w_l), (sV, sV.n)), inc=((sS, sS.tick() and 1),))
        s_wq = sS.n
        s_wprev = s_wq

        def wd_fin(v, c=c):
            v.tensor_scalar(out=wd16[:], in0=wdf[:], scalar1=-0.5, scalar2=None,
                            op0=ALU.is_lt)
            return v.scalar_tensor_tensor(
                out=wdq[:, c * D:(c + 1) * D], in0=wdf[:], scalar=0.5,
                in1=wd16[:], op0=ALU.is_gt, op1=ALU.subtract)
        kb.emit("vector", wd_fin, wait=((sS, s_wq),), inc=((sV, sV.tick() and 1),))
        v_wd_ticks.append(sV.n)
    v_wd = sV.n
    if debug_scratch:
        dma(sX2S, "gpsimd", dbg_wdq, wdq[:], wait=((sV, v_wd),))

    v_o_dones = [0, 0]
    dm_outs = [0, 0]
    tr_lasts = [0, 0]
    for gi in range(NTOK // SUB):
        x2b = x2b_[gi % 2]
        o2b = o2b_[gi % 2]
        o2b4 = o2b[:].rearrange("p (s d) -> p s d", s=SUB)
        x2b4 = x2b[:].rearrange("p (s d) -> p s d", s=SUB)
        rows = slice(gi * SUB * P, (gi + 1) * SUB * P)
        w_x2 = dma(sX2, "sync", x2b4,
                   x2_d[rows, :].rearrange("(s p) d -> p s d", p=P),
                   wait=((sV, v_o_dones[gi % 2]), (sX2S, p2_X2S)))
        def mk_dd(v, gi=gi):
            v.wait_ge(sV.h, v_wd)
            v.tensor_tensor(colt[:, 0:SUB], cwd[:].broadcast_to([P, SUB]),
                            dsg[:, gi * SUB:(gi + 1) * SUB], ALU.mult)
            return v.drain()
        kb.emit("vector", mk_dd, wait=((sS, sS.n),), inc=((sV, sV.tick() and 1),))
        v_dd = sV.n
        for il in range(SUB):
            i = gi * SUB + il
            trow = i * P
            gq_t = gqT if il % 2 == 0 else gqT2
            gstg = gstg_[i % 2]
            wvg = dma(sGQT, "sync", gstg[:], gq_d[trow:trow + P, :],
                      wait=(((sT, tr_lasts[i % 2]), (sGQS, p3_GQS))))
            for c in range(IH):
                pt = pC if c % 2 == 0 else pD
                bankw = ((sS, s_gq_evacs[-2]) if len(s_gq_evacs) >= 2 else (sS, 0))
                kb.emit("tensor", lambda t, c=c, pt=pt, gq_t=gq_t, gstg=gstg:
                    t.transpose(
                        pt[:, 0:64].bitcast(BF16), gstg[:, c * P:(c + 1) * P],
                        idb4[:]),
                    wait=((sGQT, wvg), (sV, v_idb4), bankw, (sT, sT.n)),
                    inc=((sT, sT.tick() and 1),))
                t_tr = sT.n
                kb.emit("scalar", lambda sc, c=c, pt=pt, gq_t=gq_t: sc.activation(
                    out=gq_t[:, c * P:(c + 1) * P], in_=pt[:, 0:64].bitcast(BF16),
                    func=AF.Copy), wait=((sT, t_tr),), inc=((sS, sS.tick() and 1),))
                s_gq_evacs.append(sS.n)
            tr_lasts[i % 2] = sT.n
            for g2 in range(2):
                pq = pA if (g2 % 2 == 0) else pB
                mmw = None
                wdw = ((sV, v_wd),) if (gi == 0 and il == 0) else ()
                mm_group([(pq[:], gq_t[:, c * P:(c + 1) * P],
                           wdq[:, c * D + g2 * TT:c * D + (g2 + 1) * TT])
                          for c in range(IH)],
                         wait=((sS, s_gq_evacs[-1]),) + wdw,
                         inc=((sT, sT.tick() and 1),), mm_waits=mmw)
                t_d = sT.n
                kb.emit("scalar", lambda sc, il=il, g2=g2, pq=pq, o2b=o2b:
                    sc.activation(
                        out=o2b[:, il * D + g2 * TT:il * D + (g2 + 1) * TT],
                        in_=pq[:], func=AF.Copy, scale=colt[:, il:il + 1]),
                    wait=((sT, t_d), (sV, v_dd), (sOUT, dm_outs[gi % 2])),
                    inc=((sS, sS.tick() and 1),))
        s_o2 = sS.n
        for s_ in range(SUB):
            kb.emit("scalar", lambda sc, s_=s_, o2b4=o2b4: sc.activation(
                out=sqs[:], in_=o2b4[:, s_, :], func=AF.Square,
                accum_out=lncol[:, s_:s_ + 1]),
                wait=((sV, sV.n),) if s_ == 0 else (), inc=((sS, sS.tick() and 1),))
        v_cols = ln_cols4(o2b4, sS.n)
        for s_ in range(SUB):
            kb.emit("scalar", lambda sc, s_=s_, o2b4=o2b4: sc.activation(
                out=o2b4[:, s_, :], in_=o2b4[:, s_, :], func=AF.Identity,
                scale=lncol[:, s_:s_ + 1], bias=lncol[:, 4 + s_:5 + s_]),
                wait=((sV, v_cols),) if s_ == 0 else (), inc=((sS, sS.tick() and 1),))
        s_ln4 = sS.n
        kb.emit("vector", lambda v, o2b=o2b, x2b=x2b: v.tensor_add(
            o2b[:], o2b[:], x2b[:]),
            wait=((sS, s_ln4), (sX2, w_x2)), inc=((sV, sV.tick() and 1),))
        v_o_dones[gi % 2] = sV.n
        dm_outs[gi % 2] = dma(sOUT, "gpsimd",
                              out_h[rows, :].rearrange("(s p) d -> p s d", p=P),
                              o2b4, wait=((sV, sV.n),))

    kb.emit("gpsimd", lambda g: None, wait=((sOUT, sOUT.n),))
    p4.close()

    with top:
        with nc.Block() as block:
            kb.run(block)
    return nc


_NC_CACHE = None


def kernel(**inputs):
    global _NC_CACHE
    f32 = np.float32
    T = lambda a: np.ascontiguousarray(np.asarray(a, dtype=f32).T)
    C = lambda a: np.ascontiguousarray(np.asarray(a, dtype=f32))
    x = C(inputs["L0_attn_x"])
    common = {
        "conv_w": C(np.asarray(inputs["L0_attn_conv_w"], dtype=f32)[:, 0, :]),
        "conv_b": C(inputs["L0_attn_conv_b"]),
        "A": C(inputs["L0_attn_A"]),
        "BpT": T(inputs["L0_attn_Bp_w"]),
        "Bpb": C(inputs["L0_attn_Bp_b"]),
        "CpT": T(inputs["L0_attn_Cp_w"]),
        "Cpb": C(inputs["L0_attn_Cp_b"]),
        "Dd": C(inputs["L0_attn_D"]),
        "gwT": T(inputs["L0_attn_gate_w"]),
        "gb": C(inputs["L0_attn_gate_b"]),
        "owT": T(inputs["L0_attn_out_w"]),
        "ob": C(inputs["L0_attn_out_b"]),
        "lng": C(inputs["L0_attn_ln_g"]),
        "lnb": C(inputs["L0_attn_ln_b"]),
        "wgT": T(inputs["L0_ffn_w_gate"]),
        "wuT": T(inputs["L0_ffn_w_up"]),
        "wdT": T(inputs["L0_ffn_w_down"]),
    }
    in_maps = [dict(common, x=x[c]) for c in range(8)]
    if _NC_CACHE is None:
        _NC_CACHE = build_core()
    res = run_bass_kernel_spmd(_NC_CACHE, in_maps, core_ids=list(range(8)))
    return np.stack([np.asarray(r["out"], dtype=f32) for r in res.results])
